# version 03
# baseline (speedup 1.0000x reference)
import numpy as np
from contextlib import ExitStack

import concourse.bass as bass
import concourse.mybir as mybir
from concourse.bass_utils import run_bass_kernel_spmd

F32 = mybir.dt.float32
BF16 = mybir.dt.bfloat16
ALU = mybir.AluOpType
AF = mybir.ActivationFunctionType
AX = mybir.AxisListType

T, D, I = 2048, 1024, 2816
NT, ND, NI = T // 128, D // 128, I // 128
CH = 64
LN8 = float(np.log(np.float64(8.0)))
R6 = float(np.float32(1.0 / 6.0))
R1024 = float(np.float32(1.0 / 1024.0))
EPS = 1e-5
ICHUNKS = [(k * 512, 512) for k in range(5)] + [(2560, 256)]
ENGS = ("sync", "gpsimd", "tensor", "vector", "scalar")


def V(ap):
    sh = ap.shape
    if len(sh) == 2 and sh[1] > 512:
        n = sh[1]
        z = max(d for d in range(1, 513) if n % d == 0)
        return ap.rearrange("p (a z) -> p a z", z=z)
    return ap


class B:

    def __init__(self, nc, es):
        self.nc = nc
        self.q = {e: [] for e in ENGS}
        names = ("s_ld", "s_g", "s_pe", "s_dve", "s_act", "s_sp", "s_og")
        self.sem = {n: es.enter_context(nc.semaphore(n)) for n in names}
        self.cnt = {n: 0 for n in names}
        self.res = {}

    def on(self, eng, fn, inc=None, inc_n=None):
        if inc is None:
            self.q[eng].append(fn)
            return None
        n = inc_n if inc_n is not None else (16 if eng in ("sync", "gpsimd") else 1)
        self.cnt[inc] += n

        def thunk(e, fn=fn, s=self.sem[inc], n=n):
            fn(e).then_inc(s, n)
        self.q[eng].append(thunk)
        return self.cnt[inc]

    def now(self, eng):
        s = {"sync": "s_ld", "gpsimd": "s_g", "tensor": "s_pe",
             "vector": "s_dve", "scalar": "s_act"}[eng]
        return (s, self.cnt[s])

    def set(self, res, eng=None, at=None):
        self.res[res] = at if at is not None else self.now(eng)

    def wait(self, eng, res):
        if res not in self.res:
            return
        sname, val = self.res[res]
        if val > 0:
            self.q[eng].append(lambda e, s=self.sem[sname], v=val: e.wait_ge(s, v))

    def waits(self, eng, *ress):
        best = {}
        for r in ress:
            if r in self.res:
                sname, val = self.res[r]
                best[sname] = max(best.get(sname, 0), val)
        for sname, val in best.items():
            if val > 0:
                self.q[eng].append(lambda e, s=self.sem[sname], v=val: e.wait_ge(s, v))

    def replay(self, blk):
        for eng in ENGS:
            thunks = self.q[eng]

            def body(e, thunks=thunks):
                for t in thunks:
                    t(e)
            getattr(blk, eng)(body)


def build(stage="full"):
    nc = bass.Bass("TRN2", target_bir_lowering=False, debug=False)

    x_in = nc.dram_tensor("x_in", [T, D], F32, kind="ExternalInput").ap()
    y_out = nc.dram_tensor("y_out", [T, D], F32, kind="ExternalOutput").ap()
    wg_d = nc.dram_tensor("w_gate", [I, D], F32, kind="ExternalInput").ap()
    wu_d = nc.dram_tensor("w_up", [I, D], F32, kind="ExternalInput").ap()
    wd_d = nc.dram_tensor("w_down", [D, I], F32, kind="ExternalInput").ap()
    W_d = {f: nc.dram_tensor(f, [D, D], F32, kind="ExternalInput").ap()
           for f in ("Wq", "Wk", "Wv", "Wo", "Wg")}
    w1_d = nc.dram_tensor("maa_w1", [D, 160], F32, kind="ExternalInput").ap()
    w2_d = nc.dram_tensor("maa_w2", [5, 32, D], F32, kind="ExternalInput").ap()
    dw1_d = nc.dram_tensor("dec_w1", [D, 64], F32, kind="ExternalInput").ap()
    dw2_d = nc.dram_tensor("dec_w2", [64, D], F32, kind="ExternalInput").ap()
    vecs_d = nc.dram_tensor("vecs", [128, 64], F32, kind="ExternalInput").ap()
    ident_d = nc.dram_tensor("ident", [128, 128], F32, kind="ExternalInput").ap()
    mask_d = nc.dram_tensor("mask", [128, CH], F32, kind="ExternalInput").ap()

    wgT_s = nc.dram_tensor("wgT_s", [D, I], BF16, kind="Internal").ap()
    wuT_s = nc.dram_tensor("wuT_s", [D, I], BF16, kind="Internal").ap()
    wdT_s = nc.dram_tensor("wdT_s", [I, D], BF16, kind="Internal").ap()
    xsp_s = nc.dram_tensor("xsp_s", [128, NT * D], F32, kind="Internal").ap()
    gsp_s = nc.dram_tensor("gsp_s", [128, ND * T], BF16, kind="Internal").ap()

    es = ExitStack()

    def sbt(name, p, n, dt=F32):
        return es.enter_context(nc.sbuf_tensor(name, [p, n], dt))

    def pst_(name, p, n, dt=F32):
        return es.enter_context(nc.psum_tensor(name, [p, n], dt))

    AR0 = sbt("AR0", 128, 16384)
    AR1 = sbt("AR1", 128, 8192)
    AR2 = sbt("AR2", 128, 8192)
    AR3 = sbt("AR3", 128, 8192)
    AR4 = sbt("AR4", 128, 4096)

    ident = sbt("identsb", 128, 128, BF16)
    maskt = sbt("masksb", 128, CH)
    vecs = sbt("vecssb", 128, 64)
    w1sb = sbt("w1sb", 128, 8 * 160, BF16)
    w2sb = sbt("w2sb", 32, 5 * 1024, BF16)
    dw1sb = sbt("dw1sb", 128, 8 * 64, BF16)
    dw2sb = sbt("dw2sb", 64, 1024, BF16)
    tanh1 = sbt("tanh1", 64, T, BF16)
    Ssb = sbt("Ssb", 128, 512, BF16)
    Stmp = sbt("Stmp", 128, 512, BF16)
    mixsb = sbt("mixsb", 128, 512, BF16)
    mix2sb = sbt("mix2sb", 32, 512, BF16)
    cbsb = sbt("cbsb", 128, 8 * CH)
    smalls = sbt("smalls", 128, 64)
    sttsb = sbt("sttsb", 128, 512)
    gstage = sbt("gstage", 128, 512, BF16)
    lnmean = sbt("lnmean", 128, NT)
    lnsum2 = sbt("lnsum2", 128, NT)
    lnrstd = sbt("lnrstd", 128, NT)
    lnnmr = sbt("lnnmr", 128, NT)
    onessb = sbt("onessb", 128, 1)
    q_bm = sbt("q_bm", 128, 88)
    q_sc = sbt("q_sc", 128, 88)
    q_rs = sbt("q_rs", 128, 88)
    q_hs = sbt("q_hs", 128, 88, BF16)

    psA = pst_("psA", 128, 512)
    psB = pst_("psB", 128, 512)
    psC = pst_("psC", 128, 1024)
    psD = pst_("psD", 128, 512)
    psE = pst_("psE", 128, 512)
    psT = pst_("psT", 128, 1024, BF16)

    b = B(nc, es)
    cstsb = sbt("cstsb", 128, 2)
    nc.const_aps.register(F32, EPS, cstsb[:, 0:1])
    nc.const_aps.register(F32, -LN8, cstsb[:, 1:2])

    AR0b = AR0[:].bitcast(BF16)
    AR1b = AR1[:].bitcast(BF16)
    AR2b = AR2[:].bitcast(BF16)
    AR3b = AR3[:].bitcast(BF16)
    AR4b = AR4[:].bitcast(BF16)
    xres = AR0[:]
    q_fm = AR0b[:, 0:16384]
    k_fm = AR0b[:, 16384:32768]
    xq_fm = AR1b[:, 0:16384]
    ln1_fm = AR1b[:, 0:16384]
    og2_fm = AR1b[:, 0:8192]
    og1_fm = AR3b[:, 8192:16384]
    attw = AR3b[:, 0:8192]
    v_tm = AR2b[:, 0:8192]
    dec_g = AR2[:, 4096:6144]
    egd_g = AR2[:, 6144:8192]
    wch_g = AR2b[:, 0:4096]
    wch_u = AR2b[:, 4096:8192]
    wch_d = AR2b[:, 8192:12288]
    mixf = AR2b[0:32, 8192:13312]
    ffnsil = AR4[:, 2560:3072]
    expsc = AR4[:, 1024:1536]
    expsc2 = AR4[:, 1536:2048]
    ktm = AR4b[:, 4096:5120]
    qsc = AR4b[:, 5120:5632]
    qss = AR4b[:, 5632:6144]
    ksc = AR4b[:, 6144:6656]
    khc = AR3b[:, 0:512]
    amskA = AR3b[:, 512:1024]
    amskB = AR3b[:, 1024:1536]
    q_y2 = AR2[:, 6144:7552]
    q_y = AR3[:, 0:1408]
    q_a = AR3[:, 1408:2816]
    q_b = AR3[:, 2816:4224]
    q_A = AR3[:, 4224:5632]
    q_sg = AR3b[:, 11264:12672]
    q_Q = AR3b[:, 12672:14080]
    wq_tm = AR3b[:, 14080:16384]
    lnsq = AR4[:, 0:1024]
    ln_t = AR4[:, 1024:2048]
    xrld = AR4[:, 1024:2048]
    hbuf = AR4[:, 2048:2560]
    lnb_t = AR4b[:, 5120:6144]
    hq_tm = AR4b[:, 6144:6656]
    hq_fm = AR4b[:, 6656:7168]
    tstage = AR4b[:, 7168:8192]
    dx_r = AR4b[:, 0:4096]
    xx_r = AR4b[:, 4096:8192]
    xf_r = AR4b[:, 4096:8192]
    gat_r = AR4b[:, 0:2048]

    def vsl(i):
        return vecs[:, i * 8:(i + 1) * 8]

    maa_x = vsl(0)
    maa_of = {"r": vsl(1), "k": vsl(2), "v": vsl(3), "w": vsl(4), "g": vsl(5)}
    tdec, bgv = vsl(6), vsl(7)

    def fm3(ap, c0, n):
        return ap.rearrange("p (j t) -> p j t", j=ND)[:, :, c0:c0 + n]

    b.on("gpsimd", lambda g: g.dma_start(out=ident[:], in_=ident_d), inc="s_g")
    b.on("gpsimd", lambda g: g.dma_start(out=maskt[:], in_=mask_d), inc="s_g")
    b.on("gpsimd", lambda g: g.dma_start(out=vecs[:], in_=vecs_d), inc="s_g")
    b.on("gpsimd", lambda g: g.dma_start(
        out=w1sb[:].rearrange("p (j n) -> p j n", j=ND),
        in_=w1_d.rearrange("(j p) n -> p j n", p=128)), inc="s_g")
    b.on("gpsimd", lambda g: g.dma_start(
        out=w2sb[:].rearrange("p (f d) -> p f d", f=5),
        in_=w2_d.rearrange("f l d -> l f d")), inc="s_g")
    b.on("gpsimd", lambda g: g.dma_start(
        out=dw1sb[:].rearrange("p (j n) -> p j n", j=ND),
        in_=dw1_d.rearrange("(j p) n -> p j n", p=128)), inc="s_g")
    b.on("gpsimd", lambda g: g.dma_start(out=dw2sb[:], in_=dw2_d), inc="s_g")
    b.set("consts", "gpsimd")

    b.on("sync", lambda s: s.dma_start(
        out=xres.rearrange("p (i d) -> p i d", d=D),
        in_=x_in.rearrange("(i p) d -> p i d", p=128)), inc="s_ld")
    b.set("xload", "sync")
    b.on("vector", lambda v: v.memset(onessb[:], 1.0))
    b.on("vector", lambda v: v.memset(cstsb[:, 0:1], EPS))
    b.on("vector", lambda v: v.memset(cstsb[:, 1:2], -LN8))
    b.on("vector", lambda v: v.memset(Ssb[:], 0.0), inc="s_dve")
    b.set("init", "vector")
    b.set("S_ok", "vector")

    qn = [0]

    def emit_quant(x_ap, out_bf, n, ready_res, out_res, free_deps=()):
        p = qn[0] & 1
        qn[0] += 1
        nb = n // 16
        qy = q_y if p == 0 else q_y2
        b.waits("scalar", ready_res, "q_yokA")
        b.on("scalar", lambda s: s.activation(q_a[:, 0:n], x_ap, AF.Abs),
             inc="s_act")
        b.set("q_abs", "scalar")
        b.waits("scalar", "q_poolA")
        b.on("scalar", lambda s: s.activation(q_sg[:, 0:n], x_ap, AF.Sign),
             inc="s_act")
        b.set("q_act", "scalar")
        b.waits("vector", ready_res)
        x3 = x_ap.rearrange("p (z k) -> p z k", k=16)
        b.on("vector", lambda v: v.tensor_reduce(
            q_bm[:, 0:nb], x3, axis=AX.X, op=ALU.max, apply_absolute_value=True),
            inc="s_dve")
        b.set("q_bmr", "vector")
        b.wait("vector", "q_bmr")
        b.on("vector", lambda v: v.tensor_scalar(
            q_sc[:, 0:nb], q_bm[:, 0:nb], R6, 1e-12, ALU.mult, ALU.max),
            inc="s_dve")
        b.set("q_scd", "vector")
        b.wait("vector", "q_scd")
        b.waits("vector", "q_yokA")
        b.on("vector", lambda v: v.reciprocal(q_rs[:, 0:nb], q_sc[:, 0:nb]),
             inc="s_dve")
        b.set("q_rsd", "vector")
        a3 = q_a[:, 0:n].rearrange("p (z k) -> p z k", k=16)
        y3 = qy[:, 0:n].rearrange("p (z k) -> p z k", k=16)
        b.waits("gpsimd", "q_abs", "q_rsd", f"q_chain{p}")
        vy = b.on("gpsimd", lambda g, y3=y3, a3=a3: g.tensor_tensor(
            y3, a3, q_rs[:, 0:nb].unsqueeze(2).broadcast_to([128, nb, 16]),
            ALU.mult), inc="s_og", inc_n=1)
        b.set("q_yok", at=("s_og", vy))
        b.set("q_yokA", at=("s_og", vy))
        y, Bc, Ac = V(qy[:, 0:n]), V(q_b[:, 0:n]), V(q_A[:, 0:n])
        b.waits("vector", "q_yok", "q_poolQ")
        b.on("vector", lambda v, y=y, Bc=Bc: v.tensor_single_scalar(
            Bc, y, 5.0, ALU.is_gt))
        b.on("vector", lambda v, y=y, Ac=Ac: v.tensor_single_scalar(
            Ac, y, 0.25, ALU.is_gt))
        for ci, (thB, thA) in enumerate(((5.0, 0.75), (3.5, 1.25), (2.5, 1.75))):
            b.on("vector", lambda v, th=thB, y=y, Bc=Bc: v.scalar_tensor_tensor(
                Bc, y, th, Bc, ALU.is_gt, ALU.add))
            b.on("vector", lambda v, th=thA, y=y, Ac=Ac: v.scalar_tensor_tensor(
                Ac, y, th, Ac, ALU.is_gt, ALU.add),
                inc="s_dve" if ci == 2 else None)
        b.set("q_chain", "vector")
        b.set(f"q_chain{p}", "vector")
        b.waits("vector", "q_s3A")
        b.on("vector", lambda v: v.tensor_scalar(
            q_hs[:, 0:nb], q_sc[:, 0:nb], 0.5, None, ALU.mult), inc="s_dve")
        b.set("q_hsd", "vector")
        s3 = q_sg[:, 0:n].rearrange("p (z k) -> p z k", k=16)
        b.waits("gpsimd", "q_hsd", "q_act", "q_chain", *free_deps)
        b.on("gpsimd", lambda g, Bc=Bc: g.tensor_tensor(
            V(q_Q[:, 0:n]), Bc, Bc, ALU.add), inc="s_og", inc_n=1)
        vs3 = b.on("gpsimd", lambda g, s3=s3: g.tensor_tensor(
            s3, s3, q_hs[:, 0:nb].unsqueeze(2).broadcast_to([128, nb, 16]),
            ALU.mult), inc="s_og", inc_n=1)
        b.set("q_s3A", at=("s_og", vs3))
        vq = b.on("gpsimd", lambda g, Ac=Ac: g.tensor_tensor(
            V(q_Q[:, 0:n]), V(q_Q[:, 0:n]), Ac, ALU.add), inc="s_og", inc_n=1)
        b.set("q_poolQ", at=("s_og", vq))
        b.on("gpsimd", lambda g, vq=vq: g.wait_ge(b.sem["s_og"], vq))
        vout = b.on("gpsimd", lambda g, out_bf=out_bf: g.tensor_tensor(
            V(out_bf), V(q_Q[:, 0:n]), V(q_sg[:, 0:n]), ALU.mult),
            inc="s_og", inc_n=1)
        b.set(out_res, at=("s_og", vout))
        b.set(f"q_pool{p}", at=("s_og", vout))
        b.set("q_poolA", at=("s_og", vout))
        b.set("q_scratch", at=("s_og", vout))

    wplan = ([("g", ti) for ti in range(NI)] + [("u", ti) for ti in range(NI)]
             + [("d", ti) for ti in range(ND)])
    for (w, ti) in wplan:
        parts = [(0, 1024)] if w in "gu" else [(0, 1408), (1408, 1408)]
        for (c0, n) in parts:
            src = {"g": wg_d, "u": wu_d, "d": wd_d}[w]
            p_ = qn[0] & 1
            qld = q_y if p_ == 0 else q_y2
            b.waits("sync", f"q_chain{p_}")
            b.on("sync", lambda s, src=src, ti=ti, c0=c0, n=n, qld=qld:
                 s.dma_start(out=qld[:, 0:n],
                             in_=src[ti * 128:(ti + 1) * 128, c0:c0 + n]),
                 inc="s_ld")
            b.set("wload", "sync")
            emit_quant(qld[:, 0:n], wq_tm[:, 0:n], n, "wload", "wq",
                       free_deps=("wtp_done",))
            nblk = n // 128
            for g0 in range(0, nblk, 8):
                gn = min(8, nblk - g0)
                b.waits("tensor", "wq", "psT_free")
                for bb in range(gn):
                    b.on("tensor", lambda t, bb=bb, g0=g0: t.matmul(
                        psT[:, bb * 128:(bb + 1) * 128],
                        wq_tm[:, (g0 + bb) * 128:(g0 + bb + 1) * 128],
                        ident[:], start=True, stop=True, is_transpose=True),
                        inc="s_pe" if bb == gn - 1 else None)
                b.set("wtp", "tensor")
                if g0 + 8 >= nblk:
                    b.set("wtp_done", "tensor")
                b.waits("vector", "wtp", "tstage_free")
                b.on("vector", lambda v, gn=gn: v.tensor_copy(
                    tstage[:, 0:gn * 128], psT[:, 0:gn * 128]), inc="s_dve")
                b.set("tev", "vector")
                b.set("psT_free", "vector")
                if w in "gu":
                    dst = wgT_s if w == "g" else wuT_s
                    dview = dst.rearrange("(z p) i -> p z i", p=128)[
                        :, g0:g0 + gn, ti * 128:(ti + 1) * 128]
                else:
                    i0 = (c0 // 128) + g0
                    dview = wdT_s.rearrange("(z p) dd -> p z dd", p=128)[
                        :, i0:i0 + gn, ti * 128:(ti + 1) * 128]
                b.wait("sync", "tev")
                b.on("sync", lambda s, dv=dview, gn=gn: s.dma_start(
                    out=dv, in_=tstage[:, 0:gn * 128].rearrange(
                        "p (z m) -> p z m", m=128)), inc="s_sp")
                b.set("tstage_free", at=("s_sp", b.cnt["s_sp"]))
    b.set("wq_all_st", at=("s_sp", b.cnt["s_sp"]))

    def emit_ln(x_src_res):
        b.waits("vector", x_src_res)
        b.on("vector", lambda v: v.tensor_reduce(
            lnmean[:], xres.rearrange("p (i d) -> p i d", d=D),
            axis=AX.X, op=ALU.add), inc="s_dve")
        b.set("ln_red", "vector")
        b.wait("vector", "ln_red")
        b.waits("scalar", x_src_res, "lnsq_free")
        for t in range(NT):
            b.on("scalar", lambda s, t=t: s.activation(
                lnsq[:], xres[:, t * D:(t + 1) * D], AF.Square,
                accum_out=lnsum2[:, t:t + 1]),
                inc="s_act" if t == NT - 1 else None)
        b.set("ln_sq", "scalar")
        b.on("vector", lambda v: v.tensor_scalar(
            lnmean[:], lnmean[:], R1024, None, ALU.mult), inc="s_dve")
        b.set("ln_m2", "vector")
        b.wait("vector", "ln_sq")
        b.on("vector", lambda v: v.tensor_scalar(
            lnsum2[:], lnsum2[:], R1024, None, ALU.mult), inc="s_dve")
        b.set("ln_s2", "vector")
        b.wait("vector", "ln_m2")
        b.wait("vector", "ln_s2")
        b.on("vector", lambda v: v.tensor_tensor(
            lnrstd[:], lnmean[:], lnmean[:], ALU.mult), inc="s_dve")
        b.set("ln_msq", "vector")
        b.wait("vector", "ln_msq")
        b.on("vector", lambda v: v.tensor_tensor(
            lnrstd[:], lnsum2[:], lnrstd[:], ALU.subtract), inc="s_dve")
        b.set("ln_var", "vector")
        b.wait("scalar", "ln_var")
        b.on("scalar", lambda s: s.activation(
            lnrstd[:], lnrstd[:], AF.Ln, bias=EPS), inc="s_act")
        b.set("ln_ln", "scalar")
        b.wait("scalar", "ln_ln")
        b.on("scalar", lambda s: s.activation(
            lnrstd[:], lnrstd[:], AF.Exp, scale=-0.5), inc="s_act")
        b.set("ln_rstd", "scalar")
        b.wait("vector", "ln_rstd")
        b.on("vector", lambda v: v.tensor_tensor(
            lnnmr[:], lnmean[:], lnrstd[:], ALU.mult), inc="s_dve")
        b.set("ln_nm1", "vector")
        b.wait("vector", "ln_nm1")
        b.on("vector", lambda v: v.tensor_scalar(
            lnnmr[:], lnnmr[:], -1.0, None, ALU.mult), inc="s_dve")
        b.set("ln_ok", "vector")

    def emit_ln_apply(t, out_ap, out_res):
        b.waits("scalar", "ln_ok", f"lnout_free")
        b.on("scalar", lambda s, t=t: s.activation(
            out_ap, xres[:, t * D:(t + 1) * D], AF.Identity,
            bias=lnnmr[:, t:t + 1], scale=lnrstd[:, t:t + 1]), inc="s_act")
        b.set(out_res, "scalar")

    def emit_tp(src_bf, nblk, dst_fn, dep_res, out_res):
        b.waits("tensor", dep_res, "psT_free")
        for bb in range(nblk):
            b.on("tensor", lambda te, bb=bb: te.matmul(
                psT[:, bb * 128:(bb + 1) * 128],
                src_bf[:, bb * 128:(bb + 1) * 128],
                ident[:], start=True, stop=True, is_transpose=True),
                inc="s_pe" if bb == nblk - 1 else None)
        b.set("tp_mm", "tensor")
        b.wait("vector", "tp_mm")
        b.on("vector", dst_fn, inc="s_dve")
        b.set(out_res, "vector")
        b.set("psT_free", "vector")

    def emit_ffn(tag, x_res):
        emit_ln(x_res)
        for t in range(NT):
            emit_ln_apply(t, ln_t[:], "ln_ap")
            emit_quant(ln_t[:], lnb_t[:], D, "ln_ap", "xq_q",
                       free_deps=("lnb_free",))
            b.set("lnout_free", at=b.res["q_bmr"])
            emit_tp(lnb_t, 8,
                    lambda v, t=t: v.tensor_copy(
                        fm3(xq_fm, t * 128, 128),
                        psT[:].rearrange("p (z m) -> p z m", m=128)),
                    "xq_q", "xq_ev")
            b.set("lnb_free", at=b.res["tp_mm"])
        b.set("xq_all", at=b.res["xq_ev"])
        if stage == "dbgx":
            return "DBGX"
        for ic, (i0, ilen) in enumerate(ICHUNKS):
            nseg = ilen // 128
            b.waits("sync", "wch_free", "wq_all_st")
            b.on("sync", lambda s, i0=i0, ilen=ilen: s.dma_start(
                out=wch_g[:, 0:8 * ilen].rearrange("p (j n) -> p j n", j=ND),
                in_=wgT_s.rearrange("(j p) i -> p j i", p=128)[:, :, i0:i0 + ilen]),
                inc="s_ld")
            b.on("sync", lambda s, i0=i0, ilen=ilen: s.dma_start(
                out=wch_u[:, 0:8 * ilen].rearrange("p (j n) -> p j n", j=ND),
                in_=wuT_s.rearrange("(j p) i -> p j i", p=128)[:, :, i0:i0 + ilen]),
                inc="s_ld")
            b.on("sync", lambda s, i0=i0, nseg=nseg: s.dma_start(
                out=wch_d[:, 0:nseg * 1024].rearrange("p (m n) -> p m n", n=1024),
                in_=wdT_s.rearrange("(m p) dd -> p m dd", p=128)[
                    :, i0 // 128:i0 // 128 + nseg, :]), inc="s_ld")
            b.set("wch", "sync")
            for t in range(NT):
                b.waits("tensor", "wch", "xq_all", "psA_free", "psB_free")
                for j in range(ND):
                    b.on("tensor", lambda te, j=j, t=t, ilen=ilen: te.matmul(
                        psA[:, 0:ilen],
                        xq_fm[:, j * T + t * 128: j * T + (t + 1) * 128],
                        wch_g[:, j * ilen:(j + 1) * ilen],
                        start=(j == 0), stop=(j == ND - 1)))
                for j in range(ND):
                    b.on("tensor", lambda te, j=j, t=t, ilen=ilen: te.matmul(
                        psB[:, 0:ilen],
                        xq_fm[:, j * T + t * 128: j * T + (t + 1) * 128],
                        wch_u[:, j * ilen:(j + 1) * ilen],
                        start=(j == 0), stop=(j == ND - 1)),
                        inc="s_pe" if j == ND - 1 else None)
                b.set("gu_mm", "tensor")
                if stage in ("dbgg", "dbgu") and t == 0:
                    b.wait("vector", "gu_mm")
                    pp_dbg = psA if stage == "dbgg" else psB
                    b.on("vector", lambda v, pp_dbg=pp_dbg, ilen=ilen: v.tensor_copy(
                        sttsb[:, 0:ilen], pp_dbg[:, 0:ilen]), inc="s_dve")
                    b.set("dbg_ev", "vector")
                    b.wait("sync", "dbg_ev")
                    b.on("sync", lambda s, ilen=ilen: s.dma_start(
                        out=y_out.rearrange("(i p) d -> p (i d)", p=128)[:, 0:ilen],
                        in_=sttsb[:, 0:ilen]), inc="s_sp")
                    ov = b.cnt["s_sp"]
                    b.on("sync", lambda s, v=ov: s.wait_ge(b.sem["s_sp"], v))
                    return "DBG"
                b.waits("scalar", "gu_mm", "sil_free")
                b.on("scalar", lambda s, ilen=ilen: s.activation(
                    ffnsil[:, 0:ilen], psA[:, 0:ilen], AF.Silu), inc="s_act")
                b.set("sil", "scalar")
                b.set("psA_free", "scalar")
                b.waits("vector", "sil", "hbuf_free")
                b.on("vector", lambda v, ilen=ilen: v.scalar_tensor_tensor(
                    hbuf[:, 0:ilen], ffnsil[:, 0:ilen], 0.5, psB[:, 0:ilen],
                    ALU.mult, ALU.mult), inc="s_dve")
                b.set("hgot", "vector")
                b.set("psB_free", "vector")
                b.set("sil_free", "vector")
                if stage == "dbgh" and t == 0:
                    b.wait("sync", "hgot")
                    b.on("sync", lambda s, ilen=ilen: s.dma_start(
                        out=y_out.rearrange("(i p) d -> p (i d)", p=128)[:, 0:ilen],
                        in_=hbuf[:, 0:ilen]), inc="s_sp")
                    ov = b.cnt["s_sp"]
                    b.on("sync", lambda s, v=ov: s.wait_ge(b.sem["s_sp"], v))
                    return "DBG"
                emit_quant(hbuf[:, 0:ilen], hq_tm[:, 0:ilen], ilen, "hgot",
                           "hq_q", free_deps=("hq_tm_free",))
                b.set("hbuf_free", at=b.res["q_act"])
                emit_tp(hq_tm, nseg,
                        lambda v, nseg=nseg: v.tensor_copy(
                            hq_fm[:, 0:nseg * 128], psT[:, 0:nseg * 128]),
                        "hq_q", "hq_ev")
                b.set("hq_tm_free", at=b.res["tp_mm"])
                b.waits("tensor", "hq_ev", "psC_free")
                for nh in range(2):
                    for m in range(nseg):
                        b.on("tensor", lambda te, m=m, nh=nh, nseg=nseg: te.matmul(
                            psC[:, nh * 512:(nh + 1) * 512],
                            hq_fm[:, m * 128:(m + 1) * 128],
                            wch_d[:, m * 1024 + nh * 512: m * 1024 + (nh + 1) * 512],
                            start=(m == 0), stop=(m == nseg - 1)),
                            inc="s_pe" if (nh == 1 and m == nseg - 1) else None)
                b.set("dn_mm", "tensor")
                b.wait("vector", "dn_mm")
                b.on("vector", lambda v, t=t: v.tensor_tensor(
                    V(xres[:, t * D:(t + 1) * D]), V(xres[:, t * D:(t + 1) * D]),
                    V(psC[:]), ALU.add), inc="s_dve")
                b.set("x_acc", "vector")
                b.set("psC_free", "vector")
            b.set("wch_free", at=b.res["dn_mm"])
        b.set(f"{tag}_done", at=b.res["x_acc"])

    def dump_and_finish(src_ap, n_bf16_cols, deps):
        b.waits("gpsimd", *deps)
        ni = max(1, n_bf16_cols // D)
        b.on("gpsimd", lambda g: g.dma_start(
            out=y_out.rearrange("(i p) d -> p i d", p=128)[:, 0:ni, :],
            in_=src_ap[:, 0:n_bf16_cols].rearrange("p (i d) -> p i d", d=min(D, n_bf16_cols))), inc="s_g")
        ov = b.cnt["s_g"]
        b.on("gpsimd", lambda g, v=ov: g.wait_ge(b.sem["s_g"], v))
        nc.detect_race_conditions = False
        blk2 = es.enter_context(nc.Block())
        with es:
            b.replay(blk2)
        return nc

    def dump_dram_and_finish(dram_ap, deps):
        b.waits("gpsimd", *deps)
        b.on("gpsimd", lambda g: g.dma_start(
            out=y_out[0:1024, :], in_=dram_ap), inc="s_g")
        ov = b.cnt["s_g"]
        b.on("gpsimd", lambda g, v=ov: g.wait_ge(b.sem["s_g"], v))
        nc.detect_race_conditions = False
        blk2 = es.enter_context(nc.Block())
        with es:
            b.replay(blk2)
        return nc

    def finish_here():
        b.waits("sync", "wq_all_st", "x_acc", "xload", "x2_done", "f2_done")
        b.on("sync", lambda s: s.dma_start(
            out=y_out.rearrange("(i p) d -> p i d", p=128),
            in_=xres.rearrange("p (i d) -> p i d", d=D)), inc="s_sp")
        ov = b.cnt["s_sp"]
        b.on("sync", lambda s, v=ov: s.wait_ge(b.sem["s_sp"], v))
        nc.detect_race_conditions = False
        blk2 = es.enter_context(nc.Block())
        with es:
            b.replay(blk2)
        return nc

    b.wait("vector", "xload")
    b.wait("scalar", "xload")
    b.wait("vector", "consts")
    b.wait("scalar", "consts")
    if stage == "p0":
        return finish_here()
    if stage == "dbgw":
        return dump_dram_and_finish(wgT_s[0:1024, 0:1024], ["wq_all_st"])
    if stage == "dbgwd":
        return dump_dram_and_finish(wdT_s[0:1024, 0:1024], ["wq_all_st"])
    if stage == "dbgln":
        emit_ln("xload")
        b.waits("sync", "ln_ok", "ln_rstd")
        for idx, tsr in enumerate((lnmean, lnsum2, lnrstd, lnnmr)):
            b.on("sync", lambda s, t=tsr, idx=idx: s.dma_start(
                out=y_out.rearrange("(i p) d -> p i d", p=128)[:, idx, 0:NT],
                in_=t[:]), inc="s_sp")
        for k in range(4):
            b.on("sync", lambda s, k=k: s.dma_start(
                out=y_out.rearrange("(i p) d -> p i d", p=128)[:, 4 + k, 0:NT],
                in_=xres[:, (12 + k) * 1024:(12 + k) * 1024 + NT]), inc="s_sp")
        ov = b.cnt["s_sp"]
        b.on("sync", lambda s, v=ov: s.wait_ge(b.sem["s_sp"], v))
        nc.detect_race_conditions = False
        blk2 = es.enter_context(nc.Block())
        with es:
            b.replay(blk2)
        return nc
    rr = emit_ffn("f1", "xload")
    if rr == "DBG":
        nc.detect_race_conditions = False
        blk2 = es.enter_context(nc.Block())
        with es:
            b.replay(blk2)
        return nc
    if rr == "DBGX":
        return dump_and_finish(xq_fm, 16384, ["xq_all"])
    if stage == "f1":
        return finish_here()

    emit_ln("f1_done")
    for t in range(NT):
        b.waits("scalar", "lnb_free")
        emit_ln_apply(t, lnb_t[:], "l1_ap")
        b.set("lnout_free", at=b.res["l1_ap"])
        emit_tp(lnb_t, 8,
                lambda v, t=t: v.tensor_copy(
                    fm3(ln1_fm, t * 128, 128),
                    psT[:].rearrange("p (z m) -> p z m", m=128)),
                "l1_ap", "l1_ev")
        b.set("lnb_free", at=b.res["tp_mm"])
    b.set("ln1_all", at=b.res["l1_ev"])
    b.waits("sync", "f1_done")
    b.on("sync", lambda s: s.dma_start(out=xsp_s, in_=xres), inc="s_sp")
    b.set("xspill", at=("s_sp", b.cnt["s_sp"]))
    if stage == "ln1":
        return finish_here()

    FIDX = {"r": 0, "k": 1, "v": 2, "g": 3, "w": 4}

    def emit_dx(c0):
        dx3 = dx_r.rearrange("p (j t) -> p j t", j=ND)
        ln3 = ln1_fm.rearrange("p (j t) -> p j t", j=ND)
        if c0 == 0:
            b.on("vector", lambda v: v.tensor_scalar(
                dx3[:, :, 0:1], ln3[:, :, 0:1], -1.0, None, ALU.mult))
            b.on("vector", lambda v: v.tensor_tensor(
                dx3[:, :, 1:512], ln3[:, :, 0:511], ln3[:, :, 1:512],
                ALU.subtract), inc="s_dve")
        else:
            b.on("vector", lambda v, c0=c0: v.tensor_tensor(
                dx3[:, :, 0:512], ln3[:, :, c0 - 1:c0 + 511],
                ln3[:, :, c0:c0 + 512], ALU.subtract), inc="s_dve")
        b.set("dx", "vector")

    def emit_mixpre(half, tc):
        c0 = half * 1024 + tc * 512
        b.waits("vector", "ln1_all", "dx_free", "xx_free", "su_mm")
        emit_dx(c0)
        xx3 = xx_r.rearrange("p (j t) -> p j t", j=ND)
        dx3 = dx_r.rearrange("p (j t) -> p j t", j=ND)
        ln3 = ln1_fm.rearrange("p (j t) -> p j t", j=ND)
        b.on("vector", lambda v: v.tensor_tensor(
            xx3, dx3, maa_x.unsqueeze(2).broadcast_to([128, ND, 512]), ALU.mult))
        b.on("vector", lambda v, c0=c0: v.tensor_tensor(
            xx3, xx3, ln3[:, :, c0:c0 + 512], ALU.add), inc="s_dve")
        b.set("xxx", "vector")
        b.set("dx_free", "vector")
        b.waits("tensor", "xxx", "consts", "psA_free", "psE_free")
        for j in range(ND):
            b.on("tensor", lambda te, j=j: te.matmul(
                psA[:], w1sb[:, j * 160:j * 160 + 128],
                xx_r[:, j * 512:(j + 1) * 512],
                start=(j == 0), stop=(j == ND - 1)))
        for j in range(ND):
            b.on("tensor", lambda te, j=j: te.matmul(
                psE[0:32, :], w1sb[:, j * 160 + 128:(j + 1) * 160],
                xx_r[:, j * 512:(j + 1) * 512],
                start=(j == 0), stop=(j == ND - 1)),
                inc="s_pe" if j == ND - 1 else None)
        b.set("mix_mm", "tensor")
        b.set("xx_free", "tensor")
        b.waits("scalar", "mix_mm", "mixsb_free")
        b.on("scalar", lambda s: s.activation(mixsb[:], psA[:], AF.Tanh))
        b.on("scalar", lambda s: s.activation(mix2sb[:], psE[0:32, :], AF.Tanh),
             inc="s_act")
        b.set("mix_ev", "scalar")
        b.set("psA_free", "scalar")
        b.set("psE_free", "scalar")
        b.wait("sync", "mix_ev")
        for fb in range(4):
            b.on("sync", lambda s, fb=fb, tc=tc: s.dma_start(
                out=mixf[:, (fb * 2 + tc) * 512:(fb * 2 + tc + 1) * 512],
                in_=mixsb[fb * 32:(fb + 1) * 32, :]), inc="s_ld")
        b.on("sync", lambda s, tc=tc: s.dma_start(
            out=mixf[:, (4 * 2 + tc) * 512:(4 * 2 + tc + 1) * 512],
            in_=mix2sb[:]), inc="s_ld")
        b.set("mix_dma", "sync")
        b.set("mixsb_free", at=b.res["mix_dma"])

    def emit_xf(f, half, tc):
        c0 = half * 1024 + tc * 512
        b.waits("vector", "xf_free", "dx_free", "su_mm")
        emit_dx(c0)
        fi = FIDX[f]
        for j in range(ND):
            pp = psD if j % 2 == 0 else psB
            pfree = "psD_free" if j % 2 == 0 else "psB_free"
            b.waits("tensor", pfree, "mix_dma", "consts",
                    "dx" if j == 0 else "noop")
            b.on("tensor", lambda te, pp=pp, j=j, fi=fi, tc=tc: te.matmul(
                pp[:], w2sb[:, fi * 1024 + j * 128:fi * 1024 + (j + 1) * 128],
                mixf[:, (fi * 2 + tc) * 512:(fi * 2 + tc + 1) * 512],
                start=True, stop=True), inc="s_pe")
            b.set("mf_mm", "tensor")
            b.waits("vector", "mf_mm", "sttsb_free")
            b.on("vector", lambda v, pp=pp, f=f, j=j: v.scalar_tensor_tensor(
                sttsb[:], pp[:], maa_of[f][:, j:j + 1],
                dx_r[:, j * 512:(j + 1) * 512], ALU.add, ALU.mult),
                inc="s_dve")
            b.set("xf_stt", "vector")
            b.wait("vector", "xf_stt")
            b.on("vector", lambda v, j=j, c0=c0: v.tensor_tensor(
                xf_r[:, j * 512:(j + 1) * 512], sttsb[:],
                ln1_fm[:, j * T + c0:j * T + c0 + 512], ALU.add), inc="s_dve")
            b.set(pfree, "vector")
            b.set("sttsb_free", "vector")
        b.set("xf", at=b.res["sttsb_free"])
        b.set("dx_free", at=b.res["sttsb_free"])

    def emit_proj(f, half, tc):
        c0 = half * 1024 + tc * 512
        if f == "w":
            b.waits("tensor", "xf", "psE_free")
            for j in range(ND):
                b.on("tensor", lambda te, j=j: te.matmul(
                    psE[0:64, :], dw1sb[:, j * 64:(j + 1) * 64],
                    xf_r[:, j * 512:(j + 1) * 512],
                    start=(j == 0), stop=(j == ND - 1)),
                    inc="s_pe" if j == ND - 1 else None)
            b.set("t1_mm", "tensor")
            b.set("xf_free", "tensor")
            b.wait("scalar", "t1_mm")
            b.on("scalar", lambda s, c0=c0: s.activation(
                tanh1[:, c0:c0 + 512], psE[0:64, :], AF.Tanh), inc="s_act")
            b.set("t1_ev", "scalar")
            b.set("psE_free", "scalar")
            return
        if f == "v":
            for st in range(4):
                b.waits("tensor", "xf" if st == 0 else "noop", "psC_free")
                for j in range(ND):
                    for nh in range(2):
                        b.on("tensor", lambda te, j=j, st=st, nh=nh: te.matmul(
                            psC[:, nh * 512:(nh + 1) * 512],
                            xf_r[:, j * 512 + st * 128: j * 512 + (st + 1) * 128],
                            attw[:, j * 1024 + nh * 512: j * 1024 + (nh + 1) * 512],
                            start=(j == 0), stop=(j == ND - 1)),
                            inc="s_pe" if (j == ND - 1 and nh == 1) else None)
                b.set("v_mm", "tensor")
                if st == 3:
                    b.set("xf_free", "tensor")
                b.wait("scalar", "v_mm")
                b.on("scalar", lambda s, st=st, tc=tc: s.activation(
                    v_tm[:, (tc * 4 + st) * 1024:(tc * 4 + st + 1) * 1024],
                    psC[:], AF.Copy), inc="s_act")
                b.set("v_ev", "scalar")
                b.set("psC_free", "scalar")
            return
        for jj in range(ND):
            pp = psA if jj % 2 == 0 else psD
            pfree = "psA_free" if jj % 2 == 0 else "psD_free"
            b.waits("tensor", pfree, "xf" if jj == 0 else "noop")
            for j in range(ND):
                b.on("tensor", lambda te, pp=pp, j=j, jj=jj: te.matmul(
                    pp[:], attw[:, j * 1024 + jj * 128: j * 1024 + (jj + 1) * 128],
                    xf_r[:, j * 512:(j + 1) * 512],
                    start=(j == 0), stop=(j == ND - 1)),
                    inc="s_pe" if j == ND - 1 else None)
            b.set("p_mm", "tensor")
            if jj == ND - 1:
                b.set("xf_free", "tensor")
            b.wait("scalar", "p_mm")
            if f == "g":
                b.waits("scalar", "g_dma")
                b.on("scalar", lambda s, pp=pp, jj=jj: s.activation(
                    gstage[:], pp[:], AF.Silu, bias=bgv[:, jj:jj + 1]),
                    inc="s_act")
                b.set("g_ev", "scalar")
                b.set(pfree, "scalar")
                b.wait("sync", "g_ev")
                b.on("sync", lambda s, jj=jj, c0=c0: s.dma_start(
                    out=gsp_s[:, jj * T + c0: jj * T + c0 + 512],
                    in_=gstage[:]), inc="s_ld")
                b.set("g_dma", "sync")
            else:
                dst = q_fm if f == "r" else k_fm
                b.on("scalar", lambda s, pp=pp, jj=jj, c0=c0, dst=dst: s.activation(
                    dst[:, jj * T + c0: jj * T + c0 + 512], pp[:], AF.Copy),
                    inc="s_act")
                b.set("p_ev", "scalar")
                b.set(pfree, "scalar")

    def emit_S_half(half):
        b.wait("scalar", "xspill")
        for tc in range(2):
            emit_mixpre(half, tc)
        for f in ("w", "r", "k", "v", "g"):
            if f != "w":
                wname = {"r": "Wq", "k": "Wk", "v": "Wv", "g": "Wg"}[f]
                b.waits("gpsimd", "attw_free", "q_scratch", "q_act",
                        "su_mm", "o_mm")
                b.on("gpsimd", lambda g, wname=wname: g.dma_start(
                    out=attw[:].rearrange("p (j n) -> p j n", j=ND),
                    in_=W_d[wname].rearrange("(j p) n -> p j n", p=128)),
                    inc="s_g")
                b.set("w_att", "gpsimd")
            for tc in range(2):
                emit_xf(f, half, tc)
                if f != "w":
                    b.wait("tensor", "w_att")
                emit_proj(f, half, tc)
            if f != "w":
                b.set("attw_free", at=b.res["xf_free"])

    import os
    E_NGRP = int(os.environ.get("KE_NGRP", "4"))
    E_NCH = int(os.environ.get("KE_NCH", "4"))
    E_PARTS = os.environ.get("KE_PARTS", "sakouz123")

    def emit_E_half(half):
        for grp in range(E_NGRP):
            g0 = half * 1024 + grp * 256
            lg = grp * 256
            for j in range(ND):
                pp = psE if j % 2 == 0 else psD
                pfree = "psE_free" if j % 2 == 0 else "psD_free"
                b.waits("tensor", pfree, "t1_ev", "dec_free", "consts")
                b.on("tensor", lambda te, j=j, g0=g0, pp=pp: te.matmul(
                    pp[:, 0:256], dw2sb[:, j * 128:(j + 1) * 128],
                    tanh1[:, g0:g0 + 256], start=True, stop=True), inc="s_pe")
                b.set("dec_mm", "tensor")
                b.wait("vector", "dec_mm")
                b.on("vector", lambda v, j=j, pp=pp: v.tensor_scalar(
                    dec_g[:, j * 256:(j + 1) * 256], pp[:, 0:256],
                    tdec[:, j:j + 1], None, ALU.add), inc="s_dve")
                b.set(pfree, "vector")
                b.set("dec_j", "vector")
            b.set("dec_done", at=b.res["dec_j"])
            b.waits("scalar", "dec_done", "egd_free")
            b.on("scalar", lambda s: s.activation(egd_g[:], dec_g[:], AF.Exp),
                 inc="s_act")
            b.set("egd", "scalar")
            b.wait("scalar", "egd")
            b.on("scalar", lambda s: s.activation(
                dec_g[:], egd_g[:], AF.Exp, scale=-1.0), inc="s_act")
            b.set("emg", "scalar")
            b.set("dec_free", at=("s_act", b.cnt["s_act"]))
            b.wait("vector", "emg")
            b.on("vector", lambda v: v.tensor_scalar(
                V(dec_g[:]), V(dec_g[:]), -1.0, 1.0, ALU.mult, ALU.add))
            kf3 = fm3(k_fm, g0, 256)
            dg3 = dec_g[:].rearrange("p (j t) -> p j t", j=ND)
            b.on("vector", lambda v, kf3=kf3, dg3=dg3: v.tensor_tensor(
                kf3, kf3, dg3, ALU.mult), inc="s_dve")
            b.set("kp", "vector")
            if os.environ.get("KE_DUMMYDMA"):
                b.waits("sync", "g_dma")
                b.on("sync", lambda s, g0=g0: s.dma_start(
                    out=gat_r.rearrange("p (j t) -> p j t", j=ND),
                    in_=gsp_s.rearrange("p (j t) -> p j t", j=ND)[:, :, g0:g0 + 256]),
                    inc="s_ld")
                b.set("g_rl", "sync")
            for cc in range(E_NCH):
                c0 = g0 + cc * 64
                l0 = cc * 64
                par = (c0 // 64) % 2
                tloc = (c0 - half * 1024) // 128
                if os.environ.get("KE_ESER"):
                    b.waits("tensor", "og_ev", "S_ok", "o_sb")
                b.waits("vector", "egd", "cb_free", "expq")
                for j in range(ND):
                    b.on("vector", lambda v, j=j, l0=l0: v.tensor_tensor_scan(
                        cbsb[:, j * 64:(j + 1) * 64],
                        onessb[:].broadcast_to([128, 64]),
                        egd_g[:, j * 256 + l0:j * 256 + l0 + 64],
                        0.0, ALU.mult, ALU.add),
                        inc="s_dve" if j == ND - 1 else None)
                b.set("cb", "vector")
                b.wait("vector", "cb")
                cb3 = cbsb[:].rearrange("p (j t) -> p j t", j=ND)
                cbL = cb3[:, :, 63:64].squeeze(2)
                b.wait("vector", "sm_exp")
                b.on("vector", lambda v, cbL=cbL: v.tensor_scalar(
                    smalls[:, 0:8], cbL, 0.5, None, ALU.mult), inc="s_dve")
                b.set("chalf", "vector")
                b.wait("scalar", "chalf")
                b.on("scalar", lambda s: s.activation(
                    smalls[:, 8:16], smalls[:, 0:8], AF.Exp))
                b.on("scalar", lambda s: s.activation(
                    smalls[:, 16:24], smalls[:, 0:8], AF.Exp, scale=-1.0))
                b.on("scalar", lambda s: s.activation(
                    smalls[:, 24:32], smalls[:, 0:8], AF.Exp, scale=-2.0),
                    inc="s_act")
                b.set("sm_exp", "scalar")
                b.waits("scalar", "expq_free")
                b.on("scalar", lambda s: s.activation(
                    expsc[:], cbsb[:], AF.Exp, scale=-1.0, bias=-LN8),
                    inc="s_act")
                b.set("expq", "scalar")
                b.waits("vector", "chalf", "expk_free")
                b.on("vector", lambda v: v.tensor_tensor(
                    expsc2[:].rearrange("p (j t) -> p j t", j=ND), cb3,
                    smalls[:, 0:8].unsqueeze(2).broadcast_to([128, ND, 64]),
                    ALU.subtract), inc="s_dve")
                b.set("argk", "vector")
                b.set("cb_free", "vector")
                b.wait("scalar", "argk")
                b.on("scalar", lambda s: s.activation(
                    expsc2[:], expsc2[:], AF.Exp), inc="s_act")
                b.set("expk", "scalar")
                q3 = fm3(q_fm, c0, 64)
                k3 = fm3(k_fm, c0, 64)
                e1 = expsc[:].rearrange("p (j t) -> p j t", j=ND)
                e2 = expsc2[:].rearrange("p (j t) -> p j t", j=ND)
                qs3 = qsc[:].rearrange("p (j t) -> p j t", j=ND)
                qss3 = qss[:].rearrange("p (j t) -> p j t", j=ND)
                ks3 = ksc[:].rearrange("p (j t) -> p j t", j=ND)
                kh3 = khc[:].rearrange("p (j t) -> p j t", j=ND)
                b.waits("vector", "expq", "expk", "sm_exp", "kp",
                        "qk_free", "p_mm", "v_mm", "t1_mm")
                b.on("vector", lambda v, qs3=qs3, q3=q3, e1=e1: v.tensor_tensor(
                    qs3, q3, e1, ALU.mult))
                b.on("vector", lambda v, ks3=ks3, k3=k3, e2=e2: v.tensor_tensor(
                    ks3, k3, e2, ALU.mult))
                b.on("vector", lambda v, qss3=qss3, qs3=qs3: v.tensor_tensor(
                    qss3, qs3, smalls[:, 8:16].unsqueeze(2).broadcast_to(
                        [128, ND, 64]), ALU.mult))
                b.on("vector", lambda v, kh3=kh3, ks3=ks3: v.tensor_tensor(
                    kh3, ks3, smalls[:, 16:24].unsqueeze(2).broadcast_to(
                        [128, ND, 64]), ALU.mult), inc="s_dve")
                b.set("qk", "vector")
                b.set("expq_free", "vector")
                b.set("expk_free", "vector")
                if "a" not in E_PARTS:
                    continue
                b.waits("tensor", "qk", "psA_free", "psD_free")
                for hp in range(ND):
                    b.on("tensor", lambda te, hp=hp, par=par: te.matmul(
                        psA[par * 64:par * 64 + 64, hp * 64:(hp + 1) * 64],
                        ksc[0:64, hp * 64:(hp + 1) * 64],
                        qss[0:64, hp * 64:(hp + 1) * 64],
                        start=True, stop=True, tile_position=(0, par * 64)))
                    b.on("tensor", lambda te, hp=hp, par=par: te.matmul(
                        psD[par * 64:par * 64 + 64, hp * 64:(hp + 1) * 64],
                        ksc[64:128, hp * 64:(hp + 1) * 64],
                        qss[64:128, hp * 64:(hp + 1) * 64],
                        start=True, stop=True, tile_position=(64, par * 64)),
                        inc="s_pe" if hp == ND - 1 else None)
                b.set("a_mm", "tensor")
                b.waits("vector", "a_mm", "amsk_free")
                mrow = maskt[64 * par:64 * par + 64, :].unsqueeze(1)\
                    .broadcast_to([64, ND, 64])
                b.on("vector", lambda v, par=par, mrow=mrow: v.tensor_tensor(
                    amskA[64 * par:64 * par + 64, :].rearrange(
                        "p (j t) -> p j t", j=ND),
                    psA[64 * par:64 * par + 64, :].rearrange(
                        "p (j t) -> p j t", j=ND), mrow, ALU.mult))
                b.on("vector", lambda v, par=par, mrow=mrow: v.tensor_tensor(
                    amskB[64 * par:64 * par + 64, :].rearrange(
                        "p (j t) -> p j t", j=ND),
                    psD[64 * par:64 * par + 64, :].rearrange(
                        "p (j t) -> p j t", j=ND), mrow, ALU.mult),
                    inc="s_dve")
                b.set("am_ev", "vector")
                b.set("psA_free", "vector")
                b.set("psD_free", "vector")
                if "k" not in E_PARTS:
                    continue
                b.waits("tensor", "qk", "psT_free")
                for hp in range(ND):
                    b.on("tensor", lambda te, hp=hp, par=par: te.matmul(
                        psT[par * 64:par * 64 + 64, hp * 128:(hp + 1) * 128],
                        khc[:, hp * 64:(hp + 1) * 64], ident[:],
                        start=True, stop=True, is_transpose=True,
                        tile_position=(0, par * 64)),
                        inc="s_pe" if hp == ND - 1 else None)
                b.set("kt_mm", "tensor")
                b.waits("vector", "kt_mm", "ktm_free")
                b.on("vector", lambda v, par=par: v.tensor_copy(
                    ktm[par * 64:par * 64 + 64, :],
                    psT[par * 64:par * 64 + 64, :]), inc="s_dve")
                b.set("kt_ev", "vector")
                b.set("psT_free", "vector")
                if "o" not in E_PARTS:
                    continue
                b.waits("tensor", "am_ev", "psB_free", "S_ok", "v_ev",
                        "psC_free")
                for hp in range(ND):
                    b.on("tensor", lambda te, hp=hp: te.matmul(
                        psB[0:64, hp * 64:(hp + 1) * 64],
                        Ssb[0:64, hp * 64:(hp + 1) * 64],
                        qsc[0:64, hp * 64:(hp + 1) * 64],
                        start=True, stop=True, tile_position=(0, 0)))
                    b.on("tensor", lambda te, hp=hp: te.matmul(
                        psB[64:128, hp * 64:(hp + 1) * 64],
                        Ssb[64:128, hp * 64:(hp + 1) * 64],
                        qsc[64:128, hp * 64:(hp + 1) * 64],
                        start=True, stop=True, tile_position=(64, 64)))
                    b.on("tensor", lambda te, hp=hp, par=par, tloc=tloc: te.matmul(
                        psC[0:64, hp * 64:(hp + 1) * 64],
                        v_tm[par * 64:par * 64 + 64,
                             tloc * 1024 + hp * 128: tloc * 1024 + hp * 128 + 64],
                        amskA[par * 64:par * 64 + 64, hp * 64:(hp + 1) * 64],
                        start=True, stop=True, tile_position=(par * 64, 0)))
                    b.on("tensor", lambda te, hp=hp, par=par, tloc=tloc: te.matmul(
                        psC[64:128, hp * 64:(hp + 1) * 64],
                        v_tm[par * 64:par * 64 + 64,
                             tloc * 1024 + hp * 128 + 64:
                             tloc * 1024 + hp * 128 + 128],
                        amskB[par * 64:par * 64 + 64, hp * 64:(hp + 1) * 64],
                        start=True, stop=True, tile_position=(par * 64, 64)),
                        inc="s_pe" if hp == ND - 1 else None)
                b.set("o_mm", "tensor")
                og_fm = og1_fm if half == 0 else og2_fm
                cloc = (c0 - half * 1024) // 64
                if "u" not in E_PARTS:
                    continue
                b.waits("tensor", "kt_ev", "psE_free", "S_scaled_read",
                        "v_ev")
                for hp in range(ND):
                    b.on("tensor", lambda te, hp=hp, par=par, tloc=tloc: te.matmul(
                        psE[0:64, hp * 64:(hp + 1) * 64],
                        ktm[par * 64:par * 64 + 64, hp * 128:hp * 128 + 64],
                        v_tm[par * 64:par * 64 + 64,
                             tloc * 1024 + hp * 128: tloc * 1024 + hp * 128 + 64],
                        start=True, stop=True, tile_position=(par * 64, 0)))
                    b.on("tensor", lambda te, hp=hp, par=par, tloc=tloc: te.matmul(
                        psE[64:128, hp * 64:(hp + 1) * 64],
                        ktm[par * 64:par * 64 + 64, hp * 128 + 64:hp * 128 + 128],
                        v_tm[par * 64:par * 64 + 64,
                             tloc * 1024 + hp * 128 + 64:
                             tloc * 1024 + hp * 128 + 128],
                        start=True, stop=True, tile_position=(par * 64, 64)),
                        inc="s_pe" if hp == ND - 1 else None)
                b.set("su_mm", "tensor")
                b.set("ktm_free", "tensor")
                b.set("qk_free", "tensor")
                if "z" not in E_PARTS:
                    continue
                if "1" not in E_PARTS:
                    continue
                b.waits("vector", "o_mm", "sm_exp")
                b.on("vector", lambda v: v.tensor_tensor(
                    Stmp[:].rearrange("p (j t) -> p j t", j=ND),
                    Ssb[:].rearrange("p (j t) -> p j t", j=ND),
                    smalls[:, 24:32].unsqueeze(2).broadcast_to([128, ND, 64]),
                    ALU.mult), inc="s_dve")
                b.set("S_scaled_read", "vector")
                if "2" not in E_PARTS:
                    continue
                b.waits("vector", "o_mm")
                b.on("vector", lambda v: v.tensor_copy(sttsb[:], psB[:]),
                     inc="s_dve")
                b.set("o_sb", "vector")
                b.set("psB_free", "vector")
                if "3" not in E_PARTS:
                    continue
                if os.environ.get("KE_OGMEMSET"):
                    b.on("vector", lambda v, og_fm=og_fm, cloc=cloc: v.memset(
                        og_fm[:, cloc * 512:(cloc + 1) * 512],
                        float(100 + cloc)), inc="s_dve")
                else:
                    b.on("vector", lambda v, og_fm=og_fm, cloc=cloc: v.tensor_tensor(
                        V(og_fm[:, cloc * 512:(cloc + 1) * 512]), V(sttsb[:]),
                        V(psC[:, 0:512]), ALU.add), inc="s_dve")
                b.set("og_ev", "vector")
                b.set("psC_free", "vector")
                b.waits("vector", "su_mm")
                b.on("vector", lambda v: v.tensor_tensor(
                    V(Ssb[:]), V(Stmp[:]), V(psE[:]), ALU.add), inc="s_dve")
                b.set("S_ok", "vector")
                b.set("psE_free", "vector")
                b.set("amsk_free", at=b.res["o_mm"])
            b.set("egd_free", at=b.res["cb"])
        b.set(f"og_half{half}", at=b.res["og_ev"])

    b.res.setdefault("noop", ("s_dve", 0))

    for half in range(2):
        emit_S_half(half)
        if stage == f"s{half}":
            return finish_here()
        emit_E_half(half)
        if stage == f"e{half}":
            if os.environ.get("KE_DUMPOG"):
                return dump_and_finish(og1_fm if half == 0 else og2_fm, 8192,
                                       ["og_ev"])
            return finish_here()

    b.waits("gpsimd", "attw_free", "q_scratch", "q_act", "su_mm", "o_mm")
    b.on("gpsimd", lambda g: g.dma_start(
        out=attw[:].rearrange("p (j n) -> p j n", j=ND),
        in_=W_d["Wo"].rearrange("(j p) n -> p j n", p=128)), inc="s_g")
    b.set("wo_ld", "gpsimd")
    gatp5 = AR4b[:, 0:1024]
    _p5ret = [None]
    for t in range(NT):
        og_fm = og1_fm if t < 8 else og2_fm
        tl = t % 8
        b.waits("sync", "xspill", "xrld_free", "qk", "gatp5_free")
        b.on("sync", lambda s, t=t: s.dma_start(
            out=xrld[:], in_=xsp_s[:, t * D:(t + 1) * D]), inc="s_ld")
        for c_ in range(2):
            b.on("sync", lambda s, t=t, c_=c_: s.dma_start(
                out=gatp5[:, c_ * 512:(c_ + 1) * 512].rearrange(
                    "p (j z) -> p j z", j=ND),
                in_=gsp_s.rearrange("p (j tt) -> p j tt", j=ND)[
                    :, :, t * 128 + c_ * 64: t * 128 + c_ * 64 + 64]),
                inc="s_ld")
        b.set("x_rl", "sync")
        b.waits("vector", "x_rl", f"og_half{0 if t < 8 else 1}", "wo_mm")
        b.on("vector", lambda v, og_fm=og_fm, tl=tl: v.tensor_tensor(
            V(og_fm[:, tl * 1024:(tl + 1) * 1024]),
            V(og_fm[:, tl * 1024:(tl + 1) * 1024]), V(gatp5[:]), ALU.mult),
            inc="s_dve")
        b.set("og_gated", "vector")
        b.set("gatp5_free", "vector")
        if stage == "dbgp5" and t == int(os.environ.get("KE_P5T", "1")):
            return dump_and_finish(og_fm[:, tl * 1024:tl * 1024 + 1024], 1024,
                                   ["og_gated"])
        b.waits("tensor", "wo_ld", "psC_free", "og_gated")
        for c_ in range(2):
            for nh in range(2):
                for j in range(ND):
                    b.on("tensor", lambda te, j=j, nh=nh, tl=tl, c_=c_,
                         og_fm=og_fm: te.matmul(
                        psC[c_ * 64:(c_ + 1) * 64, nh * 512:(nh + 1) * 512],
                        og_fm[:, tl * 1024 + c_ * 512 + j * 64:
                              tl * 1024 + c_ * 512 + (j + 1) * 64],
                        attw[:, j * 1024 + nh * 512: j * 1024 + (nh + 1) * 512],
                        start=(j == 0), stop=(j == ND - 1),
                        tile_position=(0, c_ * 64)),
                        inc="s_pe" if (c_ == 1 and j == ND - 1 and nh == 1) else None)
        b.set("wo_mm", "tensor")
        b.waits("vector", "wo_mm", "x_rl")
        b.on("vector", lambda v, t=t: v.tensor_tensor(
            V(xres[:, t * D:(t + 1) * D]), V(xrld[:]), V(psC[:]), ALU.add),
            inc="s_dve")
        b.set("x2_acc", "vector")
        b.set("psC_free", "vector")
        b.set("xrld_free", "vector")
    b.set("x2_done", at=b.res["x2_acc"])
    if stage == "p5":
        return finish_here()
    if stage == "dbgp5":
        raise RuntimeError("dbgp5 tile index out of range")

    for eng in ENGS:
        b.waits(eng, "x2_done", "wo_mm")
    emit_ffn("f2", "x2_done")
    b.waits("sync", "f2_done")
    b.on("sync", lambda s: s.dma_start(
        out=y_out.rearrange("(i p) d -> p i d", p=128),
        in_=xres.rearrange("p (i d) -> p i d", d=D)), inc="s_sp")
    out_val = b.cnt["s_sp"]
    b.on("sync", lambda s, v=out_val: s.wait_ge(b.sem["s_sp"], v))

    nc.detect_race_conditions = False
    blk = es.enter_context(nc.Block())
    with es:
        b.replay(blk)
    return nc


_NC_CACHE = [None]


def kernel(L0_attn_hidden_states, L0_attn_time_maa_x, L0_attn_time_maa_r,
           L0_attn_time_maa_k, L0_attn_time_maa_v, L0_attn_time_maa_w,
           L0_attn_time_maa_g, L0_attn_time_maa_w1, L0_attn_time_maa_w2,
           L0_attn_time_decay, L0_attn_time_decay_w1, L0_attn_time_decay_w2,
           L0_attn_Wq, L0_attn_Wk, L0_attn_Wv, L0_attn_Wo, L0_attn_Wg,
           L0_attn_bg, L0_ffn_w_gate, L0_ffn_w_up, L0_ffn_w_down):
    x = np.ascontiguousarray(np.asarray(L0_attn_hidden_states, dtype=np.float32))
    assert x.shape == (8, T, D)
    vecs0 = np.stack([
        np.asarray(L0_attn_time_maa_x, np.float32).reshape(D),
        np.asarray(L0_attn_time_maa_r, np.float32).reshape(D),
        np.asarray(L0_attn_time_maa_k, np.float32).reshape(D),
        np.asarray(L0_attn_time_maa_v, np.float32).reshape(D),
        np.asarray(L0_attn_time_maa_w, np.float32).reshape(D),
        np.asarray(L0_attn_time_maa_g, np.float32).reshape(D),
        np.asarray(L0_attn_time_decay, np.float32).reshape(D),
        np.asarray(L0_attn_bg, np.float32).reshape(D),
    ], axis=0)
    vecs = np.ascontiguousarray(
        vecs0.reshape(8, 8, 128).transpose(2, 0, 1).reshape(128, 64))
    ident = np.eye(128, dtype=np.float32)
    mask = np.tile(np.tril(np.ones((CH, CH), np.float32)).T * 0 +
                   (np.arange(CH)[:, None] <= np.arange(CH)[None, :]).astype(np.float32),
                   (2, 1))
    common = {
        "w_gate": np.ascontiguousarray(np.asarray(L0_ffn_w_gate, np.float32)),
        "w_up": np.ascontiguousarray(np.asarray(L0_ffn_w_up, np.float32)),
        "w_down": np.ascontiguousarray(np.asarray(L0_ffn_w_down, np.float32)),
        "Wq": np.ascontiguousarray(np.asarray(L0_attn_Wq, np.float32)),
        "Wk": np.ascontiguousarray(np.asarray(L0_attn_Wk, np.float32)),
        "Wv": np.ascontiguousarray(np.asarray(L0_attn_Wv, np.float32)),
        "Wo": np.ascontiguousarray(np.asarray(L0_attn_Wo, np.float32)),
        "Wg": np.ascontiguousarray(np.asarray(L0_attn_Wg, np.float32)),
        "maa_w1": np.ascontiguousarray(np.asarray(L0_attn_time_maa_w1, np.float32)),
        "maa_w2": np.ascontiguousarray(np.asarray(L0_attn_time_maa_w2, np.float32)),
        "dec_w1": np.ascontiguousarray(np.asarray(L0_attn_time_decay_w1, np.float32)),
        "dec_w2": np.ascontiguousarray(np.asarray(L0_attn_time_decay_w2, np.float32)),
        "vecs": vecs,
        "ident": ident,
        "mask": np.ascontiguousarray(mask),
    }
    in_maps = [dict(common, x_in=np.ascontiguousarray(x[i])) for i in range(8)]
    nc = build()
    res = run_bass_kernel_spmd(nc, in_maps, core_ids=list(range(8)))
    out = np.stack([np.asarray(r["y_out"]) for r in res.results], axis=0)
    return out.astype(np.float32)


if __name__ == "__main__":
    import sys
    if "--build" in sys.argv:
        nc = build()
        total = sum(len(v) for v in nc.m.funcs[0].blocks[0].instructions) if False else None
        print("build ok")


# version 04
# speedup vs baseline: 1.4434x; 1.4434x over previous
import numpy as np
from contextlib import ExitStack

import concourse.bass as bass
import concourse.mybir as mybir
from concourse.bass_utils import run_bass_kernel_spmd

F32 = mybir.dt.float32
BF16 = mybir.dt.bfloat16
ALU = mybir.AluOpType
AF = mybir.ActivationFunctionType
AX = mybir.AxisListType

T, D, I = 2048, 1024, 2816
NT, ND, NI = T // 128, D // 128, I // 128
CH = 64
LN8 = float(np.log(np.float64(8.0)))
R6 = float(np.float32(1.0 / 6.0))
R1024 = float(np.float32(1.0 / 1024.0))
EPS = 1e-5
ICHUNKS = [(k * 512, 512) for k in range(5)] + [(2560, 256)]
ENGS = ("sync", "gpsimd", "tensor", "vector", "scalar")


def V(ap):
    sh = ap.shape
    if len(sh) == 2 and sh[1] > 512:
        n = sh[1]
        z = max(d for d in range(1, 513) if n % d == 0)
        return ap.rearrange("p (a z) -> p a z", z=z)
    return ap


class B:

    def __init__(self, nc, es):
        self.nc = nc
        self.q = {e: [] for e in ENGS}
        names = ("s_ld", "s_g", "s_pe", "s_dve", "s_act", "s_sp", "s_og")
        self.sem = {n: es.enter_context(nc.semaphore(n)) for n in names}
        self.cnt = {n: 0 for n in names}
        self.res = {}

    def on(self, eng, fn, inc=None, inc_n=None):
        if inc is None:
            self.q[eng].append(fn)
            return None
        n = inc_n if inc_n is not None else (16 if eng in ("sync", "gpsimd") else 1)
        self.cnt[inc] += n

        def thunk(e, fn=fn, s=self.sem[inc], n=n):
            fn(e).then_inc(s, n)
        self.q[eng].append(thunk)
        return self.cnt[inc]

    def now(self, eng):
        s = {"sync": "s_ld", "gpsimd": "s_g", "tensor": "s_pe",
             "vector": "s_dve", "scalar": "s_act"}[eng]
        return (s, self.cnt[s])

    def set(self, res, eng=None, at=None):
        self.res[res] = at if at is not None else self.now(eng)

    def wait(self, eng, res):
        if res not in self.res:
            return
        sname, val = self.res[res]
        if val > 0:
            self.q[eng].append(lambda e, s=self.sem[sname], v=val: e.wait_ge(s, v))

    def waits(self, eng, *ress):
        best = {}
        for r in ress:
            if r in self.res:
                sname, val = self.res[r]
                best[sname] = max(best.get(sname, 0), val)
        for sname, val in best.items():
            if val > 0:
                self.q[eng].append(lambda e, s=self.sem[sname], v=val: e.wait_ge(s, v))

    def replay(self, blk):
        for eng in ENGS:
            thunks = self.q[eng]

            def body(e, thunks=thunks):
                for t in thunks:
                    t(e)
            getattr(blk, eng)(body)


def build(stage="full"):
    nc = bass.Bass("TRN2", target_bir_lowering=False, debug=False)

    x_in = nc.dram_tensor("x_in", [T, D], F32, kind="ExternalInput").ap()
    y_out = nc.dram_tensor("y_out", [T, D], F32, kind="ExternalOutput").ap()
    wg_d = nc.dram_tensor("w_gate", [I, D], F32, kind="ExternalInput").ap()
    wu_d = nc.dram_tensor("w_up", [I, D], F32, kind="ExternalInput").ap()
    wd_d = nc.dram_tensor("w_down", [D, I], F32, kind="ExternalInput").ap()
    W_d = {f: nc.dram_tensor(f, [D, D], F32, kind="ExternalInput").ap()
           for f in ("Wq", "Wk", "Wv", "Wo", "Wg")}
    w1_d = nc.dram_tensor("maa_w1", [D, 160], F32, kind="ExternalInput").ap()
    w2_d = nc.dram_tensor("maa_w2", [5, 32, D], F32, kind="ExternalInput").ap()
    dw1_d = nc.dram_tensor("dec_w1", [D, 64], F32, kind="ExternalInput").ap()
    dw2_d = nc.dram_tensor("dec_w2", [64, D], F32, kind="ExternalInput").ap()
    vecs_d = nc.dram_tensor("vecs", [128, 64], F32, kind="ExternalInput").ap()
    ident_d = nc.dram_tensor("ident", [128, 128], F32, kind="ExternalInput").ap()
    mask_d = nc.dram_tensor("mask", [128, CH], F32, kind="ExternalInput").ap()

    wgT_s = nc.dram_tensor("wgT_s", [D, I], BF16, kind="Internal").ap()
    wuT_s = nc.dram_tensor("wuT_s", [D, I], BF16, kind="Internal").ap()
    wdT_s = nc.dram_tensor("wdT_s", [I, D], BF16, kind="Internal").ap()
    xsp_s = nc.dram_tensor("xsp_s", [128, NT * D], F32, kind="Internal").ap()
    gsp_s = nc.dram_tensor("gsp_s", [128, ND * T], BF16, kind="Internal").ap()

    es = ExitStack()

    def sbt(name, p, n, dt=F32):
        return es.enter_context(nc.sbuf_tensor(name, [p, n], dt))

    def pst_(name, p, n, dt=F32):
        return es.enter_context(nc.psum_tensor(name, [p, n], dt))

    AR0 = sbt("AR0", 128, 16384)
    AR1 = sbt("AR1", 128, 8192)
    AR2 = sbt("AR2", 128, 8192)
    AR3 = sbt("AR3", 128, 8192)
    AR4 = sbt("AR4", 128, 4096)

    ident = sbt("identsb", 128, 128, BF16)
    maskt = sbt("masksb", 128, CH)
    vecs = sbt("vecssb", 128, 64)
    w1sb = sbt("w1sb", 128, 8 * 160, BF16)
    w2sb = sbt("w2sb", 32, 5 * 1024, BF16)
    dw1sb = sbt("dw1sb", 128, 8 * 64, BF16)
    dw2sb = sbt("dw2sb", 64, 1024, BF16)
    tanh1 = sbt("tanh1", 64, T, BF16)
    Ssb = sbt("Ssb", 128, 512, BF16)
    Stmp = sbt("Stmp", 128, 512, BF16)
    mixsb = sbt("mixsb", 128, 512, BF16)
    mix2sb = sbt("mix2sb", 32, 512, BF16)
    cbsb = sbt("cbsb", 128, 8 * CH)
    smalls = sbt("smalls", 128, 64)
    sttsb = sbt("sttsb", 128, 512)
    gstage = sbt("gstage", 128, 512, BF16)
    lnmean = sbt("lnmean", 128, NT)
    lnsum2 = sbt("lnsum2", 128, NT)
    lnrstd = sbt("lnrstd", 128, NT)
    lnnmr = sbt("lnnmr", 128, NT)
    onessb = sbt("onessb", 128, 1)
    q_bm = sbt("q_bm", 128, 88)
    q_sc = sbt("q_sc", 128, 88)
    q_rs = sbt("q_rs", 128, 88)
    q_hs = sbt("q_hs", 128, 88, BF16)
    thrsb = sbt("thrsb", 128, 2)

    psA = pst_("psA", 128, 512)
    psB = pst_("psB", 128, 512)
    psC = pst_("psC", 128, 1024)
    psD = pst_("psD", 128, 512)
    psE = pst_("psE", 128, 512)
    psT = pst_("psT", 128, 1024, BF16)

    b = B(nc, es)
    cstsb = sbt("cstsb", 128, 9)
    nc.const_aps.register(F32, EPS, cstsb[:, 0:1])
    nc.const_aps.register(F32, -LN8, cstsb[:, 1:2])
    QTHR = (5.0, 3.5, 2.5, 1.75, 1.25, 0.75, 0.25)
    for qi_, qt_ in enumerate(QTHR):
        nc.const_aps.register(F32, -qt_, cstsb[:, 2 + qi_:3 + qi_])

    AR0b = AR0[:].bitcast(BF16)
    AR1b = AR1[:].bitcast(BF16)
    AR2b = AR2[:].bitcast(BF16)
    AR3b = AR3[:].bitcast(BF16)
    AR4b = AR4[:].bitcast(BF16)
    xres = AR0[:]
    q_fm = AR0b[:, 0:16384]
    k_fm = AR0b[:, 16384:32768]
    xq_fm = AR1b[:, 0:16384]
    ln1_fm = AR1b[:, 0:16384]
    og2_fm = AR1b[:, 0:8192]
    og1_fm = AR3b[:, 8192:16384]
    attw = AR3b[:, 0:8192]
    v_tm = AR2b[:, 0:8192]
    dec_g = AR2[:, 4096:6144]
    egd_g = AR2[:, 6144:8192]
    wch_g = AR2b[:, 0:4096]
    wch_u = AR2b[:, 4096:8192]
    wch_d = AR2b[:, 8192:12288]
    mixf = AR2b[0:32, 8192:13312]
    ffnsil = AR4[:, 2560:3072]
    expsc = AR4[:, 1024:1536]
    expsc2 = AR4[:, 1536:2048]
    ktm = AR4b[:, 4096:5120]
    qsc = AR4b[:, 5120:5632]
    qss = AR4b[:, 5632:6144]
    ksc = AR4b[:, 6144:6656]
    khc = AR3b[:, 0:512]
    amskA = AR3b[:, 512:1024]
    amskB = AR3b[:, 1024:1536]
    q_y2 = AR2[:, 6144:7552]
    q_sg2 = AR2[:].bitcast(BF16)[:, 15104:16128]
    q_y = AR3[:, 0:1408]
    q_a = AR3[:, 1408:2816]
    q_b = AR3[:, 2816:4224]
    q_A = AR3[:, 4224:5632]
    q_sg = AR3b[:, 11264:12672]
    q_Q = AR3b[:, 12672:14080]
    wq_tm = AR3b[:, 14080:16384]
    lnsq = AR4[:, 0:1024]
    ln_t = AR4[:, 1024:2048]
    xrld = AR4[:, 1024:2048]
    hbuf = AR4[:, 2048:2560]
    lnb_t = AR4b[:, 5120:6144]
    hq_tm = AR4b[:, 6144:6656]
    hq_tm2 = AR3b[:, 15488:16000]
    hq_fm = AR4b[:, 6656:7168]
    tstage = AR4b[:, 7168:8192]
    dx_r = AR4b[:, 0:4096]
    xx_r = AR4b[:, 4096:8192]
    xf_r = AR4b[:, 4096:8192]
    gat_r = AR4b[:, 0:2048]

    def vsl(i):
        return vecs[:, i * 8:(i + 1) * 8]

    maa_x = vsl(0)
    maa_of = {"r": vsl(1), "k": vsl(2), "v": vsl(3), "w": vsl(4), "g": vsl(5)}
    tdec, bgv = vsl(6), vsl(7)

    def fm3(ap, c0, n):
        return ap.rearrange("p (j t) -> p j t", j=ND)[:, :, c0:c0 + n]

    b.on("gpsimd", lambda g: g.dma_start(out=ident[:], in_=ident_d), inc="s_g")
    b.on("gpsimd", lambda g: g.dma_start(out=maskt[:], in_=mask_d), inc="s_g")
    b.on("gpsimd", lambda g: g.dma_start(out=vecs[:], in_=vecs_d), inc="s_g")
    b.on("gpsimd", lambda g: g.dma_start(
        out=w1sb[:].rearrange("p (j n) -> p j n", j=ND),
        in_=w1_d.rearrange("(j p) n -> p j n", p=128)), inc="s_g")
    b.on("gpsimd", lambda g: g.dma_start(
        out=w2sb[:].rearrange("p (f d) -> p f d", f=5),
        in_=w2_d.rearrange("f l d -> l f d")), inc="s_g")
    b.on("gpsimd", lambda g: g.dma_start(
        out=dw1sb[:].rearrange("p (j n) -> p j n", j=ND),
        in_=dw1_d.rearrange("(j p) n -> p j n", p=128)), inc="s_g")
    b.on("gpsimd", lambda g: g.dma_start(out=dw2sb[:], in_=dw2_d), inc="s_g")
    b.set("consts", "gpsimd")

    b.on("sync", lambda s: s.dma_start(
        out=xres.rearrange("p (i d) -> p i d", d=D),
        in_=x_in.rearrange("(i p) d -> p i d", p=128)), inc="s_ld")
    b.set("xload", "sync")
    b.on("vector", lambda v: v.memset(onessb[:], 1.0))
    b.on("vector", lambda v: v.memset(cstsb[:, 0:1], EPS))
    b.on("vector", lambda v: v.memset(cstsb[:, 1:2], -LN8))
    for qi_, qt_ in enumerate(QTHR):
        b.on("vector", lambda v, qi_=qi_, qt_=qt_: v.memset(
            cstsb[:, 2 + qi_:3 + qi_], -qt_))
    for ti_, tv_ in enumerate((1.0, 0.5)):
        b.on("vector", lambda v, ti_=ti_, tv_=tv_: v.memset(
            thrsb[:, ti_:ti_ + 1], tv_))
    b.on("vector", lambda v: v.memset(Ssb[:], 0.0), inc="s_dve")
    b.set("init", "vector")
    b.set("S_ok", "vector")

    qn = [0]

    def emit_quant(x_ap, out_bf, n, ready_res, out_res, ybuf=None):
        p = qn[0] & 1
        qn[0] += 1
        nb = n // 16
        qsg = q_sg if p == 0 else q_sg2
        if ybuf is None:
            ybuf = x_ap
        b.waits("scalar", ready_res, "q_dy")
        b.on("scalar", lambda s: s.activation(q_a[:, 0:n], x_ap, AF.Abs),
             inc="s_act")
        b.set("q_abs", "scalar")
        b.waits("scalar", f"q_sgf{p}")
        b.on("scalar", lambda s, qsg=qsg: s.activation(
            qsg[:, 0:n], x_ap, AF.Sign), inc="s_act")
        b.set("q_act", "scalar")
        b.waits("vector", ready_res)
        x3 = x_ap.rearrange("p (z k) -> p z k", k=16)
        b.on("vector", lambda v: v.tensor_reduce(
            q_bm[:, 0:nb], x3, axis=AX.X, op=ALU.max, apply_absolute_value=True),
            inc="s_dve")
        b.set("q_bmr", "vector")
        b.wait("vector", "q_bmr")
        b.on("vector", lambda v: v.tensor_scalar(
            q_sc[:, 0:nb], q_bm[:, 0:nb], R6, 1e-12, ALU.mult, ALU.max),
            inc="s_dve")
        b.set("q_scd", "vector")
        b.wait("vector", "q_scd")
        b.on("vector", lambda v: v.reciprocal(q_rs[:, 0:nb], q_sc[:, 0:nb]),
             inc="s_dve")
        b.set("q_rsd", "vector")
        b.on("vector", lambda v: v.tensor_scalar(
            q_hs[:, 0:nb], q_sc[:, 0:nb], 0.5, None, ALU.mult))
        b.wait("vector", "q_rsd")
        b.wait("vector", "q_act")
        a3 = q_a[:, 0:n].rearrange("p (z k) -> p z k", k=16)
        y3 = ybuf[:, 0:n].rearrange("p (z k) -> p z k", k=16)
        b.on("vector", lambda v, y3=y3: v.tensor_tensor(
            y3, a3, q_rs[:, 0:nb].unsqueeze(2).broadcast_to([128, nb, 16]),
            ALU.mult), inc="s_dve")
        b.set("q_dy", "vector")
        y, Bc, Ac = V(ybuf[:, 0:n]), V(q_b[:, 0:n]), V(q_A[:, 0:n])
        b.on("vector", lambda v: v.tensor_single_scalar(Bc, y, 5.0, ALU.is_gt))
        b.on("vector", lambda v: v.tensor_single_scalar(Ac, y, 0.25, ALU.is_gt))
        for thB, thA in ((5.0, 0.75), (3.5, 1.25), (2.5, 1.75)):
            b.on("vector", lambda v, th=thB: v.scalar_tensor_tensor(
                Bc, y, th, Bc, ALU.is_gt, ALU.add))
            b.on("vector", lambda v, th=thA: v.scalar_tensor_tensor(
                Ac, y, th, Ac, ALU.is_gt, ALU.add))
        s3 = qsg[:, 0:n].rearrange("p (z k) -> p z k", k=16)
        b.on("vector", lambda v, s3=s3: v.tensor_tensor(
            s3, s3, q_hs[:, 0:nb].unsqueeze(2).broadcast_to([128, nb, 16]),
            ALU.mult))
        b.on("vector", lambda v: v.scalar_tensor_tensor(
            V(q_Q[:, 0:n]), Bc, 2.0, Ac, ALU.mult, ALU.add))
        b.on("vector", lambda v, qsg=qsg: v.tensor_tensor(
            V(out_bf), V(q_Q[:, 0:n]), V(qsg[:, 0:n]), ALU.mult), inc="s_dve")
        b.set(out_res, "vector")
        b.set(f"q_sgf{p}", "vector")
        b.set("q_scratch", "vector")

    def emit_wquant(w, ti, parts):
        for (c0, n) in parts:
            src = {"g": wg_d, "u": wu_d, "d": wd_d}[w]
            p_ = qn[0] & 1
            qld = q_y if p_ == 0 else q_y2
            b.waits("sync", f"q_sgf{p_}")
            b.on("sync", lambda s, src=src, ti=ti, c0=c0, n=n, qld=qld:
                 s.dma_start(out=qld[:, 0:n],
                             in_=src[ti * 128:(ti + 1) * 128, c0:c0 + n]),
                 inc="s_ld")
            b.set("wload", "sync")
            b.wait("vector", "wtp_done")
            emit_quant(qld[:, 0:n], wq_tm[:, 0:n], n, "wload", "wq")
            nblk = n // 128
            for g0 in range(0, nblk, 8):
                gn = min(8, nblk - g0)
                b.waits("tensor", "wq", "psT_free")
                for bb in range(gn):
                    b.on("tensor", lambda t, bb=bb, g0=g0: t.matmul(
                        psT[:, bb * 128:(bb + 1) * 128],
                        wq_tm[:, (g0 + bb) * 128:(g0 + bb + 1) * 128],
                        ident[:], start=True, stop=True, is_transpose=True),
                        inc="s_pe" if bb == gn - 1 else None)
                b.set("wtp", "tensor")
                if g0 + 8 >= nblk:
                    b.set("wtp_done", "tensor")
                b.waits("scalar", "wtp", "tstage_free")
                b.on("scalar", lambda sc, gn=gn: sc.activation(
                    tstage[:, 0:gn * 128], psT[:, 0:gn * 128], AF.Copy),
                    inc="s_act")
                b.set("tev", "scalar")
                b.set("psT_free", "scalar")
                if w in "gu":
                    dst = wgT_s if w == "g" else wuT_s
                    dview = dst.rearrange("(z p) i -> p z i", p=128)[
                        :, g0:g0 + gn, ti * 128:(ti + 1) * 128]
                else:
                    i0 = (c0 // 128) + g0
                    dview = wdT_s.rearrange("(z p) dd -> p z dd", p=128)[
                        :, i0:i0 + gn, ti * 128:(ti + 1) * 128]
                b.wait("sync", "tev")
                b.on("sync", lambda s, dv=dview, gn=gn: s.dma_start(
                    out=dv, in_=tstage[:, 0:gn * 128].rearrange(
                        "p (z m) -> p z m", m=128)), inc="s_sp")
                b.set("tstage_free", at=("s_sp", b.cnt["s_sp"]))
            b.set(f"wst_{w}{ti}p{c0}", at=("s_sp", b.cnt["s_sp"]))

    def emit_ln(x_src_res):
        b.waits("vector", x_src_res)
        b.on("vector", lambda v: v.tensor_reduce(
            lnmean[:], xres.rearrange("p (i d) -> p i d", d=D),
            axis=AX.X, op=ALU.add), inc="s_dve")
        b.set("ln_red", "vector")
        b.wait("vector", "ln_red")
        b.waits("scalar", x_src_res, "lnsq_free")
        for t in range(NT):
            b.on("scalar", lambda s, t=t: s.activation(
                lnsq[:], xres[:, t * D:(t + 1) * D], AF.Square,
                accum_out=lnsum2[:, t:t + 1]),
                inc="s_act" if t == NT - 1 else None)
        b.set("ln_sq", "scalar")
        b.on("vector", lambda v: v.tensor_scalar(
            lnmean[:], lnmean[:], R1024, None, ALU.mult), inc="s_dve")
        b.set("ln_m2", "vector")
        b.wait("vector", "ln_sq")
        b.on("vector", lambda v: v.tensor_scalar(
            lnsum2[:], lnsum2[:], R1024, None, ALU.mult), inc="s_dve")
        b.set("ln_s2", "vector")
        b.wait("vector", "ln_m2")
        b.wait("vector", "ln_s2")
        b.on("vector", lambda v: v.tensor_tensor(
            lnrstd[:], lnmean[:], lnmean[:], ALU.mult), inc="s_dve")
        b.set("ln_msq", "vector")
        b.wait("vector", "ln_msq")
        b.on("vector", lambda v: v.tensor_tensor(
            lnrstd[:], lnsum2[:], lnrstd[:], ALU.subtract), inc="s_dve")
        b.set("ln_var", "vector")
        b.wait("scalar", "ln_var")
        b.on("scalar", lambda s: s.activation(
            lnrstd[:], lnrstd[:], AF.Ln, bias=EPS), inc="s_act")
        b.set("ln_ln", "scalar")
        b.wait("scalar", "ln_ln")
        b.on("scalar", lambda s: s.activation(
            lnrstd[:], lnrstd[:], AF.Exp, scale=-0.5), inc="s_act")
        b.set("ln_rstd", "scalar")
        b.wait("vector", "ln_rstd")
        b.on("vector", lambda v: v.tensor_tensor(
            lnnmr[:], lnmean[:], lnrstd[:], ALU.mult), inc="s_dve")
        b.set("ln_nm1", "vector")
        b.wait("vector", "ln_nm1")
        b.on("vector", lambda v: v.tensor_scalar(
            lnnmr[:], lnnmr[:], -1.0, None, ALU.mult), inc="s_dve")
        b.set("ln_ok", "vector")

    def emit_ln_apply(t, out_ap, out_res):
        b.waits("scalar", "ln_ok", f"lnout_free")
        b.on("scalar", lambda s, t=t: s.activation(
            out_ap, xres[:, t * D:(t + 1) * D], AF.Identity,
            bias=lnnmr[:, t:t + 1], scale=lnrstd[:, t:t + 1]), inc="s_act")
        b.set(out_res, "scalar")

    def emit_tp(src_bf, nblk, dst_fn, dep_res, out_res):
        deps = dep_res if isinstance(dep_res, tuple) else (dep_res,)
        b.waits("tensor", *deps, "psT_free")
        for bb in range(nblk):
            b.on("tensor", lambda te, bb=bb: te.matmul(
                psT[:, bb * 128:(bb + 1) * 128],
                src_bf[:, bb * 128:(bb + 1) * 128],
                ident[:], start=True, stop=True, is_transpose=True),
                inc="s_pe" if bb == nblk - 1 else None)
        b.set("tp_mm", "tensor")
        b.wait("scalar", "tp_mm")
        b.on("scalar", dst_fn, inc="s_act")
        b.set(out_res, "scalar")
        b.set("psT_free", "scalar")

    def emit_ffn(tag, x_res, wprep=False):
        emit_ln(x_res)
        for t in range(NT):
            emit_ln_apply(t, ln_t[:], "ln_ap")
            b.waits("vector", "lnb_free")
            emit_quant(ln_t[:], lnb_t[:], D, "ln_ap", "xq_q", ybuf=lnsq)
            b.set("lnout_free", at=b.res["q_bmr"])
            emit_tp(lnb_t, 8,
                    lambda sc, t=t: sc.activation(
                        fm3(xq_fm, t * 128, 128),
                        psT[:].rearrange("p (z m) -> p z m", m=128), AF.Copy),
                    "xq_q", "xq_ev")
            b.set("lnb_free", at=b.res["tp_mm"])
        b.set("xq_all", at=b.res["xq_ev"])
        if stage == "dbgx":
            return "DBGX"
        pend = [None]
        for ic, (i0, ilen) in enumerate(ICHUNKS):
            nseg = ilen // 128
            if wprep:
                if ic == 0:
                    for ti in range(ND):
                        emit_wquant("d", ti, [(0, 1024)])
                for ti in range(4 * ic, min(4 * (ic + 1), NI)):
                    emit_wquant("g", ti, [(0, 1024)])
                    emit_wquant("u", ti, [(0, 1024)])
                if ic in (0, 1):
                    for ti in range(4 * ic, 4 * ic + 4):
                        emit_wquant("d", ti, [(1024, 896), (1920, 896)])
                last_ti = min(4 * (ic + 1), NI) - 1
                wdep = "wst_d7p0" if ic <= 1 else "wst_d7p1920"
                b.waits("sync", "wch_free", f"wst_u{last_ti}p0", wdep)
            else:
                b.waits("sync", "wch_free", "wq_all_st")
            b.on("sync", lambda s, i0=i0, ilen=ilen: s.dma_start(
                out=wch_g[:, 0:8 * ilen].rearrange("p (j n) -> p j n", j=ND),
                in_=wgT_s.rearrange("(j p) i -> p j i", p=128)[:, :, i0:i0 + ilen]),
                inc="s_ld")
            b.on("sync", lambda s, i0=i0, ilen=ilen: s.dma_start(
                out=wch_u[:, 0:8 * ilen].rearrange("p (j n) -> p j n", j=ND),
                in_=wuT_s.rearrange("(j p) i -> p j i", p=128)[:, :, i0:i0 + ilen]),
                inc="s_ld")
            b.on("sync", lambda s, i0=i0, nseg=nseg: s.dma_start(
                out=wch_d[:, 0:nseg * 1024].rearrange("p (m n) -> p m n", n=1024),
                in_=wdT_s.rearrange("(m p) dd -> p m dd", p=128)[
                    :, i0 // 128:i0 // 128 + nseg, :]), inc="s_ld")
            b.set("wch", "sync")
            for t in range(NT):
                b.waits("tensor", "wch", "xq_all", "psA_free", "psB_free")
                for j in range(ND):
                    b.on("tensor", lambda te, j=j, t=t, ilen=ilen: te.matmul(
                        psA[:, 0:ilen],
                        xq_fm[:, j * T + t * 128: j * T + (t + 1) * 128],
                        wch_g[:, j * ilen:(j + 1) * ilen],
                        start=(j == 0), stop=(j == ND - 1)))
                for j in range(ND):
                    b.on("tensor", lambda te, j=j, t=t, ilen=ilen: te.matmul(
                        psB[:, 0:ilen],
                        xq_fm[:, j * T + t * 128: j * T + (t + 1) * 128],
                        wch_u[:, j * ilen:(j + 1) * ilen],
                        start=(j == 0), stop=(j == ND - 1)),
                        inc="s_pe" if j == ND - 1 else None)
                b.set("gu_mm", "tensor")
                if stage in ("dbgg", "dbgu") and t == 0:
                    b.wait("vector", "gu_mm")
                    pp_dbg = psA if stage == "dbgg" else psB
                    b.on("vector", lambda v, pp_dbg=pp_dbg, ilen=ilen: v.tensor_copy(
                        sttsb[:, 0:ilen], pp_dbg[:, 0:ilen]), inc="s_dve")
                    b.set("dbg_ev", "vector")
                    b.wait("sync", "dbg_ev")
                    b.on("sync", lambda s, ilen=ilen: s.dma_start(
                        out=y_out.rearrange("(i p) d -> p (i d)", p=128)[:, 0:ilen],
                        in_=sttsb[:, 0:ilen]), inc="s_sp")
                    ov = b.cnt["s_sp"]
                    b.on("sync", lambda s, v=ov: s.wait_ge(b.sem["s_sp"], v))
                    return "DBG"
                b.waits("scalar", "gu_mm", "sil_free")
                b.on("scalar", lambda s, ilen=ilen: s.activation(
                    ffnsil[:, 0:ilen], psA[:, 0:ilen], AF.Silu), inc="s_act")
                b.set("sil", "scalar")
                b.set("psA_free", "scalar")
                b.waits("vector", "sil", "hbuf_free")
                b.on("vector", lambda v, ilen=ilen: v.scalar_tensor_tensor(
                    hbuf[:, 0:ilen], ffnsil[:, 0:ilen], 0.5, psB[:, 0:ilen],
                    ALU.mult, ALU.mult), inc="s_dve")
                b.set("hgot", "vector")
                b.set("psB_free", "vector")
                b.set("sil_free", "vector")
                hq = hq_tm if t % 2 == 0 else hq_tm2
                if stage == "dbgh" and t == 0:
                    b.wait("sync", "hgot")
                    b.on("sync", lambda s, ilen=ilen: s.dma_start(
                        out=y_out.rearrange("(i p) d -> p (i d)", p=128)[:, 0:ilen],
                        in_=hbuf[:, 0:ilen]), inc="s_sp")
                    ov = b.cnt["s_sp"]
                    b.on("sync", lambda s, v=ov: s.wait_ge(b.sem["s_sp"], v))
                    return "DBG"
                b.waits("vector", f"hqf{t % 2}")
                emit_quant(hbuf[:, 0:ilen], hq[:, 0:ilen], ilen, "hgot",
                           f"hq_q{t % 2}")
                b.set("hbuf_free", at=b.res["q_act"])

                def emit_tail(t=t, hq=hq, nseg=nseg, ilen=ilen):
                    emit_tp(hq, nseg,
                            lambda sc, nseg=nseg: sc.activation(
                                hq_fm[:, 0:nseg * 128], psT[:, 0:nseg * 128],
                                AF.Copy),
                            f"hq_q{t % 2}", "hq_ev")
                    b.set(f"hqf{t % 2}", at=b.res["tp_mm"])
                    b.waits("tensor", "hq_ev", "psC_free")
                    for nh in range(2):
                        for m in range(nseg):
                            b.on("tensor", lambda te, m=m, nh=nh, nseg=nseg:
                                 te.matmul(
                                     psC[:, nh * 512:(nh + 1) * 512],
                                     hq_fm[:, m * 128:(m + 1) * 128],
                                     wch_d[:, m * 1024 + nh * 512:
                                           m * 1024 + (nh + 1) * 512],
                                     start=(m == 0), stop=(m == nseg - 1)),
                                 inc="s_pe" if (nh == 1 and m == nseg - 1)
                                 else None)
                    b.set("dn_mm", "tensor")
                    b.wait("vector", "dn_mm")
                    b.on("vector", lambda v, t=t: v.tensor_tensor(
                        V(xres[:, t * D:(t + 1) * D]),
                        V(xres[:, t * D:(t + 1) * D]),
                        V(psC[:]), ALU.add), inc="s_dve")
                    b.set("x_acc", "vector")
                    b.set("psC_free", "vector")

                if pend[0] is not None:
                    pend[0]()
                pend[0] = emit_tail
            pend[0]()
            pend[0] = None
            b.set("wch_free", at=b.res["dn_mm"])
        b.set(f"{tag}_done", at=b.res["x_acc"])

    def dump_and_finish(src_ap, n_bf16_cols, deps):
        b.waits("gpsimd", *deps)
        ni = max(1, n_bf16_cols // D)
        b.on("gpsimd", lambda g: g.dma_start(
            out=y_out.rearrange("(i p) d -> p i d", p=128)[:, 0:ni, :],
            in_=src_ap[:, 0:n_bf16_cols].rearrange("p (i d) -> p i d", d=min(D, n_bf16_cols))), inc="s_g")
        ov = b.cnt["s_g"]
        b.on("gpsimd", lambda g, v=ov: g.wait_ge(b.sem["s_g"], v))
        nc.detect_race_conditions = False
        blk2 = es.enter_context(nc.Block())
        with es:
            b.replay(blk2)
        return nc

    def dump_dram_and_finish(dram_ap, deps):
        b.waits("gpsimd", *deps)
        b.on("gpsimd", lambda g: g.dma_start(
            out=y_out[0:1024, :], in_=dram_ap), inc="s_g")
        ov = b.cnt["s_g"]
        b.on("gpsimd", lambda g, v=ov: g.wait_ge(b.sem["s_g"], v))
        nc.detect_race_conditions = False
        blk2 = es.enter_context(nc.Block())
        with es:
            b.replay(blk2)
        return nc

    def finish_here():
        b.waits("sync", "wq_all_st", "x_acc", "xload", "x2_done", "f2_done")
        b.on("sync", lambda s: s.dma_start(
            out=y_out.rearrange("(i p) d -> p i d", p=128),
            in_=xres.rearrange("p (i d) -> p i d", d=D)), inc="s_sp")
        ov = b.cnt["s_sp"]
        b.on("sync", lambda s, v=ov: s.wait_ge(b.sem["s_sp"], v))
        nc.detect_race_conditions = False
        blk2 = es.enter_context(nc.Block())
        with es:
            b.replay(blk2)
        return nc

    b.wait("vector", "xload")
    b.wait("scalar", "xload")
    b.wait("vector", "consts")
    b.wait("scalar", "consts")
    if stage == "p0":
        return finish_here()
    if stage == "dbgln":
        emit_ln("xload")
        b.waits("sync", "ln_ok", "ln_rstd")
        for idx, tsr in enumerate((lnmean, lnsum2, lnrstd, lnnmr)):
            b.on("sync", lambda s, t=tsr, idx=idx: s.dma_start(
                out=y_out.rearrange("(i p) d -> p i d", p=128)[:, idx, 0:NT],
                in_=t[:]), inc="s_sp")
        for k in range(4):
            b.on("sync", lambda s, k=k: s.dma_start(
                out=y_out.rearrange("(i p) d -> p i d", p=128)[:, 4 + k, 0:NT],
                in_=xres[:, (12 + k) * 1024:(12 + k) * 1024 + NT]), inc="s_sp")
        ov = b.cnt["s_sp"]
        b.on("sync", lambda s, v=ov: s.wait_ge(b.sem["s_sp"], v))
        nc.detect_race_conditions = False
        blk2 = es.enter_context(nc.Block())
        with es:
            b.replay(blk2)
        return nc
    rr = emit_ffn("f1", "xload", wprep=True)
    b.set("wq_all_st", at=("s_sp", b.cnt["s_sp"]))
    if rr == "DBG":
        nc.detect_race_conditions = False
        blk2 = es.enter_context(nc.Block())
        with es:
            b.replay(blk2)
        return nc
    if rr == "DBGX":
        return dump_and_finish(xq_fm, 16384, ["xq_all"])
    if stage == "dbgw":
        return dump_dram_and_finish(wgT_s[0:1024, 0:1024], ["wq_all_st"])
    if stage == "dbgwd":
        return dump_dram_and_finish(wdT_s[0:1024, 0:1024], ["wq_all_st"])
    if stage == "f1":
        return finish_here()

    emit_ln("f1_done")
    for t in range(NT):
        b.waits("scalar", "lnb_free")
        emit_ln_apply(t, lnb_t[:], "l1_ap")
        b.set("lnout_free", at=b.res["l1_ap"])
        emit_tp(lnb_t, 8,
                lambda sc, t=t: sc.activation(
                    fm3(ln1_fm, t * 128, 128),
                    psT[:].rearrange("p (z m) -> p z m", m=128), AF.Copy),
                "l1_ap", "l1_ev")
        b.set("lnb_free", at=b.res["tp_mm"])
    b.set("ln1_all", at=b.res["l1_ev"])
    b.waits("sync", "f1_done")
    b.on("sync", lambda s: s.dma_start(out=xsp_s, in_=xres), inc="s_sp")
    b.set("xspill", at=("s_sp", b.cnt["s_sp"]))
    if stage == "ln1":
        return finish_here()

    FIDX = {"r": 0, "k": 1, "v": 2, "g": 3, "w": 4}

    def emit_dx(c0):
        dx3 = dx_r.rearrange("p (j t) -> p j t", j=ND)
        ln3 = ln1_fm.rearrange("p (j t) -> p j t", j=ND)
        if c0 == 0:
            b.on("vector", lambda v: v.tensor_scalar(
                dx3[:, :, 0:1], ln3[:, :, 0:1], -1.0, None, ALU.mult))
            b.on("vector", lambda v: v.tensor_tensor(
                dx3[:, :, 1:512], ln3[:, :, 0:511], ln3[:, :, 1:512],
                ALU.subtract), inc="s_dve")
        else:
            b.on("vector", lambda v, c0=c0: v.tensor_tensor(
                dx3[:, :, 0:512], ln3[:, :, c0 - 1:c0 + 511],
                ln3[:, :, c0:c0 + 512], ALU.subtract), inc="s_dve")
        b.set("dx", "vector")

    def emit_mixpre(half, tc):
        c0 = half * 1024 + tc * 512
        b.waits("vector", "ln1_all", "dx_free", "xx_free", "su_mm")
        emit_dx(c0)
        xx3 = xx_r.rearrange("p (j t) -> p j t", j=ND)
        dx3 = dx_r.rearrange("p (j t) -> p j t", j=ND)
        ln3 = ln1_fm.rearrange("p (j t) -> p j t", j=ND)
        b.on("vector", lambda v: v.tensor_tensor(
            xx3, dx3, maa_x.unsqueeze(2).broadcast_to([128, ND, 512]), ALU.mult))
        b.on("vector", lambda v, c0=c0: v.tensor_tensor(
            xx3, xx3, ln3[:, :, c0:c0 + 512], ALU.add), inc="s_dve")
        b.set("xxx", "vector")
        b.set("dx_free", "vector")
        b.waits("tensor", "xxx", "consts", "psA_free", "psE_free")
        for j in range(ND):
            b.on("tensor", lambda te, j=j: te.matmul(
                psA[:], w1sb[:, j * 160:j * 160 + 128],
                xx_r[:, j * 512:(j + 1) * 512],
                start=(j == 0), stop=(j == ND - 1)))
        for j in range(ND):
            b.on("tensor", lambda te, j=j: te.matmul(
                psE[0:32, :], w1sb[:, j * 160 + 128:(j + 1) * 160],
                xx_r[:, j * 512:(j + 1) * 512],
                start=(j == 0), stop=(j == ND - 1)),
                inc="s_pe" if j == ND - 1 else None)
        b.set("mix_mm", "tensor")
        b.set("xx_free", "tensor")
        b.waits("scalar", "mix_mm", "mixsb_free")
        b.on("scalar", lambda s: s.activation(mixsb[:], psA[:], AF.Tanh))
        b.on("scalar", lambda s: s.activation(mix2sb[:], psE[0:32, :], AF.Tanh),
             inc="s_act")
        b.set("mix_ev", "scalar")
        b.set("psA_free", "scalar")
        b.set("psE_free", "scalar")
        b.wait("sync", "mix_ev")
        for fb in range(4):
            b.on("sync", lambda s, fb=fb, tc=tc: s.dma_start(
                out=mixf[:, (fb * 2 + tc) * 512:(fb * 2 + tc + 1) * 512],
                in_=mixsb[fb * 32:(fb + 1) * 32, :]), inc="s_ld")
        b.on("sync", lambda s, tc=tc: s.dma_start(
            out=mixf[:, (4 * 2 + tc) * 512:(4 * 2 + tc + 1) * 512],
            in_=mix2sb[:]), inc="s_ld")
        b.set("mix_dma", "sync")
        b.set("mixsb_free", at=b.res["mix_dma"])

    def emit_xf(f, half, tc):
        c0 = half * 1024 + tc * 512
        b.waits("vector", "xf_free", "dx_free", "su_mm")
        emit_dx(c0)
        fi = FIDX[f]
        for j in range(ND):
            pp = psD if j % 2 == 0 else psB
            pfree = "psD_free" if j % 2 == 0 else "psB_free"
            b.waits("tensor", pfree, "mix_dma", "consts",
                    "dx" if j == 0 else "noop")
            b.on("tensor", lambda te, pp=pp, j=j, fi=fi, tc=tc: te.matmul(
                pp[:], w2sb[:, fi * 1024 + j * 128:fi * 1024 + (j + 1) * 128],
                mixf[:, (fi * 2 + tc) * 512:(fi * 2 + tc + 1) * 512],
                start=True, stop=True), inc="s_pe")
            b.set("mf_mm", "tensor")
            b.waits("vector", "mf_mm", "sttsb_free")
            b.on("vector", lambda v, pp=pp, f=f, j=j: v.scalar_tensor_tensor(
                sttsb[:], pp[:], maa_of[f][:, j:j + 1],
                dx_r[:, j * 512:(j + 1) * 512], ALU.add, ALU.mult),
                inc="s_dve")
            b.set("xf_stt", "vector")
            b.wait("vector", "xf_stt")
            b.on("vector", lambda v, j=j, c0=c0: v.tensor_tensor(
                xf_r[:, j * 512:(j + 1) * 512], sttsb[:],
                ln1_fm[:, j * T + c0:j * T + c0 + 512], ALU.add), inc="s_dve")
            b.set(pfree, "vector")
            b.set("sttsb_free", "vector")
        b.set("xf", at=b.res["sttsb_free"])
        b.set("dx_free", at=b.res["sttsb_free"])

    def emit_proj(f, half, tc):
        c0 = half * 1024 + tc * 512
        if f == "w":
            b.waits("tensor", "xf", "psE_free")
            for j in range(ND):
                b.on("tensor", lambda te, j=j: te.matmul(
                    psE[0:64, :], dw1sb[:, j * 64:(j + 1) * 64],
                    xf_r[:, j * 512:(j + 1) * 512],
                    start=(j == 0), stop=(j == ND - 1)),
                    inc="s_pe" if j == ND - 1 else None)
            b.set("t1_mm", "tensor")
            b.set("xf_free", "tensor")
            b.wait("scalar", "t1_mm")
            b.on("scalar", lambda s, c0=c0: s.activation(
                tanh1[:, c0:c0 + 512], psE[0:64, :], AF.Tanh), inc="s_act")
            b.set("t1_ev", "scalar")
            b.set("psE_free", "scalar")
            return
        if f == "v":
            for st in range(4):
                b.waits("tensor", "xf" if st == 0 else "noop", "psC_free")
                for j in range(ND):
                    for nh in range(2):
                        b.on("tensor", lambda te, j=j, st=st, nh=nh: te.matmul(
                            psC[:, nh * 512:(nh + 1) * 512],
                            xf_r[:, j * 512 + st * 128: j * 512 + (st + 1) * 128],
                            attw[:, j * 1024 + nh * 512: j * 1024 + (nh + 1) * 512],
                            start=(j == 0), stop=(j == ND - 1)),
                            inc="s_pe" if (j == ND - 1 and nh == 1) else None)
                b.set("v_mm", "tensor")
                if st == 3:
                    b.set("xf_free", "tensor")
                b.wait("scalar", "v_mm")
                b.on("scalar", lambda s, st=st, tc=tc: s.activation(
                    v_tm[:, (tc * 4 + st) * 1024:(tc * 4 + st + 1) * 1024],
                    psC[:], AF.Copy), inc="s_act")
                b.set("v_ev", "scalar")
                b.set("psC_free", "scalar")
            return
        for jj in range(ND):
            pp = psA if jj % 2 == 0 else psD
            pfree = "psA_free" if jj % 2 == 0 else "psD_free"
            b.waits("tensor", pfree, "xf" if jj == 0 else "noop")
            for j in range(ND):
                b.on("tensor", lambda te, pp=pp, j=j, jj=jj: te.matmul(
                    pp[:], attw[:, j * 1024 + jj * 128: j * 1024 + (jj + 1) * 128],
                    xf_r[:, j * 512:(j + 1) * 512],
                    start=(j == 0), stop=(j == ND - 1)),
                    inc="s_pe" if j == ND - 1 else None)
            b.set("p_mm", "tensor")
            if jj == ND - 1:
                b.set("xf_free", "tensor")
            b.wait("scalar", "p_mm")
            if f == "g":
                b.waits("scalar", "g_dma")
                b.on("scalar", lambda s, pp=pp, jj=jj: s.activation(
                    gstage[:], pp[:], AF.Silu, bias=bgv[:, jj:jj + 1]),
                    inc="s_act")
                b.set("g_ev", "scalar")
                b.set(pfree, "scalar")
                b.wait("sync", "g_ev")
                b.on("sync", lambda s, jj=jj, c0=c0: s.dma_start(
                    out=gsp_s[:, jj * T + c0: jj * T + c0 + 512],
                    in_=gstage[:]), inc="s_ld")
                b.set("g_dma", "sync")
            else:
                dst = q_fm if f == "r" else k_fm
                b.on("scalar", lambda s, pp=pp, jj=jj, c0=c0, dst=dst: s.activation(
                    dst[:, jj * T + c0: jj * T + c0 + 512], pp[:], AF.Copy),
                    inc="s_act")
                b.set("p_ev", "scalar")
                b.set(pfree, "scalar")

    def emit_S_half(half):
        b.wait("scalar", "xspill")
        for tc in range(2):
            emit_mixpre(half, tc)
        for f in ("w", "r", "k", "v", "g"):
            if f != "w":
                wname = {"r": "Wq", "k": "Wk", "v": "Wv", "g": "Wg"}[f]
                b.waits("gpsimd", "attw_free", "q_scratch", "q_act",
                        "su_mm", "o_mm")
                b.on("gpsimd", lambda g, wname=wname: g.dma_start(
                    out=attw[:].rearrange("p (j n) -> p j n", j=ND),
                    in_=W_d[wname].rearrange("(j p) n -> p j n", p=128)),
                    inc="s_g")
                b.set("w_att", "gpsimd")
            for tc in range(2):
                emit_xf(f, half, tc)
                if f != "w":
                    b.wait("tensor", "w_att")
                emit_proj(f, half, tc)
            if f != "w":
                b.set("attw_free", at=b.res["xf_free"])

    import os
    E_NGRP = int(os.environ.get("KE_NGRP", "4"))
    E_NCH = int(os.environ.get("KE_NCH", "4"))
    E_PARTS = os.environ.get("KE_PARTS", "sakouz123")

    def emit_E_half(half):
        for grp in range(E_NGRP):
            g0 = half * 1024 + grp * 256
            lg = grp * 256
            for j in range(ND):
                pp = psE if j % 2 == 0 else psD
                pfree = "psE_free" if j % 2 == 0 else "psD_free"
                b.waits("tensor", pfree, "t1_ev", "dec_free", "consts")
                b.on("tensor", lambda te, j=j, g0=g0, pp=pp: te.matmul(
                    pp[:, 0:256], dw2sb[:, j * 128:(j + 1) * 128],
                    tanh1[:, g0:g0 + 256], start=True, stop=True), inc="s_pe")
                b.set("dec_mm", "tensor")
                b.wait("vector", "dec_mm")
                b.on("vector", lambda v, j=j, pp=pp: v.tensor_scalar(
                    dec_g[:, j * 256:(j + 1) * 256], pp[:, 0:256],
                    tdec[:, j:j + 1], None, ALU.add), inc="s_dve")
                b.set(pfree, "vector")
                b.set("dec_j", "vector")
            b.set("dec_done", at=b.res["dec_j"])
            b.waits("scalar", "dec_done", "egd_free")
            b.on("scalar", lambda s: s.activation(egd_g[:], dec_g[:], AF.Exp),
                 inc="s_act")
            b.set("egd", "scalar")
            b.wait("scalar", "egd")
            b.on("scalar", lambda s: s.activation(
                dec_g[:], egd_g[:], AF.Exp, scale=-1.0), inc="s_act")
            b.set("emg", "scalar")
            b.set("dec_free", at=("s_act", b.cnt["s_act"]))
            b.wait("vector", "emg")
            b.on("vector", lambda v: v.tensor_scalar(
                V(dec_g[:]), V(dec_g[:]), -1.0, 1.0, ALU.mult, ALU.add))
            kf3 = fm3(k_fm, g0, 256)
            dg3 = dec_g[:].rearrange("p (j t) -> p j t", j=ND)
            b.on("vector", lambda v, kf3=kf3, dg3=dg3: v.tensor_tensor(
                kf3, kf3, dg3, ALU.mult), inc="s_dve")
            b.set("kp", "vector")
            if os.environ.get("KE_DUMMYDMA"):
                b.waits("sync", "g_dma")
                b.on("sync", lambda s, g0=g0: s.dma_start(
                    out=gat_r.rearrange("p (j t) -> p j t", j=ND),
                    in_=gsp_s.rearrange("p (j t) -> p j t", j=ND)[:, :, g0:g0 + 256]),
                    inc="s_ld")
                b.set("g_rl", "sync")
            for cc in range(E_NCH):
                c0 = g0 + cc * 64
                l0 = cc * 64
                par = (c0 // 64) % 2
                tloc = (c0 - half * 1024) // 128
                if os.environ.get("KE_ESER"):
                    b.waits("tensor", "og_ev", "S_ok", "o_sb")
                b.waits("vector", "egd", "cb_free", "expq")
                for j in range(ND):
                    b.on("vector", lambda v, j=j, l0=l0: v.tensor_tensor_scan(
                        cbsb[:, j * 64:(j + 1) * 64],
                        onessb[:].broadcast_to([128, 64]),
                        egd_g[:, j * 256 + l0:j * 256 + l0 + 64],
                        0.0, ALU.mult, ALU.add),
                        inc="s_dve" if j == ND - 1 else None)
                b.set("cb", "vector")
                b.wait("vector", "cb")
                cb3 = cbsb[:].rearrange("p (j t) -> p j t", j=ND)
                cbL = cb3[:, :, 63:64].squeeze(2)
                b.wait("vector", "sm_exp")
                b.on("vector", lambda v, cbL=cbL: v.tensor_scalar(
                    smalls[:, 0:8], cbL, 0.5, None, ALU.mult), inc="s_dve")
                b.set("chalf", "vector")
                b.wait("scalar", "chalf")
                b.on("scalar", lambda s: s.activation(
                    smalls[:, 8:16], smalls[:, 0:8], AF.Exp))
                b.on("scalar", lambda s: s.activation(
                    smalls[:, 16:24], smalls[:, 0:8], AF.Exp, scale=-1.0))
                b.on("scalar", lambda s: s.activation(
                    smalls[:, 24:32], smalls[:, 0:8], AF.Exp, scale=-2.0),
                    inc="s_act")
                b.set("sm_exp", "scalar")
                b.waits("scalar", "expq_free")
                b.on("scalar", lambda s: s.activation(
                    expsc[:], cbsb[:], AF.Exp, scale=-1.0, bias=-LN8),
                    inc="s_act")
                b.set("expq", "scalar")
                b.waits("vector", "chalf", "expk_free")
                b.on("vector", lambda v: v.tensor_tensor(
                    expsc2[:].rearrange("p (j t) -> p j t", j=ND), cb3,
                    smalls[:, 0:8].unsqueeze(2).broadcast_to([128, ND, 64]),
                    ALU.subtract), inc="s_dve")
                b.set("argk", "vector")
                b.set("cb_free", "vector")
                b.wait("scalar", "argk")
                b.on("scalar", lambda s: s.activation(
                    expsc2[:], expsc2[:], AF.Exp), inc="s_act")
                b.set("expk", "scalar")
                q3 = fm3(q_fm, c0, 64)
                k3 = fm3(k_fm, c0, 64)
                e1 = expsc[:].rearrange("p (j t) -> p j t", j=ND)
                e2 = expsc2[:].rearrange("p (j t) -> p j t", j=ND)
                qs3 = qsc[:].rearrange("p (j t) -> p j t", j=ND)
                qss3 = qss[:].rearrange("p (j t) -> p j t", j=ND)
                ks3 = ksc[:].rearrange("p (j t) -> p j t", j=ND)
                kh3 = khc[:].rearrange("p (j t) -> p j t", j=ND)
                b.waits("vector", "expq", "expk", "sm_exp", "kp",
                        "qk_free", "p_mm", "v_mm", "t1_mm")
                b.on("vector", lambda v, qs3=qs3, q3=q3, e1=e1: v.tensor_tensor(
                    qs3, q3, e1, ALU.mult))
                b.on("vector", lambda v, ks3=ks3, k3=k3, e2=e2: v.tensor_tensor(
                    ks3, k3, e2, ALU.mult))
                b.on("vector", lambda v, qss3=qss3, qs3=qs3: v.tensor_tensor(
                    qss3, qs3, smalls[:, 8:16].unsqueeze(2).broadcast_to(
                        [128, ND, 64]), ALU.mult))
                b.on("vector", lambda v, kh3=kh3, ks3=ks3: v.tensor_tensor(
                    kh3, ks3, smalls[:, 16:24].unsqueeze(2).broadcast_to(
                        [128, ND, 64]), ALU.mult), inc="s_dve")
                b.set("qk", "vector")
                b.set("expq_free", "vector")
                b.set("expk_free", "vector")
                if "a" not in E_PARTS:
                    continue
                b.waits("tensor", "qk", "psA_free", "psD_free")
                for hp in range(ND):
                    b.on("tensor", lambda te, hp=hp, par=par: te.matmul(
                        psA[par * 64:par * 64 + 64, hp * 64:(hp + 1) * 64],
                        ksc[0:64, hp * 64:(hp + 1) * 64],
                        qss[0:64, hp * 64:(hp + 1) * 64],
                        start=True, stop=True, tile_position=(0, par * 64)))
                    b.on("tensor", lambda te, hp=hp, par=par: te.matmul(
                        psD[par * 64:par * 64 + 64, hp * 64:(hp + 1) * 64],
                        ksc[64:128, hp * 64:(hp + 1) * 64],
                        qss[64:128, hp * 64:(hp + 1) * 64],
                        start=True, stop=True, tile_position=(64, par * 64)),
                        inc="s_pe" if hp == ND - 1 else None)
                b.set("a_mm", "tensor")
                b.waits("vector", "a_mm", "amsk_free")
                mrow = maskt[64 * par:64 * par + 64, :].unsqueeze(1)\
                    .broadcast_to([64, ND, 64])
                b.on("vector", lambda v, par=par, mrow=mrow: v.tensor_tensor(
                    amskA[64 * par:64 * par + 64, :].rearrange(
                        "p (j t) -> p j t", j=ND),
                    psA[64 * par:64 * par + 64, :].rearrange(
                        "p (j t) -> p j t", j=ND), mrow, ALU.mult))
                b.on("vector", lambda v, par=par, mrow=mrow: v.tensor_tensor(
                    amskB[64 * par:64 * par + 64, :].rearrange(
                        "p (j t) -> p j t", j=ND),
                    psD[64 * par:64 * par + 64, :].rearrange(
                        "p (j t) -> p j t", j=ND), mrow, ALU.mult),
                    inc="s_dve")
                b.set("am_ev", "vector")
                b.set("psA_free", "vector")
                b.set("psD_free", "vector")
                if "k" not in E_PARTS:
                    continue
                b.waits("tensor", "qk", "psT_free")
                for hp in range(ND):
                    b.on("tensor", lambda te, hp=hp, par=par: te.matmul(
                        psT[par * 64:par * 64 + 64, hp * 128:(hp + 1) * 128],
                        khc[:, hp * 64:(hp + 1) * 64], ident[:],
                        start=True, stop=True, is_transpose=True,
                        tile_position=(0, par * 64)),
                        inc="s_pe" if hp == ND - 1 else None)
                b.set("kt_mm", "tensor")
                b.waits("vector", "kt_mm", "ktm_free")
                b.on("vector", lambda v, par=par: v.tensor_copy(
                    ktm[par * 64:par * 64 + 64, :],
                    psT[par * 64:par * 64 + 64, :]), inc="s_dve")
                b.set("kt_ev", "vector")
                b.set("psT_free", "vector")
                if "o" not in E_PARTS:
                    continue
                b.waits("tensor", "am_ev", "psB_free", "S_ok", "v_ev",
                        "psC_free")
                for hp in range(ND):
                    b.on("tensor", lambda te, hp=hp: te.matmul(
                        psB[0:64, hp * 64:(hp + 1) * 64],
                        Ssb[0:64, hp * 64:(hp + 1) * 64],
                        qsc[0:64, hp * 64:(hp + 1) * 64],
                        start=True, stop=True, tile_position=(0, 0)))
                    b.on("tensor", lambda te, hp=hp: te.matmul(
                        psB[64:128, hp * 64:(hp + 1) * 64],
                        Ssb[64:128, hp * 64:(hp + 1) * 64],
                        qsc[64:128, hp * 64:(hp + 1) * 64],
                        start=True, stop=True, tile_position=(64, 64)))
                    b.on("tensor", lambda te, hp=hp, par=par, tloc=tloc: te.matmul(
                        psC[0:64, hp * 64:(hp + 1) * 64],
                        v_tm[par * 64:par * 64 + 64,
                             tloc * 1024 + hp * 128: tloc * 1024 + hp * 128 + 64],
                        amskA[par * 64:par * 64 + 64, hp * 64:(hp + 1) * 64],
                        start=True, stop=True, tile_position=(par * 64, 0)))
                    b.on("tensor", lambda te, hp=hp, par=par, tloc=tloc: te.matmul(
                        psC[64:128, hp * 64:(hp + 1) * 64],
                        v_tm[par * 64:par * 64 + 64,
                             tloc * 1024 + hp * 128 + 64:
                             tloc * 1024 + hp * 128 + 128],
                        amskB[par * 64:par * 64 + 64, hp * 64:(hp + 1) * 64],
                        start=True, stop=True, tile_position=(par * 64, 64)),
                        inc="s_pe" if hp == ND - 1 else None)
                b.set("o_mm", "tensor")
                og_fm = og1_fm if half == 0 else og2_fm
                cloc = (c0 - half * 1024) // 64
                if "u" not in E_PARTS:
                    continue
                b.waits("tensor", "kt_ev", "psE_free", "S_scaled_read",
                        "v_ev")
                for hp in range(ND):
                    b.on("tensor", lambda te, hp=hp, par=par, tloc=tloc: te.matmul(
                        psE[0:64, hp * 64:(hp + 1) * 64],
                        ktm[par * 64:par * 64 + 64, hp * 128:hp * 128 + 64],
                        v_tm[par * 64:par * 64 + 64,
                             tloc * 1024 + hp * 128: tloc * 1024 + hp * 128 + 64],
                        start=True, stop=True, tile_position=(par * 64, 0)))
                    b.on("tensor", lambda te, hp=hp, par=par, tloc=tloc: te.matmul(
                        psE[64:128, hp * 64:(hp + 1) * 64],
                        ktm[par * 64:par * 64 + 64, hp * 128 + 64:hp * 128 + 128],
                        v_tm[par * 64:par * 64 + 64,
                             tloc * 1024 + hp * 128 + 64:
                             tloc * 1024 + hp * 128 + 128],
                        start=True, stop=True, tile_position=(par * 64, 64)),
                        inc="s_pe" if hp == ND - 1 else None)
                b.set("su_mm", "tensor")
                b.set("ktm_free", "tensor")
                b.set("qk_free", "tensor")
                if "z" not in E_PARTS:
                    continue
                if "1" not in E_PARTS:
                    continue
                b.waits("vector", "o_mm", "sm_exp")
                b.on("vector", lambda v: v.tensor_tensor(
                    Stmp[:].rearrange("p (j t) -> p j t", j=ND),
                    Ssb[:].rearrange("p (j t) -> p j t", j=ND),
                    smalls[:, 24:32].unsqueeze(2).broadcast_to([128, ND, 64]),
                    ALU.mult), inc="s_dve")
                b.set("S_scaled_read", "vector")
                if "2" not in E_PARTS:
                    continue
                b.waits("vector", "o_mm")
                b.on("vector", lambda v: v.tensor_copy(sttsb[:], psB[:]),
                     inc="s_dve")
                b.set("o_sb", "vector")
                b.set("psB_free", "vector")
                if "3" not in E_PARTS:
                    continue
                if os.environ.get("KE_OGMEMSET"):
                    b.on("vector", lambda v, og_fm=og_fm, cloc=cloc: v.memset(
                        og_fm[:, cloc * 512:(cloc + 1) * 512],
                        float(100 + cloc)), inc="s_dve")
                else:
                    b.on("vector", lambda v, og_fm=og_fm, cloc=cloc: v.tensor_tensor(
                        V(og_fm[:, cloc * 512:(cloc + 1) * 512]), V(sttsb[:]),
                        V(psC[:, 0:512]), ALU.add), inc="s_dve")
                b.set("og_ev", "vector")
                b.set("psC_free", "vector")
                b.waits("vector", "su_mm")
                b.on("vector", lambda v: v.tensor_tensor(
                    V(Ssb[:]), V(Stmp[:]), V(psE[:]), ALU.add), inc="s_dve")
                b.set("S_ok", "vector")
                b.set("psE_free", "vector")
                b.set("amsk_free", at=b.res["o_mm"])
            b.set("egd_free", at=b.res["cb"])
        b.set(f"og_half{half}", at=b.res["og_ev"])

    b.res.setdefault("noop", ("s_dve", 0))

    for half in range(2):
        emit_S_half(half)
        if stage == f"s{half}":
            return finish_here()
        emit_E_half(half)
        if stage == f"e{half}":
            if os.environ.get("KE_DUMPOG"):
                return dump_and_finish(og1_fm if half == 0 else og2_fm, 8192,
                                       ["og_ev"])
            return finish_here()

    b.waits("gpsimd", "attw_free", "q_scratch", "q_act", "su_mm", "o_mm")
    b.on("gpsimd", lambda g: g.dma_start(
        out=attw[:].rearrange("p (j n) -> p j n", j=ND),
        in_=W_d["Wo"].rearrange("(j p) n -> p j n", p=128)), inc="s_g")
    b.set("wo_ld", "gpsimd")
    gatp5 = AR4b[:, 0:1024]
    _p5ret = [None]
    for t in range(NT):
        og_fm = og1_fm if t < 8 else og2_fm
        tl = t % 8
        b.waits("sync", "xspill", "xrld_free", "qk", "gatp5_free")
        b.on("sync", lambda s, t=t: s.dma_start(
            out=xrld[:], in_=xsp_s[:, t * D:(t + 1) * D]), inc="s_ld")
        for c_ in range(2):
            b.on("sync", lambda s, t=t, c_=c_: s.dma_start(
                out=gatp5[:, c_ * 512:(c_ + 1) * 512].rearrange(
                    "p (j z) -> p j z", j=ND),
                in_=gsp_s.rearrange("p (j tt) -> p j tt", j=ND)[
                    :, :, t * 128 + c_ * 64: t * 128 + c_ * 64 + 64]),
                inc="s_ld")
        b.set("x_rl", "sync")
        b.waits("vector", "x_rl", f"og_half{0 if t < 8 else 1}", "wo_mm")
        b.on("vector", lambda v, og_fm=og_fm, tl=tl: v.tensor_tensor(
            V(og_fm[:, tl * 1024:(tl + 1) * 1024]),
            V(og_fm[:, tl * 1024:(tl + 1) * 1024]), V(gatp5[:]), ALU.mult),
            inc="s_dve")
        b.set("og_gated", "vector")
        b.set("gatp5_free", "vector")
        if stage == "dbgp5" and t == int(os.environ.get("KE_P5T", "1")):
            return dump_and_finish(og_fm[:, tl * 1024:tl * 1024 + 1024], 1024,
                                   ["og_gated"])
        b.waits("tensor", "wo_ld", "psC_free", "og_gated")
        for c_ in range(2):
            for nh in range(2):
                for j in range(ND):
                    b.on("tensor", lambda te, j=j, nh=nh, tl=tl, c_=c_,
                         og_fm=og_fm: te.matmul(
                        psC[c_ * 64:(c_ + 1) * 64, nh * 512:(nh + 1) * 512],
                        og_fm[:, tl * 1024 + c_ * 512 + j * 64:
                              tl * 1024 + c_ * 512 + (j + 1) * 64],
                        attw[:, j * 1024 + nh * 512: j * 1024 + (nh + 1) * 512],
                        start=(j == 0), stop=(j == ND - 1),
                        tile_position=(0, c_ * 64)),
                        inc="s_pe" if (c_ == 1 and j == ND - 1 and nh == 1) else None)
        b.set("wo_mm", "tensor")
        b.waits("vector", "wo_mm", "x_rl")
        b.on("vector", lambda v, t=t: v.tensor_tensor(
            V(xres[:, t * D:(t + 1) * D]), V(xrld[:]), V(psC[:]), ALU.add),
            inc="s_dve")
        b.set("x2_acc", "vector")
        b.set("psC_free", "vector")
        b.set("xrld_free", "vector")
    b.set("x2_done", at=b.res["x2_acc"])
    if stage == "p5":
        return finish_here()
    if stage == "dbgp5":
        raise RuntimeError("dbgp5 tile index out of range")

    for eng in ENGS:
        b.waits(eng, "x2_done", "wo_mm")
    emit_ffn("f2", "x2_done")
    b.waits("sync", "f2_done")
    b.on("sync", lambda s: s.dma_start(
        out=y_out.rearrange("(i p) d -> p i d", p=128),
        in_=xres.rearrange("p (i d) -> p i d", d=D)), inc="s_sp")
    out_val = b.cnt["s_sp"]
    b.on("sync", lambda s, v=out_val: s.wait_ge(b.sem["s_sp"], v))

    nc.detect_race_conditions = False
    blk = es.enter_context(nc.Block())
    with es:
        b.replay(blk)
    return nc


_NC_CACHE = [None]


def kernel(L0_attn_hidden_states, L0_attn_time_maa_x, L0_attn_time_maa_r,
           L0_attn_time_maa_k, L0_attn_time_maa_v, L0_attn_time_maa_w,
           L0_attn_time_maa_g, L0_attn_time_maa_w1, L0_attn_time_maa_w2,
           L0_attn_time_decay, L0_attn_time_decay_w1, L0_attn_time_decay_w2,
           L0_attn_Wq, L0_attn_Wk, L0_attn_Wv, L0_attn_Wo, L0_attn_Wg,
           L0_attn_bg, L0_ffn_w_gate, L0_ffn_w_up, L0_ffn_w_down):
    x = np.ascontiguousarray(np.asarray(L0_attn_hidden_states, dtype=np.float32))
    assert x.shape == (8, T, D)
    vecs0 = np.stack([
        np.asarray(L0_attn_time_maa_x, np.float32).reshape(D),
        np.asarray(L0_attn_time_maa_r, np.float32).reshape(D),
        np.asarray(L0_attn_time_maa_k, np.float32).reshape(D),
        np.asarray(L0_attn_time_maa_v, np.float32).reshape(D),
        np.asarray(L0_attn_time_maa_w, np.float32).reshape(D),
        np.asarray(L0_attn_time_maa_g, np.float32).reshape(D),
        np.asarray(L0_attn_time_decay, np.float32).reshape(D),
        np.asarray(L0_attn_bg, np.float32).reshape(D),
    ], axis=0)
    vecs = np.ascontiguousarray(
        vecs0.reshape(8, 8, 128).transpose(2, 0, 1).reshape(128, 64))
    ident = np.eye(128, dtype=np.float32)
    mask = np.tile(np.tril(np.ones((CH, CH), np.float32)).T * 0 +
                   (np.arange(CH)[:, None] <= np.arange(CH)[None, :]).astype(np.float32),
                   (2, 1))
    common = {
        "w_gate": np.ascontiguousarray(np.asarray(L0_ffn_w_gate, np.float32)),
        "w_up": np.ascontiguousarray(np.asarray(L0_ffn_w_up, np.float32)),
        "w_down": np.ascontiguousarray(np.asarray(L0_ffn_w_down, np.float32)),
        "Wq": np.ascontiguousarray(np.asarray(L0_attn_Wq, np.float32)),
        "Wk": np.ascontiguousarray(np.asarray(L0_attn_Wk, np.float32)),
        "Wv": np.ascontiguousarray(np.asarray(L0_attn_Wv, np.float32)),
        "Wo": np.ascontiguousarray(np.asarray(L0_attn_Wo, np.float32)),
        "Wg": np.ascontiguousarray(np.asarray(L0_attn_Wg, np.float32)),
        "maa_w1": np.ascontiguousarray(np.asarray(L0_attn_time_maa_w1, np.float32)),
        "maa_w2": np.ascontiguousarray(np.asarray(L0_attn_time_maa_w2, np.float32)),
        "dec_w1": np.ascontiguousarray(np.asarray(L0_attn_time_decay_w1, np.float32)),
        "dec_w2": np.ascontiguousarray(np.asarray(L0_attn_time_decay_w2, np.float32)),
        "vecs": vecs,
        "ident": ident,
        "mask": np.ascontiguousarray(mask),
    }
    in_maps = [dict(common, x_in=np.ascontiguousarray(x[i])) for i in range(8)]
    nc = build()
    res = run_bass_kernel_spmd(nc, in_maps, core_ids=list(range(8)))
    out = np.stack([np.asarray(r["y_out"]) for r in res.results], axis=0)
    return out.astype(np.float32)


if __name__ == "__main__":
    import sys
    if "--build" in sys.argv:
        nc = build()
        total = sum(len(v) for v in nc.m.funcs[0].blocks[0].instructions) if False else None
        print("build ok")


# version 05
# speedup vs baseline: 1.4508x; 1.0051x over previous
import numpy as np
from contextlib import ExitStack

import concourse.bass as bass
import concourse.mybir as mybir
from concourse.bass_utils import run_bass_kernel_spmd

F32 = mybir.dt.float32
BF16 = mybir.dt.bfloat16
ALU = mybir.AluOpType
AF = mybir.ActivationFunctionType
AX = mybir.AxisListType

T, D, I = 2048, 1024, 2816
NT, ND, NI = T // 128, D // 128, I // 128
CH = 64
LN8 = float(np.log(np.float64(8.0)))
R6 = float(np.float32(1.0 / 6.0))
R1024 = float(np.float32(1.0 / 1024.0))
EPS = 1e-5
ICHUNKS = [(k * 512, 512) for k in range(5)] + [(2560, 256)]
ENGS = ("sync", "gpsimd", "tensor", "vector", "scalar")


def V(ap):
    sh = ap.shape
    if len(sh) == 2 and sh[1] > 512:
        n = sh[1]
        z = max(d for d in range(1, 513) if n % d == 0)
        return ap.rearrange("p (a z) -> p a z", z=z)
    return ap


class B:

    def __init__(self, nc, es):
        self.nc = nc
        self.q = {e: [] for e in ENGS}
        names = ("s_ld", "s_g", "s_pe", "s_dve", "s_act", "s_sp", "s_og")
        self.sem = {n: es.enter_context(nc.semaphore(n)) for n in names}
        self.cnt = {n: 0 for n in names}
        self.res = {}

    def on(self, eng, fn, inc=None, inc_n=None):
        if inc is None:
            self.q[eng].append(fn)
            return None
        n = inc_n if inc_n is not None else (16 if eng in ("sync", "gpsimd") else 1)
        self.cnt[inc] += n

        def thunk(e, fn=fn, s=self.sem[inc], n=n):
            fn(e).then_inc(s, n)
        self.q[eng].append(thunk)
        return self.cnt[inc]

    def now(self, eng):
        s = {"sync": "s_ld", "gpsimd": "s_g", "tensor": "s_pe",
             "vector": "s_dve", "scalar": "s_act"}[eng]
        return (s, self.cnt[s])

    def set(self, res, eng=None, at=None):
        self.res[res] = at if at is not None else self.now(eng)

    def wait(self, eng, res):
        if res not in self.res:
            return
        sname, val = self.res[res]
        if val > 0:
            self.q[eng].append(lambda e, s=self.sem[sname], v=val: e.wait_ge(s, v))

    def waits(self, eng, *ress):
        best = {}
        for r in ress:
            if r in self.res:
                sname, val = self.res[r]
                best[sname] = max(best.get(sname, 0), val)
        for sname, val in best.items():
            if val > 0:
                self.q[eng].append(lambda e, s=self.sem[sname], v=val: e.wait_ge(s, v))

    def replay(self, blk):
        for eng in ENGS:
            thunks = self.q[eng]

            def body(e, thunks=thunks):
                for t in thunks:
                    t(e)
            getattr(blk, eng)(body)


def build(stage="full"):
    nc = bass.Bass("TRN2", target_bir_lowering=False, debug=False)

    x_in = nc.dram_tensor("x_in", [T, D], F32, kind="ExternalInput").ap()
    y_out = nc.dram_tensor("y_out", [T, D], F32, kind="ExternalOutput").ap()
    wg_d = nc.dram_tensor("w_gate", [I, D], F32, kind="ExternalInput").ap()
    wu_d = nc.dram_tensor("w_up", [I, D], F32, kind="ExternalInput").ap()
    wd_d = nc.dram_tensor("w_down", [D, I], F32, kind="ExternalInput").ap()
    W_d = {f: nc.dram_tensor(f, [D, D], F32, kind="ExternalInput").ap()
           for f in ("Wq", "Wk", "Wv", "Wo", "Wg")}
    w1_d = nc.dram_tensor("maa_w1", [D, 160], F32, kind="ExternalInput").ap()
    w2_d = nc.dram_tensor("maa_w2", [5, 32, D], F32, kind="ExternalInput").ap()
    dw1_d = nc.dram_tensor("dec_w1", [D, 64], F32, kind="ExternalInput").ap()
    dw2_d = nc.dram_tensor("dec_w2", [64, D], F32, kind="ExternalInput").ap()
    vecs_d = nc.dram_tensor("vecs", [128, 64], F32, kind="ExternalInput").ap()
    ident_d = nc.dram_tensor("ident", [128, 128], F32, kind="ExternalInput").ap()
    mask_d = nc.dram_tensor("mask", [128, CH], F32, kind="ExternalInput").ap()

    wgT_s = nc.dram_tensor("wgT_s", [D, I], BF16, kind="Internal").ap()
    wuT_s = nc.dram_tensor("wuT_s", [D, I], BF16, kind="Internal").ap()
    wdT_s = nc.dram_tensor("wdT_s", [I, D], BF16, kind="Internal").ap()
    xsp_s = nc.dram_tensor("xsp_s", [128, NT * D], F32, kind="Internal").ap()
    gsp_s = nc.dram_tensor("gsp_s", [128, ND * T], BF16, kind="Internal").ap()

    es = ExitStack()

    def sbt(name, p, n, dt=F32):
        return es.enter_context(nc.sbuf_tensor(name, [p, n], dt))

    def pst_(name, p, n, dt=F32):
        return es.enter_context(nc.psum_tensor(name, [p, n], dt))

    AR0 = sbt("AR0", 128, 16384)
    AR1 = sbt("AR1", 128, 8192)
    AR2 = sbt("AR2", 128, 8192)
    AR3 = sbt("AR3", 128, 8192)
    AR4 = sbt("AR4", 128, 4096)

    ident = sbt("identsb", 128, 128, BF16)
    maskt = sbt("masksb", 128, CH)
    vecs = sbt("vecssb", 128, 64)
    w1sb = sbt("w1sb", 128, 8 * 160, BF16)
    w2sb = sbt("w2sb", 32, 5 * 1024, BF16)
    dw1sb = sbt("dw1sb", 128, 8 * 64, BF16)
    dw2sb = sbt("dw2sb", 64, 1024, BF16)
    tanh1 = sbt("tanh1", 64, T, BF16)
    Ssb = sbt("Ssb", 128, 512, BF16)
    Stmp = sbt("Stmp", 128, 512, BF16)
    mixsb = sbt("mixsb", 128, 512, BF16)
    mix2sb = sbt("mix2sb", 32, 512, BF16)
    cbsb = sbt("cbsb", 128, 8 * CH)
    smalls = sbt("smalls", 128, 64)
    sttsb = sbt("sttsb", 128, 512)
    gstage = sbt("gstage", 128, 512, BF16)
    lnmean = sbt("lnmean", 128, NT)
    lnsum2 = sbt("lnsum2", 128, NT)
    lnrstd = sbt("lnrstd", 128, NT)
    lnnmr = sbt("lnnmr", 128, NT)
    onessb = sbt("onessb", 128, 1)
    q_bm = sbt("q_bm", 128, 88)
    q_sc = sbt("q_sc", 128, 88)
    q_rs = sbt("q_rs", 128, 88)
    q_hs = sbt("q_hs", 128, 88, BF16)
    thrsb = sbt("thrsb", 128, 2)

    psA = pst_("psA", 128, 512)
    psB = pst_("psB", 128, 512)
    psC = pst_("psC", 128, 1024)
    psD = pst_("psD", 128, 512)
    psE = pst_("psE", 128, 512)
    psT = pst_("psT", 128, 1024, BF16)

    b = B(nc, es)
    cstsb = sbt("cstsb", 128, 9)
    nc.const_aps.register(F32, EPS, cstsb[:, 0:1])
    nc.const_aps.register(F32, -LN8, cstsb[:, 1:2])
    QTHR = (5.0, 3.5, 2.5, 1.75, 1.25, 0.75, 0.25)
    for qi_, qt_ in enumerate(QTHR):
        nc.const_aps.register(F32, -qt_, cstsb[:, 2 + qi_:3 + qi_])

    AR0b = AR0[:].bitcast(BF16)
    AR1b = AR1[:].bitcast(BF16)
    AR2b = AR2[:].bitcast(BF16)
    AR3b = AR3[:].bitcast(BF16)
    AR4b = AR4[:].bitcast(BF16)
    xres = AR0[:]
    q_fm = AR0b[:, 0:16384]
    k_fm = AR0b[:, 16384:32768]
    xq_fm = AR1b[:, 0:16384]
    ln1_fm = AR1b[:, 0:16384]
    og2_fm = AR1b[:, 0:8192]
    og1_fm = AR3b[:, 8192:16384]
    attw = AR3b[:, 0:8192]
    v_tm = AR2b[:, 0:8192]
    dec_g = AR2[:, 4096:6144]
    egd_g = AR2[:, 6144:8192]
    wch_g = AR2b[:, 0:4096]
    wch_u = AR2b[:, 4096:8192]
    wch_d = AR2b[:, 8192:12288]
    mixf = AR2b[0:32, 8192:13312]
    ffnsil = AR4[:, 2560:3072]
    expsc = AR4[:, 1024:1536]
    expsc2 = AR4[:, 1536:2048]
    ktm = AR4b[:, 4096:5120]
    qsc = AR4b[:, 5120:5632]
    qss = AR4b[:, 5632:6144]
    ksc = AR4b[:, 6144:6656]
    khc = AR3b[:, 0:512]
    amskA = AR3b[:, 512:1024]
    amskB = AR3b[:, 1024:1536]
    q_y2 = AR2[:, 6144:7552]
    q_sg2 = AR2[:].bitcast(BF16)[:, 15104:16128]
    q_y = AR3[:, 0:1408]
    q_a = AR3[:, 1408:2816]
    q_b = AR3[:, 2816:4224]
    q_A = AR3[:, 4224:5632]
    q_sg = AR3b[:, 11264:12672]
    q_Q = AR3b[:, 12672:14080]
    wq_tm = AR3b[:, 14080:16384]
    lnsq = AR4[:, 0:1024]
    ln_t = AR4[:, 1024:2048]
    xrld = AR4[:, 1024:2048]
    hbuf = AR4[:, 2048:2560]
    lnb_t = AR4b[:, 5120:6144]
    hq_tm = AR4b[:, 6144:6656]
    hq_tm2 = AR3b[:, 15488:16000]
    hq_fm = AR4b[:, 6656:7168]
    tstage = AR4b[:, 7168:8192]
    dx_r = AR4b[:, 0:4096]
    xx_r = AR4b[:, 4096:8192]
    xf_r = AR4b[:, 4096:8192]
    gat_r = AR4b[:, 0:2048]

    def vsl(i):
        return vecs[:, i * 8:(i + 1) * 8]

    maa_x = vsl(0)
    maa_of = {"r": vsl(1), "k": vsl(2), "v": vsl(3), "w": vsl(4), "g": vsl(5)}
    tdec, bgv = vsl(6), vsl(7)

    def fm3(ap, c0, n):
        return ap.rearrange("p (j t) -> p j t", j=ND)[:, :, c0:c0 + n]

    b.on("gpsimd", lambda g: g.dma_start(out=ident[:], in_=ident_d), inc="s_g")
    b.on("gpsimd", lambda g: g.dma_start(out=maskt[:], in_=mask_d), inc="s_g")
    b.on("gpsimd", lambda g: g.dma_start(out=vecs[:], in_=vecs_d), inc="s_g")
    b.on("gpsimd", lambda g: g.dma_start(
        out=w1sb[:].rearrange("p (j n) -> p j n", j=ND),
        in_=w1_d.rearrange("(j p) n -> p j n", p=128)), inc="s_g")
    b.on("gpsimd", lambda g: g.dma_start(
        out=w2sb[:].rearrange("p (f d) -> p f d", f=5),
        in_=w2_d.rearrange("f l d -> l f d")), inc="s_g")
    b.on("gpsimd", lambda g: g.dma_start(
        out=dw1sb[:].rearrange("p (j n) -> p j n", j=ND),
        in_=dw1_d.rearrange("(j p) n -> p j n", p=128)), inc="s_g")
    b.on("gpsimd", lambda g: g.dma_start(out=dw2sb[:], in_=dw2_d), inc="s_g")
    b.set("consts", "gpsimd")

    b.on("sync", lambda s: s.dma_start(
        out=xres.rearrange("p (i d) -> p i d", d=D),
        in_=x_in.rearrange("(i p) d -> p i d", p=128)), inc="s_ld")
    b.set("xload", "sync")
    b.on("vector", lambda v: v.memset(onessb[:], 1.0))
    b.on("vector", lambda v: v.memset(cstsb[:, 0:1], EPS))
    b.on("vector", lambda v: v.memset(cstsb[:, 1:2], -LN8))
    for qi_, qt_ in enumerate(QTHR):
        b.on("vector", lambda v, qi_=qi_, qt_=qt_: v.memset(
            cstsb[:, 2 + qi_:3 + qi_], -qt_))
    for ti_, tv_ in enumerate((1.0, 0.5)):
        b.on("vector", lambda v, ti_=ti_, tv_=tv_: v.memset(
            thrsb[:, ti_:ti_ + 1], tv_))
    b.on("vector", lambda v: v.memset(Ssb[:], 0.0), inc="s_dve")
    b.set("init", "vector")
    b.set("S_ok", "vector")

    qn = [0]

    def emit_quant(x_ap, out_bf, n, ready_res, out_res, ybuf=None):
        p = qn[0] & 1
        qn[0] += 1
        nb = n // 16
        qsg = q_sg if p == 0 else q_sg2
        if ybuf is None:
            ybuf = x_ap
        b.waits("scalar", ready_res, "q_dy")
        b.on("scalar", lambda s: s.activation(q_a[:, 0:n], x_ap, AF.Abs),
             inc="s_act")
        b.set("q_abs", "scalar")
        b.waits("scalar", f"q_sgf{p}")
        b.on("scalar", lambda s, qsg=qsg: s.activation(
            qsg[:, 0:n], x_ap, AF.Sign), inc="s_act")
        b.set("q_act", "scalar")
        b.waits("vector", ready_res)
        x3 = x_ap.rearrange("p (z k) -> p z k", k=16)
        b.on("vector", lambda v: v.tensor_reduce(
            q_bm[:, 0:nb], x3, axis=AX.X, op=ALU.max, apply_absolute_value=True),
            inc="s_dve")
        b.set("q_bmr", "vector")
        import os as _os
        QSY = not _os.environ.get("KE_NOQSY")
        if QSY:
            b.wait("vector", "q_bmr")
        b.on("vector", lambda v: v.tensor_scalar(
            q_sc[:, 0:nb], q_bm[:, 0:nb], R6, 1e-12, ALU.mult, ALU.max),
            inc="s_dve")
        b.set("q_scd", "vector")
        if QSY:
            b.wait("vector", "q_scd")
        b.on("vector", lambda v: v.reciprocal(q_rs[:, 0:nb], q_sc[:, 0:nb]),
             inc="s_dve")
        b.set("q_rsd", "vector")
        b.on("vector", lambda v: v.tensor_scalar(
            q_hs[:, 0:nb], q_sc[:, 0:nb], 0.5, None, ALU.mult))
        if QSY:
            b.wait("vector", "q_rsd")
        b.wait("vector", "q_act")
        a3 = q_a[:, 0:n].rearrange("p (z k) -> p z k", k=16)
        y3 = ybuf[:, 0:n].rearrange("p (z k) -> p z k", k=16)
        b.on("vector", lambda v, y3=y3: v.tensor_tensor(
            y3, a3, q_rs[:, 0:nb].unsqueeze(2).broadcast_to([128, nb, 16]),
            ALU.mult), inc="s_dve")
        b.set("q_dy", "vector")
        y, Bc, Ac = V(ybuf[:, 0:n]), V(q_b[:, 0:n]), V(q_A[:, 0:n])
        b.on("vector", lambda v: v.tensor_single_scalar(Bc, y, 5.0, ALU.is_gt))
        b.on("vector", lambda v: v.tensor_single_scalar(Ac, y, 0.25, ALU.is_gt))
        for thB, thA in ((5.0, 0.75), (3.5, 1.25), (2.5, 1.75)):
            b.on("vector", lambda v, th=thB: v.scalar_tensor_tensor(
                Bc, y, th, Bc, ALU.is_gt, ALU.add))
            b.on("vector", lambda v, th=thA: v.scalar_tensor_tensor(
                Ac, y, th, Ac, ALU.is_gt, ALU.add))
        s3 = qsg[:, 0:n].rearrange("p (z k) -> p z k", k=16)
        b.on("vector", lambda v, s3=s3: v.tensor_tensor(
            s3, s3, q_hs[:, 0:nb].unsqueeze(2).broadcast_to([128, nb, 16]),
            ALU.mult))
        b.on("vector", lambda v: v.scalar_tensor_tensor(
            V(q_Q[:, 0:n]), Bc, 2.0, Ac, ALU.mult, ALU.add))
        b.on("vector", lambda v, qsg=qsg: v.tensor_tensor(
            V(out_bf), V(q_Q[:, 0:n]), V(qsg[:, 0:n]), ALU.mult), inc="s_dve")
        b.set(out_res, "vector")
        b.set(f"q_sgf{p}", "vector")
        b.set("q_scratch", "vector")

    def emit_wquant(w, ti, parts):
        for (c0, n) in parts:
            src = {"g": wg_d, "u": wu_d, "d": wd_d}[w]
            p_ = qn[0] & 1
            qld = q_y if p_ == 0 else q_y2
            b.waits("sync", f"q_sgf{p_}")
            b.on("sync", lambda s, src=src, ti=ti, c0=c0, n=n, qld=qld:
                 s.dma_start(out=qld[:, 0:n],
                             in_=src[ti * 128:(ti + 1) * 128, c0:c0 + n]),
                 inc="s_ld")
            b.set("wload", "sync")
            b.wait("vector", "wtp_done")
            emit_quant(qld[:, 0:n], wq_tm[:, 0:n], n, "wload", "wq")
            nblk = n // 128
            for g0 in range(0, nblk, 8):
                gn = min(8, nblk - g0)
                b.waits("tensor", "wq", "psT_free")
                for bb in range(gn):
                    b.on("tensor", lambda t, bb=bb, g0=g0: t.matmul(
                        psT[:, bb * 128:(bb + 1) * 128],
                        wq_tm[:, (g0 + bb) * 128:(g0 + bb + 1) * 128],
                        ident[:], start=True, stop=True, is_transpose=True),
                        inc="s_pe" if bb == gn - 1 else None)
                b.set("wtp", "tensor")
                if g0 + 8 >= nblk:
                    b.set("wtp_done", "tensor")
                b.waits("scalar", "wtp", "tstage_free")
                b.on("scalar", lambda sc, gn=gn: sc.activation(
                    tstage[:, 0:gn * 128], psT[:, 0:gn * 128], AF.Copy),
                    inc="s_act")
                b.set("tev", "scalar")
                b.set("psT_free", "scalar")
                if w in "gu":
                    dst = wgT_s if w == "g" else wuT_s
                    dview = dst.rearrange("(z p) i -> p z i", p=128)[
                        :, g0:g0 + gn, ti * 128:(ti + 1) * 128]
                else:
                    i0 = (c0 // 128) + g0
                    dview = wdT_s.rearrange("(z p) dd -> p z dd", p=128)[
                        :, i0:i0 + gn, ti * 128:(ti + 1) * 128]
                b.wait("sync", "tev")
                b.on("sync", lambda s, dv=dview, gn=gn: s.dma_start(
                    out=dv, in_=tstage[:, 0:gn * 128].rearrange(
                        "p (z m) -> p z m", m=128)), inc="s_sp")
                b.set("tstage_free", at=("s_sp", b.cnt["s_sp"]))
            b.set(f"wst_{w}{ti}p{c0}", at=("s_sp", b.cnt["s_sp"]))

    def emit_ln(x_src_res):
        b.waits("vector", x_src_res)
        b.on("vector", lambda v: v.tensor_reduce(
            lnmean[:], xres.rearrange("p (i d) -> p i d", d=D),
            axis=AX.X, op=ALU.add), inc="s_dve")
        b.set("ln_red", "vector")
        b.wait("vector", "ln_red")
        b.waits("scalar", x_src_res, "lnsq_free")
        for t in range(NT):
            b.on("scalar", lambda s, t=t: s.activation(
                lnsq[:], xres[:, t * D:(t + 1) * D], AF.Square,
                accum_out=lnsum2[:, t:t + 1]),
                inc="s_act" if t == NT - 1 else None)
        b.set("ln_sq", "scalar")
        b.on("vector", lambda v: v.tensor_scalar(
            lnmean[:], lnmean[:], R1024, None, ALU.mult), inc="s_dve")
        b.set("ln_m2", "vector")
        b.wait("vector", "ln_sq")
        b.on("vector", lambda v: v.tensor_scalar(
            lnsum2[:], lnsum2[:], R1024, None, ALU.mult), inc="s_dve")
        b.set("ln_s2", "vector")
        b.wait("vector", "ln_m2")
        b.wait("vector", "ln_s2")
        b.on("vector", lambda v: v.tensor_tensor(
            lnrstd[:], lnmean[:], lnmean[:], ALU.mult), inc="s_dve")
        b.set("ln_msq", "vector")
        b.wait("vector", "ln_msq")
        b.on("vector", lambda v: v.tensor_tensor(
            lnrstd[:], lnsum2[:], lnrstd[:], ALU.subtract), inc="s_dve")
        b.set("ln_var", "vector")
        b.wait("scalar", "ln_var")
        b.on("scalar", lambda s: s.activation(
            lnrstd[:], lnrstd[:], AF.Ln, bias=EPS), inc="s_act")
        b.set("ln_ln", "scalar")
        b.wait("scalar", "ln_ln")
        b.on("scalar", lambda s: s.activation(
            lnrstd[:], lnrstd[:], AF.Exp, scale=-0.5), inc="s_act")
        b.set("ln_rstd", "scalar")
        b.wait("vector", "ln_rstd")
        b.on("vector", lambda v: v.tensor_tensor(
            lnnmr[:], lnmean[:], lnrstd[:], ALU.mult), inc="s_dve")
        b.set("ln_nm1", "vector")
        b.wait("vector", "ln_nm1")
        b.on("vector", lambda v: v.tensor_scalar(
            lnnmr[:], lnnmr[:], -1.0, None, ALU.mult), inc="s_dve")
        b.set("ln_ok", "vector")

    def emit_ln_apply(t, out_ap, out_res):
        b.waits("scalar", "ln_ok", f"lnout_free")
        b.on("scalar", lambda s, t=t: s.activation(
            out_ap, xres[:, t * D:(t + 1) * D], AF.Identity,
            bias=lnnmr[:, t:t + 1], scale=lnrstd[:, t:t + 1]), inc="s_act")
        b.set(out_res, "scalar")

    def emit_tp(src_bf, nblk, dst_fn, dep_res, out_res):
        deps = dep_res if isinstance(dep_res, tuple) else (dep_res,)
        b.waits("tensor", *deps, "psT_free")
        for bb in range(nblk):
            b.on("tensor", lambda te, bb=bb: te.matmul(
                psT[:, bb * 128:(bb + 1) * 128],
                src_bf[:, bb * 128:(bb + 1) * 128],
                ident[:], start=True, stop=True, is_transpose=True),
                inc="s_pe" if bb == nblk - 1 else None)
        b.set("tp_mm", "tensor")
        b.wait("scalar", "tp_mm")
        b.on("scalar", dst_fn, inc="s_act")
        b.set(out_res, "scalar")
        b.set("psT_free", "scalar")

    def emit_ffn(tag, x_res, wprep=False):
        emit_ln(x_res)
        for t in range(NT):
            emit_ln_apply(t, ln_t[:], "ln_ap")
            b.waits("vector", "lnb_free")
            emit_quant(ln_t[:], lnb_t[:], D, "ln_ap", "xq_q", ybuf=lnsq)
            b.set("lnout_free", at=b.res["q_bmr"])
            emit_tp(lnb_t, 8,
                    lambda sc, t=t: sc.activation(
                        fm3(xq_fm, t * 128, 128),
                        psT[:].rearrange("p (z m) -> p z m", m=128), AF.Copy),
                    "xq_q", "xq_ev")
            b.set("lnb_free", at=b.res["tp_mm"])
        b.set("xq_all", at=b.res["xq_ev"])
        if stage == "dbgx":
            return "DBGX"
        pend = [None]
        for ic, (i0, ilen) in enumerate(ICHUNKS):
            nseg = ilen // 128
            if wprep:
                if ic == 0:
                    for ti in range(ND):
                        emit_wquant("d", ti, [(0, 1024)])
                for ti in range(4 * ic, min(4 * (ic + 1), NI)):
                    emit_wquant("g", ti, [(0, 1024)])
                    emit_wquant("u", ti, [(0, 1024)])
                if ic in (0, 1):
                    for ti in range(4 * ic, 4 * ic + 4):
                        emit_wquant("d", ti, [(1024, 896), (1920, 896)])
                last_ti = min(4 * (ic + 1), NI) - 1
                wdep = "wst_d7p0" if ic <= 1 else "wst_d7p1920"
                b.waits("sync", f"wst_u{last_ti}p0", wdep)
            else:
                b.waits("sync", "wq_all_st")
            b.waits("sync", "wgu_free")
            b.on("sync", lambda s, i0=i0, ilen=ilen: s.dma_start(
                out=wch_g[:, 0:8 * ilen].rearrange("p (j n) -> p j n", j=ND),
                in_=wgT_s.rearrange("(j p) i -> p j i", p=128)[:, :, i0:i0 + ilen]),
                inc="s_ld")
            b.on("sync", lambda s, i0=i0, ilen=ilen: s.dma_start(
                out=wch_u[:, 0:8 * ilen].rearrange("p (j n) -> p j n", j=ND),
                in_=wuT_s.rearrange("(j p) i -> p j i", p=128)[:, :, i0:i0 + ilen]),
                inc="s_ld")
            b.set("wchgu", "sync")
            b.waits("sync", "wch_free")
            b.on("sync", lambda s, i0=i0, nseg=nseg: s.dma_start(
                out=wch_d[:, 0:nseg * 1024].rearrange("p (m n) -> p m n", n=1024),
                in_=wdT_s.rearrange("(m p) dd -> p m dd", p=128)[
                    :, i0 // 128:i0 // 128 + nseg, :]), inc="s_ld")
            b.set("wchd", "sync")
            for t in range(NT):
                b.waits("tensor", "wchgu", "wchd", "xq_all", "psA_free",
                        "psB_free")
                for j in range(ND):
                    b.on("tensor", lambda te, j=j, t=t, ilen=ilen: te.matmul(
                        psA[:, 0:ilen],
                        xq_fm[:, j * T + t * 128: j * T + (t + 1) * 128],
                        wch_g[:, j * ilen:(j + 1) * ilen],
                        start=(j == 0), stop=(j == ND - 1)))
                for j in range(ND):
                    b.on("tensor", lambda te, j=j, t=t, ilen=ilen: te.matmul(
                        psB[:, 0:ilen],
                        xq_fm[:, j * T + t * 128: j * T + (t + 1) * 128],
                        wch_u[:, j * ilen:(j + 1) * ilen],
                        start=(j == 0), stop=(j == ND - 1)),
                        inc="s_pe" if j == ND - 1 else None)
                b.set("gu_mm", "tensor")
                if t == NT - 1:
                    b.set("wgu_free", "tensor")
                if stage in ("dbgg", "dbgu") and t == 0:
                    b.wait("vector", "gu_mm")
                    pp_dbg = psA if stage == "dbgg" else psB
                    b.on("vector", lambda v, pp_dbg=pp_dbg, ilen=ilen: v.tensor_copy(
                        sttsb[:, 0:ilen], pp_dbg[:, 0:ilen]), inc="s_dve")
                    b.set("dbg_ev", "vector")
                    b.wait("sync", "dbg_ev")
                    b.on("sync", lambda s, ilen=ilen: s.dma_start(
                        out=y_out.rearrange("(i p) d -> p (i d)", p=128)[:, 0:ilen],
                        in_=sttsb[:, 0:ilen]), inc="s_sp")
                    ov = b.cnt["s_sp"]
                    b.on("sync", lambda s, v=ov: s.wait_ge(b.sem["s_sp"], v))
                    return "DBG"
                b.waits("scalar", "gu_mm", "sil_free")
                b.on("scalar", lambda s, ilen=ilen: s.activation(
                    ffnsil[:, 0:ilen], psA[:, 0:ilen], AF.Silu), inc="s_act")
                b.set("sil", "scalar")
                b.set("psA_free", "scalar")
                b.waits("vector", "sil", "hbuf_free")
                b.on("vector", lambda v, ilen=ilen: v.scalar_tensor_tensor(
                    hbuf[:, 0:ilen], ffnsil[:, 0:ilen], 0.5, psB[:, 0:ilen],
                    ALU.mult, ALU.mult), inc="s_dve")
                b.set("hgot", "vector")
                b.set("psB_free", "vector")
                b.set("sil_free", "vector")
                hq = hq_tm if t % 2 == 0 else hq_tm2
                if stage == "dbgh" and t == 0:
                    b.wait("sync", "hgot")
                    b.on("sync", lambda s, ilen=ilen: s.dma_start(
                        out=y_out.rearrange("(i p) d -> p (i d)", p=128)[:, 0:ilen],
                        in_=hbuf[:, 0:ilen]), inc="s_sp")
                    ov = b.cnt["s_sp"]
                    b.on("sync", lambda s, v=ov: s.wait_ge(b.sem["s_sp"], v))
                    return "DBG"
                b.waits("vector", f"hqf{t % 2}")
                emit_quant(hbuf[:, 0:ilen], hq[:, 0:ilen], ilen, "hgot",
                           f"hq_q{t % 2}")
                b.set("hbuf_free", at=b.res["q_act"])

                def emit_tail(t=t, hq=hq, nseg=nseg, ilen=ilen):
                    emit_tp(hq, nseg,
                            lambda sc, nseg=nseg: sc.activation(
                                hq_fm[:, 0:nseg * 128], psT[:, 0:nseg * 128],
                                AF.Copy),
                            f"hq_q{t % 2}", "hq_ev")
                    b.set(f"hqf{t % 2}", at=b.res["tp_mm"])
                    b.waits("tensor", "hq_ev", "psC_free")
                    for nh in range(2):
                        for m in range(nseg):
                            b.on("tensor", lambda te, m=m, nh=nh, nseg=nseg:
                                 te.matmul(
                                     psC[:, nh * 512:(nh + 1) * 512],
                                     hq_fm[:, m * 128:(m + 1) * 128],
                                     wch_d[:, m * 1024 + nh * 512:
                                           m * 1024 + (nh + 1) * 512],
                                     start=(m == 0), stop=(m == nseg - 1)),
                                 inc="s_pe" if (nh == 1 and m == nseg - 1)
                                 else None)
                    b.set("dn_mm", "tensor")
                    b.wait("vector", "dn_mm")
                    b.on("vector", lambda v, t=t: v.tensor_tensor(
                        V(xres[:, t * D:(t + 1) * D]),
                        V(xres[:, t * D:(t + 1) * D]),
                        V(psC[:]), ALU.add), inc="s_dve")
                    b.set("x_acc", "vector")
                    b.set("psC_free", "vector")

                if pend[0] is not None:
                    pend[0]()
                pend[0] = emit_tail
            pend[0]()
            pend[0] = None
            b.set("wch_free", at=b.res["dn_mm"])
        b.set(f"{tag}_done", at=b.res["x_acc"])

    def dump_and_finish(src_ap, n_bf16_cols, deps):
        b.waits("gpsimd", *deps)
        ni = max(1, n_bf16_cols // D)
        b.on("gpsimd", lambda g: g.dma_start(
            out=y_out.rearrange("(i p) d -> p i d", p=128)[:, 0:ni, :],
            in_=src_ap[:, 0:n_bf16_cols].rearrange("p (i d) -> p i d", d=min(D, n_bf16_cols))), inc="s_g")
        ov = b.cnt["s_g"]
        b.on("gpsimd", lambda g, v=ov: g.wait_ge(b.sem["s_g"], v))
        nc.detect_race_conditions = False
        blk2 = es.enter_context(nc.Block())
        with es:
            b.replay(blk2)
        return nc

    def dump_dram_and_finish(dram_ap, deps):
        b.waits("gpsimd", *deps)
        b.on("gpsimd", lambda g: g.dma_start(
            out=y_out[0:1024, :], in_=dram_ap), inc="s_g")
        ov = b.cnt["s_g"]
        b.on("gpsimd", lambda g, v=ov: g.wait_ge(b.sem["s_g"], v))
        nc.detect_race_conditions = False
        blk2 = es.enter_context(nc.Block())
        with es:
            b.replay(blk2)
        return nc

    def finish_here():
        b.waits("sync", "wq_all_st", "x_acc", "xload", "x2_done", "f2_done")
        b.on("sync", lambda s: s.dma_start(
            out=y_out.rearrange("(i p) d -> p i d", p=128),
            in_=xres.rearrange("p (i d) -> p i d", d=D)), inc="s_sp")
        ov = b.cnt["s_sp"]
        b.on("sync", lambda s, v=ov: s.wait_ge(b.sem["s_sp"], v))
        nc.detect_race_conditions = False
        blk2 = es.enter_context(nc.Block())
        with es:
            b.replay(blk2)
        return nc

    b.wait("vector", "xload")
    b.wait("scalar", "xload")
    b.wait("vector", "consts")
    b.wait("scalar", "consts")
    if stage == "p0":
        return finish_here()
    if stage == "dbgln":
        emit_ln("xload")
        b.waits("sync", "ln_ok", "ln_rstd")
        for idx, tsr in enumerate((lnmean, lnsum2, lnrstd, lnnmr)):
            b.on("sync", lambda s, t=tsr, idx=idx: s.dma_start(
                out=y_out.rearrange("(i p) d -> p i d", p=128)[:, idx, 0:NT],
                in_=t[:]), inc="s_sp")
        for k in range(4):
            b.on("sync", lambda s, k=k: s.dma_start(
                out=y_out.rearrange("(i p) d -> p i d", p=128)[:, 4 + k, 0:NT],
                in_=xres[:, (12 + k) * 1024:(12 + k) * 1024 + NT]), inc="s_sp")
        ov = b.cnt["s_sp"]
        b.on("sync", lambda s, v=ov: s.wait_ge(b.sem["s_sp"], v))
        nc.detect_race_conditions = False
        blk2 = es.enter_context(nc.Block())
        with es:
            b.replay(blk2)
        return nc
    rr = emit_ffn("f1", "xload", wprep=True)
    b.set("wq_all_st", at=("s_sp", b.cnt["s_sp"]))
    if rr == "DBG":
        nc.detect_race_conditions = False
        blk2 = es.enter_context(nc.Block())
        with es:
            b.replay(blk2)
        return nc
    if rr == "DBGX":
        return dump_and_finish(xq_fm, 16384, ["xq_all"])
    if stage == "dbgw":
        return dump_dram_and_finish(wgT_s[0:1024, 0:1024], ["wq_all_st"])
    if stage == "dbgwd":
        return dump_dram_and_finish(wdT_s[0:1024, 0:1024], ["wq_all_st"])
    if stage == "f1":
        return finish_here()

    emit_ln("f1_done")
    for t in range(NT):
        b.waits("scalar", "lnb_free")
        emit_ln_apply(t, lnb_t[:], "l1_ap")
        b.set("lnout_free", at=b.res["l1_ap"])
        emit_tp(lnb_t, 8,
                lambda sc, t=t: sc.activation(
                    fm3(ln1_fm, t * 128, 128),
                    psT[:].rearrange("p (z m) -> p z m", m=128), AF.Copy),
                "l1_ap", "l1_ev")
        b.set("lnb_free", at=b.res["tp_mm"])
    b.set("ln1_all", at=b.res["l1_ev"])
    b.waits("sync", "f1_done")
    b.on("sync", lambda s: s.dma_start(out=xsp_s, in_=xres), inc="s_sp")
    b.set("xspill", at=("s_sp", b.cnt["s_sp"]))
    if stage == "ln1":
        return finish_here()

    FIDX = {"r": 0, "k": 1, "v": 2, "g": 3, "w": 4}

    def emit_dx(c0):
        dx3 = dx_r.rearrange("p (j t) -> p j t", j=ND)
        ln3 = ln1_fm.rearrange("p (j t) -> p j t", j=ND)
        if c0 == 0:
            b.on("vector", lambda v: v.tensor_scalar(
                dx3[:, :, 0:1], ln3[:, :, 0:1], -1.0, None, ALU.mult))
            b.on("vector", lambda v: v.tensor_tensor(
                dx3[:, :, 1:512], ln3[:, :, 0:511], ln3[:, :, 1:512],
                ALU.subtract), inc="s_dve")
        else:
            b.on("vector", lambda v, c0=c0: v.tensor_tensor(
                dx3[:, :, 0:512], ln3[:, :, c0 - 1:c0 + 511],
                ln3[:, :, c0:c0 + 512], ALU.subtract), inc="s_dve")
        b.set("dx", "vector")

    def emit_mixpre(half, tc):
        c0 = half * 1024 + tc * 512
        b.waits("vector", "ln1_all", "dx_free", "xx_free", "su_mm")
        emit_dx(c0)
        xx3 = xx_r.rearrange("p (j t) -> p j t", j=ND)
        dx3 = dx_r.rearrange("p (j t) -> p j t", j=ND)
        ln3 = ln1_fm.rearrange("p (j t) -> p j t", j=ND)
        b.on("vector", lambda v: v.tensor_tensor(
            xx3, dx3, maa_x.unsqueeze(2).broadcast_to([128, ND, 512]), ALU.mult))
        b.on("vector", lambda v, c0=c0: v.tensor_tensor(
            xx3, xx3, ln3[:, :, c0:c0 + 512], ALU.add), inc="s_dve")
        b.set("xxx", "vector")
        b.set("dx_free", "vector")
        b.waits("tensor", "xxx", "consts", "psA_free", "psE_free")
        for j in range(ND):
            b.on("tensor", lambda te, j=j: te.matmul(
                psA[:], w1sb[:, j * 160:j * 160 + 128],
                xx_r[:, j * 512:(j + 1) * 512],
                start=(j == 0), stop=(j == ND - 1)))
        for j in range(ND):
            b.on("tensor", lambda te, j=j: te.matmul(
                psE[0:32, :], w1sb[:, j * 160 + 128:(j + 1) * 160],
                xx_r[:, j * 512:(j + 1) * 512],
                start=(j == 0), stop=(j == ND - 1)),
                inc="s_pe" if j == ND - 1 else None)
        b.set("mix_mm", "tensor")
        b.set("xx_free", "tensor")
        b.waits("scalar", "mix_mm", "mixsb_free")
        b.on("scalar", lambda s: s.activation(mixsb[:], psA[:], AF.Tanh))
        b.on("scalar", lambda s: s.activation(mix2sb[:], psE[0:32, :], AF.Tanh),
             inc="s_act")
        b.set("mix_ev", "scalar")
        b.set("psA_free", "scalar")
        b.set("psE_free", "scalar")
        b.wait("sync", "mix_ev")
        for fb in range(4):
            b.on("sync", lambda s, fb=fb, tc=tc: s.dma_start(
                out=mixf[:, (fb * 2 + tc) * 512:(fb * 2 + tc + 1) * 512],
                in_=mixsb[fb * 32:(fb + 1) * 32, :]), inc="s_ld")
        b.on("sync", lambda s, tc=tc: s.dma_start(
            out=mixf[:, (4 * 2 + tc) * 512:(4 * 2 + tc + 1) * 512],
            in_=mix2sb[:]), inc="s_ld")
        b.set("mix_dma", "sync")
        b.set("mixsb_free", at=b.res["mix_dma"])

    def emit_xf(f, half, tc):
        c0 = half * 1024 + tc * 512
        b.waits("vector", "xf_free", "dx_free", "su_mm")
        emit_dx(c0)
        fi = FIDX[f]
        for j in range(ND):
            pp = psD if j % 2 == 0 else psB
            pfree = "psD_free" if j % 2 == 0 else "psB_free"
            b.waits("tensor", pfree, "mix_dma", "consts",
                    "dx" if j == 0 else "noop")
            b.on("tensor", lambda te, pp=pp, j=j, fi=fi, tc=tc: te.matmul(
                pp[:], w2sb[:, fi * 1024 + j * 128:fi * 1024 + (j + 1) * 128],
                mixf[:, (fi * 2 + tc) * 512:(fi * 2 + tc + 1) * 512],
                start=True, stop=True), inc="s_pe")
            b.set("mf_mm", "tensor")
            b.waits("vector", "mf_mm", "sttsb_free")
            b.on("vector", lambda v, pp=pp, f=f, j=j: v.scalar_tensor_tensor(
                sttsb[:], pp[:], maa_of[f][:, j:j + 1],
                dx_r[:, j * 512:(j + 1) * 512], ALU.add, ALU.mult),
                inc="s_dve")
            b.set("xf_stt", "vector")
            b.wait("vector", "xf_stt")
            b.on("vector", lambda v, j=j, c0=c0: v.tensor_tensor(
                xf_r[:, j * 512:(j + 1) * 512], sttsb[:],
                ln1_fm[:, j * T + c0:j * T + c0 + 512], ALU.add), inc="s_dve")
            b.set(pfree, "vector")
            b.set("sttsb_free", "vector")
        b.set("xf", at=b.res["sttsb_free"])
        b.set("dx_free", at=b.res["sttsb_free"])

    def emit_proj(f, half, tc):
        c0 = half * 1024 + tc * 512
        if f == "w":
            b.waits("tensor", "xf", "psE_free")
            for j in range(ND):
                b.on("tensor", lambda te, j=j: te.matmul(
                    psE[0:64, :], dw1sb[:, j * 64:(j + 1) * 64],
                    xf_r[:, j * 512:(j + 1) * 512],
                    start=(j == 0), stop=(j == ND - 1)),
                    inc="s_pe" if j == ND - 1 else None)
            b.set("t1_mm", "tensor")
            b.set("xf_free", "tensor")
            b.wait("scalar", "t1_mm")
            b.on("scalar", lambda s, c0=c0: s.activation(
                tanh1[:, c0:c0 + 512], psE[0:64, :], AF.Tanh), inc="s_act")
            b.set("t1_ev", "scalar")
            b.set("psE_free", "scalar")
            return
        if f == "v":
            for st in range(4):
                b.waits("tensor", "xf" if st == 0 else "noop", "psC_free")
                for j in range(ND):
                    for nh in range(2):
                        b.on("tensor", lambda te, j=j, st=st, nh=nh: te.matmul(
                            psC[:, nh * 512:(nh + 1) * 512],
                            xf_r[:, j * 512 + st * 128: j * 512 + (st + 1) * 128],
                            attw[:, j * 1024 + nh * 512: j * 1024 + (nh + 1) * 512],
                            start=(j == 0), stop=(j == ND - 1)),
                            inc="s_pe" if (j == ND - 1 and nh == 1) else None)
                b.set("v_mm", "tensor")
                if st == 3:
                    b.set("xf_free", "tensor")
                b.wait("scalar", "v_mm")
                b.on("scalar", lambda s, st=st, tc=tc: s.activation(
                    v_tm[:, (tc * 4 + st) * 1024:(tc * 4 + st + 1) * 1024],
                    psC[:], AF.Copy), inc="s_act")
                b.set("v_ev", "scalar")
                b.set("psC_free", "scalar")
            return
        for jj in range(ND):
            pp = psA if jj % 2 == 0 else psD
            pfree = "psA_free" if jj % 2 == 0 else "psD_free"
            b.waits("tensor", pfree, "xf" if jj == 0 else "noop")
            for j in range(ND):
                b.on("tensor", lambda te, pp=pp, j=j, jj=jj: te.matmul(
                    pp[:], attw[:, j * 1024 + jj * 128: j * 1024 + (jj + 1) * 128],
                    xf_r[:, j * 512:(j + 1) * 512],
                    start=(j == 0), stop=(j == ND - 1)),
                    inc="s_pe" if j == ND - 1 else None)
            b.set("p_mm", "tensor")
            if jj == ND - 1:
                b.set("xf_free", "tensor")
            b.wait("scalar", "p_mm")
            if f == "g":
                b.waits("scalar", "g_dma")
                b.on("scalar", lambda s, pp=pp, jj=jj: s.activation(
                    gstage[:], pp[:], AF.Silu, bias=bgv[:, jj:jj + 1]),
                    inc="s_act")
                b.set("g_ev", "scalar")
                b.set(pfree, "scalar")
                b.wait("sync", "g_ev")
                b.on("sync", lambda s, jj=jj, c0=c0: s.dma_start(
                    out=gsp_s[:, jj * T + c0: jj * T + c0 + 512],
                    in_=gstage[:]), inc="s_ld")
                b.set("g_dma", "sync")
            else:
                dst = q_fm if f == "r" else k_fm
                b.on("scalar", lambda s, pp=pp, jj=jj, c0=c0, dst=dst: s.activation(
                    dst[:, jj * T + c0: jj * T + c0 + 512], pp[:], AF.Copy),
                    inc="s_act")
                b.set("p_ev", "scalar")
                b.set(pfree, "scalar")

    def emit_S_half(half):
        b.wait("scalar", "xspill")
        for tc in range(2):
            emit_mixpre(half, tc)
        for f in ("w", "r", "k", "v", "g"):
            if f != "w":
                wname = {"r": "Wq", "k": "Wk", "v": "Wv", "g": "Wg"}[f]
                b.waits("gpsimd", "attw_free", "q_scratch", "q_act",
                        "su_mm", "o_mm")
                b.on("gpsimd", lambda g, wname=wname: g.dma_start(
                    out=attw[:].rearrange("p (j n) -> p j n", j=ND),
                    in_=W_d[wname].rearrange("(j p) n -> p j n", p=128)),
                    inc="s_g")
                b.set("w_att", "gpsimd")
            for tc in range(2):
                emit_xf(f, half, tc)
                if f != "w":
                    b.wait("tensor", "w_att")
                emit_proj(f, half, tc)
            if f != "w":
                b.set("attw_free", at=b.res["xf_free"])

    import os
    E_NGRP = int(os.environ.get("KE_NGRP", "4"))
    E_NCH = int(os.environ.get("KE_NCH", "4"))
    E_PARTS = os.environ.get("KE_PARTS", "sakouz123")

    def emit_E_half(half):
        for grp in range(E_NGRP):
            g0 = half * 1024 + grp * 256
            lg = grp * 256
            for j in range(ND):
                pp = psE if j % 2 == 0 else psD
                pfree = "psE_free" if j % 2 == 0 else "psD_free"
                b.waits("tensor", pfree, "t1_ev", "dec_free", "consts")
                b.on("tensor", lambda te, j=j, g0=g0, pp=pp: te.matmul(
                    pp[:, 0:256], dw2sb[:, j * 128:(j + 1) * 128],
                    tanh1[:, g0:g0 + 256], start=True, stop=True), inc="s_pe")
                b.set("dec_mm", "tensor")
                b.wait("vector", "dec_mm")
                b.on("vector", lambda v, j=j, pp=pp: v.tensor_scalar(
                    dec_g[:, j * 256:(j + 1) * 256], pp[:, 0:256],
                    tdec[:, j:j + 1], None, ALU.add), inc="s_dve")
                b.set(pfree, "vector")
                b.set("dec_j", "vector")
            b.set("dec_done", at=b.res["dec_j"])
            b.waits("scalar", "dec_done", "egd_free")
            b.on("scalar", lambda s: s.activation(egd_g[:], dec_g[:], AF.Exp),
                 inc="s_act")
            b.set("egd", "scalar")
            b.wait("scalar", "egd")
            b.on("scalar", lambda s: s.activation(
                dec_g[:], egd_g[:], AF.Exp, scale=-1.0), inc="s_act")
            b.set("emg", "scalar")
            b.set("dec_free", at=("s_act", b.cnt["s_act"]))
            b.wait("vector", "emg")
            b.on("vector", lambda v: v.tensor_scalar(
                V(dec_g[:]), V(dec_g[:]), -1.0, 1.0, ALU.mult, ALU.add))
            kf3 = fm3(k_fm, g0, 256)
            dg3 = dec_g[:].rearrange("p (j t) -> p j t", j=ND)
            b.on("vector", lambda v, kf3=kf3, dg3=dg3: v.tensor_tensor(
                kf3, kf3, dg3, ALU.mult), inc="s_dve")
            b.set("kp", "vector")
            if os.environ.get("KE_DUMMYDMA"):
                b.waits("sync", "g_dma")
                b.on("sync", lambda s, g0=g0: s.dma_start(
                    out=gat_r.rearrange("p (j t) -> p j t", j=ND),
                    in_=gsp_s.rearrange("p (j t) -> p j t", j=ND)[:, :, g0:g0 + 256]),
                    inc="s_ld")
                b.set("g_rl", "sync")
            for cc in range(E_NCH):
                c0 = g0 + cc * 64
                l0 = cc * 64
                par = (c0 // 64) % 2
                tloc = (c0 - half * 1024) // 128
                if os.environ.get("KE_ESER"):
                    b.waits("tensor", "og_ev", "S_ok", "o_sb")
                b.waits("vector", "egd", "cb_free", "expq")
                for j in range(ND):
                    b.on("vector", lambda v, j=j, l0=l0: v.tensor_tensor_scan(
                        cbsb[:, j * 64:(j + 1) * 64],
                        onessb[:].broadcast_to([128, 64]),
                        egd_g[:, j * 256 + l0:j * 256 + l0 + 64],
                        0.0, ALU.mult, ALU.add),
                        inc="s_dve" if j == ND - 1 else None)
                b.set("cb", "vector")
                b.wait("vector", "cb")
                cb3 = cbsb[:].rearrange("p (j t) -> p j t", j=ND)
                cbL = cb3[:, :, 63:64].squeeze(2)
                b.wait("vector", "sm_exp")
                b.on("vector", lambda v, cbL=cbL: v.tensor_scalar(
                    smalls[:, 0:8], cbL, 0.5, None, ALU.mult), inc="s_dve")
                b.set("chalf", "vector")
                b.wait("scalar", "chalf")
                b.on("scalar", lambda s: s.activation(
                    smalls[:, 8:16], smalls[:, 0:8], AF.Exp))
                b.on("scalar", lambda s: s.activation(
                    smalls[:, 16:24], smalls[:, 0:8], AF.Exp, scale=-1.0))
                b.on("scalar", lambda s: s.activation(
                    smalls[:, 24:32], smalls[:, 0:8], AF.Exp, scale=-2.0),
                    inc="s_act")
                b.set("sm_exp", "scalar")
                b.waits("scalar", "expq_free")
                b.on("scalar", lambda s: s.activation(
                    expsc[:], cbsb[:], AF.Exp, scale=-1.0, bias=-LN8),
                    inc="s_act")
                b.set("expq", "scalar")
                b.waits("vector", "chalf", "expk_free")
                b.on("vector", lambda v: v.tensor_tensor(
                    expsc2[:].rearrange("p (j t) -> p j t", j=ND), cb3,
                    smalls[:, 0:8].unsqueeze(2).broadcast_to([128, ND, 64]),
                    ALU.subtract), inc="s_dve")
                b.set("argk", "vector")
                b.set("cb_free", "vector")
                b.wait("scalar", "argk")
                b.on("scalar", lambda s: s.activation(
                    expsc2[:], expsc2[:], AF.Exp), inc="s_act")
                b.set("expk", "scalar")
                q3 = fm3(q_fm, c0, 64)
                k3 = fm3(k_fm, c0, 64)
                e1 = expsc[:].rearrange("p (j t) -> p j t", j=ND)
                e2 = expsc2[:].rearrange("p (j t) -> p j t", j=ND)
                qs3 = qsc[:].rearrange("p (j t) -> p j t", j=ND)
                qss3 = qss[:].rearrange("p (j t) -> p j t", j=ND)
                ks3 = ksc[:].rearrange("p (j t) -> p j t", j=ND)
                kh3 = khc[:].rearrange("p (j t) -> p j t", j=ND)
                b.waits("vector", "expq", "expk", "sm_exp", "kp",
                        "qk_free", "p_mm", "v_mm", "t1_mm")
                b.on("vector", lambda v, qs3=qs3, q3=q3, e1=e1: v.tensor_tensor(
                    qs3, q3, e1, ALU.mult))
                b.on("vector", lambda v, ks3=ks3, k3=k3, e2=e2: v.tensor_tensor(
                    ks3, k3, e2, ALU.mult))
                b.on("vector", lambda v, qss3=qss3, qs3=qs3: v.tensor_tensor(
                    qss3, qs3, smalls[:, 8:16].unsqueeze(2).broadcast_to(
                        [128, ND, 64]), ALU.mult))
                b.on("vector", lambda v, kh3=kh3, ks3=ks3: v.tensor_tensor(
                    kh3, ks3, smalls[:, 16:24].unsqueeze(2).broadcast_to(
                        [128, ND, 64]), ALU.mult), inc="s_dve")
                b.set("qk", "vector")
                b.set("expq_free", "vector")
                b.set("expk_free", "vector")
                if "a" not in E_PARTS:
                    continue
                b.waits("tensor", "qk", "psA_free", "psD_free")
                for hp in range(ND):
                    b.on("tensor", lambda te, hp=hp, par=par: te.matmul(
                        psA[par * 64:par * 64 + 64, hp * 64:(hp + 1) * 64],
                        ksc[0:64, hp * 64:(hp + 1) * 64],
                        qss[0:64, hp * 64:(hp + 1) * 64],
                        start=True, stop=True, tile_position=(0, par * 64)))
                    b.on("tensor", lambda te, hp=hp, par=par: te.matmul(
                        psD[par * 64:par * 64 + 64, hp * 64:(hp + 1) * 64],
                        ksc[64:128, hp * 64:(hp + 1) * 64],
                        qss[64:128, hp * 64:(hp + 1) * 64],
                        start=True, stop=True, tile_position=(64, par * 64)),
                        inc="s_pe" if hp == ND - 1 else None)
                b.set("a_mm", "tensor")
                b.waits("vector", "a_mm", "amsk_free")
                mrow = maskt[64 * par:64 * par + 64, :].unsqueeze(1)\
                    .broadcast_to([64, ND, 64])
                b.on("vector", lambda v, par=par, mrow=mrow: v.tensor_tensor(
                    amskA[64 * par:64 * par + 64, :].rearrange(
                        "p (j t) -> p j t", j=ND),
                    psA[64 * par:64 * par + 64, :].rearrange(
                        "p (j t) -> p j t", j=ND), mrow, ALU.mult))
                b.on("vector", lambda v, par=par, mrow=mrow: v.tensor_tensor(
                    amskB[64 * par:64 * par + 64, :].rearrange(
                        "p (j t) -> p j t", j=ND),
                    psD[64 * par:64 * par + 64, :].rearrange(
                        "p (j t) -> p j t", j=ND), mrow, ALU.mult),
                    inc="s_dve")
                b.set("am_ev", "vector")
                b.set("psA_free", "vector")
                b.set("psD_free", "vector")
                if "k" not in E_PARTS:
                    continue
                b.waits("tensor", "qk", "psT_free")
                for hp in range(ND):
                    b.on("tensor", lambda te, hp=hp, par=par: te.matmul(
                        psT[par * 64:par * 64 + 64, hp * 128:(hp + 1) * 128],
                        khc[:, hp * 64:(hp + 1) * 64], ident[:],
                        start=True, stop=True, is_transpose=True,
                        tile_position=(0, par * 64)),
                        inc="s_pe" if hp == ND - 1 else None)
                b.set("kt_mm", "tensor")
                b.waits("vector", "kt_mm", "ktm_free")
                b.on("vector", lambda v, par=par: v.tensor_copy(
                    ktm[par * 64:par * 64 + 64, :],
                    psT[par * 64:par * 64 + 64, :]), inc="s_dve")
                b.set("kt_ev", "vector")
                b.set("psT_free", "vector")
                if "o" not in E_PARTS:
                    continue
                b.waits("tensor", "am_ev", "psB_free", "S_ok", "v_ev",
                        "psC_free")
                for hp in range(ND):
                    b.on("tensor", lambda te, hp=hp: te.matmul(
                        psB[0:64, hp * 64:(hp + 1) * 64],
                        Ssb[0:64, hp * 64:(hp + 1) * 64],
                        qsc[0:64, hp * 64:(hp + 1) * 64],
                        start=True, stop=True, tile_position=(0, 0)))
                    b.on("tensor", lambda te, hp=hp: te.matmul(
                        psB[64:128, hp * 64:(hp + 1) * 64],
                        Ssb[64:128, hp * 64:(hp + 1) * 64],
                        qsc[64:128, hp * 64:(hp + 1) * 64],
                        start=True, stop=True, tile_position=(64, 64)))
                    b.on("tensor", lambda te, hp=hp, par=par, tloc=tloc: te.matmul(
                        psC[0:64, hp * 64:(hp + 1) * 64],
                        v_tm[par * 64:par * 64 + 64,
                             tloc * 1024 + hp * 128: tloc * 1024 + hp * 128 + 64],
                        amskA[par * 64:par * 64 + 64, hp * 64:(hp + 1) * 64],
                        start=True, stop=True, tile_position=(par * 64, 0)))
                    b.on("tensor", lambda te, hp=hp, par=par, tloc=tloc: te.matmul(
                        psC[64:128, hp * 64:(hp + 1) * 64],
                        v_tm[par * 64:par * 64 + 64,
                             tloc * 1024 + hp * 128 + 64:
                             tloc * 1024 + hp * 128 + 128],
                        amskB[par * 64:par * 64 + 64, hp * 64:(hp + 1) * 64],
                        start=True, stop=True, tile_position=(par * 64, 64)),
                        inc="s_pe" if hp == ND - 1 else None)
                b.set("o_mm", "tensor")
                og_fm = og1_fm if half == 0 else og2_fm
                cloc = (c0 - half * 1024) // 64
                if "u" not in E_PARTS:
                    continue
                b.waits("tensor", "kt_ev", "psE_free", "S_scaled_read",
                        "v_ev")
                for hp in range(ND):
                    b.on("tensor", lambda te, hp=hp, par=par, tloc=tloc: te.matmul(
                        psE[0:64, hp * 64:(hp + 1) * 64],
                        ktm[par * 64:par * 64 + 64, hp * 128:hp * 128 + 64],
                        v_tm[par * 64:par * 64 + 64,
                             tloc * 1024 + hp * 128: tloc * 1024 + hp * 128 + 64],
                        start=True, stop=True, tile_position=(par * 64, 0)))
                    b.on("tensor", lambda te, hp=hp, par=par, tloc=tloc: te.matmul(
                        psE[64:128, hp * 64:(hp + 1) * 64],
                        ktm[par * 64:par * 64 + 64, hp * 128 + 64:hp * 128 + 128],
                        v_tm[par * 64:par * 64 + 64,
                             tloc * 1024 + hp * 128 + 64:
                             tloc * 1024 + hp * 128 + 128],
                        start=True, stop=True, tile_position=(par * 64, 64)),
                        inc="s_pe" if hp == ND - 1 else None)
                b.set("su_mm", "tensor")
                b.set("ktm_free", "tensor")
                b.set("qk_free", "tensor")
                if "z" not in E_PARTS:
                    continue
                if "1" not in E_PARTS:
                    continue
                b.waits("vector", "o_mm", "sm_exp")
                b.on("vector", lambda v: v.tensor_tensor(
                    Stmp[:].rearrange("p (j t) -> p j t", j=ND),
                    Ssb[:].rearrange("p (j t) -> p j t", j=ND),
                    smalls[:, 24:32].unsqueeze(2).broadcast_to([128, ND, 64]),
                    ALU.mult), inc="s_dve")
                b.set("S_scaled_read", "vector")
                if "2" not in E_PARTS:
                    continue
                b.waits("vector", "o_mm")
                b.on("vector", lambda v: v.tensor_copy(sttsb[:], psB[:]),
                     inc="s_dve")
                b.set("o_sb", "vector")
                b.set("psB_free", "vector")
                if "3" not in E_PARTS:
                    continue
                if os.environ.get("KE_OGMEMSET"):
                    b.on("vector", lambda v, og_fm=og_fm, cloc=cloc: v.memset(
                        og_fm[:, cloc * 512:(cloc + 1) * 512],
                        float(100 + cloc)), inc="s_dve")
                else:
                    b.on("vector", lambda v, og_fm=og_fm, cloc=cloc: v.tensor_tensor(
                        V(og_fm[:, cloc * 512:(cloc + 1) * 512]), V(sttsb[:]),
                        V(psC[:, 0:512]), ALU.add), inc="s_dve")
                b.set("og_ev", "vector")
                b.set("psC_free", "vector")
                b.waits("vector", "su_mm")
                b.on("vector", lambda v: v.tensor_tensor(
                    V(Ssb[:]), V(Stmp[:]), V(psE[:]), ALU.add), inc="s_dve")
                b.set("S_ok", "vector")
                b.set("psE_free", "vector")
                b.set("amsk_free", at=b.res["o_mm"])
            b.set("egd_free", at=b.res["cb"])
        b.set(f"og_half{half}", at=b.res["og_ev"])

    b.res.setdefault("noop", ("s_dve", 0))

    for half in range(2):
        emit_S_half(half)
        if stage == f"s{half}":
            return finish_here()
        emit_E_half(half)
        if stage == f"e{half}":
            if os.environ.get("KE_DUMPOG"):
                return dump_and_finish(og1_fm if half == 0 else og2_fm, 8192,
                                       ["og_ev"])
            return finish_here()

    b.waits("gpsimd", "attw_free", "q_scratch", "q_act", "su_mm", "o_mm")
    b.on("gpsimd", lambda g: g.dma_start(
        out=attw[:].rearrange("p (j n) -> p j n", j=ND),
        in_=W_d["Wo"].rearrange("(j p) n -> p j n", p=128)), inc="s_g")
    b.set("wo_ld", "gpsimd")
    gatp5 = AR4b[:, 0:1024]
    _p5ret = [None]
    for t in range(NT):
        og_fm = og1_fm if t < 8 else og2_fm
        tl = t % 8
        b.waits("sync", "xspill", "xrld_free", "qk", "gatp5_free")
        b.on("sync", lambda s, t=t: s.dma_start(
            out=xrld[:], in_=xsp_s[:, t * D:(t + 1) * D]), inc="s_ld")
        for c_ in range(2):
            b.on("sync", lambda s, t=t, c_=c_: s.dma_start(
                out=gatp5[:, c_ * 512:(c_ + 1) * 512].rearrange(
                    "p (j z) -> p j z", j=ND),
                in_=gsp_s.rearrange("p (j tt) -> p j tt", j=ND)[
                    :, :, t * 128 + c_ * 64: t * 128 + c_ * 64 + 64]),
                inc="s_ld")
        b.set("x_rl", "sync")
        b.waits("vector", "x_rl", f"og_half{0 if t < 8 else 1}", "wo_mm")
        b.on("vector", lambda v, og_fm=og_fm, tl=tl: v.tensor_tensor(
            V(og_fm[:, tl * 1024:(tl + 1) * 1024]),
            V(og_fm[:, tl * 1024:(tl + 1) * 1024]), V(gatp5[:]), ALU.mult),
            inc="s_dve")
        b.set("og_gated", "vector")
        b.set("gatp5_free", "vector")
        if stage == "dbgp5" and t == int(os.environ.get("KE_P5T", "1")):
            return dump_and_finish(og_fm[:, tl * 1024:tl * 1024 + 1024], 1024,
                                   ["og_gated"])
        b.waits("tensor", "wo_ld", "psC_free", "og_gated")
        for c_ in range(2):
            for nh in range(2):
                for j in range(ND):
                    b.on("tensor", lambda te, j=j, nh=nh, tl=tl, c_=c_,
                         og_fm=og_fm: te.matmul(
                        psC[c_ * 64:(c_ + 1) * 64, nh * 512:(nh + 1) * 512],
                        og_fm[:, tl * 1024 + c_ * 512 + j * 64:
                              tl * 1024 + c_ * 512 + (j + 1) * 64],
                        attw[:, j * 1024 + nh * 512: j * 1024 + (nh + 1) * 512],
                        start=(j == 0), stop=(j == ND - 1),
                        tile_position=(0, c_ * 64)),
                        inc="s_pe" if (c_ == 1 and j == ND - 1 and nh == 1) else None)
        b.set("wo_mm", "tensor")
        b.waits("vector", "wo_mm", "x_rl")
        b.on("vector", lambda v, t=t: v.tensor_tensor(
            V(xres[:, t * D:(t + 1) * D]), V(xrld[:]), V(psC[:]), ALU.add),
            inc="s_dve")
        b.set("x2_acc", "vector")
        b.set("psC_free", "vector")
        b.set("xrld_free", "vector")
    b.set("x2_done", at=b.res["x2_acc"])
    if stage == "p5":
        return finish_here()
    if stage == "dbgp5":
        raise RuntimeError("dbgp5 tile index out of range")

    for eng in ENGS:
        b.waits(eng, "x2_done", "wo_mm")
    emit_ffn("f2", "x2_done")
    b.waits("sync", "f2_done")
    b.on("sync", lambda s: s.dma_start(
        out=y_out.rearrange("(i p) d -> p i d", p=128),
        in_=xres.rearrange("p (i d) -> p i d", d=D)), inc="s_sp")
    out_val = b.cnt["s_sp"]
    b.on("sync", lambda s, v=out_val: s.wait_ge(b.sem["s_sp"], v))

    nc.detect_race_conditions = False
    blk = es.enter_context(nc.Block())
    with es:
        b.replay(blk)
    return nc


_NC_CACHE = [None]


def kernel(L0_attn_hidden_states, L0_attn_time_maa_x, L0_attn_time_maa_r,
           L0_attn_time_maa_k, L0_attn_time_maa_v, L0_attn_time_maa_w,
           L0_attn_time_maa_g, L0_attn_time_maa_w1, L0_attn_time_maa_w2,
           L0_attn_time_decay, L0_attn_time_decay_w1, L0_attn_time_decay_w2,
           L0_attn_Wq, L0_attn_Wk, L0_attn_Wv, L0_attn_Wo, L0_attn_Wg,
           L0_attn_bg, L0_ffn_w_gate, L0_ffn_w_up, L0_ffn_w_down):
    x = np.ascontiguousarray(np.asarray(L0_attn_hidden_states, dtype=np.float32))
    assert x.shape == (8, T, D)
    vecs0 = np.stack([
        np.asarray(L0_attn_time_maa_x, np.float32).reshape(D),
        np.asarray(L0_attn_time_maa_r, np.float32).reshape(D),
        np.asarray(L0_attn_time_maa_k, np.float32).reshape(D),
        np.asarray(L0_attn_time_maa_v, np.float32).reshape(D),
        np.asarray(L0_attn_time_maa_w, np.float32).reshape(D),
        np.asarray(L0_attn_time_maa_g, np.float32).reshape(D),
        np.asarray(L0_attn_time_decay, np.float32).reshape(D),
        np.asarray(L0_attn_bg, np.float32).reshape(D),
    ], axis=0)
    vecs = np.ascontiguousarray(
        vecs0.reshape(8, 8, 128).transpose(2, 0, 1).reshape(128, 64))
    ident = np.eye(128, dtype=np.float32)
    mask = np.tile(np.tril(np.ones((CH, CH), np.float32)).T * 0 +
                   (np.arange(CH)[:, None] <= np.arange(CH)[None, :]).astype(np.float32),
                   (2, 1))
    common = {
        "w_gate": np.ascontiguousarray(np.asarray(L0_ffn_w_gate, np.float32)),
        "w_up": np.ascontiguousarray(np.asarray(L0_ffn_w_up, np.float32)),
        "w_down": np.ascontiguousarray(np.asarray(L0_ffn_w_down, np.float32)),
        "Wq": np.ascontiguousarray(np.asarray(L0_attn_Wq, np.float32)),
        "Wk": np.ascontiguousarray(np.asarray(L0_attn_Wk, np.float32)),
        "Wv": np.ascontiguousarray(np.asarray(L0_attn_Wv, np.float32)),
        "Wo": np.ascontiguousarray(np.asarray(L0_attn_Wo, np.float32)),
        "Wg": np.ascontiguousarray(np.asarray(L0_attn_Wg, np.float32)),
        "maa_w1": np.ascontiguousarray(np.asarray(L0_attn_time_maa_w1, np.float32)),
        "maa_w2": np.ascontiguousarray(np.asarray(L0_attn_time_maa_w2, np.float32)),
        "dec_w1": np.ascontiguousarray(np.asarray(L0_attn_time_decay_w1, np.float32)),
        "dec_w2": np.ascontiguousarray(np.asarray(L0_attn_time_decay_w2, np.float32)),
        "vecs": vecs,
        "ident": ident,
        "mask": np.ascontiguousarray(mask),
    }
    in_maps = [dict(common, x_in=np.ascontiguousarray(x[i])) for i in range(8)]
    nc = build()
    res = run_bass_kernel_spmd(nc, in_maps, core_ids=list(range(8)))
    out = np.stack([np.asarray(r["y_out"]) for r in res.results], axis=0)
    return out.astype(np.float32)


if __name__ == "__main__":
    import sys
    if "--build" in sys.argv:
        nc = build()
        total = sum(len(v) for v in nc.m.funcs[0].blocks[0].instructions) if False else None
        print("build ok")


# version 06
# speedup vs baseline: 1.4720x; 1.0146x over previous
import numpy as np
from contextlib import ExitStack

import concourse.bass as bass
import concourse.mybir as mybir
from concourse.bass_utils import run_bass_kernel_spmd

F32 = mybir.dt.float32
BF16 = mybir.dt.bfloat16
ALU = mybir.AluOpType
AF = mybir.ActivationFunctionType
AX = mybir.AxisListType

T, D, I = 2048, 1024, 2816
NT, ND, NI = T // 128, D // 128, I // 128
CH = 64
LN8 = float(np.log(np.float64(8.0)))
R6 = float(np.float32(1.0 / 6.0))
R1024 = float(np.float32(1.0 / 1024.0))
EPS = 1e-5
ICHUNKS = [(k * 512, 512) for k in range(5)] + [(2560, 256)]
ENGS = ("sync", "gpsimd", "tensor", "vector", "scalar")


def V(ap):
    sh = ap.shape
    if len(sh) == 2 and sh[1] > 512:
        n = sh[1]
        z = max(d for d in range(1, 513) if n % d == 0)
        return ap.rearrange("p (a z) -> p a z", z=z)
    return ap


class B:

    def __init__(self, nc, es):
        self.nc = nc
        self.q = {e: [] for e in ENGS}
        names = ("s_ld", "s_g", "s_pe", "s_dve", "s_act", "s_sp", "s_og")
        self.sem = {n: es.enter_context(nc.semaphore(n)) for n in names}
        self.cnt = {n: 0 for n in names}
        self.res = {}

    def on(self, eng, fn, inc=None, inc_n=None):
        if inc is None:
            self.q[eng].append(fn)
            return None
        n = inc_n if inc_n is not None else (16 if eng in ("sync", "gpsimd") else 1)
        self.cnt[inc] += n

        def thunk(e, fn=fn, s=self.sem[inc], n=n):
            fn(e).then_inc(s, n)
        self.q[eng].append(thunk)
        return self.cnt[inc]

    def now(self, eng):
        s = {"sync": "s_ld", "gpsimd": "s_g", "tensor": "s_pe",
             "vector": "s_dve", "scalar": "s_act"}[eng]
        return (s, self.cnt[s])

    def set(self, res, eng=None, at=None):
        self.res[res] = at if at is not None else self.now(eng)

    def wait(self, eng, res):
        if res not in self.res:
            return
        sname, val = self.res[res]
        if val > 0:
            self.q[eng].append(lambda e, s=self.sem[sname], v=val: e.wait_ge(s, v))

    def waits(self, eng, *ress):
        best = {}
        for r in ress:
            if r in self.res:
                sname, val = self.res[r]
                best[sname] = max(best.get(sname, 0), val)
        for sname, val in best.items():
            if val > 0:
                self.q[eng].append(lambda e, s=self.sem[sname], v=val: e.wait_ge(s, v))

    def replay(self, blk):
        for eng in ENGS:
            thunks = self.q[eng]

            def body(e, thunks=thunks):
                for t in thunks:
                    t(e)
            getattr(blk, eng)(body)


def build(stage="full"):
    nc = bass.Bass("TRN2", target_bir_lowering=False, debug=False)

    x_in = nc.dram_tensor("x_in", [T, D], F32, kind="ExternalInput").ap()
    y_out = nc.dram_tensor("y_out", [T, D], F32, kind="ExternalOutput").ap()
    wg_d = nc.dram_tensor("w_gate", [I, D], F32, kind="ExternalInput").ap()
    wu_d = nc.dram_tensor("w_up", [I, D], F32, kind="ExternalInput").ap()
    wd_d = nc.dram_tensor("w_down", [D, I], F32, kind="ExternalInput").ap()
    W_d = {f: nc.dram_tensor(f, [D, D], F32, kind="ExternalInput").ap()
           for f in ("Wq", "Wk", "Wv", "Wo", "Wg")}
    w1_d = nc.dram_tensor("maa_w1", [D, 160], F32, kind="ExternalInput").ap()
    w2_d = nc.dram_tensor("maa_w2", [5, 32, D], F32, kind="ExternalInput").ap()
    dw1_d = nc.dram_tensor("dec_w1", [D, 64], F32, kind="ExternalInput").ap()
    dw2_d = nc.dram_tensor("dec_w2", [64, D], F32, kind="ExternalInput").ap()
    vecs_d = nc.dram_tensor("vecs", [128, 64], F32, kind="ExternalInput").ap()
    ident_d = nc.dram_tensor("ident", [128, 128], F32, kind="ExternalInput").ap()
    mask_d = nc.dram_tensor("mask", [128, CH], F32, kind="ExternalInput").ap()

    wgT_s = nc.dram_tensor("wgT_s", [D, I], BF16, kind="Internal").ap()
    wuT_s = nc.dram_tensor("wuT_s", [D, I], BF16, kind="Internal").ap()
    wdT_s = nc.dram_tensor("wdT_s", [I, D], BF16, kind="Internal").ap()
    xsp_s = nc.dram_tensor("xsp_s", [128, NT * D], F32, kind="Internal").ap()
    gsp_s = nc.dram_tensor("gsp_s", [128, ND * T], BF16, kind="Internal").ap()

    es = ExitStack()

    def sbt(name, p, n, dt=F32):
        return es.enter_context(nc.sbuf_tensor(name, [p, n], dt))

    def pst_(name, p, n, dt=F32):
        return es.enter_context(nc.psum_tensor(name, [p, n], dt))

    AR0 = sbt("AR0", 128, 16384)
    AR1 = sbt("AR1", 128, 8192)
    AR2 = sbt("AR2", 128, 8192)
    AR3 = sbt("AR3", 128, 8192)
    AR4 = sbt("AR4", 128, 4096)

    ident = sbt("identsb", 128, 128, BF16)
    maskt = sbt("masksb", 128, CH)
    vecs = sbt("vecssb", 128, 64)
    w1sb = sbt("w1sb", 128, 8 * 160, BF16)
    w2sb = sbt("w2sb", 32, 5 * 1024, BF16)
    dw1sb = sbt("dw1sb", 128, 8 * 64, BF16)
    dw2sb = sbt("dw2sb", 64, 1024, BF16)
    tanh1 = sbt("tanh1", 64, T, BF16)
    Ssb = sbt("Ssb", 128, 512, BF16)
    Stmp = sbt("Stmp", 128, 512, BF16)
    mixsb = sbt("mixsb", 128, 512, BF16)
    mix2sb = sbt("mix2sb", 32, 512, BF16)
    cbsb = sbt("cbsb", 128, 8 * CH)
    smalls = sbt("smalls", 128, 64)
    sttsb = sbt("sttsb", 128, 512)
    gstage = sbt("gstage", 128, 512, BF16)
    lnmean = sbt("lnmean", 128, NT)
    lnsum2 = sbt("lnsum2", 128, NT)
    lnrstd = sbt("lnrstd", 128, NT)
    lnnmr = sbt("lnnmr", 128, NT)
    onessb = sbt("onessb", 128, 1)
    q_bm = sbt("q_bm", 128, 88)
    q_sc = sbt("q_sc", 128, 88)
    q_rs = sbt("q_rs", 128, 88)
    q_hs = sbt("q_hs", 128, 88, BF16)
    thrsb = sbt("thrsb", 128, 2)

    psA = pst_("psA", 128, 512)
    psB = pst_("psB", 128, 512)
    psC = pst_("psC", 128, 1024)
    psD = pst_("psD", 128, 512)
    psE = pst_("psE", 128, 512)
    psT = pst_("psT", 128, 1024, BF16)

    b = B(nc, es)
    cstsb = sbt("cstsb", 128, 9)
    nc.const_aps.register(F32, EPS, cstsb[:, 0:1])
    nc.const_aps.register(F32, -LN8, cstsb[:, 1:2])
    QTHR = (5.0, 3.5, 2.5, 1.75, 1.25, 0.75, 0.25)
    for qi_, qt_ in enumerate(QTHR):
        nc.const_aps.register(F32, -qt_, cstsb[:, 2 + qi_:3 + qi_])

    AR0b = AR0[:].bitcast(BF16)
    AR1b = AR1[:].bitcast(BF16)
    AR2b = AR2[:].bitcast(BF16)
    AR3b = AR3[:].bitcast(BF16)
    AR4b = AR4[:].bitcast(BF16)
    xres = AR0[:]
    q_fm = AR0b[:, 0:16384]
    k_fm = AR0b[:, 16384:32768]
    xq_fm = AR1b[:, 0:16384]
    ln1_fm = AR1b[:, 0:16384]
    og2_fm = AR1b[:, 0:8192]
    og1_fm = AR3b[:, 8192:16384]
    attw = AR3b[:, 0:8192]
    v_tm = AR2b[:, 0:8192]
    dec_g = AR2[:, 4096:6144]
    egd_g = AR2[:, 6144:8192]
    wch_g = AR2b[:, 0:4096]
    wch_u = AR2b[:, 4096:8192]
    wch_d = AR2b[:, 8192:12288]
    mixf = AR2b[0:32, 8192:13312]
    ffnsil = AR4[:, 2560:3072]
    expsc = AR4[:, 1024:1536]
    expsc2 = AR4[:, 1536:2048]
    ktm = AR4b[:, 4096:5120]
    qsc = AR4b[:, 5120:5632]
    qss = AR4b[:, 5632:6144]
    ksc = AR4b[:, 6144:6656]
    khc = AR3b[:, 0:512]
    amskA = AR3b[:, 512:1024]
    amskB = AR3b[:, 1024:1536]
    q_y2 = AR2[:, 6144:7552]
    q_sg2 = AR2[:].bitcast(BF16)[:, 15104:16128]
    q_y = AR3[:, 0:1408]
    q_a = AR3[:, 1408:2816]
    q_b = AR3[:, 2816:4224]
    q_A = AR3[:, 4224:5632]
    q_sg = AR3b[:, 11264:12672]
    q_Q = AR3b[:, 12672:14080]
    wq_tm = AR3b[:, 14080:16384]
    lnsq = AR4[:, 0:1024]
    ln_t = AR4[:, 1024:2048]
    xrld = AR4[:, 1024:2048]
    hbuf = AR4[:, 2048:2560]
    lnb_t = AR4b[:, 5120:6144]
    hq_tm = AR4b[:, 6144:6656]
    hq_tm2 = AR3b[:, 15488:16000]
    hq_fm = AR4b[:, 6656:7168]
    tstage = AR4b[:, 7168:8192]
    dx_r = AR4b[:, 0:4096]
    xx_r = AR4b[:, 4096:8192]
    xf_r = AR4b[:, 4096:8192]
    gat_r = AR4b[:, 0:2048]

    def vsl(i):
        return vecs[:, i * 8:(i + 1) * 8]

    maa_x = vsl(0)
    maa_of = {"r": vsl(1), "k": vsl(2), "v": vsl(3), "w": vsl(4), "g": vsl(5)}
    tdec, bgv = vsl(6), vsl(7)

    def fm3(ap, c0, n):
        return ap.rearrange("p (j t) -> p j t", j=ND)[:, :, c0:c0 + n]

    b.on("gpsimd", lambda g: g.dma_start(out=ident[:], in_=ident_d), inc="s_g")
    b.on("gpsimd", lambda g: g.dma_start(out=maskt[:], in_=mask_d), inc="s_g")
    b.on("gpsimd", lambda g: g.dma_start(out=vecs[:], in_=vecs_d), inc="s_g")
    b.on("gpsimd", lambda g: g.dma_start(
        out=w1sb[:].rearrange("p (j n) -> p j n", j=ND),
        in_=w1_d.rearrange("(j p) n -> p j n", p=128)), inc="s_g")
    b.on("gpsimd", lambda g: g.dma_start(
        out=w2sb[:].rearrange("p (f d) -> p f d", f=5),
        in_=w2_d.rearrange("f l d -> l f d")), inc="s_g")
    b.on("gpsimd", lambda g: g.dma_start(
        out=dw1sb[:].rearrange("p (j n) -> p j n", j=ND),
        in_=dw1_d.rearrange("(j p) n -> p j n", p=128)), inc="s_g")
    b.on("gpsimd", lambda g: g.dma_start(out=dw2sb[:], in_=dw2_d), inc="s_g")
    b.set("consts", "gpsimd")

    b.on("sync", lambda s: s.dma_start(
        out=xres.rearrange("p (i d) -> p i d", d=D),
        in_=x_in.rearrange("(i p) d -> p i d", p=128)), inc="s_ld")
    b.set("xload", "sync")
    b.on("vector", lambda v: v.memset(onessb[:], 1.0))
    b.on("vector", lambda v: v.memset(cstsb[:, 0:1], EPS))
    b.on("vector", lambda v: v.memset(cstsb[:, 1:2], -LN8))
    for qi_, qt_ in enumerate(QTHR):
        b.on("vector", lambda v, qi_=qi_, qt_=qt_: v.memset(
            cstsb[:, 2 + qi_:3 + qi_], -qt_))
    for ti_, tv_ in enumerate((1.0, 0.5)):
        b.on("vector", lambda v, ti_=ti_, tv_=tv_: v.memset(
            thrsb[:, ti_:ti_ + 1], tv_))
    b.on("vector", lambda v: v.memset(Ssb[:], 0.0), inc="s_dve")
    b.set("init", "vector")
    b.set("S_ok", "vector")

    qn = [0]

    def emit_quant(x_ap, out_bf, n, ready_res, out_res, ybuf=None):
        p = qn[0] & 1
        qn[0] += 1
        nb = n // 16
        qsg = q_sg if p == 0 else q_sg2
        if ybuf is None:
            ybuf = x_ap
        b.waits("scalar", ready_res, "q_dy")
        b.on("scalar", lambda s: s.activation(q_a[:, 0:n], x_ap, AF.Abs),
             inc="s_act")
        b.set("q_abs", "scalar")
        b.waits("scalar", f"q_sgf{p}")
        b.on("scalar", lambda s, qsg=qsg: s.activation(
            qsg[:, 0:n], x_ap, AF.Sign), inc="s_act")
        b.set("q_act", "scalar")
        b.waits("vector", ready_res)
        x3 = x_ap.rearrange("p (z k) -> p z k", k=16)
        b.on("vector", lambda v: v.tensor_reduce(
            q_bm[:, 0:nb], x3, axis=AX.X, op=ALU.max, apply_absolute_value=True),
            inc="s_dve")
        b.set("q_bmr", "vector")
        import os as _os
        QSY = not _os.environ.get("KE_NOQSY")
        if QSY:
            b.wait("vector", "q_bmr")
        b.on("vector", lambda v: v.tensor_scalar(
            q_sc[:, 0:nb], q_bm[:, 0:nb], R6, 1e-12, ALU.mult, ALU.max),
            inc="s_dve")
        b.set("q_scd", "vector")
        if QSY:
            b.wait("vector", "q_scd")
        b.on("vector", lambda v: v.reciprocal(q_rs[:, 0:nb], q_sc[:, 0:nb]),
             inc="s_dve")
        b.set("q_rsd", "vector")
        b.on("vector", lambda v: v.tensor_scalar(
            q_hs[:, 0:nb], q_sc[:, 0:nb], 0.5, None, ALU.mult))
        if QSY:
            b.wait("vector", "q_rsd")
        b.wait("vector", "q_act")
        a3 = q_a[:, 0:n].rearrange("p (z k) -> p z k", k=16)
        y3 = ybuf[:, 0:n].rearrange("p (z k) -> p z k", k=16)
        b.on("vector", lambda v, y3=y3: v.tensor_tensor(
            y3, a3, q_rs[:, 0:nb].unsqueeze(2).broadcast_to([128, nb, 16]),
            ALU.mult), inc="s_dve")
        b.set("q_dy", "vector")
        y, Bc, Ac = V(ybuf[:, 0:n]), V(q_b[:, 0:n]), V(q_A[:, 0:n])
        b.on("vector", lambda v: v.tensor_single_scalar(Bc, y, 5.0, ALU.is_gt))
        b.on("vector", lambda v: v.tensor_single_scalar(Ac, y, 0.25, ALU.is_gt))
        for thB, thA in ((5.0, 0.75), (3.5, 1.25), (2.5, 1.75)):
            b.on("vector", lambda v, th=thB: v.scalar_tensor_tensor(
                Bc, y, th, Bc, ALU.is_gt, ALU.add))
            b.on("vector", lambda v, th=thA: v.scalar_tensor_tensor(
                Ac, y, th, Ac, ALU.is_gt, ALU.add))
        s3 = qsg[:, 0:n].rearrange("p (z k) -> p z k", k=16)
        b.on("vector", lambda v, s3=s3: v.tensor_tensor(
            s3, s3, q_hs[:, 0:nb].unsqueeze(2).broadcast_to([128, nb, 16]),
            ALU.mult))
        b.on("vector", lambda v: v.scalar_tensor_tensor(
            V(q_Q[:, 0:n]), Bc, 2.0, Ac, ALU.mult, ALU.add))
        b.on("vector", lambda v, qsg=qsg: v.tensor_tensor(
            V(out_bf), V(q_Q[:, 0:n]), V(qsg[:, 0:n]), ALU.mult), inc="s_dve")
        b.set(out_res, "vector")
        b.set(f"q_sgf{p}", "vector")
        b.set("q_scratch", "vector")

    def emit_wquant(w, ti, parts):
        for (c0, n) in parts:
            src = {"g": wg_d, "u": wu_d, "d": wd_d}[w]
            p_ = qn[0] & 1
            qld = q_y if p_ == 0 else q_y2
            b.waits("sync", f"q_sgf{p_}")
            b.on("sync", lambda s, src=src, ti=ti, c0=c0, n=n, qld=qld:
                 s.dma_start(out=qld[:, 0:n],
                             in_=src[ti * 128:(ti + 1) * 128, c0:c0 + n]),
                 inc="s_ld")
            b.set("wload", "sync")
            b.wait("vector", "wtp_done")
            emit_quant(qld[:, 0:n], wq_tm[:, 0:n], n, "wload", "wq")
            nblk = n // 128
            for g0 in range(0, nblk, 8):
                gn = min(8, nblk - g0)
                b.waits("tensor", "wq", "psT_free")
                for bb in range(gn):
                    b.on("tensor", lambda t, bb=bb, g0=g0: t.matmul(
                        psT[:, bb * 128:(bb + 1) * 128],
                        wq_tm[:, (g0 + bb) * 128:(g0 + bb + 1) * 128],
                        ident[:], start=True, stop=True, is_transpose=True),
                        inc="s_pe" if bb == gn - 1 else None)
                b.set("wtp", "tensor")
                if g0 + 8 >= nblk:
                    b.set("wtp_done", "tensor")
                b.waits("scalar", "wtp", "tstage_free")
                b.on("scalar", lambda sc, gn=gn: sc.activation(
                    tstage[:, 0:gn * 128], psT[:, 0:gn * 128], AF.Copy),
                    inc="s_act")
                b.set("tev", "scalar")
                b.set("psT_free", "scalar")
                if w in "gu":
                    dst = wgT_s if w == "g" else wuT_s
                    dview = dst.rearrange("(z p) i -> p z i", p=128)[
                        :, g0:g0 + gn, ti * 128:(ti + 1) * 128]
                else:
                    i0 = (c0 // 128) + g0
                    dview = wdT_s.rearrange("(z p) dd -> p z dd", p=128)[
                        :, i0:i0 + gn, ti * 128:(ti + 1) * 128]
                b.wait("sync", "tev")
                b.on("sync", lambda s, dv=dview, gn=gn: s.dma_start(
                    out=dv, in_=tstage[:, 0:gn * 128].rearrange(
                        "p (z m) -> p z m", m=128)), inc="s_sp")
                b.set("tstage_free", at=("s_sp", b.cnt["s_sp"]))
            b.set(f"wst_{w}{ti}p{c0}", at=("s_sp", b.cnt["s_sp"]))

    def emit_ln(x_src_res):
        b.waits("vector", x_src_res)
        b.on("vector", lambda v: v.tensor_reduce(
            lnmean[:], xres.rearrange("p (i d) -> p i d", d=D),
            axis=AX.X, op=ALU.add), inc="s_dve")
        b.set("ln_red", "vector")
        b.wait("vector", "ln_red")
        b.waits("scalar", x_src_res, "lnsq_free")
        for t in range(NT):
            b.on("scalar", lambda s, t=t: s.activation(
                lnsq[:], xres[:, t * D:(t + 1) * D], AF.Square,
                accum_out=lnsum2[:, t:t + 1]),
                inc="s_act" if t == NT - 1 else None)
        b.set("ln_sq", "scalar")
        b.on("vector", lambda v: v.tensor_scalar(
            lnmean[:], lnmean[:], R1024, None, ALU.mult), inc="s_dve")
        b.set("ln_m2", "vector")
        b.wait("vector", "ln_sq")
        b.on("vector", lambda v: v.tensor_scalar(
            lnsum2[:], lnsum2[:], R1024, None, ALU.mult), inc="s_dve")
        b.set("ln_s2", "vector")
        b.wait("vector", "ln_m2")
        b.wait("vector", "ln_s2")
        b.on("vector", lambda v: v.tensor_tensor(
            lnrstd[:], lnmean[:], lnmean[:], ALU.mult), inc="s_dve")
        b.set("ln_msq", "vector")
        b.wait("vector", "ln_msq")
        b.on("vector", lambda v: v.tensor_tensor(
            lnrstd[:], lnsum2[:], lnrstd[:], ALU.subtract), inc="s_dve")
        b.set("ln_var", "vector")
        b.wait("scalar", "ln_var")
        b.on("scalar", lambda s: s.activation(
            lnrstd[:], lnrstd[:], AF.Ln, bias=EPS), inc="s_act")
        b.set("ln_ln", "scalar")
        b.wait("scalar", "ln_ln")
        b.on("scalar", lambda s: s.activation(
            lnrstd[:], lnrstd[:], AF.Exp, scale=-0.5), inc="s_act")
        b.set("ln_rstd", "scalar")
        b.wait("vector", "ln_rstd")
        b.on("vector", lambda v: v.tensor_tensor(
            lnnmr[:], lnmean[:], lnrstd[:], ALU.mult), inc="s_dve")
        b.set("ln_nm1", "vector")
        b.wait("vector", "ln_nm1")
        b.on("vector", lambda v: v.tensor_scalar(
            lnnmr[:], lnnmr[:], -1.0, None, ALU.mult), inc="s_dve")
        b.set("ln_ok", "vector")

    def emit_ln_apply(t, out_ap, out_res):
        b.waits("scalar", "ln_ok", f"lnout_free")
        b.on("scalar", lambda s, t=t: s.activation(
            out_ap, xres[:, t * D:(t + 1) * D], AF.Identity,
            bias=lnnmr[:, t:t + 1], scale=lnrstd[:, t:t + 1]), inc="s_act")
        b.set(out_res, "scalar")

    def emit_tp(src_bf, nblk, dst_fn, dep_res, out_res):
        deps = dep_res if isinstance(dep_res, tuple) else (dep_res,)
        b.waits("tensor", *deps, "psT_free")
        for bb in range(nblk):
            b.on("tensor", lambda te, bb=bb: te.matmul(
                psT[:, bb * 128:(bb + 1) * 128],
                src_bf[:, bb * 128:(bb + 1) * 128],
                ident[:], start=True, stop=True, is_transpose=True),
                inc="s_pe" if bb == nblk - 1 else None)
        b.set("tp_mm", "tensor")
        b.wait("scalar", "tp_mm")
        b.on("scalar", dst_fn, inc="s_act")
        b.set(out_res, "scalar")
        b.set("psT_free", "scalar")

    def emit_ffn(tag, x_res, wprep=False):
        emit_ln(x_res)
        for t in range(NT):
            emit_ln_apply(t, ln_t[:], "ln_ap")
            b.waits("vector", "lnb_free")
            emit_quant(ln_t[:], lnb_t[:], D, "ln_ap", "xq_q", ybuf=lnsq)
            b.set("lnout_free", at=b.res["q_bmr"])
            emit_tp(lnb_t, 8,
                    lambda sc, t=t: sc.activation(
                        fm3(xq_fm, t * 128, 128),
                        psT[:].rearrange("p (z m) -> p z m", m=128), AF.Copy),
                    "xq_q", "xq_ev")
            b.set("lnb_free", at=b.res["tp_mm"])
        b.set("xq_all", at=b.res["xq_ev"])
        if stage == "dbgx":
            return "DBGX"
        pend = [None]
        for ic, (i0, ilen) in enumerate(ICHUNKS):
            nseg = ilen // 128
            if wprep:
                if ic == 0:
                    for ti in range(ND):
                        emit_wquant("d", ti, [(0, 1024)])
                for ti in range(4 * ic, min(4 * (ic + 1), NI)):
                    emit_wquant("g", ti, [(0, 1024)])
                    emit_wquant("u", ti, [(0, 1024)])
                if ic in (0, 1):
                    for ti in range(4 * ic, 4 * ic + 4):
                        emit_wquant("d", ti, [(1024, 896), (1920, 896)])
                last_ti = min(4 * (ic + 1), NI) - 1
                wdep = "wst_d7p0" if ic <= 1 else "wst_d7p1920"
                b.waits("sync", f"wst_u{last_ti}p0", wdep)
            else:
                b.waits("sync", "wq_all_st")
            b.waits("sync", "wgu_free")
            b.on("sync", lambda s, i0=i0, ilen=ilen: s.dma_start(
                out=wch_g[:, 0:8 * ilen].rearrange("p (j n) -> p j n", j=ND),
                in_=wgT_s.rearrange("(j p) i -> p j i", p=128)[:, :, i0:i0 + ilen]),
                inc="s_ld")
            b.on("sync", lambda s, i0=i0, ilen=ilen: s.dma_start(
                out=wch_u[:, 0:8 * ilen].rearrange("p (j n) -> p j n", j=ND),
                in_=wuT_s.rearrange("(j p) i -> p j i", p=128)[:, :, i0:i0 + ilen]),
                inc="s_ld")
            b.set("wchgu", "sync")
            b.waits("sync", "wch_free")
            b.on("sync", lambda s, i0=i0, nseg=nseg: s.dma_start(
                out=wch_d[:, 0:nseg * 1024].rearrange("p (m n) -> p m n", n=1024),
                in_=wdT_s.rearrange("(m p) dd -> p m dd", p=128)[
                    :, i0 // 128:i0 // 128 + nseg, :]), inc="s_ld")
            b.set("wchd", "sync")
            for t in range(NT):
                b.waits("tensor", "wchgu", "wchd", "xq_all", "psA_free",
                        "psB_free")
                for j in range(ND):
                    b.on("tensor", lambda te, j=j, t=t, ilen=ilen: te.matmul(
                        psA[:, 0:ilen],
                        xq_fm[:, j * T + t * 128: j * T + (t + 1) * 128],
                        wch_g[:, j * ilen:(j + 1) * ilen],
                        start=(j == 0), stop=(j == ND - 1)))
                for j in range(ND):
                    b.on("tensor", lambda te, j=j, t=t, ilen=ilen: te.matmul(
                        psB[:, 0:ilen],
                        xq_fm[:, j * T + t * 128: j * T + (t + 1) * 128],
                        wch_u[:, j * ilen:(j + 1) * ilen],
                        start=(j == 0), stop=(j == ND - 1)),
                        inc="s_pe" if j == ND - 1 else None)
                b.set("gu_mm", "tensor")
                if t == NT - 1:
                    b.set("wgu_free", "tensor")
                if stage in ("dbgg", "dbgu") and t == 0:
                    b.wait("vector", "gu_mm")
                    pp_dbg = psA if stage == "dbgg" else psB
                    b.on("vector", lambda v, pp_dbg=pp_dbg, ilen=ilen: v.tensor_copy(
                        sttsb[:, 0:ilen], pp_dbg[:, 0:ilen]), inc="s_dve")
                    b.set("dbg_ev", "vector")
                    b.wait("sync", "dbg_ev")
                    b.on("sync", lambda s, ilen=ilen: s.dma_start(
                        out=y_out.rearrange("(i p) d -> p (i d)", p=128)[:, 0:ilen],
                        in_=sttsb[:, 0:ilen]), inc="s_sp")
                    ov = b.cnt["s_sp"]
                    b.on("sync", lambda s, v=ov: s.wait_ge(b.sem["s_sp"], v))
                    return "DBG"
                b.waits("scalar", "gu_mm", "sil_free")
                b.on("scalar", lambda s, ilen=ilen: s.activation(
                    ffnsil[:, 0:ilen], psA[:, 0:ilen], AF.Silu), inc="s_act")
                b.set("sil", "scalar")
                b.set("psA_free", "scalar")
                b.waits("vector", "sil", "hbuf_free")
                b.on("vector", lambda v, ilen=ilen: v.scalar_tensor_tensor(
                    hbuf[:, 0:ilen], ffnsil[:, 0:ilen], 0.5, psB[:, 0:ilen],
                    ALU.mult, ALU.mult), inc="s_dve")
                b.set("hgot", "vector")
                b.set("psB_free", "vector")
                b.set("sil_free", "vector")
                hq = hq_tm if t % 2 == 0 else hq_tm2
                if stage == "dbgh" and t == 0:
                    b.wait("sync", "hgot")
                    b.on("sync", lambda s, ilen=ilen: s.dma_start(
                        out=y_out.rearrange("(i p) d -> p (i d)", p=128)[:, 0:ilen],
                        in_=hbuf[:, 0:ilen]), inc="s_sp")
                    ov = b.cnt["s_sp"]
                    b.on("sync", lambda s, v=ov: s.wait_ge(b.sem["s_sp"], v))
                    return "DBG"
                b.waits("vector", f"hqf{t % 2}")
                emit_quant(hbuf[:, 0:ilen], hq[:, 0:ilen], ilen, "hgot",
                           f"hq_q{t % 2}")
                b.set("hbuf_free", at=b.res["q_act"])

                def emit_tail(t=t, hq=hq, nseg=nseg, ilen=ilen):
                    emit_tp(hq, nseg,
                            lambda sc, nseg=nseg: sc.activation(
                                hq_fm[:, 0:nseg * 128], psT[:, 0:nseg * 128],
                                AF.Copy),
                            f"hq_q{t % 2}", "hq_ev")
                    b.set(f"hqf{t % 2}", at=b.res["tp_mm"])
                    b.waits("tensor", "hq_ev", "psC_free")
                    for nh in range(2):
                        for m in range(nseg):
                            b.on("tensor", lambda te, m=m, nh=nh, nseg=nseg:
                                 te.matmul(
                                     psC[:, nh * 512:(nh + 1) * 512],
                                     hq_fm[:, m * 128:(m + 1) * 128],
                                     wch_d[:, m * 1024 + nh * 512:
                                           m * 1024 + (nh + 1) * 512],
                                     start=(m == 0), stop=(m == nseg - 1)),
                                 inc="s_pe" if (nh == 1 and m == nseg - 1)
                                 else None)
                    b.set("dn_mm", "tensor")
                    b.wait("vector", "dn_mm")
                    b.on("vector", lambda v, t=t: v.tensor_tensor(
                        V(xres[:, t * D:(t + 1) * D]),
                        V(xres[:, t * D:(t + 1) * D]),
                        V(psC[:]), ALU.add), inc="s_dve")
                    b.set("x_acc", "vector")
                    b.set("psC_free", "vector")

                if pend[0] is not None:
                    pend[0]()
                pend[0] = emit_tail
            pend[0]()
            pend[0] = None
            b.set("wch_free", at=b.res["dn_mm"])
        b.set(f"{tag}_done", at=b.res["x_acc"])

    def dump_and_finish(src_ap, n_bf16_cols, deps):
        b.waits("gpsimd", *deps)
        ni = max(1, n_bf16_cols // D)
        b.on("gpsimd", lambda g: g.dma_start(
            out=y_out.rearrange("(i p) d -> p i d", p=128)[:, 0:ni, :],
            in_=src_ap[:, 0:n_bf16_cols].rearrange("p (i d) -> p i d", d=min(D, n_bf16_cols))), inc="s_g")
        ov = b.cnt["s_g"]
        b.on("gpsimd", lambda g, v=ov: g.wait_ge(b.sem["s_g"], v))
        nc.detect_race_conditions = False
        blk2 = es.enter_context(nc.Block())
        with es:
            b.replay(blk2)
        return nc

    def dump_dram_and_finish(dram_ap, deps):
        b.waits("gpsimd", *deps)
        b.on("gpsimd", lambda g: g.dma_start(
            out=y_out[0:1024, :], in_=dram_ap), inc="s_g")
        ov = b.cnt["s_g"]
        b.on("gpsimd", lambda g, v=ov: g.wait_ge(b.sem["s_g"], v))
        nc.detect_race_conditions = False
        blk2 = es.enter_context(nc.Block())
        with es:
            b.replay(blk2)
        return nc

    def finish_here():
        b.waits("sync", "wq_all_st", "x_acc", "xload", "x2_done", "f2_done")
        b.on("sync", lambda s: s.dma_start(
            out=y_out.rearrange("(i p) d -> p i d", p=128),
            in_=xres.rearrange("p (i d) -> p i d", d=D)), inc="s_sp")
        ov = b.cnt["s_sp"]
        b.on("sync", lambda s, v=ov: s.wait_ge(b.sem["s_sp"], v))
        nc.detect_race_conditions = False
        blk2 = es.enter_context(nc.Block())
        with es:
            b.replay(blk2)
        return nc

    b.wait("vector", "xload")
    b.wait("scalar", "xload")
    b.wait("vector", "consts")
    b.wait("scalar", "consts")
    if stage == "p0":
        return finish_here()
    if stage == "dbgln":
        emit_ln("xload")
        b.waits("sync", "ln_ok", "ln_rstd")
        for idx, tsr in enumerate((lnmean, lnsum2, lnrstd, lnnmr)):
            b.on("sync", lambda s, t=tsr, idx=idx: s.dma_start(
                out=y_out.rearrange("(i p) d -> p i d", p=128)[:, idx, 0:NT],
                in_=t[:]), inc="s_sp")
        for k in range(4):
            b.on("sync", lambda s, k=k: s.dma_start(
                out=y_out.rearrange("(i p) d -> p i d", p=128)[:, 4 + k, 0:NT],
                in_=xres[:, (12 + k) * 1024:(12 + k) * 1024 + NT]), inc="s_sp")
        ov = b.cnt["s_sp"]
        b.on("sync", lambda s, v=ov: s.wait_ge(b.sem["s_sp"], v))
        nc.detect_race_conditions = False
        blk2 = es.enter_context(nc.Block())
        with es:
            b.replay(blk2)
        return nc
    rr = emit_ffn("f1", "xload", wprep=True)
    b.set("wq_all_st", at=("s_sp", b.cnt["s_sp"]))
    if rr == "DBG":
        nc.detect_race_conditions = False
        blk2 = es.enter_context(nc.Block())
        with es:
            b.replay(blk2)
        return nc
    if rr == "DBGX":
        return dump_and_finish(xq_fm, 16384, ["xq_all"])
    if stage == "dbgw":
        return dump_dram_and_finish(wgT_s[0:1024, 0:1024], ["wq_all_st"])
    if stage == "dbgwd":
        return dump_dram_and_finish(wdT_s[0:1024, 0:1024], ["wq_all_st"])
    if stage == "f1":
        return finish_here()

    emit_ln("f1_done")
    for t in range(NT):
        b.waits("scalar", "lnb_free")
        emit_ln_apply(t, lnb_t[:], "l1_ap")
        b.set("lnout_free", at=b.res["l1_ap"])
        emit_tp(lnb_t, 8,
                lambda sc, t=t: sc.activation(
                    fm3(ln1_fm, t * 128, 128),
                    psT[:].rearrange("p (z m) -> p z m", m=128), AF.Copy),
                "l1_ap", "l1_ev")
        b.set("lnb_free", at=b.res["tp_mm"])
    b.set("ln1_all", at=b.res["l1_ev"])
    b.waits("sync", "f1_done")
    b.on("sync", lambda s: s.dma_start(out=xsp_s, in_=xres), inc="s_sp")
    b.set("xspill", at=("s_sp", b.cnt["s_sp"]))
    if stage == "ln1":
        return finish_here()

    FIDX = {"r": 0, "k": 1, "v": 2, "g": 3, "w": 4}

    def emit_dx(c0):
        dx3 = dx_r.rearrange("p (j t) -> p j t", j=ND)
        ln3 = ln1_fm.rearrange("p (j t) -> p j t", j=ND)
        if c0 == 0:
            b.on("vector", lambda v: v.tensor_scalar(
                dx3[:, :, 0:1], ln3[:, :, 0:1], -1.0, None, ALU.mult))
            b.on("vector", lambda v: v.tensor_tensor(
                dx3[:, :, 1:512], ln3[:, :, 0:511], ln3[:, :, 1:512],
                ALU.subtract), inc="s_dve")
        else:
            b.on("vector", lambda v, c0=c0: v.tensor_tensor(
                dx3[:, :, 0:512], ln3[:, :, c0 - 1:c0 + 511],
                ln3[:, :, c0:c0 + 512], ALU.subtract), inc="s_dve")
        b.set("dx", "vector")

    def emit_mixpre(half, tc):
        c0 = half * 1024 + tc * 512
        b.waits("vector", "ln1_all", "dx_free", "xx_free", "su_mm")
        emit_dx(c0)
        xx3 = xx_r.rearrange("p (j t) -> p j t", j=ND)
        dx3 = dx_r.rearrange("p (j t) -> p j t", j=ND)
        ln3 = ln1_fm.rearrange("p (j t) -> p j t", j=ND)
        b.on("vector", lambda v: v.tensor_tensor(
            xx3, dx3, maa_x.unsqueeze(2).broadcast_to([128, ND, 512]), ALU.mult))
        b.on("vector", lambda v, c0=c0: v.tensor_tensor(
            xx3, xx3, ln3[:, :, c0:c0 + 512], ALU.add), inc="s_dve")
        b.set("xxx", "vector")
        b.set("dx_free", "vector")
        b.waits("tensor", "xxx", "consts", "psA_free", "psE_free")
        for j in range(ND):
            b.on("tensor", lambda te, j=j: te.matmul(
                psA[:], w1sb[:, j * 160:j * 160 + 128],
                xx_r[:, j * 512:(j + 1) * 512],
                start=(j == 0), stop=(j == ND - 1)))
        for j in range(ND):
            b.on("tensor", lambda te, j=j: te.matmul(
                psE[0:32, :], w1sb[:, j * 160 + 128:(j + 1) * 160],
                xx_r[:, j * 512:(j + 1) * 512],
                start=(j == 0), stop=(j == ND - 1)),
                inc="s_pe" if j == ND - 1 else None)
        b.set("mix_mm", "tensor")
        b.set("xx_free", "tensor")
        b.waits("scalar", "mix_mm", "mixsb_free")
        b.on("scalar", lambda s: s.activation(mixsb[:], psA[:], AF.Tanh))
        b.on("scalar", lambda s: s.activation(mix2sb[:], psE[0:32, :], AF.Tanh),
             inc="s_act")
        b.set("mix_ev", "scalar")
        b.set("psA_free", "scalar")
        b.set("psE_free", "scalar")
        b.wait("sync", "mix_ev")
        for fb in range(4):
            b.on("sync", lambda s, fb=fb, tc=tc: s.dma_start(
                out=mixf[:, (fb * 2 + tc) * 512:(fb * 2 + tc + 1) * 512],
                in_=mixsb[fb * 32:(fb + 1) * 32, :]), inc="s_ld")
        b.on("sync", lambda s, tc=tc: s.dma_start(
            out=mixf[:, (4 * 2 + tc) * 512:(4 * 2 + tc + 1) * 512],
            in_=mix2sb[:]), inc="s_ld")
        b.set("mix_dma", "sync")
        b.set("mixsb_free", at=b.res["mix_dma"])

    def emit_xf(f, half, tc):
        c0 = half * 1024 + tc * 512
        b.waits("vector", "xf_free", "dx_free", "su_mm")
        emit_dx(c0)
        fi = FIDX[f]
        for j in range(ND):
            pp = psD if j % 2 == 0 else psB
            pfree = "psD_free" if j % 2 == 0 else "psB_free"
            b.waits("tensor", pfree, "mix_dma", "consts",
                    "dx" if j == 0 else "noop")
            b.on("tensor", lambda te, pp=pp, j=j, fi=fi, tc=tc: te.matmul(
                pp[:], w2sb[:, fi * 1024 + j * 128:fi * 1024 + (j + 1) * 128],
                mixf[:, (fi * 2 + tc) * 512:(fi * 2 + tc + 1) * 512],
                start=True, stop=True), inc="s_pe")
            b.set("mf_mm", "tensor")
            b.waits("vector", "mf_mm", "sttsb_free")
            b.on("vector", lambda v, pp=pp, f=f, j=j: v.scalar_tensor_tensor(
                sttsb[:], pp[:], maa_of[f][:, j:j + 1],
                dx_r[:, j * 512:(j + 1) * 512], ALU.add, ALU.mult),
                inc="s_dve")
            b.set("xf_stt", "vector")
            b.wait("vector", "xf_stt")
            b.on("vector", lambda v, j=j, c0=c0: v.tensor_tensor(
                xf_r[:, j * 512:(j + 1) * 512], sttsb[:],
                ln1_fm[:, j * T + c0:j * T + c0 + 512], ALU.add), inc="s_dve")
            b.set(pfree, "vector")
            b.set("sttsb_free", "vector")
        b.set("xf", at=b.res["sttsb_free"])
        b.set("dx_free", at=b.res["sttsb_free"])

    def emit_proj(f, half, tc):
        c0 = half * 1024 + tc * 512
        if f == "w":
            b.waits("tensor", "xf", "psE_free")
            for j in range(ND):
                b.on("tensor", lambda te, j=j: te.matmul(
                    psE[0:64, :], dw1sb[:, j * 64:(j + 1) * 64],
                    xf_r[:, j * 512:(j + 1) * 512],
                    start=(j == 0), stop=(j == ND - 1)),
                    inc="s_pe" if j == ND - 1 else None)
            b.set("t1_mm", "tensor")
            b.set("xf_free", "tensor")
            b.wait("scalar", "t1_mm")
            b.on("scalar", lambda s, c0=c0: s.activation(
                tanh1[:, c0:c0 + 512], psE[0:64, :], AF.Tanh), inc="s_act")
            b.set("t1_ev", "scalar")
            b.set("psE_free", "scalar")
            return
        if f == "v":
            for st in range(4):
                b.waits("tensor", "xf" if st == 0 else "noop", "psC_free")
                for j in range(ND):
                    for nh in range(2):
                        b.on("tensor", lambda te, j=j, st=st, nh=nh: te.matmul(
                            psC[:, nh * 512:(nh + 1) * 512],
                            xf_r[:, j * 512 + st * 128: j * 512 + (st + 1) * 128],
                            attw[:, j * 1024 + nh * 512: j * 1024 + (nh + 1) * 512],
                            start=(j == 0), stop=(j == ND - 1)),
                            inc="s_pe" if (j == ND - 1 and nh == 1) else None)
                b.set("v_mm", "tensor")
                if st == 3:
                    b.set("xf_free", "tensor")
                b.wait("scalar", "v_mm")
                b.on("scalar", lambda s, st=st, tc=tc: s.activation(
                    v_tm[:, (tc * 4 + st) * 1024:(tc * 4 + st + 1) * 1024],
                    psC[:], AF.Copy), inc="s_act")
                b.set("v_ev", "scalar")
                b.set("psC_free", "scalar")
            return
        for jj in range(ND):
            pp = psA if jj % 2 == 0 else psD
            pfree = "psA_free" if jj % 2 == 0 else "psD_free"
            b.waits("tensor", pfree, "xf" if jj == 0 else "noop")
            for j in range(ND):
                b.on("tensor", lambda te, pp=pp, j=j, jj=jj: te.matmul(
                    pp[:], attw[:, j * 1024 + jj * 128: j * 1024 + (jj + 1) * 128],
                    xf_r[:, j * 512:(j + 1) * 512],
                    start=(j == 0), stop=(j == ND - 1)),
                    inc="s_pe" if j == ND - 1 else None)
            b.set("p_mm", "tensor")
            if jj == ND - 1:
                b.set("xf_free", "tensor")
            b.wait("scalar", "p_mm")
            if f == "g":
                b.waits("scalar", "g_dma")
                b.on("scalar", lambda s, pp=pp, jj=jj: s.activation(
                    gstage[:], pp[:], AF.Silu, bias=bgv[:, jj:jj + 1]),
                    inc="s_act")
                b.set("g_ev", "scalar")
                b.set(pfree, "scalar")
                b.wait("sync", "g_ev")
                b.on("sync", lambda s, jj=jj, c0=c0: s.dma_start(
                    out=gsp_s[:, jj * T + c0: jj * T + c0 + 512],
                    in_=gstage[:]), inc="s_ld")
                b.set("g_dma", "sync")
            else:
                dst = q_fm if f == "r" else k_fm
                b.on("scalar", lambda s, pp=pp, jj=jj, c0=c0, dst=dst: s.activation(
                    dst[:, jj * T + c0: jj * T + c0 + 512], pp[:], AF.Copy),
                    inc="s_act")
                b.set("p_ev", "scalar")
                b.set(pfree, "scalar")

    def emit_S_half(half):
        b.wait("scalar", "xspill")
        for tc in range(2):
            emit_mixpre(half, tc)
        for f in ("w", "r", "k", "v", "g"):
            if f != "w":
                wname = {"r": "Wq", "k": "Wk", "v": "Wv", "g": "Wg"}[f]
                b.waits("gpsimd", "attw_free", "q_scratch", "q_act",
                        "su_mm", "o_mm")
                b.on("gpsimd", lambda g, wname=wname: g.dma_start(
                    out=attw[:].rearrange("p (j n) -> p j n", j=ND),
                    in_=W_d[wname].rearrange("(j p) n -> p j n", p=128)),
                    inc="s_g")
                b.set("w_att", "gpsimd")
            for tc in range(2):
                emit_xf(f, half, tc)
                if f != "w":
                    b.wait("tensor", "w_att")
                emit_proj(f, half, tc)
            if f != "w":
                b.set("attw_free", at=b.res["xf_free"])

    import os
    E_NGRP = int(os.environ.get("KE_NGRP", "4"))
    E_NCH = int(os.environ.get("KE_NCH", "4"))
    E_PARTS = os.environ.get("KE_PARTS", "sakouz123")

    def emit_E_half(half):
        for grp in range(E_NGRP):
            g0 = half * 1024 + grp * 256
            lg = grp * 256
            for j in range(ND):
                pp = psE if j % 2 == 0 else psD
                pfree = "psE_free" if j % 2 == 0 else "psD_free"
                b.waits("tensor", pfree, "t1_ev", "dec_free", "consts")
                b.on("tensor", lambda te, j=j, g0=g0, pp=pp: te.matmul(
                    pp[:, 0:256], dw2sb[:, j * 128:(j + 1) * 128],
                    tanh1[:, g0:g0 + 256], start=True, stop=True), inc="s_pe")
                b.set("dec_mm", "tensor")
                b.wait("vector", "dec_mm")
                b.on("vector", lambda v, j=j, pp=pp: v.tensor_scalar(
                    dec_g[:, j * 256:(j + 1) * 256], pp[:, 0:256],
                    tdec[:, j:j + 1], None, ALU.add), inc="s_dve")
                b.set(pfree, "vector")
                b.set("dec_j", "vector")
            b.set("dec_done", at=b.res["dec_j"])
            b.waits("scalar", "dec_done", "egd_free")
            b.on("scalar", lambda s: s.activation(egd_g[:], dec_g[:], AF.Exp),
                 inc="s_act")
            b.set("egd", "scalar")
            b.wait("scalar", "egd")
            b.on("scalar", lambda s: s.activation(
                dec_g[:], egd_g[:], AF.Exp, scale=-1.0), inc="s_act")
            b.set("emg", "scalar")
            b.set("dec_free", at=("s_act", b.cnt["s_act"]))
            b.wait("vector", "emg")
            b.on("vector", lambda v: v.tensor_scalar(
                V(dec_g[:]), V(dec_g[:]), -1.0, 1.0, ALU.mult, ALU.add))
            kf3 = fm3(k_fm, g0, 256)
            dg3 = dec_g[:].rearrange("p (j t) -> p j t", j=ND)
            b.on("vector", lambda v, kf3=kf3, dg3=dg3: v.tensor_tensor(
                kf3, kf3, dg3, ALU.mult), inc="s_dve")
            b.set("kp", "vector")
            if os.environ.get("KE_DUMMYDMA"):
                b.waits("sync", "g_dma")
                b.on("sync", lambda s, g0=g0: s.dma_start(
                    out=gat_r.rearrange("p (j t) -> p j t", j=ND),
                    in_=gsp_s.rearrange("p (j t) -> p j t", j=ND)[:, :, g0:g0 + 256]),
                    inc="s_ld")
                b.set("g_rl", "sync")
            for cc in range(E_NCH):
                c0 = g0 + cc * 64
                l0 = cc * 64
                par = (c0 // 64) % 2
                tloc = (c0 - half * 1024) // 128
                if os.environ.get("KE_ESER"):
                    b.waits("tensor", "og_ev", "S_ok", "o_sb")
                b.waits("vector", "egd", "cb_free", "expq")
                for j in range(ND):
                    b.on("vector", lambda v, j=j, l0=l0: v.tensor_tensor_scan(
                        cbsb[:, j * 64:(j + 1) * 64],
                        onessb[:].broadcast_to([128, 64]),
                        egd_g[:, j * 256 + l0:j * 256 + l0 + 64],
                        0.0, ALU.mult, ALU.add),
                        inc="s_dve" if j == ND - 1 else None)
                b.set("cb", "vector")
                b.wait("vector", "cb")
                cb3 = cbsb[:].rearrange("p (j t) -> p j t", j=ND)
                cbL = cb3[:, :, 63:64].squeeze(2)
                b.wait("vector", "sm_exp")
                b.on("vector", lambda v, cbL=cbL: v.tensor_scalar(
                    smalls[:, 0:8], cbL, 0.5, None, ALU.mult), inc="s_dve")
                b.set("chalf", "vector")
                b.wait("scalar", "chalf")
                b.on("scalar", lambda s: s.activation(
                    smalls[:, 8:16], smalls[:, 0:8], AF.Exp))
                b.on("scalar", lambda s: s.activation(
                    smalls[:, 16:24], smalls[:, 0:8], AF.Exp, scale=-1.0))
                b.on("scalar", lambda s: s.activation(
                    smalls[:, 24:32], smalls[:, 0:8], AF.Exp, scale=-2.0),
                    inc="s_act")
                b.set("sm_exp", "scalar")
                b.waits("scalar", "expq_free")
                b.on("scalar", lambda s: s.activation(
                    expsc[:], cbsb[:], AF.Exp, scale=-1.0, bias=-LN8),
                    inc="s_act")
                b.set("expq", "scalar")
                b.waits("vector", "chalf", "expk_free")
                b.on("vector", lambda v: v.tensor_tensor(
                    expsc2[:].rearrange("p (j t) -> p j t", j=ND), cb3,
                    smalls[:, 0:8].unsqueeze(2).broadcast_to([128, ND, 64]),
                    ALU.subtract), inc="s_dve")
                b.set("argk", "vector")
                b.set("cb_free", "vector")
                b.wait("scalar", "argk")
                b.on("scalar", lambda s: s.activation(
                    expsc2[:], expsc2[:], AF.Exp), inc="s_act")
                b.set("expk", "scalar")
                q3 = fm3(q_fm, c0, 64)
                k3 = fm3(k_fm, c0, 64)
                e1 = expsc[:].rearrange("p (j t) -> p j t", j=ND)
                e2 = expsc2[:].rearrange("p (j t) -> p j t", j=ND)
                qs3 = qsc[:].rearrange("p (j t) -> p j t", j=ND)
                qss3 = qss[:].rearrange("p (j t) -> p j t", j=ND)
                ks3 = ksc[:].rearrange("p (j t) -> p j t", j=ND)
                kh3 = khc[:].rearrange("p (j t) -> p j t", j=ND)
                b.waits("vector", "expq", "expk", "sm_exp", "kp",
                        "qk_free", "p_mm", "v_mm", "t1_mm")
                b.on("vector", lambda v, qs3=qs3, q3=q3, e1=e1: v.tensor_tensor(
                    qs3, q3, e1, ALU.mult))
                b.on("vector", lambda v, ks3=ks3, k3=k3, e2=e2: v.tensor_tensor(
                    ks3, k3, e2, ALU.mult))
                b.on("vector", lambda v, qss3=qss3, qs3=qs3: v.tensor_tensor(
                    qss3, qs3, smalls[:, 8:16].unsqueeze(2).broadcast_to(
                        [128, ND, 64]), ALU.mult))
                b.on("vector", lambda v, kh3=kh3, ks3=ks3: v.tensor_tensor(
                    kh3, ks3, smalls[:, 16:24].unsqueeze(2).broadcast_to(
                        [128, ND, 64]), ALU.mult), inc="s_dve")
                b.set("qk", "vector")
                b.set("expq_free", "vector")
                b.set("expk_free", "vector")
                if "a" not in E_PARTS:
                    continue
                b.waits("tensor", "qk", "psA_free", "psD_free")
                for hp in range(ND):
                    b.on("tensor", lambda te, hp=hp, par=par: te.matmul(
                        psA[par * 64:par * 64 + 64, hp * 64:(hp + 1) * 64],
                        ksc[0:64, hp * 64:(hp + 1) * 64],
                        qss[0:64, hp * 64:(hp + 1) * 64],
                        start=True, stop=True, tile_position=(0, par * 64)))
                    b.on("tensor", lambda te, hp=hp, par=par: te.matmul(
                        psD[par * 64:par * 64 + 64, hp * 64:(hp + 1) * 64],
                        ksc[64:128, hp * 64:(hp + 1) * 64],
                        qss[64:128, hp * 64:(hp + 1) * 64],
                        start=True, stop=True, tile_position=(64, par * 64)),
                        inc="s_pe" if hp == ND - 1 else None)
                b.set("a_mm", "tensor")
                b.waits("vector", "a_mm", "amsk_free")
                mrow = maskt[64 * par:64 * par + 64, :].unsqueeze(1)\
                    .broadcast_to([64, ND, 64])
                b.on("vector", lambda v, par=par, mrow=mrow: v.tensor_tensor(
                    amskA[64 * par:64 * par + 64, :].rearrange(
                        "p (j t) -> p j t", j=ND),
                    psA[64 * par:64 * par + 64, :].rearrange(
                        "p (j t) -> p j t", j=ND), mrow, ALU.mult))
                b.on("vector", lambda v, par=par, mrow=mrow: v.tensor_tensor(
                    amskB[64 * par:64 * par + 64, :].rearrange(
                        "p (j t) -> p j t", j=ND),
                    psD[64 * par:64 * par + 64, :].rearrange(
                        "p (j t) -> p j t", j=ND), mrow, ALU.mult),
                    inc="s_dve")
                b.set("am_ev", "vector")
                b.set("psA_free", "vector")
                b.set("psD_free", "vector")
                if "k" not in E_PARTS:
                    continue
                b.waits("tensor", "qk", "psT_free")
                for hp in range(ND):
                    b.on("tensor", lambda te, hp=hp, par=par: te.matmul(
                        psT[par * 64:par * 64 + 64, hp * 128:(hp + 1) * 128],
                        khc[:, hp * 64:(hp + 1) * 64], ident[:],
                        start=True, stop=True, is_transpose=True,
                        tile_position=(0, par * 64)),
                        inc="s_pe" if hp == ND - 1 else None)
                b.set("kt_mm", "tensor")
                b.waits("vector", "kt_mm", "ktm_free")
                b.on("vector", lambda v, par=par: v.tensor_copy(
                    ktm[par * 64:par * 64 + 64, :],
                    psT[par * 64:par * 64 + 64, :]), inc="s_dve")
                b.set("kt_ev", "vector")
                b.set("psT_free", "vector")
                if "o" not in E_PARTS:
                    continue
                b.waits("tensor", "am_ev", "psB_free", "S_ok", "v_ev",
                        "psC_free")
                for hp in range(ND):
                    b.on("tensor", lambda te, hp=hp: te.matmul(
                        psB[0:64, hp * 64:(hp + 1) * 64],
                        Ssb[0:64, hp * 64:(hp + 1) * 64],
                        qsc[0:64, hp * 64:(hp + 1) * 64],
                        start=True, stop=True, tile_position=(0, 0)))
                    b.on("tensor", lambda te, hp=hp: te.matmul(
                        psB[64:128, hp * 64:(hp + 1) * 64],
                        Ssb[64:128, hp * 64:(hp + 1) * 64],
                        qsc[64:128, hp * 64:(hp + 1) * 64],
                        start=True, stop=True, tile_position=(64, 64)))
                    b.on("tensor", lambda te, hp=hp, par=par, tloc=tloc: te.matmul(
                        psC[0:64, hp * 64:(hp + 1) * 64],
                        v_tm[par * 64:par * 64 + 64,
                             tloc * 1024 + hp * 128: tloc * 1024 + hp * 128 + 64],
                        amskA[par * 64:par * 64 + 64, hp * 64:(hp + 1) * 64],
                        start=True, stop=True, tile_position=(par * 64, 0)))
                    b.on("tensor", lambda te, hp=hp, par=par, tloc=tloc: te.matmul(
                        psC[64:128, hp * 64:(hp + 1) * 64],
                        v_tm[par * 64:par * 64 + 64,
                             tloc * 1024 + hp * 128 + 64:
                             tloc * 1024 + hp * 128 + 128],
                        amskB[par * 64:par * 64 + 64, hp * 64:(hp + 1) * 64],
                        start=True, stop=True, tile_position=(par * 64, 64)),
                        inc="s_pe" if hp == ND - 1 else None)
                b.set("o_mm", "tensor")
                og_fm = og1_fm if half == 0 else og2_fm
                cloc = (c0 - half * 1024) // 64
                if "u" not in E_PARTS:
                    continue
                b.waits("tensor", "kt_ev", "psE_free", "S_scaled_read",
                        "v_ev")
                for hp in range(ND):
                    b.on("tensor", lambda te, hp=hp, par=par, tloc=tloc: te.matmul(
                        psE[0:64, hp * 64:(hp + 1) * 64],
                        ktm[par * 64:par * 64 + 64, hp * 128:hp * 128 + 64],
                        v_tm[par * 64:par * 64 + 64,
                             tloc * 1024 + hp * 128: tloc * 1024 + hp * 128 + 64],
                        start=True, stop=True, tile_position=(par * 64, 0)))
                    b.on("tensor", lambda te, hp=hp, par=par, tloc=tloc: te.matmul(
                        psE[64:128, hp * 64:(hp + 1) * 64],
                        ktm[par * 64:par * 64 + 64, hp * 128 + 64:hp * 128 + 128],
                        v_tm[par * 64:par * 64 + 64,
                             tloc * 1024 + hp * 128 + 64:
                             tloc * 1024 + hp * 128 + 128],
                        start=True, stop=True, tile_position=(par * 64, 64)),
                        inc="s_pe" if hp == ND - 1 else None)
                b.set("su_mm", "tensor")
                b.set("ktm_free", "tensor")
                b.set("qk_free", "tensor")
                if "z" not in E_PARTS:
                    continue
                if "1" not in E_PARTS:
                    continue
                b.waits("vector", "o_mm", "sm_exp")
                b.on("vector", lambda v: v.tensor_tensor(
                    Stmp[:].rearrange("p (j t) -> p j t", j=ND),
                    Ssb[:].rearrange("p (j t) -> p j t", j=ND),
                    smalls[:, 24:32].unsqueeze(2).broadcast_to([128, ND, 64]),
                    ALU.mult), inc="s_dve")
                b.set("S_scaled_read", "vector")
                if "2" not in E_PARTS:
                    continue
                b.waits("vector", "o_mm")
                b.on("vector", lambda v: v.tensor_copy(sttsb[:], psB[:]),
                     inc="s_dve")
                b.set("o_sb", "vector")
                b.set("psB_free", "vector")
                if "3" not in E_PARTS:
                    continue
                if os.environ.get("KE_OGMEMSET"):
                    b.on("vector", lambda v, og_fm=og_fm, cloc=cloc: v.memset(
                        og_fm[:, cloc * 512:(cloc + 1) * 512],
                        float(100 + cloc)), inc="s_dve")
                else:
                    b.on("vector", lambda v, og_fm=og_fm, cloc=cloc: v.tensor_tensor(
                        V(og_fm[:, cloc * 512:(cloc + 1) * 512]), V(sttsb[:]),
                        V(psC[:, 0:512]), ALU.add), inc="s_dve")
                b.set("og_ev", "vector")
                b.set("psC_free", "vector")
                b.waits("vector", "su_mm")
                b.on("vector", lambda v: v.tensor_tensor(
                    V(Ssb[:]), V(Stmp[:]), V(psE[:]), ALU.add), inc="s_dve")
                b.set("S_ok", "vector")
                b.set("psE_free", "vector")
                b.set("amsk_free", at=b.res["o_mm"])
            b.set("egd_free", at=b.res["cb"])
        b.set(f"og_half{half}", at=b.res["og_ev"])

    b.res.setdefault("noop", ("s_dve", 0))

    for half in range(2):
        emit_S_half(half)
        if stage == f"s{half}":
            return finish_here()
        emit_E_half(half)
        if stage == f"e{half}":
            if os.environ.get("KE_DUMPOG"):
                return dump_and_finish(og1_fm if half == 0 else og2_fm, 8192,
                                       ["og_ev"])
            return finish_here()

    b.waits("gpsimd", "attw_free", "q_scratch", "q_act", "su_mm", "o_mm")
    b.on("gpsimd", lambda g: g.dma_start(
        out=attw[:].rearrange("p (j n) -> p j n", j=ND),
        in_=W_d["Wo"].rearrange("(j p) n -> p j n", p=128)), inc="s_g")
    b.set("wo_ld", "gpsimd")
    gatp5s = (AR4b[:, 0:1024], AR4b[:, 6144:7168])
    xrlds = (xrld, AR4[:, 2048:3072])
    _p5ret = [None]
    for t in range(NT):
        og_fm = og1_fm if t < 8 else og2_fm
        tl = t % 8
        pp5 = t % 2
        gatp5 = gatp5s[pp5]
        xrl = xrlds[pp5]
        b.waits("sync", "xspill", f"xrld_free{pp5}", "qk", f"gatp5_free{pp5}")
        b.on("sync", lambda s, t=t, xrl=xrl: s.dma_start(
            out=xrl[:], in_=xsp_s[:, t * D:(t + 1) * D]), inc="s_ld")
        for c_ in range(2):
            b.on("sync", lambda s, t=t, c_=c_, gatp5=gatp5: s.dma_start(
                out=gatp5[:, c_ * 512:(c_ + 1) * 512].rearrange(
                    "p (j z) -> p j z", j=ND),
                in_=gsp_s.rearrange("p (j tt) -> p j tt", j=ND)[
                    :, :, t * 128 + c_ * 64: t * 128 + c_ * 64 + 64]),
                inc="s_ld")
        b.set("x_rl", "sync")
        b.waits("vector", "x_rl", f"og_half{0 if t < 8 else 1}", "wo_mm")
        b.on("vector", lambda v, og_fm=og_fm, tl=tl, gatp5=gatp5: v.tensor_tensor(
            V(og_fm[:, tl * 1024:(tl + 1) * 1024]),
            V(og_fm[:, tl * 1024:(tl + 1) * 1024]), V(gatp5[:]), ALU.mult),
            inc="s_dve")
        b.set("og_gated", "vector")
        b.set(f"gatp5_free{pp5}", "vector")
        if stage == "dbgp5" and t == int(os.environ.get("KE_P5T", "1")):
            return dump_and_finish(og_fm[:, tl * 1024:tl * 1024 + 1024], 1024,
                                   ["og_gated"])
        b.waits("tensor", "wo_ld", "psC_free", "og_gated")
        for c_ in range(2):
            for nh in range(2):
                for j in range(ND):
                    b.on("tensor", lambda te, j=j, nh=nh, tl=tl, c_=c_,
                         og_fm=og_fm: te.matmul(
                        psC[c_ * 64:(c_ + 1) * 64, nh * 512:(nh + 1) * 512],
                        og_fm[:, tl * 1024 + c_ * 512 + j * 64:
                              tl * 1024 + c_ * 512 + (j + 1) * 64],
                        attw[:, j * 1024 + nh * 512: j * 1024 + (nh + 1) * 512],
                        start=(j == 0), stop=(j == ND - 1),
                        tile_position=(0, c_ * 64)),
                        inc="s_pe" if (c_ == 1 and j == ND - 1 and nh == 1) else None)
        b.set("wo_mm", "tensor")
        b.waits("vector", "wo_mm", "x_rl")
        b.on("vector", lambda v, t=t, xrl=xrl: v.tensor_tensor(
            V(xres[:, t * D:(t + 1) * D]), V(xrl[:]), V(psC[:]), ALU.add),
            inc="s_dve")
        b.set("x2_acc", "vector")
        b.set("psC_free", "vector")
        b.set(f"xrld_free{pp5}", "vector")
    b.set("x2_done", at=b.res["x2_acc"])
    if stage == "p5":
        return finish_here()
    if stage == "dbgp5":
        raise RuntimeError("dbgp5 tile index out of range")

    for eng in ENGS:
        b.waits(eng, "x2_done", "wo_mm")
    emit_ffn("f2", "x2_done")
    b.waits("sync", "f2_done")
    b.on("sync", lambda s: s.dma_start(
        out=y_out.rearrange("(i p) d -> p i d", p=128),
        in_=xres.rearrange("p (i d) -> p i d", d=D)), inc="s_sp")
    out_val = b.cnt["s_sp"]
    b.on("sync", lambda s, v=out_val: s.wait_ge(b.sem["s_sp"], v))

    nc.detect_race_conditions = False
    blk = es.enter_context(nc.Block())
    with es:
        b.replay(blk)
    return nc


_NC_CACHE = [None]


def kernel(L0_attn_hidden_states, L0_attn_time_maa_x, L0_attn_time_maa_r,
           L0_attn_time_maa_k, L0_attn_time_maa_v, L0_attn_time_maa_w,
           L0_attn_time_maa_g, L0_attn_time_maa_w1, L0_attn_time_maa_w2,
           L0_attn_time_decay, L0_attn_time_decay_w1, L0_attn_time_decay_w2,
           L0_attn_Wq, L0_attn_Wk, L0_attn_Wv, L0_attn_Wo, L0_attn_Wg,
           L0_attn_bg, L0_ffn_w_gate, L0_ffn_w_up, L0_ffn_w_down):
    x = np.ascontiguousarray(np.asarray(L0_attn_hidden_states, dtype=np.float32))
    assert x.shape == (8, T, D)
    vecs0 = np.stack([
        np.asarray(L0_attn_time_maa_x, np.float32).reshape(D),
        np.asarray(L0_attn_time_maa_r, np.float32).reshape(D),
        np.asarray(L0_attn_time_maa_k, np.float32).reshape(D),
        np.asarray(L0_attn_time_maa_v, np.float32).reshape(D),
        np.asarray(L0_attn_time_maa_w, np.float32).reshape(D),
        np.asarray(L0_attn_time_maa_g, np.float32).reshape(D),
        np.asarray(L0_attn_time_decay, np.float32).reshape(D),
        np.asarray(L0_attn_bg, np.float32).reshape(D),
    ], axis=0)
    vecs = np.ascontiguousarray(
        vecs0.reshape(8, 8, 128).transpose(2, 0, 1).reshape(128, 64))
    ident = np.eye(128, dtype=np.float32)
    mask = np.tile(np.tril(np.ones((CH, CH), np.float32)).T * 0 +
                   (np.arange(CH)[:, None] <= np.arange(CH)[None, :]).astype(np.float32),
                   (2, 1))
    common = {
        "w_gate": np.ascontiguousarray(np.asarray(L0_ffn_w_gate, np.float32)),
        "w_up": np.ascontiguousarray(np.asarray(L0_ffn_w_up, np.float32)),
        "w_down": np.ascontiguousarray(np.asarray(L0_ffn_w_down, np.float32)),
        "Wq": np.ascontiguousarray(np.asarray(L0_attn_Wq, np.float32)),
        "Wk": np.ascontiguousarray(np.asarray(L0_attn_Wk, np.float32)),
        "Wv": np.ascontiguousarray(np.asarray(L0_attn_Wv, np.float32)),
        "Wo": np.ascontiguousarray(np.asarray(L0_attn_Wo, np.float32)),
        "Wg": np.ascontiguousarray(np.asarray(L0_attn_Wg, np.float32)),
        "maa_w1": np.ascontiguousarray(np.asarray(L0_attn_time_maa_w1, np.float32)),
        "maa_w2": np.ascontiguousarray(np.asarray(L0_attn_time_maa_w2, np.float32)),
        "dec_w1": np.ascontiguousarray(np.asarray(L0_attn_time_decay_w1, np.float32)),
        "dec_w2": np.ascontiguousarray(np.asarray(L0_attn_time_decay_w2, np.float32)),
        "vecs": vecs,
        "ident": ident,
        "mask": np.ascontiguousarray(mask),
    }
    in_maps = [dict(common, x_in=np.ascontiguousarray(x[i])) for i in range(8)]
    nc = build()
    res = run_bass_kernel_spmd(nc, in_maps, core_ids=list(range(8)))
    out = np.stack([np.asarray(r["y_out"]) for r in res.results], axis=0)
    return out.astype(np.float32)


if __name__ == "__main__":
    import sys
    if "--build" in sys.argv:
        nc = build()
        total = sum(len(v) for v in nc.m.funcs[0].blocks[0].instructions) if False else None
        print("build ok")


# version 07
# speedup vs baseline: 1.4772x; 1.0035x over previous
import numpy as np
from contextlib import ExitStack

import concourse.bass as bass
import concourse.mybir as mybir
from concourse.bass_utils import run_bass_kernel_spmd

F32 = mybir.dt.float32
BF16 = mybir.dt.bfloat16
ALU = mybir.AluOpType
AF = mybir.ActivationFunctionType
AX = mybir.AxisListType

T, D, I = 2048, 1024, 2816
NT, ND, NI = T // 128, D // 128, I // 128
CH = 64
LN8 = float(np.log(np.float64(8.0)))
R6 = float(np.float32(1.0 / 6.0))
R1024 = float(np.float32(1.0 / 1024.0))
EPS = 1e-5
ICHUNKS = [(k * 512, 512) for k in range(5)] + [(2560, 256)]
ENGS = ("sync", "gpsimd", "tensor", "vector", "scalar")


def V(ap):
    sh = ap.shape
    if len(sh) == 2 and sh[1] > 512:
        n = sh[1]
        z = max(d for d in range(1, 513) if n % d == 0)
        return ap.rearrange("p (a z) -> p a z", z=z)
    return ap


class B:

    def __init__(self, nc, es):
        self.nc = nc
        self.q = {e: [] for e in ENGS}
        names = ("s_ld", "s_g", "s_pe", "s_dve", "s_act", "s_sp", "s_og")
        self.sem = {n: es.enter_context(nc.semaphore(n)) for n in names}
        self.cnt = {n: 0 for n in names}
        self.res = {}

    def on(self, eng, fn, inc=None, inc_n=None):
        if inc is None:
            self.q[eng].append(fn)
            return None
        n = inc_n if inc_n is not None else (16 if eng in ("sync", "gpsimd") else 1)
        self.cnt[inc] += n

        def thunk(e, fn=fn, s=self.sem[inc], n=n):
            fn(e).then_inc(s, n)
        self.q[eng].append(thunk)
        return self.cnt[inc]

    def now(self, eng):
        s = {"sync": "s_ld", "gpsimd": "s_g", "tensor": "s_pe",
             "vector": "s_dve", "scalar": "s_act"}[eng]
        return (s, self.cnt[s])

    def set(self, res, eng=None, at=None):
        self.res[res] = at if at is not None else self.now(eng)

    def wait(self, eng, res):
        if res not in self.res:
            return
        sname, val = self.res[res]
        if val > 0:
            self.q[eng].append(lambda e, s=self.sem[sname], v=val: e.wait_ge(s, v))

    def waits(self, eng, *ress):
        best = {}
        for r in ress:
            if r in self.res:
                sname, val = self.res[r]
                best[sname] = max(best.get(sname, 0), val)
        for sname, val in best.items():
            if val > 0:
                self.q[eng].append(lambda e, s=self.sem[sname], v=val: e.wait_ge(s, v))

    def replay(self, blk):
        for eng in ENGS:
            thunks = self.q[eng]

            def body(e, thunks=thunks):
                for t in thunks:
                    t(e)
            getattr(blk, eng)(body)


def build(stage="full"):
    nc = bass.Bass("TRN2", target_bir_lowering=False, debug=False)

    x_in = nc.dram_tensor("x_in", [T, D], F32, kind="ExternalInput").ap()
    y_out = nc.dram_tensor("y_out", [T, D], F32, kind="ExternalOutput").ap()
    wg_d = nc.dram_tensor("w_gate", [I, D], F32, kind="ExternalInput").ap()
    wu_d = nc.dram_tensor("w_up", [I, D], F32, kind="ExternalInput").ap()
    wd_d = nc.dram_tensor("w_down", [D, I], F32, kind="ExternalInput").ap()
    W_d = {f: nc.dram_tensor(f, [D, D], F32, kind="ExternalInput").ap()
           for f in ("Wq", "Wk", "Wv", "Wo", "Wg")}
    w1_d = nc.dram_tensor("maa_w1", [D, 160], F32, kind="ExternalInput").ap()
    w2_d = nc.dram_tensor("maa_w2", [5, 32, D], F32, kind="ExternalInput").ap()
    dw1_d = nc.dram_tensor("dec_w1", [D, 64], F32, kind="ExternalInput").ap()
    dw2_d = nc.dram_tensor("dec_w2", [64, D], F32, kind="ExternalInput").ap()
    vecs_d = nc.dram_tensor("vecs", [128, 64], F32, kind="ExternalInput").ap()
    ident_d = nc.dram_tensor("ident", [128, 128], F32, kind="ExternalInput").ap()
    mask_d = nc.dram_tensor("mask", [128, CH], F32, kind="ExternalInput").ap()

    wgT_s = nc.dram_tensor("wgT_s", [D, I], BF16, kind="Internal").ap()
    wuT_s = nc.dram_tensor("wuT_s", [D, I], BF16, kind="Internal").ap()
    wdT_s = nc.dram_tensor("wdT_s", [I, D], BF16, kind="Internal").ap()
    xsp_s = nc.dram_tensor("xsp_s", [128, NT * D], F32, kind="Internal").ap()
    gsp_s = nc.dram_tensor("gsp_s", [128, ND * T], BF16, kind="Internal").ap()

    es = ExitStack()

    def sbt(name, p, n, dt=F32):
        return es.enter_context(nc.sbuf_tensor(name, [p, n], dt))

    def pst_(name, p, n, dt=F32):
        return es.enter_context(nc.psum_tensor(name, [p, n], dt))

    AR0 = sbt("AR0", 128, 16384)
    AR1 = sbt("AR1", 128, 8192)
    AR2 = sbt("AR2", 128, 8192)
    AR3 = sbt("AR3", 128, 8192)
    AR4 = sbt("AR4", 128, 4096)

    ident = sbt("identsb", 128, 128, BF16)
    maskt = sbt("masksb", 128, CH)
    vecs = sbt("vecssb", 128, 64)
    w1sb = sbt("w1sb", 128, 8 * 160, BF16)
    w2sb = sbt("w2sb", 32, 5 * 1024, BF16)
    dw1sb = sbt("dw1sb", 128, 8 * 64, BF16)
    dw2sb = sbt("dw2sb", 64, 1024, BF16)
    tanh1 = sbt("tanh1", 64, T, BF16)
    Ssb = sbt("Ssb", 128, 512, BF16)
    Stmp = sbt("Stmp", 128, 512, BF16)
    mixsb = sbt("mixsb", 128, 512, BF16)
    mix2sb = sbt("mix2sb", 32, 512, BF16)
    cbsb = sbt("cbsb", 128, 8 * CH)
    smalls = sbt("smalls", 128, 64)
    sttsb = sbt("sttsb", 128, 512)
    gstage = sbt("gstage", 128, 512, BF16)
    lnmean = sbt("lnmean", 128, NT)
    lnsum2 = sbt("lnsum2", 128, NT)
    lnrstd = sbt("lnrstd", 128, NT)
    lnnmr = sbt("lnnmr", 128, NT)
    onessb = sbt("onessb", 128, 1)
    q_bm = sbt("q_bm", 128, 88)
    q_sc = sbt("q_sc", 128, 88)
    q_rs = sbt("q_rs", 128, 88)
    q_hs = sbt("q_hs", 128, 88, BF16)
    thrsb = sbt("thrsb", 128, 2)

    psA = pst_("psA", 128, 512)
    psB = pst_("psB", 128, 512)
    psC = pst_("psC", 128, 1024)
    psD = pst_("psD", 128, 512)
    psE = pst_("psE", 128, 512)
    psT = pst_("psT", 128, 1024, BF16)

    b = B(nc, es)
    cstsb = sbt("cstsb", 128, 9)
    nc.const_aps.register(F32, EPS, cstsb[:, 0:1])
    nc.const_aps.register(F32, -LN8, cstsb[:, 1:2])
    QTHR = (5.0, 3.5, 2.5, 1.75, 1.25, 0.75, 0.25)
    for qi_, qt_ in enumerate(QTHR):
        nc.const_aps.register(F32, -qt_, cstsb[:, 2 + qi_:3 + qi_])

    AR0b = AR0[:].bitcast(BF16)
    AR1b = AR1[:].bitcast(BF16)
    AR2b = AR2[:].bitcast(BF16)
    AR3b = AR3[:].bitcast(BF16)
    AR4b = AR4[:].bitcast(BF16)
    xres = AR0[:]
    q_fm = AR0b[:, 0:16384]
    k_fm = AR0b[:, 16384:32768]
    xq_fm = AR1b[:, 0:16384]
    ln1_fm = AR1b[:, 0:16384]
    og2_fm = AR1b[:, 0:8192]
    og1_fm = AR3b[:, 8192:16384]
    attw = AR3b[:, 0:8192]
    v_tm = AR2b[:, 0:8192]
    dec_g = AR2[:, 4096:6144]
    egd_g = AR2[:, 6144:8192]
    wch_g = AR2b[:, 0:4096]
    wch_u = AR2b[:, 4096:8192]
    wch_d = AR2b[:, 8192:12288]
    mixf = AR2b[0:32, 8192:13312]
    ffnsil = AR4[:, 2560:3072]
    expsc = AR4[:, 1024:1536]
    expsc2 = AR4[:, 1536:2048]
    ktm = AR4b[:, 4096:5120]
    qsc = AR4b[:, 5120:5632]
    qss = AR4b[:, 5632:6144]
    ksc = AR4b[:, 6144:6656]
    khc = AR3b[:, 0:512]
    amskA = AR3b[:, 512:1024]
    amskB = AR3b[:, 1024:1536]
    q_y2 = AR2[:, 6144:7552]
    q_sg2 = AR2[:].bitcast(BF16)[:, 15104:16128]
    q_y = AR3[:, 0:1408]
    q_a = AR3[:, 1408:2816]
    q_b = AR3[:, 2816:4224]
    q_A = AR3[:, 4224:5632]
    q_sg = AR3b[:, 11264:12672]
    q_Q = AR3b[:, 12672:14080]
    wq_tm = AR3b[:, 14080:16384]
    lnsq = AR4[:, 0:1024]
    ln_t = AR4[:, 1024:2048]
    xrld = AR4[:, 1024:2048]
    hbuf = AR4[:, 2048:2560]
    lnb_t = AR4b[:, 5120:6144]
    hq_tm = AR4b[:, 6144:6656]
    hq_tm2 = AR3b[:, 15488:16000]
    hq_fm = AR4b[:, 6656:7168]
    tstage = AR4b[:, 7168:8192]
    dx_r = AR4b[:, 0:4096]
    xx_r = AR4b[:, 4096:8192]
    xf_r = AR4b[:, 4096:8192]
    gat_r = AR4b[:, 0:2048]

    def vsl(i):
        return vecs[:, i * 8:(i + 1) * 8]

    maa_x = vsl(0)
    maa_of = {"r": vsl(1), "k": vsl(2), "v": vsl(3), "w": vsl(4), "g": vsl(5)}
    tdec, bgv = vsl(6), vsl(7)

    def fm3(ap, c0, n):
        return ap.rearrange("p (j t) -> p j t", j=ND)[:, :, c0:c0 + n]

    b.on("gpsimd", lambda g: g.dma_start(out=ident[:], in_=ident_d), inc="s_g")
    b.on("gpsimd", lambda g: g.dma_start(out=maskt[:], in_=mask_d), inc="s_g")
    b.on("gpsimd", lambda g: g.dma_start(out=vecs[:], in_=vecs_d), inc="s_g")
    b.on("gpsimd", lambda g: g.dma_start(
        out=w1sb[:].rearrange("p (j n) -> p j n", j=ND),
        in_=w1_d.rearrange("(j p) n -> p j n", p=128)), inc="s_g")
    b.on("gpsimd", lambda g: g.dma_start(
        out=w2sb[:].rearrange("p (f d) -> p f d", f=5),
        in_=w2_d.rearrange("f l d -> l f d")), inc="s_g")
    b.on("gpsimd", lambda g: g.dma_start(
        out=dw1sb[:].rearrange("p (j n) -> p j n", j=ND),
        in_=dw1_d.rearrange("(j p) n -> p j n", p=128)), inc="s_g")
    b.on("gpsimd", lambda g: g.dma_start(out=dw2sb[:], in_=dw2_d), inc="s_g")
    b.set("consts", "gpsimd")

    b.on("sync", lambda s: s.dma_start(
        out=xres.rearrange("p (i d) -> p i d", d=D)[:, 0:8, :],
        in_=x_in.rearrange("(i p) d -> p i d", p=128)[:, 0:8, :]),
        inc="s_ld")
    xl_s = b.cnt["s_ld"]
    b.on("gpsimd", lambda g: g.dma_start(
        out=xres.rearrange("p (i d) -> p i d", d=D)[:, 8:16, :],
        in_=x_in.rearrange("(i p) d -> p i d", p=128)[:, 8:16, :]),
        inc="s_g")
    xl_g = b.cnt["s_g"]
    b.set("xload", at=("s_ld", xl_s))
    b.set("xload_g", at=("s_g", xl_g))
    b.on("vector", lambda v: v.memset(onessb[:], 1.0))
    b.on("vector", lambda v: v.memset(cstsb[:, 0:1], EPS))
    b.on("vector", lambda v: v.memset(cstsb[:, 1:2], -LN8))
    for qi_, qt_ in enumerate(QTHR):
        b.on("vector", lambda v, qi_=qi_, qt_=qt_: v.memset(
            cstsb[:, 2 + qi_:3 + qi_], -qt_))
    for ti_, tv_ in enumerate((1.0, 0.5)):
        b.on("vector", lambda v, ti_=ti_, tv_=tv_: v.memset(
            thrsb[:, ti_:ti_ + 1], tv_))
    b.on("vector", lambda v: v.memset(Ssb[:], 0.0), inc="s_dve")
    b.set("init", "vector")
    b.set("S_ok", "vector")

    qn = [0]

    def emit_quant(x_ap, out_bf, n, ready_res, out_res, ybuf=None):
        p = qn[0] & 1
        qn[0] += 1
        nb = n // 16
        qsg = q_sg if p == 0 else q_sg2
        if ybuf is None:
            ybuf = x_ap
        b.waits("scalar", ready_res, "q_dy")
        b.on("scalar", lambda s: s.activation(q_a[:, 0:n], x_ap, AF.Abs),
             inc="s_act")
        b.set("q_abs", "scalar")
        b.waits("scalar", f"q_sgf{p}")
        b.on("scalar", lambda s, qsg=qsg: s.activation(
            qsg[:, 0:n], x_ap, AF.Sign), inc="s_act")
        b.set("q_act", "scalar")
        b.waits("vector", ready_res)
        x3 = x_ap.rearrange("p (z k) -> p z k", k=16)
        b.on("vector", lambda v: v.tensor_reduce(
            q_bm[:, 0:nb], x3, axis=AX.X, op=ALU.max, apply_absolute_value=True),
            inc="s_dve")
        b.set("q_bmr", "vector")
        import os as _os
        QSY = not _os.environ.get("KE_NOQSY")
        if QSY:
            b.wait("vector", "q_bmr")
        b.on("vector", lambda v: v.tensor_scalar(
            q_sc[:, 0:nb], q_bm[:, 0:nb], R6, 1e-12, ALU.mult, ALU.max),
            inc="s_dve")
        b.set("q_scd", "vector")
        if QSY:
            b.wait("vector", "q_scd")
        b.on("vector", lambda v: v.reciprocal(q_rs[:, 0:nb], q_sc[:, 0:nb]),
             inc="s_dve")
        b.set("q_rsd", "vector")
        b.on("vector", lambda v: v.tensor_scalar(
            q_hs[:, 0:nb], q_sc[:, 0:nb], 0.5, None, ALU.mult))
        if QSY:
            b.wait("vector", "q_rsd")
        b.wait("vector", "q_act")
        a3 = q_a[:, 0:n].rearrange("p (z k) -> p z k", k=16)
        y3 = ybuf[:, 0:n].rearrange("p (z k) -> p z k", k=16)
        b.on("vector", lambda v, y3=y3: v.tensor_tensor(
            y3, a3, q_rs[:, 0:nb].unsqueeze(2).broadcast_to([128, nb, 16]),
            ALU.mult), inc="s_dve")
        b.set("q_dy", "vector")
        y, Bc, Ac = V(ybuf[:, 0:n]), V(q_b[:, 0:n]), V(q_A[:, 0:n])
        b.on("vector", lambda v: v.tensor_single_scalar(Bc, y, 5.0, ALU.is_gt))
        b.on("vector", lambda v: v.tensor_single_scalar(Ac, y, 0.25, ALU.is_gt))
        for thB, thA in ((5.0, 0.75), (3.5, 1.25), (2.5, 1.75)):
            b.on("vector", lambda v, th=thB: v.scalar_tensor_tensor(
                Bc, y, th, Bc, ALU.is_gt, ALU.add))
            b.on("vector", lambda v, th=thA: v.scalar_tensor_tensor(
                Ac, y, th, Ac, ALU.is_gt, ALU.add))
        s3 = qsg[:, 0:n].rearrange("p (z k) -> p z k", k=16)
        b.on("vector", lambda v, s3=s3: v.tensor_tensor(
            s3, s3, q_hs[:, 0:nb].unsqueeze(2).broadcast_to([128, nb, 16]),
            ALU.mult))
        b.on("vector", lambda v: v.scalar_tensor_tensor(
            V(q_Q[:, 0:n]), Bc, 2.0, Ac, ALU.mult, ALU.add))
        b.on("vector", lambda v, qsg=qsg: v.tensor_tensor(
            V(out_bf), V(q_Q[:, 0:n]), V(qsg[:, 0:n]), ALU.mult), inc="s_dve")
        b.set(out_res, "vector")
        b.set(f"q_sgf{p}", "vector")
        b.set("q_scratch", "vector")

    def emit_wquant(w, ti, parts):
        for (c0, n) in parts:
            src = {"g": wg_d, "u": wu_d, "d": wd_d}[w]
            p_ = qn[0] & 1
            qld = q_y if p_ == 0 else q_y2
            b.waits("sync", f"q_sgf{p_}")
            b.on("sync", lambda s, src=src, ti=ti, c0=c0, n=n, qld=qld:
                 s.dma_start(out=qld[:, 0:n],
                             in_=src[ti * 128:(ti + 1) * 128, c0:c0 + n]),
                 inc="s_ld")
            b.set("wload", "sync")
            b.wait("vector", "wtp_done")
            emit_quant(qld[:, 0:n], wq_tm[:, 0:n], n, "wload", "wq")
            nblk = n // 128
            for g0 in range(0, nblk, 8):
                gn = min(8, nblk - g0)
                b.waits("tensor", "wq", "psT_free")
                for bb in range(gn):
                    b.on("tensor", lambda t, bb=bb, g0=g0: t.matmul(
                        psT[:, bb * 128:(bb + 1) * 128],
                        wq_tm[:, (g0 + bb) * 128:(g0 + bb + 1) * 128],
                        ident[:], start=True, stop=True, is_transpose=True),
                        inc="s_pe" if bb == gn - 1 else None)
                b.set("wtp", "tensor")
                if g0 + 8 >= nblk:
                    b.set("wtp_done", "tensor")
                b.waits("scalar", "wtp", "tstage_free")
                b.on("scalar", lambda sc, gn=gn: sc.activation(
                    tstage[:, 0:gn * 128], psT[:, 0:gn * 128], AF.Copy),
                    inc="s_act")
                b.set("tev", "scalar")
                b.set("psT_free", "scalar")
                if w in "gu":
                    dst = wgT_s if w == "g" else wuT_s
                    dview = dst.rearrange("(z p) i -> p z i", p=128)[
                        :, g0:g0 + gn, ti * 128:(ti + 1) * 128]
                else:
                    i0 = (c0 // 128) + g0
                    dview = wdT_s.rearrange("(z p) dd -> p z dd", p=128)[
                        :, i0:i0 + gn, ti * 128:(ti + 1) * 128]
                b.wait("sync", "tev")
                b.on("sync", lambda s, dv=dview, gn=gn: s.dma_start(
                    out=dv, in_=tstage[:, 0:gn * 128].rearrange(
                        "p (z m) -> p z m", m=128)), inc="s_sp")
                b.set("tstage_free", at=("s_sp", b.cnt["s_sp"]))
            b.set(f"wst_{w}{ti}p{c0}", at=("s_sp", b.cnt["s_sp"]))

    def emit_ln(x_src_res):
        if not isinstance(x_src_res, tuple):
            x_src_res = (x_src_res,)
        b.waits("vector", *x_src_res)
        b.on("vector", lambda v: v.tensor_reduce(
            lnmean[:], xres.rearrange("p (i d) -> p i d", d=D),
            axis=AX.X, op=ALU.add), inc="s_dve")
        b.set("ln_red", "vector")
        b.wait("vector", "ln_red")
        b.waits("scalar", *x_src_res, "lnsq_free")
        for t in range(NT):
            b.on("scalar", lambda s, t=t: s.activation(
                lnsq[:], xres[:, t * D:(t + 1) * D], AF.Square,
                accum_out=lnsum2[:, t:t + 1]),
                inc="s_act" if t == NT - 1 else None)
        b.set("ln_sq", "scalar")
        b.on("vector", lambda v: v.tensor_scalar(
            lnmean[:], lnmean[:], R1024, None, ALU.mult), inc="s_dve")
        b.set("ln_m2", "vector")
        b.wait("vector", "ln_sq")
        b.on("vector", lambda v: v.tensor_scalar(
            lnsum2[:], lnsum2[:], R1024, None, ALU.mult), inc="s_dve")
        b.set("ln_s2", "vector")
        b.wait("vector", "ln_m2")
        b.wait("vector", "ln_s2")
        b.on("vector", lambda v: v.tensor_tensor(
            lnrstd[:], lnmean[:], lnmean[:], ALU.mult), inc="s_dve")
        b.set("ln_msq", "vector")
        b.wait("vector", "ln_msq")
        b.on("vector", lambda v: v.tensor_tensor(
            lnrstd[:], lnsum2[:], lnrstd[:], ALU.subtract), inc="s_dve")
        b.set("ln_var", "vector")
        b.wait("scalar", "ln_var")
        b.on("scalar", lambda s: s.activation(
            lnrstd[:], lnrstd[:], AF.Ln, bias=EPS), inc="s_act")
        b.set("ln_ln", "scalar")
        b.wait("scalar", "ln_ln")
        b.on("scalar", lambda s: s.activation(
            lnrstd[:], lnrstd[:], AF.Exp, scale=-0.5), inc="s_act")
        b.set("ln_rstd", "scalar")
        b.wait("vector", "ln_rstd")
        b.on("vector", lambda v: v.tensor_tensor(
            lnnmr[:], lnmean[:], lnrstd[:], ALU.mult), inc="s_dve")
        b.set("ln_nm1", "vector")
        b.wait("vector", "ln_nm1")
        b.on("vector", lambda v: v.tensor_scalar(
            lnnmr[:], lnnmr[:], -1.0, None, ALU.mult), inc="s_dve")
        b.set("ln_ok", "vector")

    def emit_ln_apply(t, out_ap, out_res):
        b.waits("scalar", "ln_ok", f"lnout_free")
        b.on("scalar", lambda s, t=t: s.activation(
            out_ap, xres[:, t * D:(t + 1) * D], AF.Identity,
            bias=lnnmr[:, t:t + 1], scale=lnrstd[:, t:t + 1]), inc="s_act")
        b.set(out_res, "scalar")

    def emit_tp(src_bf, nblk, dst_fn, dep_res, out_res):
        deps = dep_res if isinstance(dep_res, tuple) else (dep_res,)
        b.waits("tensor", *deps, "psT_free")
        for bb in range(nblk):
            b.on("tensor", lambda te, bb=bb: te.matmul(
                psT[:, bb * 128:(bb + 1) * 128],
                src_bf[:, bb * 128:(bb + 1) * 128],
                ident[:], start=True, stop=True, is_transpose=True),
                inc="s_pe" if bb == nblk - 1 else None)
        b.set("tp_mm", "tensor")
        b.wait("scalar", "tp_mm")
        b.on("scalar", dst_fn, inc="s_act")
        b.set(out_res, "scalar")
        b.set("psT_free", "scalar")

    def emit_ffn(tag, x_res, wprep=False):
        emit_ln(x_res)
        for t in range(NT):
            emit_ln_apply(t, ln_t[:], "ln_ap")
            b.waits("vector", "lnb_free")
            emit_quant(ln_t[:], lnb_t[:], D, "ln_ap", "xq_q", ybuf=lnsq)
            b.set("lnout_free", at=b.res["q_bmr"])
            emit_tp(lnb_t, 8,
                    lambda sc, t=t: sc.activation(
                        fm3(xq_fm, t * 128, 128),
                        psT[:].rearrange("p (z m) -> p z m", m=128), AF.Copy),
                    "xq_q", "xq_ev")
            b.set("lnb_free", at=b.res["tp_mm"])
        b.set("xq_all", at=b.res["xq_ev"])
        if stage == "dbgx":
            return "DBGX"
        pend = [None]
        for ic, (i0, ilen) in enumerate(ICHUNKS):
            nseg = ilen // 128
            if wprep:
                if ic == 0:
                    for ti in range(ND):
                        emit_wquant("d", ti, [(0, 1024)])
                for ti in range(4 * ic, min(4 * (ic + 1), NI)):
                    emit_wquant("g", ti, [(0, 1024)])
                    emit_wquant("u", ti, [(0, 1024)])
                if ic in (0, 1):
                    for ti in range(4 * ic, 4 * ic + 4):
                        emit_wquant("d", ti, [(1024, 896), (1920, 896)])
                last_ti = min(4 * (ic + 1), NI) - 1
                wdep = "wst_d7p0" if ic <= 1 else "wst_d7p1920"
                b.waits("sync", f"wst_u{last_ti}p0", wdep)
            else:
                b.waits("sync", "wq_all_st")
            b.waits("sync", "wgu_free")
            b.on("sync", lambda s, i0=i0, ilen=ilen: s.dma_start(
                out=wch_g[:, 0:8 * ilen].rearrange("p (j n) -> p j n", j=ND),
                in_=wgT_s.rearrange("(j p) i -> p j i", p=128)[:, :, i0:i0 + ilen]),
                inc="s_ld")
            b.on("sync", lambda s, i0=i0, ilen=ilen: s.dma_start(
                out=wch_u[:, 0:8 * ilen].rearrange("p (j n) -> p j n", j=ND),
                in_=wuT_s.rearrange("(j p) i -> p j i", p=128)[:, :, i0:i0 + ilen]),
                inc="s_ld")
            b.set("wchgu", "sync")
            b.waits("sync", "wch_free")
            b.on("sync", lambda s, i0=i0, nseg=nseg: s.dma_start(
                out=wch_d[:, 0:nseg * 1024].rearrange("p (m n) -> p m n", n=1024),
                in_=wdT_s.rearrange("(m p) dd -> p m dd", p=128)[
                    :, i0 // 128:i0 // 128 + nseg, :]), inc="s_ld")
            b.set("wchd", "sync")
            for t in range(NT):
                b.waits("tensor", "wchgu", "wchd", "xq_all", "psA_free",
                        "psB_free")
                for j in range(ND):
                    b.on("tensor", lambda te, j=j, t=t, ilen=ilen: te.matmul(
                        psA[:, 0:ilen],
                        xq_fm[:, j * T + t * 128: j * T + (t + 1) * 128],
                        wch_g[:, j * ilen:(j + 1) * ilen],
                        start=(j == 0), stop=(j == ND - 1)))
                for j in range(ND):
                    b.on("tensor", lambda te, j=j, t=t, ilen=ilen: te.matmul(
                        psB[:, 0:ilen],
                        xq_fm[:, j * T + t * 128: j * T + (t + 1) * 128],
                        wch_u[:, j * ilen:(j + 1) * ilen],
                        start=(j == 0), stop=(j == ND - 1)),
                        inc="s_pe" if j == ND - 1 else None)
                b.set("gu_mm", "tensor")
                if t == NT - 1:
                    b.set("wgu_free", "tensor")
                if stage in ("dbgg", "dbgu") and t == 0:
                    b.wait("vector", "gu_mm")
                    pp_dbg = psA if stage == "dbgg" else psB
                    b.on("vector", lambda v, pp_dbg=pp_dbg, ilen=ilen: v.tensor_copy(
                        sttsb[:, 0:ilen], pp_dbg[:, 0:ilen]), inc="s_dve")
                    b.set("dbg_ev", "vector")
                    b.wait("sync", "dbg_ev")
                    b.on("sync", lambda s, ilen=ilen: s.dma_start(
                        out=y_out.rearrange("(i p) d -> p (i d)", p=128)[:, 0:ilen],
                        in_=sttsb[:, 0:ilen]), inc="s_sp")
                    ov = b.cnt["s_sp"]
                    b.on("sync", lambda s, v=ov: s.wait_ge(b.sem["s_sp"], v))
                    return "DBG"
                b.waits("scalar", "gu_mm", "sil_free")
                b.on("scalar", lambda s, ilen=ilen: s.activation(
                    ffnsil[:, 0:ilen], psA[:, 0:ilen], AF.Silu), inc="s_act")
                b.set("sil", "scalar")
                b.set("psA_free", "scalar")
                b.waits("vector", "sil", "hbuf_free")
                b.on("vector", lambda v, ilen=ilen: v.scalar_tensor_tensor(
                    hbuf[:, 0:ilen], ffnsil[:, 0:ilen], 0.5, psB[:, 0:ilen],
                    ALU.mult, ALU.mult), inc="s_dve")
                b.set("hgot", "vector")
                b.set("psB_free", "vector")
                b.set("sil_free", "vector")
                hq = hq_tm if t % 2 == 0 else hq_tm2
                if stage == "dbgh" and t == 0:
                    b.wait("sync", "hgot")
                    b.on("sync", lambda s, ilen=ilen: s.dma_start(
                        out=y_out.rearrange("(i p) d -> p (i d)", p=128)[:, 0:ilen],
                        in_=hbuf[:, 0:ilen]), inc="s_sp")
                    ov = b.cnt["s_sp"]
                    b.on("sync", lambda s, v=ov: s.wait_ge(b.sem["s_sp"], v))
                    return "DBG"
                b.waits("vector", f"hqf{t % 2}")
                emit_quant(hbuf[:, 0:ilen], hq[:, 0:ilen], ilen, "hgot",
                           f"hq_q{t % 2}")
                b.set("hbuf_free", at=b.res["q_act"])

                def emit_tail(t=t, hq=hq, nseg=nseg, ilen=ilen):
                    emit_tp(hq, nseg,
                            lambda sc, nseg=nseg: sc.activation(
                                hq_fm[:, 0:nseg * 128], psT[:, 0:nseg * 128],
                                AF.Copy),
                            f"hq_q{t % 2}", "hq_ev")
                    b.set(f"hqf{t % 2}", at=b.res["tp_mm"])
                    b.waits("tensor", "hq_ev", "psC_free")
                    for nh in range(2):
                        for m in range(nseg):
                            b.on("tensor", lambda te, m=m, nh=nh, nseg=nseg:
                                 te.matmul(
                                     psC[:, nh * 512:(nh + 1) * 512],
                                     hq_fm[:, m * 128:(m + 1) * 128],
                                     wch_d[:, m * 1024 + nh * 512:
                                           m * 1024 + (nh + 1) * 512],
                                     start=(m == 0), stop=(m == nseg - 1)),
                                 inc="s_pe" if (nh == 1 and m == nseg - 1)
                                 else None)
                    b.set("dn_mm", "tensor")
                    b.wait("vector", "dn_mm")
                    b.on("vector", lambda v, t=t: v.tensor_tensor(
                        V(xres[:, t * D:(t + 1) * D]),
                        V(xres[:, t * D:(t + 1) * D]),
                        V(psC[:]), ALU.add), inc="s_dve")
                    b.set("x_acc", "vector")
                    b.set("psC_free", "vector")

                if pend[0] is not None:
                    pend[0]()
                pend[0] = emit_tail
            pend[0]()
            pend[0] = None
            b.set("wch_free", at=b.res["dn_mm"])
        b.set(f"{tag}_done", at=b.res["x_acc"])

    def dump_and_finish(src_ap, n_bf16_cols, deps):
        b.waits("gpsimd", *deps)
        ni = max(1, n_bf16_cols // D)
        b.on("gpsimd", lambda g: g.dma_start(
            out=y_out.rearrange("(i p) d -> p i d", p=128)[:, 0:ni, :],
            in_=src_ap[:, 0:n_bf16_cols].rearrange("p (i d) -> p i d", d=min(D, n_bf16_cols))), inc="s_g")
        ov = b.cnt["s_g"]
        b.on("gpsimd", lambda g, v=ov: g.wait_ge(b.sem["s_g"], v))
        nc.detect_race_conditions = False
        blk2 = es.enter_context(nc.Block())
        with es:
            b.replay(blk2)
        return nc

    def dump_dram_and_finish(dram_ap, deps):
        b.waits("gpsimd", *deps)
        b.on("gpsimd", lambda g: g.dma_start(
            out=y_out[0:1024, :], in_=dram_ap), inc="s_g")
        ov = b.cnt["s_g"]
        b.on("gpsimd", lambda g, v=ov: g.wait_ge(b.sem["s_g"], v))
        nc.detect_race_conditions = False
        blk2 = es.enter_context(nc.Block())
        with es:
            b.replay(blk2)
        return nc

    def finish_here():
        b.waits("sync", "wq_all_st", "x_acc", "xload", "x2_done", "f2_done")
        b.on("sync", lambda s: s.dma_start(
            out=y_out.rearrange("(i p) d -> p i d", p=128),
            in_=xres.rearrange("p (i d) -> p i d", d=D)), inc="s_sp")
        ov = b.cnt["s_sp"]
        b.on("sync", lambda s, v=ov: s.wait_ge(b.sem["s_sp"], v))
        nc.detect_race_conditions = False
        blk2 = es.enter_context(nc.Block())
        with es:
            b.replay(blk2)
        return nc

    b.wait("vector", "xload")
    b.wait("scalar", "xload")
    b.wait("vector", "consts")
    b.wait("scalar", "consts")
    if stage == "p0":
        return finish_here()
    if stage == "dbgln":
        emit_ln(("xload", "xload_g"))
        b.waits("sync", "ln_ok", "ln_rstd")
        for idx, tsr in enumerate((lnmean, lnsum2, lnrstd, lnnmr)):
            b.on("sync", lambda s, t=tsr, idx=idx: s.dma_start(
                out=y_out.rearrange("(i p) d -> p i d", p=128)[:, idx, 0:NT],
                in_=t[:]), inc="s_sp")
        for k in range(4):
            b.on("sync", lambda s, k=k: s.dma_start(
                out=y_out.rearrange("(i p) d -> p i d", p=128)[:, 4 + k, 0:NT],
                in_=xres[:, (12 + k) * 1024:(12 + k) * 1024 + NT]), inc="s_sp")
        ov = b.cnt["s_sp"]
        b.on("sync", lambda s, v=ov: s.wait_ge(b.sem["s_sp"], v))
        nc.detect_race_conditions = False
        blk2 = es.enter_context(nc.Block())
        with es:
            b.replay(blk2)
        return nc
    rr = emit_ffn("f1", ("xload", "xload_g"), wprep=True)
    b.set("wq_all_st", at=("s_sp", b.cnt["s_sp"]))
    if rr == "DBG":
        nc.detect_race_conditions = False
        blk2 = es.enter_context(nc.Block())
        with es:
            b.replay(blk2)
        return nc
    if rr == "DBGX":
        return dump_and_finish(xq_fm, 16384, ["xq_all"])
    if stage == "dbgw":
        return dump_dram_and_finish(wgT_s[0:1024, 0:1024], ["wq_all_st"])
    if stage == "dbgwd":
        return dump_dram_and_finish(wdT_s[0:1024, 0:1024], ["wq_all_st"])
    if stage == "f1":
        return finish_here()

    emit_ln("f1_done")
    for t in range(NT):
        b.waits("scalar", "lnb_free")
        emit_ln_apply(t, lnb_t[:], "l1_ap")
        b.set("lnout_free", at=b.res["l1_ap"])
        emit_tp(lnb_t, 8,
                lambda sc, t=t: sc.activation(
                    fm3(ln1_fm, t * 128, 128),
                    psT[:].rearrange("p (z m) -> p z m", m=128), AF.Copy),
                "l1_ap", "l1_ev")
        b.set("lnb_free", at=b.res["tp_mm"])
    b.set("ln1_all", at=b.res["l1_ev"])
    b.waits("sync", "f1_done")
    b.on("sync", lambda s: s.dma_start(
        out=xsp_s[:, 0:8 * D], in_=xres[:, 0:8 * D]), inc="s_sp")
    b.set("xspill", at=("s_sp", b.cnt["s_sp"]))
    b.waits("gpsimd", "f1_done")
    b.on("gpsimd", lambda g: g.dma_start(
        out=xsp_s[:, 8 * D:16 * D], in_=xres[:, 8 * D:16 * D]), inc="s_g")
    b.set("xspill_g", at=("s_g", b.cnt["s_g"]))
    if stage == "ln1":
        return finish_here()

    FIDX = {"r": 0, "k": 1, "v": 2, "g": 3, "w": 4}

    def emit_dx(c0):
        dx3 = dx_r.rearrange("p (j t) -> p j t", j=ND)
        ln3 = ln1_fm.rearrange("p (j t) -> p j t", j=ND)
        if c0 == 0:
            b.on("vector", lambda v: v.tensor_scalar(
                dx3[:, :, 0:1], ln3[:, :, 0:1], -1.0, None, ALU.mult))
            b.on("vector", lambda v: v.tensor_tensor(
                dx3[:, :, 1:512], ln3[:, :, 0:511], ln3[:, :, 1:512],
                ALU.subtract), inc="s_dve")
        else:
            b.on("vector", lambda v, c0=c0: v.tensor_tensor(
                dx3[:, :, 0:512], ln3[:, :, c0 - 1:c0 + 511],
                ln3[:, :, c0:c0 + 512], ALU.subtract), inc="s_dve")
        b.set("dx", "vector")

    def emit_mixpre(half, tc):
        c0 = half * 1024 + tc * 512
        b.waits("vector", "ln1_all", "dx_free", "xx_free", "su_mm")
        emit_dx(c0)
        xx3 = xx_r.rearrange("p (j t) -> p j t", j=ND)
        dx3 = dx_r.rearrange("p (j t) -> p j t", j=ND)
        ln3 = ln1_fm.rearrange("p (j t) -> p j t", j=ND)
        b.on("vector", lambda v: v.tensor_tensor(
            xx3, dx3, maa_x.unsqueeze(2).broadcast_to([128, ND, 512]), ALU.mult))
        b.on("vector", lambda v, c0=c0: v.tensor_tensor(
            xx3, xx3, ln3[:, :, c0:c0 + 512], ALU.add), inc="s_dve")
        b.set("xxx", "vector")
        b.set("dx_free", "vector")
        b.waits("tensor", "xxx", "consts", "psA_free", "psE_free")
        for j in range(ND):
            b.on("tensor", lambda te, j=j: te.matmul(
                psA[:], w1sb[:, j * 160:j * 160 + 128],
                xx_r[:, j * 512:(j + 1) * 512],
                start=(j == 0), stop=(j == ND - 1)))
        for j in range(ND):
            b.on("tensor", lambda te, j=j: te.matmul(
                psE[0:32, :], w1sb[:, j * 160 + 128:(j + 1) * 160],
                xx_r[:, j * 512:(j + 1) * 512],
                start=(j == 0), stop=(j == ND - 1)),
                inc="s_pe" if j == ND - 1 else None)
        b.set("mix_mm", "tensor")
        b.set("xx_free", "tensor")
        b.waits("scalar", "mix_mm", "mixsb_free")
        b.on("scalar", lambda s: s.activation(mixsb[:], psA[:], AF.Tanh))
        b.on("scalar", lambda s: s.activation(mix2sb[:], psE[0:32, :], AF.Tanh),
             inc="s_act")
        b.set("mix_ev", "scalar")
        b.set("psA_free", "scalar")
        b.set("psE_free", "scalar")
        b.wait("sync", "mix_ev")
        for fb in range(4):
            b.on("sync", lambda s, fb=fb, tc=tc: s.dma_start(
                out=mixf[:, (fb * 2 + tc) * 512:(fb * 2 + tc + 1) * 512],
                in_=mixsb[fb * 32:(fb + 1) * 32, :]), inc="s_ld")
        b.on("sync", lambda s, tc=tc: s.dma_start(
            out=mixf[:, (4 * 2 + tc) * 512:(4 * 2 + tc + 1) * 512],
            in_=mix2sb[:]), inc="s_ld")
        b.set("mix_dma", "sync")
        b.set("mixsb_free", at=b.res["mix_dma"])

    def emit_xf(f, half, tc):
        c0 = half * 1024 + tc * 512
        b.waits("vector", "xf_free", "dx_free", "su_mm")
        emit_dx(c0)
        fi = FIDX[f]
        for j in range(ND):
            pp = psD if j % 2 == 0 else psB
            pfree = "psD_free" if j % 2 == 0 else "psB_free"
            b.waits("tensor", pfree, "mix_dma", "consts",
                    "dx" if j == 0 else "noop")
            b.on("tensor", lambda te, pp=pp, j=j, fi=fi, tc=tc: te.matmul(
                pp[:], w2sb[:, fi * 1024 + j * 128:fi * 1024 + (j + 1) * 128],
                mixf[:, (fi * 2 + tc) * 512:(fi * 2 + tc + 1) * 512],
                start=True, stop=True), inc="s_pe")
            b.set("mf_mm", "tensor")
            b.waits("vector", "mf_mm", "sttsb_free")
            b.on("vector", lambda v, pp=pp, f=f, j=j: v.scalar_tensor_tensor(
                sttsb[:], pp[:], maa_of[f][:, j:j + 1],
                dx_r[:, j * 512:(j + 1) * 512], ALU.add, ALU.mult),
                inc="s_dve")
            b.set("xf_stt", "vector")
            b.wait("vector", "xf_stt")
            b.on("vector", lambda v, j=j, c0=c0: v.tensor_tensor(
                xf_r[:, j * 512:(j + 1) * 512], sttsb[:],
                ln1_fm[:, j * T + c0:j * T + c0 + 512], ALU.add), inc="s_dve")
            b.set(pfree, "vector")
            b.set("sttsb_free", "vector")
        b.set("xf", at=b.res["sttsb_free"])
        b.set("dx_free", at=b.res["sttsb_free"])

    def emit_proj(f, half, tc):
        c0 = half * 1024 + tc * 512
        if f == "w":
            b.waits("tensor", "xf", "psE_free")
            for j in range(ND):
                b.on("tensor", lambda te, j=j: te.matmul(
                    psE[0:64, :], dw1sb[:, j * 64:(j + 1) * 64],
                    xf_r[:, j * 512:(j + 1) * 512],
                    start=(j == 0), stop=(j == ND - 1)),
                    inc="s_pe" if j == ND - 1 else None)
            b.set("t1_mm", "tensor")
            b.set("xf_free", "tensor")
            b.wait("scalar", "t1_mm")
            b.on("scalar", lambda s, c0=c0: s.activation(
                tanh1[:, c0:c0 + 512], psE[0:64, :], AF.Tanh), inc="s_act")
            b.set("t1_ev", "scalar")
            b.set("psE_free", "scalar")
            return
        if f == "v":
            for st in range(4):
                b.waits("tensor", "xf" if st == 0 else "noop", "psC_free")
                for j in range(ND):
                    for nh in range(2):
                        b.on("tensor", lambda te, j=j, st=st, nh=nh: te.matmul(
                            psC[:, nh * 512:(nh + 1) * 512],
                            xf_r[:, j * 512 + st * 128: j * 512 + (st + 1) * 128],
                            attw[:, j * 1024 + nh * 512: j * 1024 + (nh + 1) * 512],
                            start=(j == 0), stop=(j == ND - 1)),
                            inc="s_pe" if (j == ND - 1 and nh == 1) else None)
                b.set("v_mm", "tensor")
                if st == 3:
                    b.set("xf_free", "tensor")
                b.wait("scalar", "v_mm")
                b.on("scalar", lambda s, st=st, tc=tc: s.activation(
                    v_tm[:, (tc * 4 + st) * 1024:(tc * 4 + st + 1) * 1024],
                    psC[:], AF.Copy), inc="s_act")
                b.set("v_ev", "scalar")
                b.set("psC_free", "scalar")
            return
        for jj in range(ND):
            pp = psA if jj % 2 == 0 else psD
            pfree = "psA_free" if jj % 2 == 0 else "psD_free"
            b.waits("tensor", pfree, "xf" if jj == 0 else "noop")
            for j in range(ND):
                b.on("tensor", lambda te, pp=pp, j=j, jj=jj: te.matmul(
                    pp[:], attw[:, j * 1024 + jj * 128: j * 1024 + (jj + 1) * 128],
                    xf_r[:, j * 512:(j + 1) * 512],
                    start=(j == 0), stop=(j == ND - 1)),
                    inc="s_pe" if j == ND - 1 else None)
            b.set("p_mm", "tensor")
            if jj == ND - 1:
                b.set("xf_free", "tensor")
            b.wait("scalar", "p_mm")
            if f == "g":
                b.waits("scalar", "g_dma")
                b.on("scalar", lambda s, pp=pp, jj=jj: s.activation(
                    gstage[:], pp[:], AF.Silu, bias=bgv[:, jj:jj + 1]),
                    inc="s_act")
                b.set("g_ev", "scalar")
                b.set(pfree, "scalar")
                b.wait("sync", "g_ev")
                b.on("sync", lambda s, jj=jj, c0=c0: s.dma_start(
                    out=gsp_s[:, jj * T + c0: jj * T + c0 + 512],
                    in_=gstage[:]), inc="s_ld")
                b.set("g_dma", "sync")
            else:
                dst = q_fm if f == "r" else k_fm
                b.on("scalar", lambda s, pp=pp, jj=jj, c0=c0, dst=dst: s.activation(
                    dst[:, jj * T + c0: jj * T + c0 + 512], pp[:], AF.Copy),
                    inc="s_act")
                b.set("p_ev", "scalar")
                b.set(pfree, "scalar")

    def emit_S_half(half):
        b.waits("scalar", "xspill", "xspill_g")
        for tc in range(2):
            emit_mixpre(half, tc)
        for f in ("w", "r", "k", "v", "g"):
            if f != "w":
                wname = {"r": "Wq", "k": "Wk", "v": "Wv", "g": "Wg"}[f]
                b.waits("gpsimd", "attw_free", "q_scratch", "q_act",
                        "su_mm", "o_mm")
                b.on("gpsimd", lambda g, wname=wname: g.dma_start(
                    out=attw[:].rearrange("p (j n) -> p j n", j=ND),
                    in_=W_d[wname].rearrange("(j p) n -> p j n", p=128)),
                    inc="s_g")
                b.set("w_att", "gpsimd")
            for tc in range(2):
                emit_xf(f, half, tc)
                if f != "w":
                    b.wait("tensor", "w_att")
                emit_proj(f, half, tc)
            if f != "w":
                b.set("attw_free", at=b.res["xf_free"])

    import os
    E_NGRP = int(os.environ.get("KE_NGRP", "4"))
    E_NCH = int(os.environ.get("KE_NCH", "4"))
    E_PARTS = os.environ.get("KE_PARTS", "sakouz123")

    def emit_E_half(half):
        for grp in range(E_NGRP):
            g0 = half * 1024 + grp * 256
            lg = grp * 256
            for j in range(ND):
                pp = psE if j % 2 == 0 else psD
                pfree = "psE_free" if j % 2 == 0 else "psD_free"
                b.waits("tensor", pfree, "t1_ev", "dec_free", "consts")
                b.on("tensor", lambda te, j=j, g0=g0, pp=pp: te.matmul(
                    pp[:, 0:256], dw2sb[:, j * 128:(j + 1) * 128],
                    tanh1[:, g0:g0 + 256], start=True, stop=True), inc="s_pe")
                b.set("dec_mm", "tensor")
                b.wait("vector", "dec_mm")
                b.on("vector", lambda v, j=j, pp=pp: v.tensor_scalar(
                    dec_g[:, j * 256:(j + 1) * 256], pp[:, 0:256],
                    tdec[:, j:j + 1], None, ALU.add), inc="s_dve")
                b.set(pfree, "vector")
                b.set("dec_j", "vector")
            b.set("dec_done", at=b.res["dec_j"])
            b.waits("scalar", "dec_done", "egd_free")
            b.on("scalar", lambda s: s.activation(egd_g[:], dec_g[:], AF.Exp),
                 inc="s_act")
            b.set("egd", "scalar")
            b.wait("scalar", "egd")
            b.on("scalar", lambda s: s.activation(
                dec_g[:], egd_g[:], AF.Exp, scale=-1.0), inc="s_act")
            b.set("emg", "scalar")
            b.set("dec_free", at=("s_act", b.cnt["s_act"]))
            b.wait("vector", "emg")
            b.on("vector", lambda v: v.tensor_scalar(
                V(dec_g[:]), V(dec_g[:]), -1.0, 1.0, ALU.mult, ALU.add))
            kf3 = fm3(k_fm, g0, 256)
            dg3 = dec_g[:].rearrange("p (j t) -> p j t", j=ND)
            b.on("vector", lambda v, kf3=kf3, dg3=dg3: v.tensor_tensor(
                kf3, kf3, dg3, ALU.mult), inc="s_dve")
            b.set("kp", "vector")
            if os.environ.get("KE_DUMMYDMA"):
                b.waits("sync", "g_dma")
                b.on("sync", lambda s, g0=g0: s.dma_start(
                    out=gat_r.rearrange("p (j t) -> p j t", j=ND),
                    in_=gsp_s.rearrange("p (j t) -> p j t", j=ND)[:, :, g0:g0 + 256]),
                    inc="s_ld")
                b.set("g_rl", "sync")
            for cc in range(E_NCH):
                c0 = g0 + cc * 64
                l0 = cc * 64
                par = (c0 // 64) % 2
                tloc = (c0 - half * 1024) // 128
                if os.environ.get("KE_ESER"):
                    b.waits("tensor", "og_ev", "S_ok", "o_sb")
                b.waits("vector", "egd", "cb_free", "expq")
                for j in range(ND):
                    b.on("vector", lambda v, j=j, l0=l0: v.tensor_tensor_scan(
                        cbsb[:, j * 64:(j + 1) * 64],
                        onessb[:].broadcast_to([128, 64]),
                        egd_g[:, j * 256 + l0:j * 256 + l0 + 64],
                        0.0, ALU.mult, ALU.add),
                        inc="s_dve" if j == ND - 1 else None)
                b.set("cb", "vector")
                b.wait("vector", "cb")
                cb3 = cbsb[:].rearrange("p (j t) -> p j t", j=ND)
                cbL = cb3[:, :, 63:64].squeeze(2)
                b.wait("vector", "sm_exp")
                b.on("vector", lambda v, cbL=cbL: v.tensor_scalar(
                    smalls[:, 0:8], cbL, 0.5, None, ALU.mult), inc="s_dve")
                b.set("chalf", "vector")
                b.wait("scalar", "chalf")
                b.on("scalar", lambda s: s.activation(
                    smalls[:, 8:16], smalls[:, 0:8], AF.Exp))
                b.on("scalar", lambda s: s.activation(
                    smalls[:, 16:24], smalls[:, 0:8], AF.Exp, scale=-1.0))
                b.on("scalar", lambda s: s.activation(
                    smalls[:, 24:32], smalls[:, 0:8], AF.Exp, scale=-2.0),
                    inc="s_act")
                b.set("sm_exp", "scalar")
                b.waits("scalar", "expq_free")
                b.on("scalar", lambda s: s.activation(
                    expsc[:], cbsb[:], AF.Exp, scale=-1.0, bias=-LN8),
                    inc="s_act")
                b.set("expq", "scalar")
                b.waits("vector", "chalf", "expk_free")
                b.on("vector", lambda v: v.tensor_tensor(
                    expsc2[:].rearrange("p (j t) -> p j t", j=ND), cb3,
                    smalls[:, 0:8].unsqueeze(2).broadcast_to([128, ND, 64]),
                    ALU.subtract), inc="s_dve")
                b.set("argk", "vector")
                b.set("cb_free", "vector")
                b.wait("scalar", "argk")
                b.on("scalar", lambda s: s.activation(
                    expsc2[:], expsc2[:], AF.Exp), inc="s_act")
                b.set("expk", "scalar")
                q3 = fm3(q_fm, c0, 64)
                k3 = fm3(k_fm, c0, 64)
                e1 = expsc[:].rearrange("p (j t) -> p j t", j=ND)
                e2 = expsc2[:].rearrange("p (j t) -> p j t", j=ND)
                qs3 = qsc[:].rearrange("p (j t) -> p j t", j=ND)
                qss3 = qss[:].rearrange("p (j t) -> p j t", j=ND)
                ks3 = ksc[:].rearrange("p (j t) -> p j t", j=ND)
                kh3 = khc[:].rearrange("p (j t) -> p j t", j=ND)
                b.waits("vector", "expq", "expk", "sm_exp", "kp",
                        "qk_free", "p_mm", "v_mm", "t1_mm")
                b.on("vector", lambda v, qs3=qs3, q3=q3, e1=e1: v.tensor_tensor(
                    qs3, q3, e1, ALU.mult))
                b.on("vector", lambda v, ks3=ks3, k3=k3, e2=e2: v.tensor_tensor(
                    ks3, k3, e2, ALU.mult))
                b.on("vector", lambda v, qss3=qss3, qs3=qs3: v.tensor_tensor(
                    qss3, qs3, smalls[:, 8:16].unsqueeze(2).broadcast_to(
                        [128, ND, 64]), ALU.mult))
                b.on("vector", lambda v, kh3=kh3, ks3=ks3: v.tensor_tensor(
                    kh3, ks3, smalls[:, 16:24].unsqueeze(2).broadcast_to(
                        [128, ND, 64]), ALU.mult), inc="s_dve")
                b.set("qk", "vector")
                b.set("expq_free", "vector")
                b.set("expk_free", "vector")
                if "a" not in E_PARTS:
                    continue
                b.waits("tensor", "qk", "psA_free", "psD_free")
                for hp in range(ND):
                    b.on("tensor", lambda te, hp=hp, par=par: te.matmul(
                        psA[par * 64:par * 64 + 64, hp * 64:(hp + 1) * 64],
                        ksc[0:64, hp * 64:(hp + 1) * 64],
                        qss[0:64, hp * 64:(hp + 1) * 64],
                        start=True, stop=True, tile_position=(0, par * 64)))
                    b.on("tensor", lambda te, hp=hp, par=par: te.matmul(
                        psD[par * 64:par * 64 + 64, hp * 64:(hp + 1) * 64],
                        ksc[64:128, hp * 64:(hp + 1) * 64],
                        qss[64:128, hp * 64:(hp + 1) * 64],
                        start=True, stop=True, tile_position=(64, par * 64)),
                        inc="s_pe" if hp == ND - 1 else None)
                b.set("a_mm", "tensor")
                b.waits("vector", "a_mm", "amsk_free")
                mrow = maskt[64 * par:64 * par + 64, :].unsqueeze(1)\
                    .broadcast_to([64, ND, 64])
                b.on("vector", lambda v, par=par, mrow=mrow: v.tensor_tensor(
                    amskA[64 * par:64 * par + 64, :].rearrange(
                        "p (j t) -> p j t", j=ND),
                    psA[64 * par:64 * par + 64, :].rearrange(
                        "p (j t) -> p j t", j=ND), mrow, ALU.mult))
                b.on("vector", lambda v, par=par, mrow=mrow: v.tensor_tensor(
                    amskB[64 * par:64 * par + 64, :].rearrange(
                        "p (j t) -> p j t", j=ND),
                    psD[64 * par:64 * par + 64, :].rearrange(
                        "p (j t) -> p j t", j=ND), mrow, ALU.mult),
                    inc="s_dve")
                b.set("am_ev", "vector")
                b.set("psA_free", "vector")
                b.set("psD_free", "vector")
                if "k" not in E_PARTS:
                    continue
                b.waits("tensor", "qk", "psT_free")
                for hp in range(ND):
                    b.on("tensor", lambda te, hp=hp, par=par: te.matmul(
                        psT[par * 64:par * 64 + 64, hp * 128:(hp + 1) * 128],
                        khc[:, hp * 64:(hp + 1) * 64], ident[:],
                        start=True, stop=True, is_transpose=True,
                        tile_position=(0, par * 64)),
                        inc="s_pe" if hp == ND - 1 else None)
                b.set("kt_mm", "tensor")
                b.waits("vector", "kt_mm", "ktm_free")
                b.on("vector", lambda v, par=par: v.tensor_copy(
                    ktm[par * 64:par * 64 + 64, :],
                    psT[par * 64:par * 64 + 64, :]), inc="s_dve")
                b.set("kt_ev", "vector")
                b.set("psT_free", "vector")
                if "o" not in E_PARTS:
                    continue
                b.waits("tensor", "am_ev", "psB_free", "S_ok", "v_ev",
                        "psC_free")
                for hp in range(ND):
                    b.on("tensor", lambda te, hp=hp: te.matmul(
                        psB[0:64, hp * 64:(hp + 1) * 64],
                        Ssb[0:64, hp * 64:(hp + 1) * 64],
                        qsc[0:64, hp * 64:(hp + 1) * 64],
                        start=True, stop=True, tile_position=(0, 0)))
                    b.on("tensor", lambda te, hp=hp: te.matmul(
                        psB[64:128, hp * 64:(hp + 1) * 64],
                        Ssb[64:128, hp * 64:(hp + 1) * 64],
                        qsc[64:128, hp * 64:(hp + 1) * 64],
                        start=True, stop=True, tile_position=(64, 64)))
                    b.on("tensor", lambda te, hp=hp, par=par, tloc=tloc: te.matmul(
                        psC[0:64, hp * 64:(hp + 1) * 64],
                        v_tm[par * 64:par * 64 + 64,
                             tloc * 1024 + hp * 128: tloc * 1024 + hp * 128 + 64],
                        amskA[par * 64:par * 64 + 64, hp * 64:(hp + 1) * 64],
                        start=True, stop=True, tile_position=(par * 64, 0)))
                    b.on("tensor", lambda te, hp=hp, par=par, tloc=tloc: te.matmul(
                        psC[64:128, hp * 64:(hp + 1) * 64],
                        v_tm[par * 64:par * 64 + 64,
                             tloc * 1024 + hp * 128 + 64:
                             tloc * 1024 + hp * 128 + 128],
                        amskB[par * 64:par * 64 + 64, hp * 64:(hp + 1) * 64],
                        start=True, stop=True, tile_position=(par * 64, 64)),
                        inc="s_pe" if hp == ND - 1 else None)
                b.set("o_mm", "tensor")
                og_fm = og1_fm if half == 0 else og2_fm
                cloc = (c0 - half * 1024) // 64
                if "u" not in E_PARTS:
                    continue
                b.waits("tensor", "kt_ev", "psE_free", "S_scaled_read",
                        "v_ev")
                for hp in range(ND):
                    b.on("tensor", lambda te, hp=hp, par=par, tloc=tloc: te.matmul(
                        psE[0:64, hp * 64:(hp + 1) * 64],
                        ktm[par * 64:par * 64 + 64, hp * 128:hp * 128 + 64],
                        v_tm[par * 64:par * 64 + 64,
                             tloc * 1024 + hp * 128: tloc * 1024 + hp * 128 + 64],
                        start=True, stop=True, tile_position=(par * 64, 0)))
                    b.on("tensor", lambda te, hp=hp, par=par, tloc=tloc: te.matmul(
                        psE[64:128, hp * 64:(hp + 1) * 64],
                        ktm[par * 64:par * 64 + 64, hp * 128 + 64:hp * 128 + 128],
                        v_tm[par * 64:par * 64 + 64,
                             tloc * 1024 + hp * 128 + 64:
                             tloc * 1024 + hp * 128 + 128],
                        start=True, stop=True, tile_position=(par * 64, 64)),
                        inc="s_pe" if hp == ND - 1 else None)
                b.set("su_mm", "tensor")
                b.set("ktm_free", "tensor")
                b.set("qk_free", "tensor")
                if "z" not in E_PARTS:
                    continue
                if "1" not in E_PARTS:
                    continue
                b.waits("vector", "o_mm", "sm_exp")
                b.on("vector", lambda v: v.tensor_tensor(
                    Stmp[:].rearrange("p (j t) -> p j t", j=ND),
                    Ssb[:].rearrange("p (j t) -> p j t", j=ND),
                    smalls[:, 24:32].unsqueeze(2).broadcast_to([128, ND, 64]),
                    ALU.mult), inc="s_dve")
                b.set("S_scaled_read", "vector")
                if "2" not in E_PARTS:
                    continue
                b.waits("vector", "o_mm")
                b.on("vector", lambda v: v.tensor_copy(sttsb[:], psB[:]),
                     inc="s_dve")
                b.set("o_sb", "vector")
                b.set("psB_free", "vector")
                if "3" not in E_PARTS:
                    continue
                if os.environ.get("KE_OGMEMSET"):
                    b.on("vector", lambda v, og_fm=og_fm, cloc=cloc: v.memset(
                        og_fm[:, cloc * 512:(cloc + 1) * 512],
                        float(100 + cloc)), inc="s_dve")
                else:
                    b.on("vector", lambda v, og_fm=og_fm, cloc=cloc: v.tensor_tensor(
                        V(og_fm[:, cloc * 512:(cloc + 1) * 512]), V(sttsb[:]),
                        V(psC[:, 0:512]), ALU.add), inc="s_dve")
                b.set("og_ev", "vector")
                b.set("psC_free", "vector")
                b.waits("vector", "su_mm")
                b.on("vector", lambda v: v.tensor_tensor(
                    V(Ssb[:]), V(Stmp[:]), V(psE[:]), ALU.add), inc="s_dve")
                b.set("S_ok", "vector")
                b.set("psE_free", "vector")
                b.set("amsk_free", at=b.res["o_mm"])
            b.set("egd_free", at=b.res["cb"])
        b.set(f"og_half{half}", at=b.res["og_ev"])

    b.res.setdefault("noop", ("s_dve", 0))

    for half in range(2):
        emit_S_half(half)
        if stage == f"s{half}":
            return finish_here()
        emit_E_half(half)
        if stage == f"e{half}":
            if os.environ.get("KE_DUMPOG"):
                return dump_and_finish(og1_fm if half == 0 else og2_fm, 8192,
                                       ["og_ev"])
            return finish_here()

    b.waits("gpsimd", "attw_free", "q_scratch", "q_act", "su_mm", "o_mm")
    b.on("gpsimd", lambda g: g.dma_start(
        out=attw[:].rearrange("p (j n) -> p j n", j=ND),
        in_=W_d["Wo"].rearrange("(j p) n -> p j n", p=128)), inc="s_g")
    b.set("wo_ld", "gpsimd")
    gatp5s = (AR4b[:, 0:1024], AR4b[:, 6144:7168])
    xrlds = (xrld, AR4[:, 2048:3072])
    _p5ret = [None]
    for t in range(NT):
        og_fm = og1_fm if t < 8 else og2_fm
        tl = t % 8
        pp5 = t % 2
        gatp5 = gatp5s[pp5]
        xrl = xrlds[pp5]
        b.waits("sync", "xspill", "xspill_g", f"xrld_free{pp5}", "qk",
                f"gatp5_free{pp5}")
        b.on("sync", lambda s, t=t, xrl=xrl: s.dma_start(
            out=xrl[:], in_=xsp_s[:, t * D:(t + 1) * D]), inc="s_ld")
        for c_ in range(2):
            b.on("sync", lambda s, t=t, c_=c_, gatp5=gatp5: s.dma_start(
                out=gatp5[:, c_ * 512:(c_ + 1) * 512].rearrange(
                    "p (j z) -> p j z", j=ND),
                in_=gsp_s.rearrange("p (j tt) -> p j tt", j=ND)[
                    :, :, t * 128 + c_ * 64: t * 128 + c_ * 64 + 64]),
                inc="s_ld")
        b.set("x_rl", "sync")
        b.waits("vector", "x_rl", f"og_half{0 if t < 8 else 1}", "wo_mm")
        b.on("vector", lambda v, og_fm=og_fm, tl=tl, gatp5=gatp5: v.tensor_tensor(
            V(og_fm[:, tl * 1024:(tl + 1) * 1024]),
            V(og_fm[:, tl * 1024:(tl + 1) * 1024]), V(gatp5[:]), ALU.mult),
            inc="s_dve")
        b.set("og_gated", "vector")
        b.set(f"gatp5_free{pp5}", "vector")
        if stage == "dbgp5" and t == int(os.environ.get("KE_P5T", "1")):
            return dump_and_finish(og_fm[:, tl * 1024:tl * 1024 + 1024], 1024,
                                   ["og_gated"])
        b.waits("tensor", "wo_ld", "psC_free", "og_gated")
        for c_ in range(2):
            for nh in range(2):
                for j in range(ND):
                    b.on("tensor", lambda te, j=j, nh=nh, tl=tl, c_=c_,
                         og_fm=og_fm: te.matmul(
                        psC[c_ * 64:(c_ + 1) * 64, nh * 512:(nh + 1) * 512],
                        og_fm[:, tl * 1024 + c_ * 512 + j * 64:
                              tl * 1024 + c_ * 512 + (j + 1) * 64],
                        attw[:, j * 1024 + nh * 512: j * 1024 + (nh + 1) * 512],
                        start=(j == 0), stop=(j == ND - 1),
                        tile_position=(0, c_ * 64)),
                        inc="s_pe" if (c_ == 1 and j == ND - 1 and nh == 1) else None)
        b.set("wo_mm", "tensor")
        b.waits("vector", "wo_mm", "x_rl")
        b.on("vector", lambda v, t=t, xrl=xrl: v.tensor_tensor(
            V(xres[:, t * D:(t + 1) * D]), V(xrl[:]), V(psC[:]), ALU.add),
            inc="s_dve")
        b.set("x2_acc", "vector")
        b.set("psC_free", "vector")
        b.set(f"xrld_free{pp5}", "vector")
    b.set("x2_done", at=b.res["x2_acc"])
    if stage == "p5":
        return finish_here()
    if stage == "dbgp5":
        raise RuntimeError("dbgp5 tile index out of range")

    for eng in ENGS:
        b.waits(eng, "x2_done", "wo_mm")
    emit_ffn("f2", "x2_done")
    b.waits("sync", "f2_done")
    b.on("sync", lambda s: s.dma_start(
        out=y_out.rearrange("(i p) d -> p i d", p=128),
        in_=xres.rearrange("p (i d) -> p i d", d=D)), inc="s_sp")
    out_val = b.cnt["s_sp"]
    b.on("sync", lambda s, v=out_val: s.wait_ge(b.sem["s_sp"], v))

    nc.detect_race_conditions = False
    blk = es.enter_context(nc.Block())
    with es:
        b.replay(blk)
    return nc


_NC_CACHE = [None]


def kernel(L0_attn_hidden_states, L0_attn_time_maa_x, L0_attn_time_maa_r,
           L0_attn_time_maa_k, L0_attn_time_maa_v, L0_attn_time_maa_w,
           L0_attn_time_maa_g, L0_attn_time_maa_w1, L0_attn_time_maa_w2,
           L0_attn_time_decay, L0_attn_time_decay_w1, L0_attn_time_decay_w2,
           L0_attn_Wq, L0_attn_Wk, L0_attn_Wv, L0_attn_Wo, L0_attn_Wg,
           L0_attn_bg, L0_ffn_w_gate, L0_ffn_w_up, L0_ffn_w_down):
    x = np.ascontiguousarray(np.asarray(L0_attn_hidden_states, dtype=np.float32))
    assert x.shape == (8, T, D)
    vecs0 = np.stack([
        np.asarray(L0_attn_time_maa_x, np.float32).reshape(D),
        np.asarray(L0_attn_time_maa_r, np.float32).reshape(D),
        np.asarray(L0_attn_time_maa_k, np.float32).reshape(D),
        np.asarray(L0_attn_time_maa_v, np.float32).reshape(D),
        np.asarray(L0_attn_time_maa_w, np.float32).reshape(D),
        np.asarray(L0_attn_time_maa_g, np.float32).reshape(D),
        np.asarray(L0_attn_time_decay, np.float32).reshape(D),
        np.asarray(L0_attn_bg, np.float32).reshape(D),
    ], axis=0)
    vecs = np.ascontiguousarray(
        vecs0.reshape(8, 8, 128).transpose(2, 0, 1).reshape(128, 64))
    ident = np.eye(128, dtype=np.float32)
    mask = np.tile(np.tril(np.ones((CH, CH), np.float32)).T * 0 +
                   (np.arange(CH)[:, None] <= np.arange(CH)[None, :]).astype(np.float32),
                   (2, 1))
    common = {
        "w_gate": np.ascontiguousarray(np.asarray(L0_ffn_w_gate, np.float32)),
        "w_up": np.ascontiguousarray(np.asarray(L0_ffn_w_up, np.float32)),
        "w_down": np.ascontiguousarray(np.asarray(L0_ffn_w_down, np.float32)),
        "Wq": np.ascontiguousarray(np.asarray(L0_attn_Wq, np.float32)),
        "Wk": np.ascontiguousarray(np.asarray(L0_attn_Wk, np.float32)),
        "Wv": np.ascontiguousarray(np.asarray(L0_attn_Wv, np.float32)),
        "Wo": np.ascontiguousarray(np.asarray(L0_attn_Wo, np.float32)),
        "Wg": np.ascontiguousarray(np.asarray(L0_attn_Wg, np.float32)),
        "maa_w1": np.ascontiguousarray(np.asarray(L0_attn_time_maa_w1, np.float32)),
        "maa_w2": np.ascontiguousarray(np.asarray(L0_attn_time_maa_w2, np.float32)),
        "dec_w1": np.ascontiguousarray(np.asarray(L0_attn_time_decay_w1, np.float32)),
        "dec_w2": np.ascontiguousarray(np.asarray(L0_attn_time_decay_w2, np.float32)),
        "vecs": vecs,
        "ident": ident,
        "mask": np.ascontiguousarray(mask),
    }
    in_maps = [dict(common, x_in=np.ascontiguousarray(x[i])) for i in range(8)]
    nc = build()
    res = run_bass_kernel_spmd(nc, in_maps, core_ids=list(range(8)))
    out = np.stack([np.asarray(r["y_out"]) for r in res.results], axis=0)
    return out.astype(np.float32)


if __name__ == "__main__":
    import sys
    if "--build" in sys.argv:
        nc = build()
        total = sum(len(v) for v in nc.m.funcs[0].blocks[0].instructions) if False else None
        print("build ok")


# version 08
# speedup vs baseline: 1.4794x; 1.0015x over previous
import numpy as np
from contextlib import ExitStack

import concourse.bass as bass
import concourse.mybir as mybir
from concourse.bass_utils import run_bass_kernel_spmd

F32 = mybir.dt.float32
BF16 = mybir.dt.bfloat16
ALU = mybir.AluOpType
AF = mybir.ActivationFunctionType
AX = mybir.AxisListType

T, D, I = 2048, 1024, 2816
NT, ND, NI = T // 128, D // 128, I // 128
CH = 64
LN8 = float(np.log(np.float64(8.0)))
R6 = float(np.float32(1.0 / 6.0))
R1024 = float(np.float32(1.0 / 1024.0))
EPS = 1e-5
ICHUNKS = [(k * 512, 512) for k in range(5)] + [(2560, 256)]
ENGS = ("sync", "gpsimd", "tensor", "vector", "scalar")


def V(ap):
    sh = ap.shape
    if len(sh) == 2 and sh[1] > 512:
        n = sh[1]
        z = max(d for d in range(1, 513) if n % d == 0)
        return ap.rearrange("p (a z) -> p a z", z=z)
    return ap


class B:

    def __init__(self, nc, es):
        self.nc = nc
        self.q = {e: [] for e in ENGS}
        names = ("s_ld", "s_g", "s_pe", "s_dve", "s_act", "s_sp", "s_og")
        self.sem = {n: es.enter_context(nc.semaphore(n)) for n in names}
        self.cnt = {n: 0 for n in names}
        self.res = {}

    def on(self, eng, fn, inc=None, inc_n=None):
        if inc is None:
            self.q[eng].append(fn)
            return None
        n = inc_n if inc_n is not None else (16 if eng in ("sync", "gpsimd") else 1)
        self.cnt[inc] += n

        def thunk(e, fn=fn, s=self.sem[inc], n=n):
            fn(e).then_inc(s, n)
        self.q[eng].append(thunk)
        return self.cnt[inc]

    def now(self, eng):
        s = {"sync": "s_ld", "gpsimd": "s_g", "tensor": "s_pe",
             "vector": "s_dve", "scalar": "s_act"}[eng]
        return (s, self.cnt[s])

    def set(self, res, eng=None, at=None):
        self.res[res] = at if at is not None else self.now(eng)

    def wait(self, eng, res):
        if res not in self.res:
            return
        sname, val = self.res[res]
        if val > 0:
            self.q[eng].append(lambda e, s=self.sem[sname], v=val: e.wait_ge(s, v))

    def waits(self, eng, *ress):
        best = {}
        for r in ress:
            if r in self.res:
                sname, val = self.res[r]
                best[sname] = max(best.get(sname, 0), val)
        for sname, val in best.items():
            if val > 0:
                self.q[eng].append(lambda e, s=self.sem[sname], v=val: e.wait_ge(s, v))

    def replay(self, blk):
        for eng in ENGS:
            thunks = self.q[eng]

            def body(e, thunks=thunks):
                for t in thunks:
                    t(e)
            getattr(blk, eng)(body)


def build(stage="full"):
    nc = bass.Bass("TRN2", target_bir_lowering=False, debug=False)

    x_in = nc.dram_tensor("x_in", [T, D], F32, kind="ExternalInput").ap()
    y_out = nc.dram_tensor("y_out", [T, D], F32, kind="ExternalOutput").ap()
    wg_d = nc.dram_tensor("w_gate", [I, D], F32, kind="ExternalInput").ap()
    wu_d = nc.dram_tensor("w_up", [I, D], F32, kind="ExternalInput").ap()
    wd_d = nc.dram_tensor("w_down", [D, I], F32, kind="ExternalInput").ap()
    W_d = {f: nc.dram_tensor(f, [D, D], F32, kind="ExternalInput").ap()
           for f in ("Wq", "Wk", "Wv", "Wo", "Wg")}
    w1_d = nc.dram_tensor("maa_w1", [D, 160], F32, kind="ExternalInput").ap()
    w2_d = nc.dram_tensor("maa_w2", [5, 32, D], F32, kind="ExternalInput").ap()
    dw1_d = nc.dram_tensor("dec_w1", [D, 64], F32, kind="ExternalInput").ap()
    dw2_d = nc.dram_tensor("dec_w2", [64, D], F32, kind="ExternalInput").ap()
    vecs_d = nc.dram_tensor("vecs", [128, 64], F32, kind="ExternalInput").ap()
    ident_d = nc.dram_tensor("ident", [128, 128], F32, kind="ExternalInput").ap()
    mask_d = nc.dram_tensor("mask", [128, CH], F32, kind="ExternalInput").ap()

    wgT_s = nc.dram_tensor("wgT_s", [D, I], BF16, kind="Internal").ap()
    wuT_s = nc.dram_tensor("wuT_s", [D, I], BF16, kind="Internal").ap()
    wdT_s = nc.dram_tensor("wdT_s", [I, D], BF16, kind="Internal").ap()
    xsp_s = nc.dram_tensor("xsp_s", [128, NT * D], F32, kind="Internal").ap()
    gsp_s = nc.dram_tensor("gsp_s", [128, ND * T], BF16, kind="Internal").ap()

    es = ExitStack()

    def sbt(name, p, n, dt=F32):
        return es.enter_context(nc.sbuf_tensor(name, [p, n], dt))

    def pst_(name, p, n, dt=F32):
        return es.enter_context(nc.psum_tensor(name, [p, n], dt))

    AR0 = sbt("AR0", 128, 16384)
    AR1 = sbt("AR1", 128, 8192)
    AR2 = sbt("AR2", 128, 8192)
    AR3 = sbt("AR3", 128, 8192)
    AR4 = sbt("AR4", 128, 4096)

    ident = sbt("identsb", 128, 128, BF16)
    maskt = sbt("masksb", 128, CH)
    vecs = sbt("vecssb", 128, 64)
    w1sb = sbt("w1sb", 128, 8 * 160, BF16)
    w2sb = sbt("w2sb", 32, 5 * 1024, BF16)
    dw1sb = sbt("dw1sb", 128, 8 * 64, BF16)
    dw2sb = sbt("dw2sb", 64, 1024, BF16)
    tanh1 = sbt("tanh1", 64, T, BF16)
    Ssb = sbt("Ssb", 128, 512, BF16)
    Stmp = sbt("Stmp", 128, 512, BF16)
    mixsb = sbt("mixsb", 128, 512, BF16)
    mix2sb = sbt("mix2sb", 32, 512, BF16)
    cbsb = sbt("cbsb", 128, 8 * CH)
    smalls = sbt("smalls", 128, 64)
    sttsb = sbt("sttsb", 128, 512)
    gstage = sbt("gstage", 128, 512, BF16)
    lnmean = sbt("lnmean", 128, NT)
    lnsum2 = sbt("lnsum2", 128, NT)
    lnrstd = sbt("lnrstd", 128, NT)
    lnnmr = sbt("lnnmr", 128, NT)
    onessb = sbt("onessb", 128, 1)
    q_bm = sbt("q_bm", 128, 88)
    q_sc = sbt("q_sc", 128, 88)
    q_rs = sbt("q_rs", 128, 88)
    q_hs = sbt("q_hs", 128, 88, BF16)
    thrsb = sbt("thrsb", 128, 2)

    psA = pst_("psA", 128, 512)
    psB = pst_("psB", 128, 512)
    psC = pst_("psC", 128, 1024)
    psD = pst_("psD", 128, 512)
    psE = pst_("psE", 128, 512)
    psT = pst_("psT", 128, 1024, BF16)

    b = B(nc, es)
    cstsb = sbt("cstsb", 128, 9)
    nc.const_aps.register(F32, EPS, cstsb[:, 0:1])
    nc.const_aps.register(F32, -LN8, cstsb[:, 1:2])
    QTHR = (5.0, 3.5, 2.5, 1.75, 1.25, 0.75, 0.25)
    for qi_, qt_ in enumerate(QTHR):
        nc.const_aps.register(F32, -qt_, cstsb[:, 2 + qi_:3 + qi_])

    AR0b = AR0[:].bitcast(BF16)
    AR1b = AR1[:].bitcast(BF16)
    AR2b = AR2[:].bitcast(BF16)
    AR3b = AR3[:].bitcast(BF16)
    AR4b = AR4[:].bitcast(BF16)
    xres = AR0[:]
    q_fm = AR0b[:, 0:16384]
    k_fm = AR0b[:, 16384:32768]
    xq_fm = AR1b[:, 0:16384]
    ln1_fm = AR1b[:, 0:16384]
    og2_fm = AR1b[:, 0:8192]
    og1_fm = AR3b[:, 8192:16384]
    attw = AR3b[:, 0:8192]
    v_tm = AR2b[:, 0:8192]
    dec_g = AR2[:, 4096:6144]
    egd_g = AR2[:, 6144:8192]
    wch_g = AR2b[:, 0:4096]
    wch_u = AR2b[:, 4096:8192]
    wch_d = AR2b[:, 8192:12288]
    mixf = AR2b[0:32, 8192:13312]
    ffnsil = AR4[:, 2560:3072]
    expsc = AR4[:, 1024:1536]
    expsc2 = AR4[:, 1536:2048]
    ktm = AR4b[:, 4096:5120]
    qsc = AR4b[:, 5120:5632]
    qss = AR4b[:, 5632:6144]
    ksc = AR4b[:, 6144:6656]
    khc = AR3b[:, 0:512]
    amskA = AR3b[:, 512:1024]
    amskB = AR3b[:, 1024:1536]
    q_y2 = AR2[:, 6144:7552]
    q_sg2 = AR2[:].bitcast(BF16)[:, 15104:16128]
    q_y = AR3[:, 0:1408]
    q_a = AR3[:, 1408:2816]
    q_b = AR3[:, 2816:4224]
    q_A = AR3[:, 4224:5632]
    q_sg = AR3b[:, 11264:12672]
    q_Q = AR3b[:, 12672:14080]
    wq_tm = AR3b[:, 14080:16384]
    lnsq = AR4[:, 0:1024]
    ln_t = AR4[:, 1024:2048]
    xrld = AR4[:, 1024:2048]
    hbuf = AR4[:, 2048:2560]
    lnb_t = AR4b[:, 5120:6144]
    hq_tm = AR4b[:, 6144:6656]
    hq_tm2 = AR3b[:, 15488:16000]
    hq_fm = AR4b[:, 6656:7168]
    tstage = AR4b[:, 7168:8192]
    dx_r = AR4b[:, 0:4096]
    xx_r = AR4b[:, 4096:8192]
    xf_r = AR4b[:, 4096:8192]
    gat_r = AR4b[:, 0:2048]

    def vsl(i):
        return vecs[:, i * 8:(i + 1) * 8]

    maa_x = vsl(0)
    maa_of = {"r": vsl(1), "k": vsl(2), "v": vsl(3), "w": vsl(4), "g": vsl(5)}
    tdec, bgv = vsl(6), vsl(7)

    def fm3(ap, c0, n):
        return ap.rearrange("p (j t) -> p j t", j=ND)[:, :, c0:c0 + n]

    b.on("gpsimd", lambda g: g.dma_start(out=ident[:], in_=ident_d), inc="s_g")
    b.on("gpsimd", lambda g: g.dma_start(out=maskt[:], in_=mask_d), inc="s_g")
    b.on("gpsimd", lambda g: g.dma_start(out=vecs[:], in_=vecs_d), inc="s_g")
    b.on("gpsimd", lambda g: g.dma_start(
        out=w1sb[:].rearrange("p (j n) -> p j n", j=ND),
        in_=w1_d.rearrange("(j p) n -> p j n", p=128)), inc="s_g")
    b.on("gpsimd", lambda g: g.dma_start(
        out=w2sb[:].rearrange("p (f d) -> p f d", f=5),
        in_=w2_d.rearrange("f l d -> l f d")), inc="s_g")
    b.on("gpsimd", lambda g: g.dma_start(
        out=dw1sb[:].rearrange("p (j n) -> p j n", j=ND),
        in_=dw1_d.rearrange("(j p) n -> p j n", p=128)), inc="s_g")
    b.on("gpsimd", lambda g: g.dma_start(out=dw2sb[:], in_=dw2_d), inc="s_g")
    b.set("consts", "gpsimd")

    b.on("sync", lambda s: s.dma_start(
        out=xres.rearrange("p (i d) -> p i d", d=D)[:, 0:8, :],
        in_=x_in.rearrange("(i p) d -> p i d", p=128)[:, 0:8, :]),
        inc="s_ld")
    xl_s = b.cnt["s_ld"]
    b.on("gpsimd", lambda g: g.dma_start(
        out=xres.rearrange("p (i d) -> p i d", d=D)[:, 8:16, :],
        in_=x_in.rearrange("(i p) d -> p i d", p=128)[:, 8:16, :]),
        inc="s_g")
    xl_g = b.cnt["s_g"]
    b.set("xload", at=("s_ld", xl_s))
    b.set("xload_g", at=("s_g", xl_g))
    b.on("vector", lambda v: v.memset(onessb[:], 1.0))
    b.on("vector", lambda v: v.memset(cstsb[:, 0:1], EPS))
    b.on("vector", lambda v: v.memset(cstsb[:, 1:2], -LN8))
    for qi_, qt_ in enumerate(QTHR):
        b.on("vector", lambda v, qi_=qi_, qt_=qt_: v.memset(
            cstsb[:, 2 + qi_:3 + qi_], -qt_))
    for ti_, tv_ in enumerate((1.0, 0.5)):
        b.on("vector", lambda v, ti_=ti_, tv_=tv_: v.memset(
            thrsb[:, ti_:ti_ + 1], tv_))
    b.on("vector", lambda v: v.memset(Ssb[:], 0.0), inc="s_dve")
    b.set("init", "vector")
    b.set("S_ok", "vector")

    qn = [0]

    def emit_quant(x_ap, out_bf, n, ready_res, out_res, ybuf=None):
        p = qn[0] & 1
        qn[0] += 1
        nb = n // 16
        qsg = q_sg if p == 0 else q_sg2
        if ybuf is None:
            ybuf = x_ap
        b.waits("scalar", ready_res, "q_dy")
        b.on("scalar", lambda s: s.activation(q_a[:, 0:n], x_ap, AF.Abs),
             inc="s_act")
        b.set("q_abs", "scalar")
        b.waits("scalar", f"q_sgf{p}")
        b.on("scalar", lambda s, qsg=qsg: s.activation(
            qsg[:, 0:n], x_ap, AF.Sign), inc="s_act")
        b.set("q_act", "scalar")
        b.waits("vector", ready_res)
        x3 = x_ap.rearrange("p (z k) -> p z k", k=16)
        b.on("vector", lambda v: v.tensor_reduce(
            q_bm[:, 0:nb], x3, axis=AX.X, op=ALU.max, apply_absolute_value=True),
            inc="s_dve")
        b.set("q_bmr", "vector")
        import os as _os
        QSY = not _os.environ.get("KE_NOQSY")
        if QSY:
            b.wait("vector", "q_bmr")
        b.on("vector", lambda v: v.tensor_scalar(
            q_sc[:, 0:nb], q_bm[:, 0:nb], R6, 1e-12, ALU.mult, ALU.max),
            inc="s_dve")
        b.set("q_scd", "vector")
        if QSY:
            b.wait("vector", "q_scd")
        b.on("vector", lambda v: v.reciprocal(q_rs[:, 0:nb], q_sc[:, 0:nb]),
             inc="s_dve")
        b.set("q_rsd", "vector")
        b.on("vector", lambda v: v.tensor_scalar(
            q_hs[:, 0:nb], q_sc[:, 0:nb], 0.5, None, ALU.mult))
        if QSY:
            b.wait("vector", "q_rsd")
        b.wait("vector", "q_act")
        a3 = q_a[:, 0:n].rearrange("p (z k) -> p z k", k=16)
        y3 = ybuf[:, 0:n].rearrange("p (z k) -> p z k", k=16)
        b.on("vector", lambda v, y3=y3: v.tensor_tensor(
            y3, a3, q_rs[:, 0:nb].unsqueeze(2).broadcast_to([128, nb, 16]),
            ALU.mult), inc="s_dve")
        b.set("q_dy", "vector")
        y, Bc, Ac = V(ybuf[:, 0:n]), V(q_b[:, 0:n]), V(q_A[:, 0:n])
        b.on("vector", lambda v: v.tensor_single_scalar(Bc, y, 5.0, ALU.is_gt))
        b.on("vector", lambda v: v.tensor_single_scalar(Ac, y, 0.25, ALU.is_gt))
        for thB, thA in ((5.0, 0.75), (3.5, 1.25), (2.5, 1.75)):
            b.on("vector", lambda v, th=thB: v.scalar_tensor_tensor(
                Bc, y, th, Bc, ALU.is_gt, ALU.add))
            b.on("vector", lambda v, th=thA: v.scalar_tensor_tensor(
                Ac, y, th, Ac, ALU.is_gt, ALU.add))
        s3 = qsg[:, 0:n].rearrange("p (z k) -> p z k", k=16)
        b.on("vector", lambda v, s3=s3: v.tensor_tensor(
            s3, s3, q_hs[:, 0:nb].unsqueeze(2).broadcast_to([128, nb, 16]),
            ALU.mult))
        b.on("vector", lambda v: v.scalar_tensor_tensor(
            V(q_Q[:, 0:n]), Bc, 2.0, Ac, ALU.mult, ALU.add))
        b.on("vector", lambda v, qsg=qsg: v.tensor_tensor(
            V(out_bf), V(q_Q[:, 0:n]), V(qsg[:, 0:n]), ALU.mult), inc="s_dve")
        b.set(out_res, "vector")
        b.set(f"q_sgf{p}", "vector")
        b.set("q_scratch", "vector")

    def emit_wquant(w, ti, parts):
        for (c0, n) in parts:
            src = {"g": wg_d, "u": wu_d, "d": wd_d}[w]
            p_ = qn[0] & 1
            qld = q_y if p_ == 0 else q_y2
            b.waits("sync", f"q_sgf{p_}")
            b.on("sync", lambda s, src=src, ti=ti, c0=c0, n=n, qld=qld:
                 s.dma_start(out=qld[:, 0:n],
                             in_=src[ti * 128:(ti + 1) * 128, c0:c0 + n]),
                 inc="s_ld")
            b.set("wload", "sync")
            b.wait("vector", "wtp_done")
            emit_quant(qld[:, 0:n], wq_tm[:, 0:n], n, "wload", "wq")
            nblk = n // 128
            for g0 in range(0, nblk, 8):
                gn = min(8, nblk - g0)
                b.waits("tensor", "wq", "psT_free")
                for bb in range(gn):
                    b.on("tensor", lambda t, bb=bb, g0=g0: t.matmul(
                        psT[:, bb * 128:(bb + 1) * 128],
                        wq_tm[:, (g0 + bb) * 128:(g0 + bb + 1) * 128],
                        ident[:], start=True, stop=True, is_transpose=True),
                        inc="s_pe" if bb == gn - 1 else None)
                b.set("wtp", "tensor")
                if g0 + 8 >= nblk:
                    b.set("wtp_done", "tensor")
                b.waits("scalar", "wtp", "tstage_free")
                b.on("scalar", lambda sc, gn=gn: sc.activation(
                    tstage[:, 0:gn * 128], psT[:, 0:gn * 128], AF.Copy),
                    inc="s_act")
                b.set("tev", "scalar")
                b.set("psT_free", "scalar")
                if w in "gu":
                    dst = wgT_s if w == "g" else wuT_s
                    dview = dst.rearrange("(z p) i -> p z i", p=128)[
                        :, g0:g0 + gn, ti * 128:(ti + 1) * 128]
                else:
                    i0 = (c0 // 128) + g0
                    dview = wdT_s.rearrange("(z p) dd -> p z dd", p=128)[
                        :, i0:i0 + gn, ti * 128:(ti + 1) * 128]
                b.wait("sync", "tev")
                b.on("sync", lambda s, dv=dview, gn=gn: s.dma_start(
                    out=dv, in_=tstage[:, 0:gn * 128].rearrange(
                        "p (z m) -> p z m", m=128)), inc="s_sp")
                b.set("tstage_free", at=("s_sp", b.cnt["s_sp"]))
            b.set(f"wst_{w}{ti}p{c0}", at=("s_sp", b.cnt["s_sp"]))

    def emit_ln(x_src_res):
        if not isinstance(x_src_res, tuple):
            x_src_res = (x_src_res,)
        b.waits("vector", *x_src_res)
        b.on("vector", lambda v: v.tensor_reduce(
            lnmean[:], xres.rearrange("p (i d) -> p i d", d=D),
            axis=AX.X, op=ALU.add), inc="s_dve")
        b.set("ln_red", "vector")
        b.wait("vector", "ln_red")
        b.waits("scalar", *x_src_res, "lnsq_free")
        for t in range(NT):
            b.on("scalar", lambda s, t=t: s.activation(
                lnsq[:], xres[:, t * D:(t + 1) * D], AF.Square,
                accum_out=lnsum2[:, t:t + 1]),
                inc="s_act" if t == NT - 1 else None)
        b.set("ln_sq", "scalar")
        b.on("vector", lambda v: v.tensor_scalar(
            lnmean[:], lnmean[:], R1024, None, ALU.mult), inc="s_dve")
        b.set("ln_m2", "vector")
        b.wait("vector", "ln_sq")
        b.on("vector", lambda v: v.tensor_scalar(
            lnsum2[:], lnsum2[:], R1024, None, ALU.mult), inc="s_dve")
        b.set("ln_s2", "vector")
        b.wait("vector", "ln_m2")
        b.wait("vector", "ln_s2")
        b.on("vector", lambda v: v.tensor_tensor(
            lnrstd[:], lnmean[:], lnmean[:], ALU.mult), inc="s_dve")
        b.set("ln_msq", "vector")
        b.wait("vector", "ln_msq")
        b.on("vector", lambda v: v.tensor_tensor(
            lnrstd[:], lnsum2[:], lnrstd[:], ALU.subtract), inc="s_dve")
        b.set("ln_var", "vector")
        b.wait("scalar", "ln_var")
        b.on("scalar", lambda s: s.activation(
            lnrstd[:], lnrstd[:], AF.Ln, bias=EPS), inc="s_act")
        b.set("ln_ln", "scalar")
        b.wait("scalar", "ln_ln")
        b.on("scalar", lambda s: s.activation(
            lnrstd[:], lnrstd[:], AF.Exp, scale=-0.5), inc="s_act")
        b.set("ln_rstd", "scalar")
        b.wait("vector", "ln_rstd")
        b.on("vector", lambda v: v.tensor_tensor(
            lnnmr[:], lnmean[:], lnrstd[:], ALU.mult), inc="s_dve")
        b.set("ln_nm1", "vector")
        b.wait("vector", "ln_nm1")
        b.on("vector", lambda v: v.tensor_scalar(
            lnnmr[:], lnnmr[:], -1.0, None, ALU.mult), inc="s_dve")
        b.set("ln_ok", "vector")

    def emit_ln_apply(t, out_ap, out_res):
        b.waits("scalar", "ln_ok", f"lnout_free")
        b.on("scalar", lambda s, t=t: s.activation(
            out_ap, xres[:, t * D:(t + 1) * D], AF.Identity,
            bias=lnnmr[:, t:t + 1], scale=lnrstd[:, t:t + 1]), inc="s_act")
        b.set(out_res, "scalar")

    def emit_tp(src_bf, nblk, dst_fn, dep_res, out_res):
        deps = dep_res if isinstance(dep_res, tuple) else (dep_res,)
        b.waits("tensor", *deps, "psT_free")
        for bb in range(nblk):
            b.on("tensor", lambda te, bb=bb: te.matmul(
                psT[:, bb * 128:(bb + 1) * 128],
                src_bf[:, bb * 128:(bb + 1) * 128],
                ident[:], start=True, stop=True, is_transpose=True),
                inc="s_pe" if bb == nblk - 1 else None)
        b.set("tp_mm", "tensor")
        b.wait("scalar", "tp_mm")
        b.on("scalar", dst_fn, inc="s_act")
        b.set(out_res, "scalar")
        b.set("psT_free", "scalar")

    def emit_ffn(tag, x_res, wprep=False):
        emit_ln(x_res)
        for t in range(NT):
            emit_ln_apply(t, ln_t[:], "ln_ap")
            b.waits("vector", "lnb_free")
            emit_quant(ln_t[:], lnb_t[:], D, "ln_ap", "xq_q", ybuf=lnsq)
            b.set("lnout_free", at=b.res["q_bmr"])
            emit_tp(lnb_t, 8,
                    lambda sc, t=t: sc.activation(
                        fm3(xq_fm, t * 128, 128),
                        psT[:].rearrange("p (z m) -> p z m", m=128), AF.Copy),
                    "xq_q", "xq_ev")
            b.set("lnb_free", at=b.res["tp_mm"])
        b.set("xq_all", at=b.res["xq_ev"])
        if stage == "dbgx":
            return "DBGX"
        pend = [None]
        for ic, (i0, ilen) in enumerate(ICHUNKS):
            nseg = ilen // 128
            if wprep:
                if ic == 0:
                    for ti in range(ND):
                        emit_wquant("d", ti, [(0, 1024)])
                for ti in range(4 * ic, min(4 * (ic + 1), NI)):
                    emit_wquant("g", ti, [(0, 1024)])
                    emit_wquant("u", ti, [(0, 1024)])
                if ic in (0, 1):
                    for ti in range(4 * ic, 4 * ic + 4):
                        emit_wquant("d", ti, [(1024, 896), (1920, 896)])
                last_ti = min(4 * (ic + 1), NI) - 1
                wdep = "wst_d7p0" if ic <= 1 else "wst_d7p1920"
                b.waits("sync", f"wst_u{last_ti}p0", wdep)
            else:
                b.waits("sync", "wq_all_st")
            b.waits("sync", "wgu_free")
            b.on("sync", lambda s, i0=i0, ilen=ilen: s.dma_start(
                out=wch_g[:, 0:8 * ilen].rearrange("p (j n) -> p j n", j=ND),
                in_=wgT_s.rearrange("(j p) i -> p j i", p=128)[:, :, i0:i0 + ilen]),
                inc="s_ld")
            b.on("sync", lambda s, i0=i0, ilen=ilen: s.dma_start(
                out=wch_u[:, 0:8 * ilen].rearrange("p (j n) -> p j n", j=ND),
                in_=wuT_s.rearrange("(j p) i -> p j i", p=128)[:, :, i0:i0 + ilen]),
                inc="s_ld")
            b.set("wchgu", "sync")
            b.waits("sync", "wch_free")
            b.on("sync", lambda s, i0=i0, nseg=nseg: s.dma_start(
                out=wch_d[:, 0:nseg * 1024].rearrange("p (m n) -> p m n", n=1024),
                in_=wdT_s.rearrange("(m p) dd -> p m dd", p=128)[
                    :, i0 // 128:i0 // 128 + nseg, :]), inc="s_ld")
            b.set("wchd", "sync")
            for t in range(NT):
                b.waits("tensor", "wchgu", "wchd", "xq_all", "psA_free",
                        "psB_free")
                for j in range(ND):
                    b.on("tensor", lambda te, j=j, t=t, ilen=ilen: te.matmul(
                        psA[:, 0:ilen],
                        xq_fm[:, j * T + t * 128: j * T + (t + 1) * 128],
                        wch_g[:, j * ilen:(j + 1) * ilen],
                        start=(j == 0), stop=(j == ND - 1)))
                for j in range(ND):
                    b.on("tensor", lambda te, j=j, t=t, ilen=ilen: te.matmul(
                        psB[:, 0:ilen],
                        xq_fm[:, j * T + t * 128: j * T + (t + 1) * 128],
                        wch_u[:, j * ilen:(j + 1) * ilen],
                        start=(j == 0), stop=(j == ND - 1)),
                        inc="s_pe" if j == ND - 1 else None)
                b.set("gu_mm", "tensor")
                if t == NT - 1:
                    b.set("wgu_free", "tensor")
                if stage in ("dbgg", "dbgu") and t == 0:
                    b.wait("vector", "gu_mm")
                    pp_dbg = psA if stage == "dbgg" else psB
                    b.on("vector", lambda v, pp_dbg=pp_dbg, ilen=ilen: v.tensor_copy(
                        sttsb[:, 0:ilen], pp_dbg[:, 0:ilen]), inc="s_dve")
                    b.set("dbg_ev", "vector")
                    b.wait("sync", "dbg_ev")
                    b.on("sync", lambda s, ilen=ilen: s.dma_start(
                        out=y_out.rearrange("(i p) d -> p (i d)", p=128)[:, 0:ilen],
                        in_=sttsb[:, 0:ilen]), inc="s_sp")
                    ov = b.cnt["s_sp"]
                    b.on("sync", lambda s, v=ov: s.wait_ge(b.sem["s_sp"], v))
                    return "DBG"
                b.waits("scalar", "gu_mm", "sil_free")
                b.on("scalar", lambda s, ilen=ilen: s.activation(
                    ffnsil[:, 0:ilen], psA[:, 0:ilen], AF.Silu), inc="s_act")
                b.set("sil", "scalar")
                b.set("psA_free", "scalar")
                b.waits("vector", "sil", "hbuf_free")
                b.on("vector", lambda v, ilen=ilen: v.scalar_tensor_tensor(
                    hbuf[:, 0:ilen], ffnsil[:, 0:ilen], 0.5, psB[:, 0:ilen],
                    ALU.mult, ALU.mult), inc="s_dve")
                b.set("hgot", "vector")
                b.set("psB_free", "vector")
                b.set("sil_free", "vector")
                hq = hq_tm if t % 2 == 0 else hq_tm2
                if stage == "dbgh" and t == 0:
                    b.wait("sync", "hgot")
                    b.on("sync", lambda s, ilen=ilen: s.dma_start(
                        out=y_out.rearrange("(i p) d -> p (i d)", p=128)[:, 0:ilen],
                        in_=hbuf[:, 0:ilen]), inc="s_sp")
                    ov = b.cnt["s_sp"]
                    b.on("sync", lambda s, v=ov: s.wait_ge(b.sem["s_sp"], v))
                    return "DBG"
                b.waits("vector", f"hqf{t % 2}")
                emit_quant(hbuf[:, 0:ilen], hq[:, 0:ilen], ilen, "hgot",
                           f"hq_q{t % 2}")
                b.set("hbuf_free", at=b.res["q_act"])

                def emit_tail(t=t, hq=hq, nseg=nseg, ilen=ilen):
                    emit_tp(hq, nseg,
                            lambda sc, nseg=nseg: sc.activation(
                                hq_fm[:, 0:nseg * 128], psT[:, 0:nseg * 128],
                                AF.Copy),
                            f"hq_q{t % 2}", "hq_ev")
                    b.set(f"hqf{t % 2}", at=b.res["tp_mm"])
                    b.waits("tensor", "hq_ev", "psC_free")
                    for nh in range(2):
                        for m in range(nseg):
                            b.on("tensor", lambda te, m=m, nh=nh, nseg=nseg:
                                 te.matmul(
                                     psC[:, nh * 512:(nh + 1) * 512],
                                     hq_fm[:, m * 128:(m + 1) * 128],
                                     wch_d[:, m * 1024 + nh * 512:
                                           m * 1024 + (nh + 1) * 512],
                                     start=(m == 0), stop=(m == nseg - 1)),
                                 inc="s_pe" if (nh == 1 and m == nseg - 1)
                                 else None)
                    b.set("dn_mm", "tensor")
                    b.wait("vector", "dn_mm")
                    b.on("vector", lambda v, t=t: v.tensor_tensor(
                        V(xres[:, t * D:(t + 1) * D]),
                        V(xres[:, t * D:(t + 1) * D]),
                        V(psC[:]), ALU.add), inc="s_dve")
                    b.set("x_acc", "vector")
                    b.set("psC_free", "vector")

                if pend[0] is not None:
                    pend[0]()
                pend[0] = emit_tail
            pend[0]()
            pend[0] = None
            b.set("wch_free", at=b.res["dn_mm"])
        b.set(f"{tag}_done", at=b.res["x_acc"])

    def dump_and_finish(src_ap, n_bf16_cols, deps):
        b.waits("gpsimd", *deps)
        ni = max(1, n_bf16_cols // D)
        b.on("gpsimd", lambda g: g.dma_start(
            out=y_out.rearrange("(i p) d -> p i d", p=128)[:, 0:ni, :],
            in_=src_ap[:, 0:n_bf16_cols].rearrange("p (i d) -> p i d", d=min(D, n_bf16_cols))), inc="s_g")
        ov = b.cnt["s_g"]
        b.on("gpsimd", lambda g, v=ov: g.wait_ge(b.sem["s_g"], v))
        nc.detect_race_conditions = False
        blk2 = es.enter_context(nc.Block())
        with es:
            b.replay(blk2)
        return nc

    def dump_dram_and_finish(dram_ap, deps):
        b.waits("gpsimd", *deps)
        b.on("gpsimd", lambda g: g.dma_start(
            out=y_out[0:1024, :], in_=dram_ap), inc="s_g")
        ov = b.cnt["s_g"]
        b.on("gpsimd", lambda g, v=ov: g.wait_ge(b.sem["s_g"], v))
        nc.detect_race_conditions = False
        blk2 = es.enter_context(nc.Block())
        with es:
            b.replay(blk2)
        return nc

    def finish_here():
        b.waits("sync", "wq_all_st", "x_acc", "xload", "x2_done", "f2_done")
        b.on("sync", lambda s: s.dma_start(
            out=y_out.rearrange("(i p) d -> p i d", p=128),
            in_=xres.rearrange("p (i d) -> p i d", d=D)), inc="s_sp")
        ov = b.cnt["s_sp"]
        b.on("sync", lambda s, v=ov: s.wait_ge(b.sem["s_sp"], v))
        nc.detect_race_conditions = False
        blk2 = es.enter_context(nc.Block())
        with es:
            b.replay(blk2)
        return nc

    b.wait("vector", "xload")
    b.wait("scalar", "xload")
    b.wait("vector", "consts")
    b.wait("scalar", "consts")
    if stage == "p0":
        return finish_here()
    if stage == "dbgln":
        emit_ln(("xload", "xload_g"))
        b.waits("sync", "ln_ok", "ln_rstd")
        for idx, tsr in enumerate((lnmean, lnsum2, lnrstd, lnnmr)):
            b.on("sync", lambda s, t=tsr, idx=idx: s.dma_start(
                out=y_out.rearrange("(i p) d -> p i d", p=128)[:, idx, 0:NT],
                in_=t[:]), inc="s_sp")
        for k in range(4):
            b.on("sync", lambda s, k=k: s.dma_start(
                out=y_out.rearrange("(i p) d -> p i d", p=128)[:, 4 + k, 0:NT],
                in_=xres[:, (12 + k) * 1024:(12 + k) * 1024 + NT]), inc="s_sp")
        ov = b.cnt["s_sp"]
        b.on("sync", lambda s, v=ov: s.wait_ge(b.sem["s_sp"], v))
        nc.detect_race_conditions = False
        blk2 = es.enter_context(nc.Block())
        with es:
            b.replay(blk2)
        return nc
    rr = emit_ffn("f1", ("xload", "xload_g"), wprep=True)
    b.set("wq_all_st", at=("s_sp", b.cnt["s_sp"]))
    if rr == "DBG":
        nc.detect_race_conditions = False
        blk2 = es.enter_context(nc.Block())
        with es:
            b.replay(blk2)
        return nc
    if rr == "DBGX":
        return dump_and_finish(xq_fm, 16384, ["xq_all"])
    if stage == "dbgw":
        return dump_dram_and_finish(wgT_s[0:1024, 0:1024], ["wq_all_st"])
    if stage == "dbgwd":
        return dump_dram_and_finish(wdT_s[0:1024, 0:1024], ["wq_all_st"])
    if stage == "f1":
        return finish_here()

    emit_ln("f1_done")
    for t in range(NT):
        b.waits("scalar", "lnb_free")
        emit_ln_apply(t, lnb_t[:], "l1_ap")
        b.set("lnout_free", at=b.res["l1_ap"])
        emit_tp(lnb_t, 8,
                lambda sc, t=t: sc.activation(
                    fm3(ln1_fm, t * 128, 128),
                    psT[:].rearrange("p (z m) -> p z m", m=128), AF.Copy),
                "l1_ap", "l1_ev")
        b.set("lnb_free", at=b.res["tp_mm"])
    b.set("ln1_all", at=b.res["l1_ev"])
    b.waits("sync", "f1_done")
    b.on("sync", lambda s: s.dma_start(
        out=xsp_s[:, 0:8 * D], in_=xres[:, 0:8 * D]), inc="s_sp")
    b.set("xspill", at=("s_sp", b.cnt["s_sp"]))
    b.waits("gpsimd", "f1_done")
    b.on("gpsimd", lambda g: g.dma_start(
        out=xsp_s[:, 8 * D:16 * D], in_=xres[:, 8 * D:16 * D]), inc="s_g")
    b.set("xspill_g", at=("s_g", b.cnt["s_g"]))
    if stage == "ln1":
        return finish_here()

    FIDX = {"r": 0, "k": 1, "v": 2, "g": 3, "w": 4}

    def emit_dx(c0):
        dx3 = dx_r.rearrange("p (j t) -> p j t", j=ND)
        ln3 = ln1_fm.rearrange("p (j t) -> p j t", j=ND)
        if c0 == 0:
            b.on("vector", lambda v: v.tensor_scalar(
                dx3[:, :, 0:1], ln3[:, :, 0:1], -1.0, None, ALU.mult))
            b.on("vector", lambda v: v.tensor_tensor(
                dx3[:, :, 1:512], ln3[:, :, 0:511], ln3[:, :, 1:512],
                ALU.subtract), inc="s_dve")
        else:
            b.on("vector", lambda v, c0=c0: v.tensor_tensor(
                dx3[:, :, 0:512], ln3[:, :, c0 - 1:c0 + 511],
                ln3[:, :, c0:c0 + 512], ALU.subtract), inc="s_dve")
        b.set("dx", "vector")

    def emit_mixpre(half, tc):
        c0 = half * 1024 + tc * 512
        b.waits("vector", "ln1_all", "dx_free", "xx_free", "su_mm")
        emit_dx(c0)
        xx3 = xx_r.rearrange("p (j t) -> p j t", j=ND)
        dx3 = dx_r.rearrange("p (j t) -> p j t", j=ND)
        ln3 = ln1_fm.rearrange("p (j t) -> p j t", j=ND)
        b.on("vector", lambda v: v.tensor_tensor(
            xx3, dx3, maa_x.unsqueeze(2).broadcast_to([128, ND, 512]), ALU.mult))
        b.on("vector", lambda v, c0=c0: v.tensor_tensor(
            xx3, xx3, ln3[:, :, c0:c0 + 512], ALU.add), inc="s_dve")
        b.set("xxx", "vector")
        b.set("dx_free", "vector")
        b.waits("tensor", "xxx", "consts", "psA_free", "psE_free")
        for j in range(ND):
            b.on("tensor", lambda te, j=j: te.matmul(
                psA[:], w1sb[:, j * 160:j * 160 + 128],
                xx_r[:, j * 512:(j + 1) * 512],
                start=(j == 0), stop=(j == ND - 1)))
        for j in range(ND):
            b.on("tensor", lambda te, j=j: te.matmul(
                psE[0:32, :], w1sb[:, j * 160 + 128:(j + 1) * 160],
                xx_r[:, j * 512:(j + 1) * 512],
                start=(j == 0), stop=(j == ND - 1)),
                inc="s_pe" if j == ND - 1 else None)
        b.set("mix_mm", "tensor")
        b.set("xx_free", "tensor")
        b.waits("scalar", "mix_mm", "mixsb_free")
        b.on("scalar", lambda s: s.activation(mixsb[:], psA[:], AF.Tanh))
        b.on("scalar", lambda s: s.activation(mix2sb[:], psE[0:32, :], AF.Tanh),
             inc="s_act")
        b.set("mix_ev", "scalar")
        b.set("psA_free", "scalar")
        b.set("psE_free", "scalar")
        b.wait("sync", "mix_ev")
        for fb in range(4):
            b.on("sync", lambda s, fb=fb, tc=tc: s.dma_start(
                out=mixf[:, (fb * 2 + tc) * 512:(fb * 2 + tc + 1) * 512],
                in_=mixsb[fb * 32:(fb + 1) * 32, :]), inc="s_ld")
        b.on("sync", lambda s, tc=tc: s.dma_start(
            out=mixf[:, (4 * 2 + tc) * 512:(4 * 2 + tc + 1) * 512],
            in_=mix2sb[:]), inc="s_ld")
        b.set("mix_dma", "sync")
        b.set("mixsb_free", at=b.res["mix_dma"])

    def emit_xf(f, half, tc):
        c0 = half * 1024 + tc * 512
        b.waits("vector", "xf_free", "dx_free", "su_mm")
        emit_dx(c0)
        fi = FIDX[f]
        for j in range(ND):
            pp = psD if j % 2 == 0 else psB
            pfree = "psD_free" if j % 2 == 0 else "psB_free"
            b.waits("tensor", pfree, "mix_dma", "consts",
                    "dx" if j == 0 else "noop")
            b.on("tensor", lambda te, pp=pp, j=j, fi=fi, tc=tc: te.matmul(
                pp[:], w2sb[:, fi * 1024 + j * 128:fi * 1024 + (j + 1) * 128],
                mixf[:, (fi * 2 + tc) * 512:(fi * 2 + tc + 1) * 512],
                start=True, stop=True), inc="s_pe")
            b.set("mf_mm", "tensor")
            b.waits("vector", "mf_mm", "sttsb_free")
            b.on("vector", lambda v, pp=pp, f=f, j=j: v.scalar_tensor_tensor(
                sttsb[:], pp[:], maa_of[f][:, j:j + 1],
                dx_r[:, j * 512:(j + 1) * 512], ALU.add, ALU.mult),
                inc="s_dve")
            b.set("xf_stt", "vector")
            b.wait("vector", "xf_stt")
            b.on("vector", lambda v, j=j, c0=c0: v.tensor_tensor(
                xf_r[:, j * 512:(j + 1) * 512], sttsb[:],
                ln1_fm[:, j * T + c0:j * T + c0 + 512], ALU.add), inc="s_dve")
            b.set(pfree, "vector")
            b.set("sttsb_free", "vector")
        b.set("xf", at=b.res["sttsb_free"])
        b.set("dx_free", at=b.res["sttsb_free"])

    def emit_proj(f, half, tc):
        c0 = half * 1024 + tc * 512
        if f == "w":
            b.waits("tensor", "xf", "psE_free")
            for j in range(ND):
                b.on("tensor", lambda te, j=j: te.matmul(
                    psE[0:64, :], dw1sb[:, j * 64:(j + 1) * 64],
                    xf_r[:, j * 512:(j + 1) * 512],
                    start=(j == 0), stop=(j == ND - 1)),
                    inc="s_pe" if j == ND - 1 else None)
            b.set("t1_mm", "tensor")
            b.set("xf_free", "tensor")
            b.wait("scalar", "t1_mm")
            b.on("scalar", lambda s, c0=c0: s.activation(
                tanh1[:, c0:c0 + 512], psE[0:64, :], AF.Tanh), inc="s_act")
            b.set("t1_ev", "scalar")
            b.set("psE_free", "scalar")
            return
        if f == "v":
            for st in range(4):
                b.waits("tensor", "xf" if st == 0 else "noop", "psC_free")
                for j in range(ND):
                    for nh in range(2):
                        b.on("tensor", lambda te, j=j, st=st, nh=nh: te.matmul(
                            psC[:, nh * 512:(nh + 1) * 512],
                            xf_r[:, j * 512 + st * 128: j * 512 + (st + 1) * 128],
                            attw[:, j * 1024 + nh * 512: j * 1024 + (nh + 1) * 512],
                            start=(j == 0), stop=(j == ND - 1)),
                            inc="s_pe" if (j == ND - 1 and nh == 1) else None)
                b.set("v_mm", "tensor")
                if st == 3:
                    b.set("xf_free", "tensor")
                b.wait("scalar", "v_mm")
                b.on("scalar", lambda s, st=st, tc=tc: s.activation(
                    v_tm[:, (tc * 4 + st) * 1024:(tc * 4 + st + 1) * 1024],
                    psC[:], AF.Copy), inc="s_act")
                b.set("v_ev", "scalar")
                b.set("psC_free", "scalar")
            return
        for jj in range(ND):
            pp = psA if jj % 2 == 0 else psD
            pfree = "psA_free" if jj % 2 == 0 else "psD_free"
            b.waits("tensor", pfree, "xf" if jj == 0 else "noop")
            for j in range(ND):
                b.on("tensor", lambda te, pp=pp, j=j, jj=jj: te.matmul(
                    pp[:], attw[:, j * 1024 + jj * 128: j * 1024 + (jj + 1) * 128],
                    xf_r[:, j * 512:(j + 1) * 512],
                    start=(j == 0), stop=(j == ND - 1)),
                    inc="s_pe" if j == ND - 1 else None)
            b.set("p_mm", "tensor")
            if jj == ND - 1:
                b.set("xf_free", "tensor")
            b.wait("scalar", "p_mm")
            if f == "g":
                b.waits("scalar", "g_dma")
                b.on("scalar", lambda s, pp=pp, jj=jj: s.activation(
                    gstage[:], pp[:], AF.Silu, bias=bgv[:, jj:jj + 1]),
                    inc="s_act")
                b.set("g_ev", "scalar")
                b.set(pfree, "scalar")
                b.wait("sync", "g_ev")
                b.on("sync", lambda s, jj=jj, c0=c0: s.dma_start(
                    out=gsp_s[:, jj * T + c0: jj * T + c0 + 512],
                    in_=gstage[:]), inc="s_ld")
                b.set("g_dma", "sync")
            else:
                dst = q_fm if f == "r" else k_fm
                b.on("scalar", lambda s, pp=pp, jj=jj, c0=c0, dst=dst: s.activation(
                    dst[:, jj * T + c0: jj * T + c0 + 512], pp[:], AF.Copy),
                    inc="s_act")
                b.set("p_ev", "scalar")
                b.set(pfree, "scalar")

    def emit_S_half(half):
        b.waits("scalar", "xspill", "xspill_g")
        for tc in range(2):
            emit_mixpre(half, tc)
        for f in ("w", "r", "k", "v", "g"):
            if f != "w":
                wname = {"r": "Wq", "k": "Wk", "v": "Wv", "g": "Wg"}[f]
                b.waits("gpsimd", "attw_free", "q_scratch", "q_act",
                        "su_mm", "o_mm")
                b.on("gpsimd", lambda g, wname=wname: g.dma_start(
                    out=attw[:].rearrange("p (j n) -> p j n", j=ND),
                    in_=W_d[wname].rearrange("(j p) n -> p j n", p=128)),
                    inc="s_g")
                b.set("w_att", "gpsimd")
            for tc in range(2):
                emit_xf(f, half, tc)
                if f != "w":
                    b.wait("tensor", "w_att")
                emit_proj(f, half, tc)
            if f != "w":
                b.set("attw_free", at=b.res["xf_free"])

    import os
    E_NGRP = int(os.environ.get("KE_NGRP", "4"))
    E_NCH = int(os.environ.get("KE_NCH", "4"))
    E_PARTS = os.environ.get("KE_PARTS", "sakouz123")

    def emit_E_half(half):
        for grp in range(E_NGRP):
            g0 = half * 1024 + grp * 256
            lg = grp * 256
            for j in range(ND):
                pp = psE if j % 2 == 0 else psD
                pfree = "psE_free" if j % 2 == 0 else "psD_free"
                b.waits("tensor", pfree, "t1_ev", "dec_free", "consts")
                b.on("tensor", lambda te, j=j, g0=g0, pp=pp: te.matmul(
                    pp[:, 0:256], dw2sb[:, j * 128:(j + 1) * 128],
                    tanh1[:, g0:g0 + 256], start=True, stop=True), inc="s_pe")
                b.set("dec_mm", "tensor")
                b.wait("vector", "dec_mm")
                b.on("vector", lambda v, j=j, pp=pp: v.tensor_scalar(
                    dec_g[:, j * 256:(j + 1) * 256], pp[:, 0:256],
                    tdec[:, j:j + 1], None, ALU.add), inc="s_dve")
                b.set(pfree, "vector")
                b.set("dec_j", "vector")
            b.set("dec_done", at=b.res["dec_j"])
            b.waits("scalar", "dec_done", "egd_free")
            b.on("scalar", lambda s: s.activation(egd_g[:], dec_g[:], AF.Exp),
                 inc="s_act")
            b.set("egd", "scalar")
            b.wait("scalar", "egd")
            b.on("scalar", lambda s: s.activation(
                dec_g[:], egd_g[:], AF.Exp, scale=-1.0), inc="s_act")
            b.set("emg", "scalar")
            b.set("dec_free", at=("s_act", b.cnt["s_act"]))
            b.wait("vector", "emg")
            b.on("vector", lambda v: v.tensor_scalar(
                V(dec_g[:]), V(dec_g[:]), -1.0, 1.0, ALU.mult, ALU.add))
            kf3 = fm3(k_fm, g0, 256)
            dg3 = dec_g[:].rearrange("p (j t) -> p j t", j=ND)
            b.on("vector", lambda v, kf3=kf3, dg3=dg3: v.tensor_tensor(
                kf3, kf3, dg3, ALU.mult), inc="s_dve")
            b.set("kp", "vector")
            if os.environ.get("KE_DUMMYDMA"):
                b.waits("sync", "g_dma")
                b.on("sync", lambda s, g0=g0: s.dma_start(
                    out=gat_r.rearrange("p (j t) -> p j t", j=ND),
                    in_=gsp_s.rearrange("p (j t) -> p j t", j=ND)[:, :, g0:g0 + 256]),
                    inc="s_ld")
                b.set("g_rl", "sync")
            for cc in range(E_NCH):
                c0 = g0 + cc * 64
                l0 = cc * 64
                par = (c0 // 64) % 2
                tloc = (c0 - half * 1024) // 128
                if os.environ.get("KE_ESER"):
                    b.waits("tensor", "og_ev", "S_ok", "o_sb")
                b.waits("vector", "egd", "cb_free", "expq")
                for j in range(ND):
                    b.on("vector", lambda v, j=j, l0=l0: v.tensor_tensor_scan(
                        cbsb[:, j * 64:(j + 1) * 64],
                        onessb[:].broadcast_to([128, 64]),
                        egd_g[:, j * 256 + l0:j * 256 + l0 + 64],
                        0.0, ALU.mult, ALU.add),
                        inc="s_dve" if j == ND - 1 else None)
                b.set("cb", "vector")
                b.wait("vector", "cb")
                cb3 = cbsb[:].rearrange("p (j t) -> p j t", j=ND)
                cbL = cb3[:, :, 63:64].squeeze(2)
                b.wait("vector", "sm_exp")
                b.on("vector", lambda v, cbL=cbL: v.tensor_scalar(
                    smalls[:, 0:8], cbL, 0.5, None, ALU.mult), inc="s_dve")
                b.set("chalf", "vector")
                b.wait("scalar", "chalf")
                b.on("scalar", lambda s: s.activation(
                    smalls[:, 8:16], smalls[:, 0:8], AF.Exp))
                b.on("scalar", lambda s: s.activation(
                    smalls[:, 16:24], smalls[:, 0:8], AF.Exp, scale=-1.0))
                b.on("scalar", lambda s: s.activation(
                    smalls[:, 24:32], smalls[:, 0:8], AF.Exp, scale=-2.0),
                    inc="s_act")
                b.set("sm_exp", "scalar")
                b.waits("scalar", "expq_free")
                b.on("scalar", lambda s: s.activation(
                    expsc[:], cbsb[:], AF.Exp, scale=-1.0, bias=-LN8),
                    inc="s_act")
                b.set("expq", "scalar")
                b.waits("vector", "chalf", "expk_free")
                b.on("vector", lambda v: v.tensor_tensor(
                    expsc2[:].rearrange("p (j t) -> p j t", j=ND), cb3,
                    smalls[:, 0:8].unsqueeze(2).broadcast_to([128, ND, 64]),
                    ALU.subtract), inc="s_dve")
                b.set("argk", "vector")
                b.set("cb_free", "vector")
                b.wait("scalar", "argk")
                b.on("scalar", lambda s: s.activation(
                    expsc2[:], expsc2[:], AF.Exp), inc="s_act")
                b.set("expk", "scalar")
                q3 = fm3(q_fm, c0, 64)
                k3 = fm3(k_fm, c0, 64)
                e1 = expsc[:].rearrange("p (j t) -> p j t", j=ND)
                e2 = expsc2[:].rearrange("p (j t) -> p j t", j=ND)
                qs3 = qsc[:].rearrange("p (j t) -> p j t", j=ND)
                qss3 = qss[:].rearrange("p (j t) -> p j t", j=ND)
                ks3 = ksc[:].rearrange("p (j t) -> p j t", j=ND)
                kh3 = khc[:].rearrange("p (j t) -> p j t", j=ND)
                b.waits("vector", "expq", "expk", "sm_exp", "kp",
                        "qk_free", "p_mm", "v_mm", "t1_mm")
                b.on("vector", lambda v, qs3=qs3, q3=q3, e1=e1: v.tensor_tensor(
                    qs3, q3, e1, ALU.mult))
                b.on("vector", lambda v, ks3=ks3, k3=k3, e2=e2: v.tensor_tensor(
                    ks3, k3, e2, ALU.mult))
                b.on("vector", lambda v, qss3=qss3, qs3=qs3: v.tensor_tensor(
                    qss3, qs3, smalls[:, 8:16].unsqueeze(2).broadcast_to(
                        [128, ND, 64]), ALU.mult))
                b.on("vector", lambda v, kh3=kh3, ks3=ks3: v.tensor_tensor(
                    kh3, ks3, smalls[:, 16:24].unsqueeze(2).broadcast_to(
                        [128, ND, 64]), ALU.mult), inc="s_dve")
                b.set("qk", "vector")
                b.set("expq_free", "vector")
                b.set("expk_free", "vector")
                if "a" not in E_PARTS:
                    continue
                b.waits("tensor", "qk", "psA_free", "psD_free")
                for hp in range(ND):
                    b.on("tensor", lambda te, hp=hp, par=par: te.matmul(
                        psA[par * 64:par * 64 + 64, hp * 64:(hp + 1) * 64],
                        ksc[0:64, hp * 64:(hp + 1) * 64],
                        qss[0:64, hp * 64:(hp + 1) * 64],
                        start=True, stop=True, tile_position=(0, par * 64)))
                    b.on("tensor", lambda te, hp=hp, par=par: te.matmul(
                        psD[par * 64:par * 64 + 64, hp * 64:(hp + 1) * 64],
                        ksc[64:128, hp * 64:(hp + 1) * 64],
                        qss[64:128, hp * 64:(hp + 1) * 64],
                        start=True, stop=True, tile_position=(64, par * 64)),
                        inc="s_pe" if hp == ND - 1 else None)
                b.set("a_mm", "tensor")
                b.waits("vector", "a_mm", "amsk_free")
                mrow = maskt[64 * par:64 * par + 64, :].unsqueeze(1)\
                    .broadcast_to([64, ND, 64])
                b.on("vector", lambda v, par=par, mrow=mrow: v.tensor_tensor(
                    amskA[64 * par:64 * par + 64, :].rearrange(
                        "p (j t) -> p j t", j=ND),
                    psA[64 * par:64 * par + 64, :].rearrange(
                        "p (j t) -> p j t", j=ND), mrow, ALU.mult))
                b.on("vector", lambda v, par=par, mrow=mrow: v.tensor_tensor(
                    amskB[64 * par:64 * par + 64, :].rearrange(
                        "p (j t) -> p j t", j=ND),
                    psD[64 * par:64 * par + 64, :].rearrange(
                        "p (j t) -> p j t", j=ND), mrow, ALU.mult),
                    inc="s_dve")
                b.set("am_ev", "vector")
                b.set("psA_free", "vector")
                b.set("psD_free", "vector")
                if "k" not in E_PARTS:
                    continue
                b.waits("tensor", "qk", "psT_free")
                for hp in range(ND):
                    b.on("tensor", lambda te, hp=hp, par=par: te.matmul(
                        psT[par * 64:par * 64 + 64, hp * 128:(hp + 1) * 128],
                        khc[:, hp * 64:(hp + 1) * 64], ident[:],
                        start=True, stop=True, is_transpose=True,
                        tile_position=(0, par * 64)),
                        inc="s_pe" if hp == ND - 1 else None)
                b.set("kt_mm", "tensor")
                b.waits("vector", "kt_mm", "ktm_free")
                b.on("vector", lambda v, par=par: v.tensor_copy(
                    ktm[par * 64:par * 64 + 64, :],
                    psT[par * 64:par * 64 + 64, :]), inc="s_dve")
                b.set("kt_ev", "vector")
                b.set("psT_free", "vector")
                if "o" not in E_PARTS:
                    continue
                b.waits("tensor", "am_ev", "psB_free", "S_ok", "v_ev",
                        "psC_free")
                for hp in range(ND):
                    b.on("tensor", lambda te, hp=hp: te.matmul(
                        psB[0:64, hp * 64:(hp + 1) * 64],
                        Ssb[0:64, hp * 64:(hp + 1) * 64],
                        qsc[0:64, hp * 64:(hp + 1) * 64],
                        start=True, stop=True, tile_position=(0, 0)))
                    b.on("tensor", lambda te, hp=hp: te.matmul(
                        psB[64:128, hp * 64:(hp + 1) * 64],
                        Ssb[64:128, hp * 64:(hp + 1) * 64],
                        qsc[64:128, hp * 64:(hp + 1) * 64],
                        start=True, stop=True, tile_position=(64, 64)))
                    b.on("tensor", lambda te, hp=hp, par=par, tloc=tloc: te.matmul(
                        psC[0:64, hp * 64:(hp + 1) * 64],
                        v_tm[par * 64:par * 64 + 64,
                             tloc * 1024 + hp * 128: tloc * 1024 + hp * 128 + 64],
                        amskA[par * 64:par * 64 + 64, hp * 64:(hp + 1) * 64],
                        start=True, stop=True, tile_position=(par * 64, 0)))
                    b.on("tensor", lambda te, hp=hp, par=par, tloc=tloc: te.matmul(
                        psC[64:128, hp * 64:(hp + 1) * 64],
                        v_tm[par * 64:par * 64 + 64,
                             tloc * 1024 + hp * 128 + 64:
                             tloc * 1024 + hp * 128 + 128],
                        amskB[par * 64:par * 64 + 64, hp * 64:(hp + 1) * 64],
                        start=True, stop=True, tile_position=(par * 64, 64)),
                        inc="s_pe" if hp == ND - 1 else None)
                b.set("o_mm", "tensor")
                og_fm = og1_fm if half == 0 else og2_fm
                cloc = (c0 - half * 1024) // 64
                if "u" not in E_PARTS:
                    continue
                b.waits("tensor", "kt_ev", "psE_free", "S_scaled_read",
                        "v_ev")
                for hp in range(ND):
                    b.on("tensor", lambda te, hp=hp, par=par, tloc=tloc: te.matmul(
                        psE[0:64, hp * 64:(hp + 1) * 64],
                        ktm[par * 64:par * 64 + 64, hp * 128:hp * 128 + 64],
                        v_tm[par * 64:par * 64 + 64,
                             tloc * 1024 + hp * 128: tloc * 1024 + hp * 128 + 64],
                        start=True, stop=True, tile_position=(par * 64, 0)))
                    b.on("tensor", lambda te, hp=hp, par=par, tloc=tloc: te.matmul(
                        psE[64:128, hp * 64:(hp + 1) * 64],
                        ktm[par * 64:par * 64 + 64, hp * 128 + 64:hp * 128 + 128],
                        v_tm[par * 64:par * 64 + 64,
                             tloc * 1024 + hp * 128 + 64:
                             tloc * 1024 + hp * 128 + 128],
                        start=True, stop=True, tile_position=(par * 64, 64)),
                        inc="s_pe" if hp == ND - 1 else None)
                b.set("su_mm", "tensor")
                b.set("ktm_free", "tensor")
                b.set("qk_free", "tensor")
                if "z" not in E_PARTS:
                    continue
                if "1" not in E_PARTS:
                    continue
                b.waits("vector", "o_mm", "sm_exp")
                b.on("vector", lambda v: v.tensor_tensor(
                    Stmp[:].rearrange("p (j t) -> p j t", j=ND),
                    Ssb[:].rearrange("p (j t) -> p j t", j=ND),
                    smalls[:, 24:32].unsqueeze(2).broadcast_to([128, ND, 64]),
                    ALU.mult), inc="s_dve")
                b.set("S_scaled_read", "vector")
                if "2" not in E_PARTS:
                    continue
                b.waits("vector", "o_mm")
                b.on("vector", lambda v: v.tensor_copy(sttsb[:], psB[:]),
                     inc="s_dve")
                b.set("o_sb", "vector")
                b.set("psB_free", "vector")
                if "3" not in E_PARTS:
                    continue
                if os.environ.get("KE_OGMEMSET"):
                    b.on("vector", lambda v, og_fm=og_fm, cloc=cloc: v.memset(
                        og_fm[:, cloc * 512:(cloc + 1) * 512],
                        float(100 + cloc)), inc="s_dve")
                else:
                    b.on("vector", lambda v, og_fm=og_fm, cloc=cloc: v.tensor_tensor(
                        V(og_fm[:, cloc * 512:(cloc + 1) * 512]), V(sttsb[:]),
                        V(psC[:, 0:512]), ALU.add), inc="s_dve")
                b.set("og_ev", "vector")
                b.set("psC_free", "vector")
                b.waits("vector", "su_mm")
                b.on("vector", lambda v: v.tensor_tensor(
                    V(Ssb[:]), V(Stmp[:]), V(psE[:]), ALU.add), inc="s_dve")
                b.set("S_ok", "vector")
                b.set("psE_free", "vector")
                b.set("amsk_free", at=b.res["o_mm"])
            b.set("egd_free", at=b.res["cb"])
        b.set(f"og_half{half}", at=b.res["og_ev"])

    b.res.setdefault("noop", ("s_dve", 0))

    for half in range(2):
        emit_S_half(half)
        if stage == f"s{half}":
            return finish_here()
        emit_E_half(half)
        if stage == f"e{half}":
            if os.environ.get("KE_DUMPOG"):
                return dump_and_finish(og1_fm if half == 0 else og2_fm, 8192,
                                       ["og_ev"])
            return finish_here()

    b.waits("gpsimd", "attw_free", "q_scratch", "q_act")
    b.on("gpsimd", lambda g: g.dma_start(
        out=attw[:].rearrange("p (j n) -> p j n", j=ND)[:, 2:8, :],
        in_=W_d["Wo"].rearrange("(j p) n -> p j n", p=128)[:, 2:8, :]),
        inc="s_g")
    b.waits("gpsimd", "su_mm", "o_mm")
    b.on("gpsimd", lambda g: g.dma_start(
        out=attw[:].rearrange("p (j n) -> p j n", j=ND)[:, 0:2, :],
        in_=W_d["Wo"].rearrange("(j p) n -> p j n", p=128)[:, 0:2, :]),
        inc="s_g")
    b.set("wo_ld", "gpsimd")
    gatp5s = (AR4b[:, 0:1024], AR4b[:, 6144:7168])
    xrlds = (xrld, AR4[:, 2048:3072])
    _p5ret = [None]
    for t in range(NT):
        og_fm = og1_fm if t < 8 else og2_fm
        tl = t % 8
        pp5 = t % 2
        gatp5 = gatp5s[pp5]
        xrl = xrlds[pp5]
        b.waits("sync", "xspill", "xspill_g", f"xrld_free{pp5}", "qk",
                f"gatp5_free{pp5}")
        b.on("sync", lambda s, t=t, xrl=xrl: s.dma_start(
            out=xrl[:], in_=xsp_s[:, t * D:(t + 1) * D]), inc="s_ld")
        for c_ in range(2):
            b.on("sync", lambda s, t=t, c_=c_, gatp5=gatp5: s.dma_start(
                out=gatp5[:, c_ * 512:(c_ + 1) * 512].rearrange(
                    "p (j z) -> p j z", j=ND),
                in_=gsp_s.rearrange("p (j tt) -> p j tt", j=ND)[
                    :, :, t * 128 + c_ * 64: t * 128 + c_ * 64 + 64]),
                inc="s_ld")
        b.set("x_rl", "sync")
        b.waits("vector", "x_rl", f"og_half{0 if t < 8 else 1}", "wo_mm")
        b.on("vector", lambda v, og_fm=og_fm, tl=tl, gatp5=gatp5: v.tensor_tensor(
            V(og_fm[:, tl * 1024:(tl + 1) * 1024]),
            V(og_fm[:, tl * 1024:(tl + 1) * 1024]), V(gatp5[:]), ALU.mult),
            inc="s_dve")
        b.set("og_gated", "vector")
        b.set(f"gatp5_free{pp5}", "vector")
        if stage == "dbgp5" and t == int(os.environ.get("KE_P5T", "1")):
            return dump_and_finish(og_fm[:, tl * 1024:tl * 1024 + 1024], 1024,
                                   ["og_gated"])
        b.waits("tensor", "wo_ld", "psC_free", "og_gated")
        for c_ in range(2):
            for nh in range(2):
                for j in range(ND):
                    b.on("tensor", lambda te, j=j, nh=nh, tl=tl, c_=c_,
                         og_fm=og_fm: te.matmul(
                        psC[c_ * 64:(c_ + 1) * 64, nh * 512:(nh + 1) * 512],
                        og_fm[:, tl * 1024 + c_ * 512 + j * 64:
                              tl * 1024 + c_ * 512 + (j + 1) * 64],
                        attw[:, j * 1024 + nh * 512: j * 1024 + (nh + 1) * 512],
                        start=(j == 0), stop=(j == ND - 1),
                        tile_position=(0, c_ * 64)),
                        inc="s_pe" if (c_ == 1 and j == ND - 1 and nh == 1) else None)
        b.set("wo_mm", "tensor")
        b.waits("vector", "wo_mm", "x_rl")
        b.on("vector", lambda v, t=t, xrl=xrl: v.tensor_tensor(
            V(xres[:, t * D:(t + 1) * D]), V(xrl[:]), V(psC[:]), ALU.add),
            inc="s_dve")
        b.set("x2_acc", "vector")
        b.set("psC_free", "vector")
        b.set(f"xrld_free{pp5}", "vector")
    b.set("x2_done", at=b.res["x2_acc"])
    if stage == "p5":
        return finish_here()
    if stage == "dbgp5":
        raise RuntimeError("dbgp5 tile index out of range")

    for eng in ENGS:
        if eng == "sync":
            b.waits(eng, "su_mm", "o_mm")
        else:
            b.waits(eng, "x2_done", "wo_mm")
    emit_ffn("f2", "x2_done")
    b.waits("sync", "f2_done")
    b.on("sync", lambda s: s.dma_start(
        out=y_out.rearrange("(i p) d -> p i d", p=128),
        in_=xres.rearrange("p (i d) -> p i d", d=D)), inc="s_sp")
    out_val = b.cnt["s_sp"]
    b.on("sync", lambda s, v=out_val: s.wait_ge(b.sem["s_sp"], v))

    nc.detect_race_conditions = False
    blk = es.enter_context(nc.Block())
    with es:
        b.replay(blk)
    return nc


_NC_CACHE = [None]


def kernel(L0_attn_hidden_states, L0_attn_time_maa_x, L0_attn_time_maa_r,
           L0_attn_time_maa_k, L0_attn_time_maa_v, L0_attn_time_maa_w,
           L0_attn_time_maa_g, L0_attn_time_maa_w1, L0_attn_time_maa_w2,
           L0_attn_time_decay, L0_attn_time_decay_w1, L0_attn_time_decay_w2,
           L0_attn_Wq, L0_attn_Wk, L0_attn_Wv, L0_attn_Wo, L0_attn_Wg,
           L0_attn_bg, L0_ffn_w_gate, L0_ffn_w_up, L0_ffn_w_down):
    x = np.ascontiguousarray(np.asarray(L0_attn_hidden_states, dtype=np.float32))
    assert x.shape == (8, T, D)
    vecs0 = np.stack([
        np.asarray(L0_attn_time_maa_x, np.float32).reshape(D),
        np.asarray(L0_attn_time_maa_r, np.float32).reshape(D),
        np.asarray(L0_attn_time_maa_k, np.float32).reshape(D),
        np.asarray(L0_attn_time_maa_v, np.float32).reshape(D),
        np.asarray(L0_attn_time_maa_w, np.float32).reshape(D),
        np.asarray(L0_attn_time_maa_g, np.float32).reshape(D),
        np.asarray(L0_attn_time_decay, np.float32).reshape(D),
        np.asarray(L0_attn_bg, np.float32).reshape(D),
    ], axis=0)
    vecs = np.ascontiguousarray(
        vecs0.reshape(8, 8, 128).transpose(2, 0, 1).reshape(128, 64))
    ident = np.eye(128, dtype=np.float32)
    mask = np.tile(np.tril(np.ones((CH, CH), np.float32)).T * 0 +
                   (np.arange(CH)[:, None] <= np.arange(CH)[None, :]).astype(np.float32),
                   (2, 1))
    common = {
        "w_gate": np.ascontiguousarray(np.asarray(L0_ffn_w_gate, np.float32)),
        "w_up": np.ascontiguousarray(np.asarray(L0_ffn_w_up, np.float32)),
        "w_down": np.ascontiguousarray(np.asarray(L0_ffn_w_down, np.float32)),
        "Wq": np.ascontiguousarray(np.asarray(L0_attn_Wq, np.float32)),
        "Wk": np.ascontiguousarray(np.asarray(L0_attn_Wk, np.float32)),
        "Wv": np.ascontiguousarray(np.asarray(L0_attn_Wv, np.float32)),
        "Wo": np.ascontiguousarray(np.asarray(L0_attn_Wo, np.float32)),
        "Wg": np.ascontiguousarray(np.asarray(L0_attn_Wg, np.float32)),
        "maa_w1": np.ascontiguousarray(np.asarray(L0_attn_time_maa_w1, np.float32)),
        "maa_w2": np.ascontiguousarray(np.asarray(L0_attn_time_maa_w2, np.float32)),
        "dec_w1": np.ascontiguousarray(np.asarray(L0_attn_time_decay_w1, np.float32)),
        "dec_w2": np.ascontiguousarray(np.asarray(L0_attn_time_decay_w2, np.float32)),
        "vecs": vecs,
        "ident": ident,
        "mask": np.ascontiguousarray(mask),
    }
    in_maps = [dict(common, x_in=np.ascontiguousarray(x[i])) for i in range(8)]
    nc = build()
    res = run_bass_kernel_spmd(nc, in_maps, core_ids=list(range(8)))
    out = np.stack([np.asarray(r["y_out"]) for r in res.results], axis=0)
    return out.astype(np.float32)


if __name__ == "__main__":
    import sys
    if "--build" in sys.argv:
        nc = build()
        total = sum(len(v) for v in nc.m.funcs[0].blocks[0].instructions) if False else None
        print("build ok")
